# version 07
# baseline (speedup 1.0000x reference)
import math
import numpy as np
import concourse.bass as bass
import concourse.mybir as mybir
from concourse.bass_utils import run_bass_kernel_spmd

F32 = mybir.dt.float32
BF16 = mybir.dt.bfloat16
I32 = mybir.dt.int32
AF = mybir.ActivationFunctionType
ALU = mybir.AluOpType
AX = mybir.AxisListType

NCORES = 8
B, S, D = 2, 4096, 2048
TOK = 1024
NCH = D // 128
DFF = 8192
NFF = DFF // 128
HD = 128
LN_EPS = 1e-5
RSQ_HD = 1.0 / math.sqrt(float(HD))
FWHT_NORM = 1.0 / math.sqrt(float(DFF))



class Sem:
    def __init__(self, handle, name):
        self.h = handle
        self.name = name
        self.val = 0


class Ctx:
    ENGINES = ("sync", "tensor", "vector", "scalar", "gpsimd")

    def __init__(self):
        self.nc = bass.Bass("TRN2", target_bir_lowering=False)
        self._cm = []
        self.sems = {}
        self.ops = []
        self.dram = {}
        self._uid = 0
        self._last = None
        self.sq = None

    def init_serial(self):
        self.sq = self.sem("sq")

    def uid(self):
        self._uid += 1
        return self._uid

    def dram_in(self, name, shape, dt=F32):
        t = self.nc.dram_tensor(name, list(shape), dt, kind="ExternalInput").ap()
        self.dram[name] = t
        return t

    def dram_out(self, name, shape, dt=F32):
        t = self.nc.dram_tensor(name, list(shape), dt, kind="ExternalOutput").ap()
        self.dram[name] = t
        return t

    def dram_tmp(self, name, shape, dt=F32):
        t = self.nc.dram_tensor(name, list(shape), dt, kind="Internal").ap()
        self.dram[name] = t
        return t

    def sbuf(self, name, shape, dt=F32):
        cm = self.nc.sbuf_tensor(name, list(shape), dt)
        h = cm.__enter__()
        self._cm.append(cm)
        self._sbuf_cms = getattr(self, "_sbuf_cms", {})
        self._sbuf_cms[name] = cm
        return h

    def sbuf_free(self, *names):
        for name in names:
            cm = self._sbuf_cms.pop(name)
            cm.__exit__(None, None, None)
            self._cm.remove(cm)

    def psum(self, name, shape, dt=F32):
        cm = self.nc.psum_tensor(name, list(shape), dt)
        h = cm.__enter__()
        self._cm.append(cm)
        return h

    def sem(self, name):
        if name not in self.sems:
            cm = self.nc.semaphore(name)
            h = cm.__enter__()
            self._cm.append(cm)
            self.sems[name] = Sem(h, name)
        return self.sems[name]

    FULL_SERIAL = True

    def s(self, eng, fn, dma=False):
        rec = [eng, fn, [], [], dma]
        if self._last is not None:
            lrec = self._last
            if lrec[0] == eng and not lrec[4] and not dma and not self.FULL_SERIAL:
                pass
            else:
                self._ensure_inc(lrec)
                rec[2].append((self.sq, self.sq.val))
        elif self.ops:
            rec[2].append((self.sq, self.sq.val))
        if dma:
            self.sq.val += 16
            rec[3].append((self.sq, 16))
        self.ops.append(rec)
        self._last = rec
        return rec

    def _ensure_inc(self, rec):
        if not rec[3]:
            self.sq.val += 1
            rec[3].append((self.sq, 1))

    def fence(self):
        if self._last is not None:
            self._ensure_inc(self._last)
            self._last = None

    def emit(self):
        if self._last is not None:
            self._ensure_inc(self._last)
        ops = self.ops
        with self.nc.Block() as blk:
            def runner(name):
                def run(eng):
                    for eng_name, fn, waits, incs, dma in ops:
                        if eng_name != name:
                            continue
                        for s, v in waits:
                            eng.wait_ge(s.h, v)
                        if fn is None:
                            for s, k in incs:
                                eng.sem_inc(s.h, k)
                        else:
                            instr = fn(eng)
                            for s, k in incs:
                                instr.then_inc(s.h, k)
                    eng.wait_ge(self.sq.h, self.sq.val)
                return run
            blk.sync(runner("sync"))
            blk.tensor(runner("tensor"))
            blk.vector(runner("vector"))
            blk.scalar(runner("scalar"))
            blk.gpsimd(runner("gpsimd"))

    def close(self):
        for cm in reversed(self._cm):
            cm.__exit__(None, None, None)
        self._cm = []


def ceil_div(a, b):
    return -(-a // b)


def halves(n, w=512):
    return [(i, min(w, n - i)) for i in range(0, n, w)]



class Consts:

    def __init__(self, ctx):
        self.c32 = ctx.sbuf("consts32", [128, 400], F32)
        self.ident_f32 = self.c32[:, 0:128]
        self.ones_col_bf = ctx.sbuf("ones_col_bf", [128, 1], BF16)
        self.ident_bf = ctx.sbuf("ident_bf", [128, 128], BF16)
        self.h128_bf = ctx.sbuf("h128_bf", [128, 128], BF16)
        self.ones_row_f32 = self.c32[0:1, 256:384]
        self.eps_col = self.c32[:, 385:386]
        self.eps_1 = self.c32[0:1, 384:385]
        import math as _m
        ln2 = _m.log(2.0)
        for col, val in ((386, 3.0 * ln2), (387, -3.0 * ln2), (388, 0.0), (389, 1.0)):
            ctx.nc.const_aps.register(F32, float(np.float32(val)), self.c32[:, col:col + 1])

    def load(self, ctx, dram_c):
        ctx.s("sync", lambda e: e.dma_start(out=self.c32[:], in_=dram_c), dma=True)
        ctx.s("vector", lambda e: e.tensor_copy(self.ident_bf[:], self.c32[:, 0:128]))
        ctx.s("vector", lambda e: e.tensor_copy(self.h128_bf[:], self.c32[:, 128:256]))
        ctx.s("vector", lambda e: e.tensor_copy(self.ones_col_bf[:], self.c32[:, 256:257]))
        ctx.fence()

    @staticmethod
    def host_array():
        c = np.zeros((128, 400), dtype=np.float32)
        c[:, 0:128] = np.eye(128, dtype=np.float32)
        h = np.array([[1.0]], dtype=np.float32)
        while h.shape[0] < 128:
            h = np.block([[h, h], [h, -h]])
        c[:, 128:256] = h
        c[:, 256] = 1.0
        c[0, 256:384] = 1.0
        c[0, 384] = LN_EPS
        c[:, 385] = LN_EPS
        import math as _m
        ln2 = _m.log(2.0)
        c[:, 386] = np.float32(3.0 * ln2)
        c[:, 387] = np.float32(-3.0 * ln2)
        c[:, 388] = 0.0
        c[:, 389] = 1.0
        return c


class Scratch:
    def __init__(self, ctx, tok=TOK):
        t = tok
        self.x32 = ctx.sbuf("sc_x32", [128, t], F32)
        self.y32 = ctx.sbuf("sc_y32", [128, t], F32)
        self.z32 = ctx.sbuf("sc_z32", [128, t], F32)
        self.bfA = ctx.sbuf("sc_bfA", [128, 2 * t], BF16)
        self.bfB = ctx.sbuf("sc_bfB", [128, 2 * t], BF16)
        self.i32 = ctx.sbuf("sc_i32", [128, t], I32)
        self.rows = ctx.sbuf("sc_rows", [1, 8 * t], F32)
        self.bc = ctx.sbuf("sc_bc", [128, 2 * t], F32)


def bcast_row(ctx, k, row_ap, out_ap, ps, n):
    for off, w in halves(n):
        ctx.s("tensor", lambda e, off=off, w=w: e.matmul(
            ps[:, 0:w], k.ones_row_f32, row_ap[0:1, off:off + w],
            start=True, stop=True))
        ctx.s("vector", lambda e, off=off, w=w: e.tensor_copy(
            out_ap[:, off:off + w], ps[:, 0:w]))



def emit_ln(ctx, k, sc, pair_hi, pair_lo, out_hi, out_lo, ps_a, ps_b,
            gamma=None, beta=None, nch=NCH, tok=TOK, out32_cb=None):
    t = tok
    ch = lambda sb, c: sb[:, c * t:(c + 1) * t]
    for c in range(nch):
        hi, lo = ch(pair_hi, c), ch(pair_lo, c)
        ctx.s("vector", lambda e, hi=hi, lo=lo: e.tensor_tensor(sc.x32[:], hi, lo, ALU.add))
        ctx.s("scalar", lambda e: e.activation(sc.y32[:], sc.x32[:], AF.Square))
        ctx.s("vector", lambda e: e.tensor_copy(sc.bfA[:, 0:t], sc.y32[:]))
        ctx.s("vector", lambda e: e.tensor_tensor(sc.bfB[:, 0:t], sc.y32[:], sc.bfA[:, 0:t], ALU.subtract))
        ctx.fence()
        for off, w in halves(t):
            for i, (mat, r0) in enumerate(((hi, 0), (lo, 0), (sc.bfA[:, 0:t], 32), (sc.bfB[:, 0:t], 32))):
                st = (c == 0) and (i % 2 == 0)
                sp = (c == nch - 1) and (i % 2 == 1)
                ctx.s("tensor", lambda e, mat=mat, r0=r0, off=off, w=w, st=st, sp=sp: e.matmul(
                    ps_a[r0:r0 + 1, off:off + w], k.ones_col_bf[:],
                    mat[:, off:off + w], start=st, stop=sp))
        ctx.fence()
    R = sc.rows
    ctx.s("vector", lambda e: e.tensor_copy(R[0:1, 0:t], ps_a[0:1, 0:t]))
    ctx.s("vector", lambda e: e.tensor_copy(R[0:1, t:2 * t], ps_a[32:33, 0:t]))
    invd = 1.0 / float(nch * 128)
    ctx.s("vector", lambda e: e.tensor_scalar(R[0:1, 2 * t:3 * t], R[0:1, 0:t], invd, None, ALU.mult))
    ctx.s("vector", lambda e: e.tensor_scalar(R[0:1, 3 * t:4 * t], R[0:1, t:2 * t], invd, None, ALU.mult))
    ctx.s("scalar", lambda e: e.activation(R[0:1, 0:t], R[0:1, 2 * t:3 * t], AF.Square))
    ctx.s("vector", lambda e: e.tensor_tensor(R[0:1, 3 * t:4 * t], R[0:1, 3 * t:4 * t], R[0:1, 0:t], ALU.subtract))
    ctx.s("scalar", lambda e: e.activation(R[0:1, 0:t], R[0:1, 3 * t:4 * t], AF.Sqrt, bias=k.eps_1, scale=1.0))
    ctx.s("vector", lambda e: e.reciprocal(R[0:1, 4 * t:5 * t], R[0:1, 0:t]))
    ctx.s("scalar", lambda e: e.activation(R[0:1, 0:t], R[0:1, 4 * t:5 * t], AF.Square))
    ctx.s("vector", lambda e: e.tensor_scalar(R[0:1, 3 * t:4 * t], R[0:1, 3 * t:4 * t], LN_EPS, None, ALU.add))
    ctx.s("vector", lambda e: e.tensor_tensor(R[0:1, 0:t], R[0:1, 0:t], R[0:1, 3 * t:4 * t], ALU.mult))
    ctx.s("vector", lambda e: e.tensor_scalar(R[0:1, 0:t], R[0:1, 0:t], -0.5, 1.5, ALU.mult, ALU.add))
    ctx.s("vector", lambda e: e.tensor_tensor(R[0:1, 4 * t:5 * t], R[0:1, 4 * t:5 * t], R[0:1, 0:t], ALU.mult))
    ctx.s("vector", lambda e: e.scalar_tensor_tensor(
        R[0:1, 2 * t:3 * t], R[0:1, 2 * t:3 * t], -1.0, R[0:1, 4 * t:5 * t], ALU.mult, ALU.mult))
    ctx.fence()
    bcast_row(ctx, k, R[0:1, 4 * t:5 * t], sc.bc[:, 0:t], ps_b, t)
    bcast_row(ctx, k, R[0:1, 2 * t:3 * t], sc.bc[:, t:2 * t], ps_b, t)
    ctx.fence()
    for c in range(nch):
        hi, lo = ch(pair_hi, c), ch(pair_lo, c)
        ctx.s("vector", lambda e, hi=hi, lo=lo: e.tensor_tensor(sc.x32[:], hi, lo, ALU.add))
        ctx.s("vector", lambda e: e.tensor_tensor(sc.y32[:], sc.x32[:], sc.bc[:, 0:t], ALU.mult))
        ctx.s("vector", lambda e: e.tensor_tensor(sc.y32[:], sc.y32[:], sc.bc[:, t:2 * t], ALU.add))
        if gamma is not None:
            ctx.s("vector", lambda e, c=c: e.tensor_scalar(
                sc.y32[:], sc.y32[:], gamma[:, c:c + 1], beta[:, c:c + 1], ALU.mult, ALU.add))
        oh, ol = ch(out_hi, c), ch(out_lo, c)
        ctx.s("vector", lambda e, oh=oh: e.tensor_copy(oh, sc.y32[:]))
        ctx.s("vector", lambda e, oh=oh, ol=ol: e.tensor_tensor(ol, sc.y32[:], oh, ALU.subtract))
        if out32_cb is not None:
            out32_cb(c, sc.y32[:])
        ctx.fence()



def emit_proj(ctx, k, w_dram, x_hi, x_lo, n_in_ch, n_out_ch, consumer,
              tok=TOK, wbufs=None, quant_lhs=None, oc_list=None):
    if wbufs is None:
        wbufs = {}
    u = ctx.uid()
    kw = n_in_ch * 128
    w32 = wbufs.get("w32") or ctx.sbuf(f"pj{u}_w32", [128, kw], F32)
    wh = wbufs.get("wh") or ctx.sbuf(f"pj{u}_wh", [128, kw], BF16)
    wl = wbufs.get("wl") or ctx.sbuf(f"pj{u}_wl", [128, kw], BF16)
    ps = wbufs.get("ps") or ctx.psum(f"pj{u}_ps", [128, tok], F32)
    wr = w_dram.rearrange("(kc p) o -> kc p o", p=128)
    dout = w_dram.shape[1]
    for oc in (oc_list if oc_list is not None else range(n_out_ch)):
        src = wr[:, :, oc * 128:(oc + 1) * 128].transpose([1, 0, 2])
        ctx.s("sync", lambda e, src=src: e.dma_start(
            out=w32[:].rearrange("p (kc f) -> p kc f", f=128), in_=src), dma=True)
        if quant_lhs is None:
            ctx.s("vector", lambda e: e.tensor_copy(wh[:], w32[:]))
            ctx.s("vector", lambda e: e.tensor_tensor(wl[:], w32[:], wh[:], ALU.subtract))
        else:
            quant_lhs(ctx, w32, wh, wl)
        ctx.fence()
        for kc in range(n_in_ch):
            lh = wh[:, kc * 128:(kc + 1) * 128]
            ll = wl[:, kc * 128:(kc + 1) * 128]
            for off, w in halves(tok):
                st = (kc == 0)
                sp = (kc == n_in_ch - 1)
                xh = x_hi[:, kc * tok + off: kc * tok + off + w]
                if x_lo is None:
                    ctx.s("tensor", lambda e, lh=lh, xh=xh, off=off, w=w, st=st, sp=sp: e.matmul(
                        ps[:, off:off + w], lh, xh, start=st, stop=sp))
                else:
                    xl = x_lo[:, kc * tok + off: kc * tok + off + w]
                    ctx.s("tensor", lambda e, lh=lh, xh=xh, off=off, w=w, st=st: e.matmul(
                        ps[:, off:off + w], lh, xh, start=st, stop=False))
                    ctx.s("tensor", lambda e, lh=lh, xl=xl, off=off, w=w: e.matmul(
                        ps[:, off:off + w], lh, xl, start=False, stop=False))
                    ctx.s("tensor", lambda e, ll=ll, xh=xh, off=off, w=w, sp=sp: e.matmul(
                        ps[:, off:off + w], ll, xh, start=False, stop=sp))
        ctx.fence()
        consumer(oc, ps)
        ctx.fence()



def shard_tokens_fm(x):
    out = []
    for c in range(NCORES):
        b, q = divmod(c, 4)
        out.append(np.ascontiguousarray(x[b, q * TOK:(q + 1) * TOK, :].T))
    return out


def gather_tokens_fm(shards):
    x = np.empty((B, S, D), dtype=np.float32)
    for c in range(NCORES):
        b, q = divmod(c, 4)
        x[b, q * TOK:(q + 1) * TOK, :] = np.asarray(shards[c], dtype=np.float32).T
    return x





def emit_load_vec(ctx, dram_vec, sb, nch):
    src = dram_vec.rearrange("(c p) -> c p", p=128).transpose([1, 0])
    ctx.s("sync", lambda e: e.dma_start(out=sb[:, 0:nch], in_=src,
                                        allow_slow_non_contiguous=True), dma=True)


def emit_pair_load(ctx, sc, dram_fm, hi, lo, nch, tok=TOK):
    xr = dram_fm.rearrange("(c p) t -> c p t", p=128)
    for c in range(nch):
        ctx.s("sync", lambda e, c=c: e.dma_start(out=sc.x32[:], in_=xr[c]), dma=True)
        ctx.s("vector", lambda e, c=c: e.tensor_copy(hi[:, c * tok:(c + 1) * tok], sc.x32[:]))
        ctx.s("vector", lambda e, c=c: e.tensor_tensor(
            lo[:, c * tok:(c + 1) * tok], sc.x32[:], hi[:, c * tok:(c + 1) * tok], ALU.subtract))
        ctx.fence()


def emit_pair_store(ctx, hi, lo, dram_hi, dram_lo, nch, tok=TOK):
    hr = dram_hi.rearrange("(c p) t -> c p t", p=128)
    lr = dram_lo.rearrange("(c p) t -> c p t", p=128)
    for c in range(nch):
        ctx.s("gpsimd", lambda e, c=c: e.dma_start(out=hr[c], in_=hi[:, c * tok:(c + 1) * tok]), dma=True)
        ctx.s("gpsimd", lambda e, c=c: e.dma_start(out=lr[c], in_=lo[:, c * tok:(c + 1) * tok]), dma=True)



def emit_colmax_to_row(ctx, k, m_ap, row_ap, ps, tmp_col, n, clip_min=None):
    for tt in range(ceil_div(n, 128)):
        w = min(128, n - tt * 128)
        ctx.s("tensor", lambda e, tt=tt, w=w: e.transpose(
            ps[:, 0:128], m_ap[:, tt * 128:tt * 128 + w], k.ident_f32))
        ctx.s("vector", lambda e, w=w: e.tensor_reduce(
            tmp_col[0:w, 0:1], ps[0:w, 0:128], axis=AX.X, op=ALU.max))
        ctx.fence()
        ctx.s("tensor", lambda e: e.transpose(ps[0:1, 128:256], tmp_col[:, 0:1], k.ident_f32))
        ctx.s("vector", lambda e, tt=tt, w=w: e.tensor_copy(
            row_ap[0:1, tt * 128:tt * 128 + w], ps[0:1, 128:128 + w]))
        ctx.fence()
    if clip_min is not None:
        ctx.s("vector", lambda e: e.tensor_scalar(
            row_ap[0:1, 0:n], row_ap[0:1, 0:n], clip_min, None, ALU.max))
        ctx.fence()


def emit_rne_clip_bf16(ctx, sc, src32_ap, dst_bf_ap, lo=-128.0, hi=127.0):
    n = src32_ap.shape[-1]
    ctx.s("vector", lambda e: e.tensor_copy(sc.i32[:, 0:n], src32_ap))
    ctx.s("vector", lambda e: e.tensor_copy(sc.z32[:, 0:n], sc.i32[:, 0:n]))
    ctx.s("vector", lambda e: e.tensor_scalar(dst_bf_ap, sc.z32[:, 0:n], hi, lo, ALU.min, ALU.max))



R_AMAX, R_SCL, R_TMP, R_TMP2, R_AMAX2 = 0, 1, 2, 3, 4
R_CELLS, R_KQ, R_KZQ = 5, 6, 7


def rowslot(sc, i, o=0, n=1024):
    return sc.rows[0:1, i * 1024 + o:i * 1024 + o + n]


def emit_w_mean_cells(ctx, k, sc, weights, cells, cellcols, wcol32, psE):
    wstat = sc.bc[:, 0:8]
    for wi, (w_d, rows_, cols_) in enumerate(weights):
        wr = w_d.rearrange("(s p) o -> s p o", p=128)
        first = True
        for s_ in range(rows_):
            for co in range(0, cols_, 2048):
                cw = min(2048, cols_ - co)
                ctx.s("sync", lambda e, s_=s_, co=co, cw=cw, wr=wr: e.dma_start(
                    out=wcol32[:, 0:cw], in_=wr[s_][:, co:co + cw]), dma=True)
                ctx.s("vector", lambda e, cw=cw: e.tensor_reduce(
                    sc.z32[:, 0:1], wcol32[:, 0:cw], axis=AX.X, op=ALU.add,
                    apply_absolute_value=True))
                if first:
                    ctx.s("vector", lambda e, wi=wi: e.tensor_copy(wstat[:, wi:wi + 1], sc.z32[:, 0:1]))
                    first = False
                else:
                    ctx.s("vector", lambda e, wi=wi: e.tensor_tensor(
                        wstat[:, wi:wi + 1], wstat[:, wi:wi + 1], sc.z32[:, 0:1], ALU.add))
                ctx.fence()
        ctx.s("tensor", lambda e, wi=wi: e.transpose(psE[0:1, 0:128], wstat[:, wi:wi + 1], k.ident_f32))
        ctx.s("vector", lambda e, wi=wi: e.tensor_reduce(
            cells[0:1, wi:wi + 1], psE[0:1, 0:128], axis=AX.X, op=ALU.add))
        ctx.fence()
    for wi in range(3):
        ctx.s("vector", lambda e, wi=wi: e.tensor_scalar(
            cells[0:1, wi:wi + 1], cells[0:1, wi:wi + 1], 1.0 / (2048.0 * 8192.0), 1e-5,
            ALU.mult, ALU.max))
    ctx.s("vector", lambda e: e.reciprocal(cells[0:1, 3:6], cells[0:1, 0:3]))
    ctx.fence()
    for wi in range(6):
        ctx.s("tensor", lambda e, wi=wi: e.matmul(
            psE[:, 0:1], k.ones_row_f32, cells[0:1, wi:wi + 1], start=True, stop=True))
        ctx.s("vector", lambda e, wi=wi: e.tensor_copy(cellcols[:, wi:wi + 1], psE[:, 0:1]))
        ctx.fence()


def emit_actquant(ctx, k, sc, pair_hi, pair_lo, xq, nch, psD, psE):
    t = TOK
    m = sc.bc[:, t:2 * t]
    for c in range(nch):
        ctx.s("vector", lambda e, c=c: e.tensor_tensor(
            sc.x32[:], pair_hi[:, c * t:(c + 1) * t], pair_lo[:, c * t:(c + 1) * t], ALU.add))
        ctx.s("scalar", lambda e: e.activation(sc.y32[:], sc.x32[:], AF.Abs))
        if c == 0:
            ctx.s("vector", lambda e: e.tensor_copy(m[:], sc.y32[:]))
        else:
            ctx.s("vector", lambda e: e.tensor_tensor(m[:], m[:], sc.y32[:], ALU.max))
        ctx.fence()
    emit_colmax_to_row(ctx, k, m, rowslot(sc, R_AMAX), psE, sc.z32[:, 0:1], t, clip_min=1e-5)
    ctx.s("vector", lambda e: e.reciprocal(rowslot(sc, R_SCL), rowslot(sc, R_AMAX)))
    ctx.s("vector", lambda e: e.tensor_scalar(
        rowslot(sc, R_SCL), rowslot(sc, R_SCL), 128.0, None, ALU.mult))
    ctx.s("vector", lambda e: e.tensor_scalar(
        rowslot(sc, R_KQ), rowslot(sc, R_AMAX), 1.0 / 128.0, None, ALU.mult))
    ctx.fence()
    bcast_row(ctx, k, rowslot(sc, R_SCL), sc.bc[:, 0:t], psD, t)
    ctx.fence()
    for c in range(nch):
        ctx.s("vector", lambda e, c=c: e.tensor_tensor(
            sc.x32[:], pair_hi[:, c * t:(c + 1) * t], pair_lo[:, c * t:(c + 1) * t], ALU.add))
        ctx.s("vector", lambda e: e.tensor_tensor(sc.x32[:], sc.x32[:], sc.bc[:, 0:t], ALU.mult))
        emit_rne_clip_bf16(ctx, sc, sc.x32[:], xq[:, c * t:(c + 1) * t])
        ctx.fence()


def emit_tern(ctx, sc, w32ap, dst, sw_cell, n):
    ctx.s("vector", lambda e: e.tensor_scalar(
        sc.bfA[:, 0:n], w32ap, sw_cell, 0.5, ALU.mult, ALU.is_gt))
    ctx.s("vector", lambda e: e.tensor_scalar(
        sc.bfB[:, 0:n], w32ap, sw_cell, -0.5, ALU.mult, ALU.is_lt))
    ctx.s("vector", lambda e: e.tensor_tensor(dst, sc.bfA[:, 0:n], sc.bfB[:, 0:n], ALU.subtract))


def emit_bitnet_gateup(ctx, k, sc, xq, wg_d, wu_d, ispill_d, cells, cellcols,
                       wcol32, wcolh, tern2, psA, psB, psD):
    t = TOK
    nc2 = NCH * 128
    ctx.s("vector", lambda e: e.tensor_scalar(
        rowslot(sc, R_TMP), rowslot(sc, R_KQ), cells[0:1, 0:1], None, ALU.mult))
    ctx.s("vector", lambda e: e.tensor_scalar(
        rowslot(sc, R_TMP2), rowslot(sc, R_KQ), cells[0:1, 1:2], None, ALU.mult))
    ctx.fence()
    bcast_row(ctx, k, rowslot(sc, R_TMP), sc.bc[:, 0:t], psD, t)
    bcast_row(ctx, k, rowslot(sc, R_TMP2), sc.bc[:, t:2 * t], psD, t)
    ctx.fence()
    wgr = wg_d.rearrange("(kc p) o -> kc p o", p=128)
    wur = wu_d.rearrange("(kc p) o -> kc p o", p=128)
    for oc in range(NFF):
        for (wr_, dst, swi) in ((wgr, wcolh, 3), (wur, tern2, 4)):
            src = wr_[:, :, oc * 128:(oc + 1) * 128].transpose([1, 0, 2])
            ctx.s("sync", lambda e, src=src: e.dma_start(
                out=wcol32[:, 0:nc2].rearrange("p (kc f) -> p kc f", f=128), in_=src), dma=True)
            emit_tern(ctx, sc, wcol32[:, 0:nc2], dst[:, 0:nc2], cellcols[:, swi:swi + 1], nc2)
            ctx.fence()
        for kc in range(NCH):
            for off, w in halves(t):
                st, sp = kc == 0, kc == NCH - 1
                ctx.s("tensor", lambda e, kc=kc, off=off, w=w, st=st, sp=sp: e.matmul(
                    psA[:, off:off + w], wcolh[:, kc * 128:(kc + 1) * 128],
                    xq[:, kc * t + off:kc * t + off + w], start=st, stop=sp))
                ctx.s("tensor", lambda e, kc=kc, off=off, w=w, st=st, sp=sp: e.matmul(
                    psB[:, off:off + w], tern2[:, kc * 128:(kc + 1) * 128],
                    xq[:, kc * t + off:kc * t + off + w], start=st, stop=sp))
        ctx.fence()
        ctx.s("vector", lambda e: e.tensor_tensor(sc.x32[:], psA[:, 0:t], sc.bc[:, 0:t], ALU.mult))
        ctx.s("scalar", lambda e: e.activation(sc.x32[:], sc.x32[:], AF.Silu))
        ctx.s("vector", lambda e: e.tensor_tensor(sc.y32[:], psB[:, 0:t], sc.bc[:, t:2 * t], ALU.mult))
        ctx.s("vector", lambda e: e.tensor_tensor(sc.x32[:], sc.x32[:], sc.y32[:], ALU.mult))
        ctx.s("vector", lambda e: e.tensor_copy(sc.bfA[:, 0:t], sc.x32[:]))
        ctx.s("vector", lambda e: e.tensor_tensor(sc.bfB[:, 0:t], sc.x32[:], sc.bfA[:, 0:t], ALU.subtract))
        ctx.s("gpsimd", lambda e, oc=oc: e.dma_start(out=ispill_d[oc, 0], in_=sc.bfA[:, 0:t]), dma=True)
        ctx.s("gpsimd", lambda e, oc=oc: e.dma_start(out=ispill_d[oc, 1], in_=sc.bfB[:, 0:t]), dma=True)
        ctx.fence()


def emit_bitnet_stageb(ctx, k, sc, ispill_d, zqspill_d, zbuf, ihbuf, ilbuf,
                       pbuf, ztile, psD, psE):
    TT = 128
    for tile in range(8):
        t0 = tile * TT
        for hl, buf in ((0, ihbuf), (1, ilbuf)):
            for g8 in range(0, NFF, 8):
                src = ispill_d[g8:g8 + 8, hl, :, t0:t0 + TT].transpose([1, 0, 2])
                ctx.s("sync", lambda e, src=src, g8=g8, buf=buf: e.dma_start(
                    out=buf[:, g8 * TT:(g8 + 8) * TT].rearrange("p (a f) -> p a f", f=TT),
                    in_=src), dma=True)
        ctx.fence()
        for a in range(NFF):
            ctx.s("tensor", lambda e, a=a: e.matmul(
                psD[:, 0:TT], k.h128_bf[:], ihbuf[:, a * TT:(a + 1) * TT], start=True, stop=False))
            ctx.s("tensor", lambda e, a=a: e.matmul(
                psD[:, 0:TT], k.h128_bf[:], ilbuf[:, a * TT:(a + 1) * TT], start=False, stop=True))
            ctx.s("vector", lambda e, a=a: e.tensor_copy(zbuf[:, a * TT:(a + 1) * TT], psD[:, 0:TT]))
            ctx.fence()
        TB = 32
        for sp_ in range(3):
            h1, h2 = 1 << (2 * sp_), 1 << (2 * sp_ + 1)
            for tb in range(0, TT, TB):
                for (hh, src_t, dst_t) in ((h1, "z", "p"), (h2, "p", "z")):
                    def mk(which, a0, hh=hh, tb=tb):
                        tens, width, base = (zbuf, TT, tb) if which == "z" else (pbuf, TB, 0)
                        v = tens[:, :].rearrange("p (a f) -> p a f", f=width)
                        v = v[:, :, base:base + TB]
                        v = v.rearrange("p (g two j) f -> p g two j f", two=2, j=hh)
                        return v[:, :, a0]
                    in0, in1 = mk(src_t, 0), mk(src_t, 1)
                    out0, out1 = mk(dst_t, 0), mk(dst_t, 1)
                    ctx.s("vector", lambda e, out0=out0, in0=in0, in1=in1: e.tensor_tensor(
                        out0, in0, in1, ALU.add))
                    ctx.s("vector", lambda e, out1=out1, in0=in0, in1=in1: e.tensor_tensor(
                        out1, in0, in1, ALU.subtract))
                    ctx.fence()
        zv = zbuf[:, :].rearrange("p (a f) -> p f a", f=TT)
        ctx.s("vector", lambda e, zv=zv: e.tensor_reduce(
            sc.y32[:, 0:TT], zv, axis=AX.X, op=ALU.max, apply_absolute_value=True))
        ctx.fence()
        arow = rowslot(sc, R_AMAX2, t0, TT)
        emit_colmax_to_row(ctx, k, sc.y32[:, 0:TT], arow, psE, sc.z32[:, 0:1], TT)
        srow = rowslot(sc, R_TMP, t0, TT)
        kzq = rowslot(sc, R_KZQ, t0, TT)
        ctx.s("vector", lambda e, arow=arow: e.tensor_scalar(arow, arow, FWHT_NORM, 1e-5, ALU.mult, ALU.max))
        ctx.s("vector", lambda e, srow=srow, arow=arow: e.reciprocal(srow, arow))
        ctx.s("vector", lambda e, srow=srow: e.tensor_scalar(srow, srow, 128.0 * FWHT_NORM, None, ALU.mult))
        ctx.s("vector", lambda e, kzq=kzq, arow=arow: e.tensor_scalar(kzq, arow, 1.0 / 128.0, None, ALU.mult))
        ctx.fence()
        bcast_row(ctx, k, srow, sc.bc[:, 0:TT], psE, TT)
        ctx.fence()
        for a in range(NFF):
            ctx.s("vector", lambda e, a=a: e.tensor_tensor(
                sc.x32[:, 0:TT], zbuf[:, a * TT:(a + 1) * TT], sc.bc[:, 0:TT], ALU.mult))
            emit_rne_clip_bf16(ctx, sc, sc.x32[:, 0:TT], ztile[:, a * TT:(a + 1) * TT])
            ctx.fence()
        src = ztile[:, :].rearrange("p (a f) -> p a f", f=TT)
        ctx.s("gpsimd", lambda e, t0=t0, src=src: e.dma_start(
            out=zqspill_d[:, :, t0:t0 + TT].transpose([1, 0, 2]), in_=src), dma=True)
        ctx.fence()


def emit_bitnet_down(ctx, k, sc, zqspill_d, wd_d, f_d, cells, cellcols, zqh,
                     wcol32, wcolh, psC, psD):
    nc2 = NCH * 128
    wdr = wd_d.rearrange("(kc p) o -> kc p o", p=128)
    fr = f_d.rearrange("(c p) t -> c p t", p=128)
    for th in range(2):
        th0 = th * 512
        for g8 in range(0, NFF, 8):
            src = zqspill_d[g8:g8 + 8, :, th0:th0 + 512].transpose([1, 0, 2])
            ctx.s("sync", lambda e, src=src, g8=g8: e.dma_start(
                out=zqh[:, g8 * 512:(g8 + 8) * 512].rearrange("p (a f) -> p a f", f=512),
                in_=src), dma=True)
        kd = rowslot(sc, R_TMP, 0, 512)
        ctx.s("vector", lambda e, kd=kd, th0=th0: e.tensor_scalar(
            kd, rowslot(sc, R_KZQ, th0, 512), cells[0:1, 2:3], None, ALU.mult))
        ctx.fence()
        bcast_row(ctx, k, kd, sc.bc[:, 0:512], psD, 512)
        ctx.fence()
        for oc in range(NCH):
            for q4 in range(4):
                src = wdr[q4 * NCH:(q4 + 1) * NCH, :, oc * 128:(oc + 1) * 128].transpose([1, 0, 2])
                ctx.s("sync", lambda e, src=src: e.dma_start(
                    out=wcol32[:, 0:nc2].rearrange("p (kc f) -> p kc f", f=128), in_=src), dma=True)
                emit_tern(ctx, sc, wcol32[:, 0:nc2], wcolh[:, 0:nc2], cellcols[:, 5:6], nc2)
                ctx.fence()
                for kc in range(NCH):
                    a = q4 * NCH + kc
                    st = (q4 == 0 and kc == 0)
                    sp = (q4 == 3 and kc == NCH - 1)
                    ctx.s("tensor", lambda e, a=a, kc=kc, st=st, sp=sp: e.matmul(
                        psC[:, 0:512], wcolh[:, kc * 128:(kc + 1) * 128],
                        zqh[:, a * 512:(a + 1) * 512], start=st, stop=sp))
                ctx.fence()
            ctx.s("vector", lambda e: e.tensor_tensor(sc.x32[:, 0:512], psC[:, 0:512], sc.bc[:, 0:512], ALU.mult))
            ctx.s("gpsimd", lambda e, oc=oc, th0=th0: e.dma_start(
                out=fr[oc][:, th0:th0 + 512], in_=sc.x32[:, 0:512]), dma=True)
            ctx.fence()



def emit_t0(ctx, dr, k, sc, psA, psB, psC, psD, psE, vecs, cellcols,
            wcol32, wcolh, wcoll, vtm=False):
    t = TOK
    x_d = dr["x"]; wq_d = dr["wq"]; wk_d = dr["wk"]; wv_d = dr["wv"]
    wg_d = dr["wg_t"]; wu_d = dr["wu_t"]; wd_d = dr["wd_t"]
    xnh_d = dr["xnh"]; xnl_d = dr["xnl"]
    qkv_outs = {nm: dr[nm] for nm in ("qh", "ql", "kh", "kl", "vh", "vl") if nm in dr}
    f_d = dr["fout"]; ispill_d = dr["ispill"]; zqspill_d = dr["zqspill"]

    cells = rowslot(sc, R_CELLS, 0, 8)
    emit_w_mean_cells(ctx, k, sc, ((wg_d, NCH, DFF), (wu_d, NCH, DFF), (wd_d, NFF, D)),
                      cells, cellcols, wcol32, psE)
    xq = ctx.sbuf("xq", [128, NCH * t], BF16)
    xh = ctx.sbuf("xh", [128, NCH * t], BF16)
    xl = ctx.sbuf("xl", [128, NCH * t], BF16)
    emit_pair_load(ctx, sc, x_d, xh, xl, NCH)
    emit_ln(ctx, k, sc, xh, xl, xh, xl, psA, psD)
    emit_pair_store(ctx, xh, xl, xnh_d, xnl_d, NCH)
    ctx.fence()

    emit_actquant(ctx, k, sc, xh, xl, xq, NCH, psD, psE)

    g1 = vecs[:, 3 * NCH:4 * NCH]
    be1 = vecs[:, 4 * NCH:5 * NCH]
    emit_ln(ctx, k, sc, xh, xl, xh, xl, psA, psD, gamma=g1, beta=be1)
    ctx.fence()

    wbufs = {"w32": wcol32, "wh": wcolh, "wl": wcoll, "ps": psA}

    def qkv_consumer(nm_h, nm_l, bias_idx, scale):
        hr = qkv_outs[nm_h].rearrange("(c p) t -> c p t", p=128)
        lr = qkv_outs[nm_l].rearrange("(c p) t -> c p t", p=128)

        def consumer(oc, ps):
            bcol = vecs[:, bias_idx * NCH + oc:bias_idx * NCH + oc + 1]
            ctx.s("vector", lambda e, bcol=bcol: e.tensor_scalar(
                sc.z32[:, 0:1], bcol, scale, None, ALU.mult))
            ctx.fence()
            ctx.s("scalar", lambda e: e.activation(
                sc.y32[:], ps[:, 0:t], AF.Identity, bias=sc.z32[:, 0:1], scale=scale))
            ctx.s("vector", lambda e: e.tensor_copy(sc.bfA[:, 0:t], sc.y32[:]))
            ctx.s("vector", lambda e: e.tensor_tensor(sc.bfB[:, 0:t], sc.y32[:], sc.bfA[:, 0:t], ALU.subtract))
            ctx.s("gpsimd", lambda e, oc=oc, hr=hr: e.dma_start(out=hr[oc], in_=sc.bfA[:, 0:t]), dma=True)
            ctx.s("gpsimd", lambda e, oc=oc, lr=lr: e.dma_start(out=lr[oc], in_=sc.bfB[:, 0:t]), dma=True)
        return consumer

    emit_proj(ctx, k, wq_d, xh, xl, NCH, NCH, qkv_consumer("qh", "ql", 0, RSQ_HD), wbufs=wbufs)
    emit_proj(ctx, k, wk_d, xh, xl, NCH, NCH, qkv_consumer("kh", "kl", 1, 1.0), wbufs=wbufs)
    if not vtm:
        emit_proj(ctx, k, wv_d, xh, xl, NCH, NCH, qkv_consumer("vh", "vl", 2, 1.0), wbufs=wbufs)
    else:
        bvrow = sc.bc[:, 0:D]
        for cch in range(NCH):
            ctx.s("tensor", lambda e, cch=cch: e.transpose(
                psD[0:1, 0:128], vecs[:, 2 * NCH + cch:2 * NCH + cch + 1], k.ident_f32))
            ctx.s("vector", lambda e, cch=cch: e.tensor_copy(
                sc.rows[0:1, cch * 128:(cch + 1) * 128], psD[0:1, 0:128]))
            ctx.fence()
        bcast_row(ctx, k, sc.rows[0:1, 0:512], bvrow[:, 0:512], psE, 512)
        bcast_row(ctx, k, sc.rows[0:1, 512:1024], bvrow[:, 512:1024], psE, 512)
        bcast_row(ctx, k, sc.rows[0:1, 1024:1536], bvrow[:, 1024:1536], psE, 512)
        bcast_row(ctx, k, sc.rows[0:1, 1536:2048], bvrow[:, 1536:2048], psE, 512)
        ctx.fence()
        emit_vproj_tm(ctx, k, sc, wv_d, 0, xh, xl, NCH, NCH,
                      dr["vth_loc"], dr["vtl_loc"],
                      [psA, psB, psC, psD], wcol32, wcolh, wcoll, bias_col=bvrow)
    ctx.fence()
    ctx.sbuf_free("xl", "xh")

    tern2 = ctx.sbuf("tern2", [128, NCH * 128], BF16)
    emit_bitnet_gateup(ctx, k, sc, xq, wg_d, wu_d, ispill_d, cells, cellcols,
                       wcol32, wcolh, tern2, psA, psB, psD)
    ctx.sbuf_free("tern2", "xq")
    zbuf = ctx.sbuf("zbuf", [128, NFF * 128], F32)
    ihbuf = ctx.sbuf("ihbuf", [128, NFF * 128], BF16)
    ilbuf = ctx.sbuf("ilbuf", [128, NFF * 128], BF16)
    pbuf = ctx.sbuf("pbuf", [128, NFF * 32], F32)
    ztile = ctx.sbuf("ztile", [128, NFF * 128], BF16)
    emit_bitnet_stageb(ctx, k, sc, ispill_d, zqspill_d, zbuf, ihbuf, ilbuf,
                       pbuf, ztile, psD, psE)
    ctx.sbuf_free("ztile", "pbuf", "ilbuf", "ihbuf", "zbuf")
    zqh = ctx.sbuf("zqh", [128, NFF * 512], BF16)
    emit_bitnet_down(ctx, k, sc, zqspill_d, wd_d, f_d, cells, cellcols, zqh,
                     wcol32, wcolh, psC, psD)
    ctx.sbuf_free("zqh")



def build_t0(dbg=False):
    ctx = Ctx()
    ctx.init_serial()
    t = TOK
    dr = {}
    dr["x"] = ctx.dram_in("x", [D, t])
    c_d = ctx.dram_in("consts", [128, 400])
    dr["wq"] = ctx.dram_in("wq", [D, D])
    dr["wk"] = ctx.dram_in("wk", [D, D])
    dr["wv"] = ctx.dram_in("wv", [D, D])
    vec_d = ctx.dram_in("vecs", [5 * D])
    dr["wg_t"] = ctx.dram_in("wg_t", [D, DFF])
    dr["wu_t"] = ctx.dram_in("wu_t", [D, DFF])
    dr["wd_t"] = ctx.dram_in("wd_t", [DFF, D])
    dr["xnh"] = ctx.dram_out("xnh", [D, t], BF16)
    dr["xnl"] = ctx.dram_out("xnl", [D, t], BF16)
    for nm in ("qh", "ql", "kh", "kl", "vh", "vl"):
        dr[nm] = ctx.dram_out(nm, [D, t], BF16)
    dr["fout"] = ctx.dram_out("fout", [D, t])
    dr["ispill"] = ctx.dram_tmp("ispill", [NFF, 2, 128, t], BF16)
    dr["zqspill"] = ctx.dram_tmp("zqspill", [NFF, 128, t], BF16)
    k = Consts(ctx)
    sc = Scratch(ctx)
    psA = ctx.psum("psA", [128, 1024], F32)
    psB = ctx.psum("psB", [128, 1024], F32)
    psC = ctx.psum("psC", [128, 1024], F32)
    psD = ctx.psum("psD", [128, 512], F32)
    psE = ctx.psum("psE", [128, 512], F32)
    vecs = ctx.sbuf("vecs_sb", [128, 5 * NCH], F32)
    cellcols = ctx.sbuf("cellcols", [128, 8], F32)
    wcol32 = ctx.sbuf("wcol32", [128, NCH * 128], F32)
    wcolh = ctx.sbuf("wcolh", [128, NCH * 128], BF16)
    wcoll = ctx.sbuf("wcoll", [128, NCH * 128], BF16)
    k.load(ctx, c_d[:])
    emit_load_vec(ctx, vec_d, vecs, 5 * NCH)
    ctx.fence()
    emit_t0(ctx, dr, k, sc, psA, psB, psC, psD, psE, vecs, cellcols,
            wcol32, wcolh, wcoll)
    ctx.emit()
    return ctx


QT = 256
NQT = S // QT


def emit_attn(ctx, dr, k, sc, psS, psAV, psSum, psE, masks):
    qh_d, ql_d, kh_d, kl_d = dr["qh"], dr["ql"], dr["kh"], dr["kl"]
    vth_d, vtl_d, ah_d, al_d = dr["vth"], dr["vtl"], dr["ah"], dr["al"]
    u = ctx.uid()
    qhb = ctx.sbuf(f"at{u}_qhb", [128, S], BF16)
    qlb = ctx.sbuf(f"at{u}_qlb", [128, S], BF16)
    khb = ctx.sbuf(f"at{u}_khb", [128, S], BF16)
    klb = ctx.sbuf(f"at{u}_klb", [128, S], BF16)
    vhb = ctx.sbuf(f"at{u}_vhb", [128, S], BF16)
    vlb = ctx.sbuf(f"at{u}_vlb", [128, S], BF16)
    strip = ctx.sbuf(f"at{u}_strip", [128, 2 * S], F32)
    ehb = ctx.sbuf(f"at{u}_ehb", [128, 2 * S], BF16)
    elb = ctx.sbuf(f"at{u}_elb", [128, 2 * S], BF16)
    for i in range(4):
        for (dr, sb) in ((qh_d, qhb), (ql_d, qlb), (kh_d, khb), (kl_d, klb)):
            ctx.s("sync", lambda e, dr=dr, sb=sb, i=i: e.dma_start(out=sb[:], in_=dr[i]), dma=True)
        for (dr, sb) in ((vth_d, vhb), (vtl_d, vlb)):
            src = dr[i].rearrange("(kt p) d -> p kt d", p=128)
            ctx.s("sync", lambda e, src=src, sb=sb: e.dma_start(
                out=sb[:].rearrange("p (kt d) -> p kt d", d=128), in_=src), dma=True)
        ctx.fence()
        for qt in range(NQT):
            nkt = 2 * qt + 2
            q0 = qt * QT
            for kt in range(nkt):
                ctx.s("tensor", lambda e, kt=kt, q0=q0: e.matmul(
                    psS[:, 0:QT], khb[:, kt * 128:(kt + 1) * 128], qhb[:, q0:q0 + QT],
                    start=True, stop=False))
                ctx.s("tensor", lambda e, kt=kt, q0=q0: e.matmul(
                    psS[:, 0:QT], khb[:, kt * 128:(kt + 1) * 128], qlb[:, q0:q0 + QT],
                    start=False, stop=False))
                ctx.s("tensor", lambda e, kt=kt, q0=q0: e.matmul(
                    psS[:, 0:QT], klb[:, kt * 128:(kt + 1) * 128], qhb[:, q0:q0 + QT],
                    start=False, stop=True))
                if kt >= nkt - 2:
                    pat = 0 if kt == nkt - 2 else 2
                    ctx.s("vector", lambda e, kt=kt, pat=pat: e.tensor_tensor(
                        sc.x32[:, 0:QT], psS[:, 0:QT], masks[:, pat * QT:(pat + 1) * QT], ALU.mult))
                    ctx.s("vector", lambda e, kt=kt, pat=pat: e.tensor_tensor(
                        strip[:, kt * QT:(kt + 1) * QT], sc.x32[:, 0:QT],
                        masks[:, (pat + 1) * QT:(pat + 2) * QT], ALU.add))
                else:
                    ctx.s("scalar", lambda e, kt=kt: e.activation(
                        strip[:, kt * QT:(kt + 1) * QT], psS[:, 0:QT], AF.Identity))
                ctx.fence()
            ctx.s("vector", lambda e, nkt=nkt: e.tensor_reduce(
                sc.z32[:, 0:1], strip[:, 0:nkt * QT], axis=AX.X, op=ALU.max))
            ctx.fence()
            ctx.s("tensor", lambda e: e.transpose(psE[0:1, 0:128], sc.z32[:, 0:1], k.ident_f32))
            ctx.s("vector", lambda e: e.tensor_reduce(
                sc.rows[0:1, 0:1], psE[0:1, 0:128], axis=AX.X, op=ALU.max))
            ctx.s("vector", lambda e: e.tensor_scalar(
                sc.rows[0:1, 0:1], sc.rows[0:1, 0:1], -1.0, None, ALU.mult))
            ctx.fence()
            ctx.s("tensor", lambda e: e.matmul(
                psE[:, 0:1], k.ones_row_f32, sc.rows[0:1, 0:1], start=True, stop=True))
            ctx.s("vector", lambda e: e.tensor_copy(sc.z32[:, 0:1], psE[:, 0:1]))
            ctx.fence()
            for kt in range(nkt):
                sl = slice(kt * QT, (kt + 1) * QT)
                ctx.s("scalar", lambda e, sl=sl: e.activation(
                    sc.x32[:, 0:QT], strip[:, sl], AF.Exp, bias=sc.z32[:, 0:1], scale=1.0))
                ctx.s("vector", lambda e, sl=sl: e.tensor_copy(ehb[:, sl], sc.x32[:, 0:QT]))
                ctx.s("vector", lambda e, sl=sl: e.tensor_tensor(
                    elb[:, sl], sc.x32[:, 0:QT], ehb[:, sl], ALU.subtract))
                ctx.fence()
                st, sp = (kt == 0), (kt == nkt - 1)
                ctx.s("tensor", lambda e, kt=kt, sl=sl, st=st: e.matmul(
                    psAV[:, 0:QT], vhb[:, kt * 128:(kt + 1) * 128], ehb[:, sl], start=st, stop=False))
                ctx.s("tensor", lambda e, kt=kt, sl=sl: e.matmul(
                    psAV[:, 0:QT], vhb[:, kt * 128:(kt + 1) * 128], elb[:, sl], start=False, stop=False))
                ctx.s("tensor", lambda e, kt=kt, sl=sl, sp=sp: e.matmul(
                    psAV[:, 0:QT], vlb[:, kt * 128:(kt + 1) * 128], ehb[:, sl], start=False, stop=sp))
                ctx.s("tensor", lambda e, sl=sl, st=st: e.matmul(
                    psSum[0:1, 0:QT], k.ones_col_bf[:], ehb[:, sl], start=st, stop=False))
                ctx.s("tensor", lambda e, sl=sl, sp=sp: e.matmul(
                    psSum[0:1, 0:QT], k.ones_col_bf[:], elb[:, sl], start=False, stop=sp))
                ctx.fence()
            ctx.s("vector", lambda e: e.reciprocal(sc.rows[0:1, 0:QT], psSum[0:1, 0:QT]))
            ctx.fence()
            ctx.s("tensor", lambda e: e.matmul(
                psE[:, 0:QT], k.ones_row_f32, sc.rows[0:1, 0:QT], start=True, stop=True))
            ctx.s("vector", lambda e: e.tensor_copy(sc.bc[:, 0:QT], psE[:, 0:QT]))
            ctx.fence()
            ctx.s("vector", lambda e: e.tensor_tensor(
                sc.x32[:, 0:QT], psAV[:, 0:QT], sc.bc[:, 0:QT], ALU.mult))
            ctx.s("vector", lambda e: e.tensor_copy(sc.bfA[:, 0:QT], sc.x32[:, 0:QT]))
            ctx.s("vector", lambda e: e.tensor_tensor(
                sc.bfB[:, 0:QT], sc.x32[:, 0:QT], sc.bfA[:, 0:QT], ALU.subtract))
            ctx.s("gpsimd", lambda e, i=i, q0=q0: e.dma_start(
                out=ah_d[i][:, q0:q0 + QT], in_=sc.bfA[:, 0:QT]), dma=True)
            ctx.s("gpsimd", lambda e, i=i, q0=q0: e.dma_start(
                out=al_d[i][:, q0:q0 + QT], in_=sc.bfB[:, 0:QT]), dma=True)
            ctx.fence()
    for nm in ("elb", "ehb", "strip", "vlb", "vhb", "klb", "khb", "qlb", "qhb"):
        ctx.sbuf_free(f"at{u}_{nm}")


def build_attn():
    ctx = Ctx()
    ctx.init_serial()
    c_d = ctx.dram_in("consts", [128, 400])
    mask_d = ctx.dram_in("masks", [4, 128, QT])
    dr = {}
    for nm in ("qh", "ql", "kh", "kl"):
        dr[nm] = ctx.dram_in(nm, [4, 128, S], BF16)
    for nm in ("vth", "vtl"):
        dr[nm] = ctx.dram_in(nm, [4, S, 128], BF16)
    dr["ah"] = ctx.dram_out("ah", [4, 128, S], BF16)
    dr["al"] = ctx.dram_out("al", [4, 128, S], BF16)
    k = Consts(ctx)
    sc = Scratch(ctx)
    psS = ctx.psum("psS", [128, 512], F32)
    psAV = ctx.psum("psAV", [128, 512], F32)
    psSum = ctx.psum("psSum", [128, 512], F32)
    psE = ctx.psum("psE", [128, 512], F32)
    masks = ctx.sbuf("masks_sb", [128, 4 * QT], F32)
    k.load(ctx, c_d[:])
    ctx.s("sync", lambda e: e.dma_start(
        out=masks[:].rearrange("p (i f) -> p i f", f=QT), in_=mask_d.transpose([1, 0, 2])), dma=True)
    ctx.fence()
    emit_attn(ctx, dr, k, sc, psS, psAV, psSum, psE, masks)
    ctx.emit()
    return ctx


def attn_masks_host(maskval):
    out = np.zeros((4, 128, QT), dtype=np.float32)
    kloc = np.arange(128)[:, None]
    qloc = np.arange(QT)[None, :]
    mA = (kloc > qloc)
    mB = (128 + kloc > qloc)
    out[0] = (~mA).astype(np.float32)
    out[1] = np.where(mA, np.float32(maskval), np.float32(0.0))
    out[2] = (~mB).astype(np.float32)
    out[3] = np.where(mB, np.float32(maskval), np.float32(0.0))
    return out


def emit_proj_ksweep(ctx, k, sc, w_dram, rhsh_d, rhsl_d, n_in_ch, n_out_ch,
                     consumer, psW, wbuf32, wbufh, wbufl, rbufh, rbufl, tok=TOK):
    wr = w_dram.rearrange("(kc p) o -> kc p o", p=128)
    for s4 in range(0, n_out_ch, 4):
        for kc in range(n_in_ch):
            ctx.s("sync", lambda e, kc=kc, s4=s4: e.dma_start(
                out=wbuf32[:, 0:512], in_=wr[kc][:, s4 * 128:(s4 + 4) * 128]), dma=True)
            ctx.s("vector", lambda e: e.tensor_copy(wbufh[:, 0:512], wbuf32[:, 0:512]))
            ctx.s("vector", lambda e: e.tensor_tensor(
                wbufl[:, 0:512], wbuf32[:, 0:512], wbufh[:, 0:512], ALU.subtract))
            ctx.s("sync", lambda e, kc=kc: e.dma_start(out=rbufh[:, 0:tok], in_=rhsh_d[kc]), dma=True)
            ctx.s("sync", lambda e, kc=kc: e.dma_start(out=rbufl[:, 0:tok], in_=rhsl_d[kc]), dma=True)
            ctx.fence()
            for j in range(4):
                if s4 + j >= n_out_ch:
                    continue
                lh = wbufh[:, j * 128:(j + 1) * 128]
                ll = wbufl[:, j * 128:(j + 1) * 128]
                for off, w in halves(tok):
                    st, sp = kc == 0, kc == n_in_ch - 1
                    ctx.s("tensor", lambda e, j=j, lh=lh, off=off, w=w, st=st: e.matmul(
                        psW[j][:, off:off + w], lh, rbufh[:, off:off + w], start=st, stop=False))
                    ctx.s("tensor", lambda e, j=j, lh=lh, off=off, w=w: e.matmul(
                        psW[j][:, off:off + w], lh, rbufl[:, off:off + w], start=False, stop=False))
                    ctx.s("tensor", lambda e, j=j, ll=ll, off=off, w=w, sp=sp: e.matmul(
                        psW[j][:, off:off + w], ll, rbufh[:, off:off + w], start=False, stop=sp))
            ctx.fence()
        for j in range(4):
            if s4 + j >= n_out_ch:
                continue
            consumer(s4 + j, psW[j])
            ctx.fence()



LOG2E_INV = float(np.log(2.0))


def emit_fp8_quant(ctx, k, sc, pair_hi, pair_lo, out_bf, nch, inv_s, tok=TOK):
    t = tok
    for c in range(nch):
        ctx.s("vector", lambda e, c=c: e.tensor_tensor(
            sc.x32[:], pair_hi[:, c * t:(c + 1) * t], pair_lo[:, c * t:(c + 1) * t], ALU.add))
        ctx.s("scalar", lambda e: e.activation(sc.y32[:], sc.x32[:], AF.Abs, scale=inv_s))
        ctx.s("vector", lambda e: e.tensor_scalar(sc.y32[:], sc.y32[:], 448.0, None, ALU.min))
        ctx.s("scalar", lambda e: e.activation(sc.z32[:], sc.x32[:], AF.Sign))
        ctx.s("scalar", lambda e: e.activation(sc.x32[:], sc.y32[:], AF.Ln))
        ctx.s("vector", lambda e: e.tensor_scalar(
            sc.x32[:], sc.x32[:], 1.0 / LOG2E_INV, 63.5, ALU.mult, ALU.add))
        ctx.s("vector", lambda e: e.tensor_copy(sc.i32[:], sc.x32[:]))
        ctx.s("vector", lambda e: e.tensor_copy(sc.x32[:], sc.i32[:]))
        ctx.s("vector", lambda e: e.tensor_scalar(
            sc.x32[:], sc.x32[:], -64.0, -6.0, ALU.add, ALU.max))
        ctx.s("scalar", lambda e: e.activation(
            sc.bc[:, 0:t], sc.x32[:], AF.Exp, bias=float(np.float32(3.0 * LOG2E_INV)), scale=-LOG2E_INV))
        ctx.s("scalar", lambda e: e.activation(
            sc.x32[:], sc.x32[:], AF.Exp, bias=float(np.float32(-3.0 * LOG2E_INV)), scale=LOG2E_INV))
        ctx.s("vector", lambda e: e.tensor_tensor(sc.y32[:], sc.y32[:], sc.bc[:, 0:t], ALU.mult))
        ctx.s("vector", lambda e: e.tensor_copy(sc.i32[:], sc.y32[:]))
        ctx.s("vector", lambda e: e.tensor_copy(sc.y32[:], sc.i32[:]))
        ctx.s("vector", lambda e: e.tensor_tensor(sc.y32[:], sc.y32[:], sc.x32[:], ALU.mult))
        ctx.s("vector", lambda e, c=c: e.tensor_tensor(
            out_bf[:, c * t:(c + 1) * t], sc.y32[:], sc.z32[:], ALU.mult))
        ctx.fence()



def emit_t1(ctx, dr, k, sc, psW, vecs, wcol32, wcolh, wcoll, rbufh, rbufl,
            sclsb, isbf, kcell, vtm=False):
    t = TOK
    x_d = dr["x"]; xnh_d = dr["xnh"]; xnl_d = dr["xnl"]; f_d = dr["fout"]
    ath_d = dr["ath"]; atl_d = dr["atl"]; wo_d = dr["wo"]; w1_d = dr["w1"]
    w2_d = dr["w2"]; wqkv_d = dr["wqkv_t"]; w8_d = dr["w8_t"]
    ws_d = dr["wscale"]; is_d = dr["iscale"]
    accp_d = dr["accp"]; ffn2_d = dr["ffn2"]; ffn1_d = dr["ffn1"]
    q1 = {nm: dr[nm] for nm in ("q1h", "q1l", "k1h", "k1l", "v1h", "v1l")}
    hspill_d = dr["hspill"]; xasp_h = dr["xasph"]; xasp_l = dr["xaspl"]
    acc = ctx.sbuf("acc", [128, NCH * t], F32)
    xr = x_d.rearrange("(c p) t -> c p t", p=128)
    fr2 = f_d.rearrange("(c p) t -> c p t", p=128)
    for c in range(NCH):
        ctx.s("sync", lambda e, c=c: e.dma_start(out=sc.x32[:], in_=xr[c]), dma=True)
        ctx.s("sync", lambda e, c=c: e.dma_start(out=sc.y32[:], in_=fr2[c]), dma=True)
        ctx.s("vector", lambda e, c=c: e.tensor_tensor(
            acc[:, c * t:(c + 1) * t], sc.x32[:], sc.y32[:], ALU.add))
        ctx.fence()
    xnhr = xnh_d.rearrange("(c p) t -> c p t", p=128)
    xnlr = xnl_d.rearrange("(c p) t -> c p t", p=128)

    def oproj_consumer(oc, ps):
        bo = vecs[:, oc:oc + 1]
        ctx.s("scalar", lambda e, bo=bo: e.activation(
            sc.x32[:], ps[:, 0:t], AF.Identity, bias=bo, scale=1.0))
        ctx.s("sync", lambda e, oc=oc: e.dma_start(out=sc.bfA[:, 0:t], in_=xnhr[oc]), dma=True)
        ctx.s("sync", lambda e, oc=oc: e.dma_start(out=sc.bfB[:, 0:t], in_=xnlr[oc]), dma=True)
        ctx.s("vector", lambda e: e.tensor_tensor(sc.x32[:], sc.x32[:], sc.bfA[:, 0:t], ALU.add))
        ctx.s("vector", lambda e: e.tensor_tensor(sc.x32[:], sc.x32[:], sc.bfB[:, 0:t], ALU.add))
        ctx.s("vector", lambda e, oc=oc: e.tensor_tensor(
            acc[:, oc * t:(oc + 1) * t], acc[:, oc * t:(oc + 1) * t], sc.x32[:], ALU.add))
        ctx.s("vector", lambda e: e.tensor_copy(sc.bfA[:, 0:t], sc.x32[:]))
        ctx.s("vector", lambda e: e.tensor_tensor(sc.bfB[:, 0:t], sc.x32[:], sc.bfA[:, 0:t], ALU.subtract))
        ctx.s("gpsimd", lambda e, oc=oc: e.dma_start(out=xasp_h[oc], in_=sc.bfA[:, 0:t]), dma=True)
        ctx.s("gpsimd", lambda e, oc=oc: e.dma_start(out=xasp_l[oc], in_=sc.bfB[:, 0:t]), dma=True)

    emit_proj_ksweep(ctx, k, sc, wo_d, ath_d, atl_d, NCH, NCH, oproj_consumer,
                     psW, wcol32, wcolh, wcoll, rbufh, rbufl)
    ctx.fence()
    ar = accp_d.rearrange("(c p) t -> c p t", p=128)
    for c in range(NCH):
        ctx.s("gpsimd", lambda e, c=c: e.dma_start(out=ar[c], in_=acc[:, c * t:(c + 1) * t]), dma=True)
    ctx.fence()
    ctx.sbuf_free("acc")

    xah = ctx.sbuf("xah", [128, NCH * t], BF16)
    xal = ctx.sbuf("xal", [128, NCH * t], BF16)
    for c in range(NCH):
        ctx.s("sync", lambda e, c=c: e.dma_start(out=xah[:, c * t:(c + 1) * t], in_=xasp_h[c]), dma=True)
        ctx.s("sync", lambda e, c=c: e.dma_start(out=xal[:, c * t:(c + 1) * t], in_=xasp_l[c]), dma=True)
    ctx.fence()
    g2 = vecs[:, 2 * NCH:3 * NCH]
    be2 = vecs[:, 3 * NCH:4 * NCH]
    emit_ln(ctx, k, sc, xah, xal, xah, xal, psW[0], psW[1], gamma=g2, beta=be2)
    ctx.fence()
    wbufs = {"w32": wcol32, "wh": wcolh, "wl": wcoll, "ps": psW[0]}

    def w1_consumer(oc, ps):
        b1 = vecs[:, 4 * NCH + oc:4 * NCH + oc + 1]
        ctx.s("scalar", lambda e, b1=b1: e.activation(
            sc.x32[:], ps[:, 0:t], AF.Gelu, bias=b1, scale=1.0))
        ctx.s("vector", lambda e: e.tensor_copy(sc.bfA[:, 0:t], sc.x32[:]))
        ctx.s("vector", lambda e: e.tensor_tensor(sc.bfB[:, 0:t], sc.x32[:], sc.bfA[:, 0:t], ALU.subtract))
        ctx.s("gpsimd", lambda e, oc=oc: e.dma_start(out=hspill_d[oc, 0], in_=sc.bfA[:, 0:t]), dma=True)
        ctx.s("gpsimd", lambda e, oc=oc: e.dma_start(out=hspill_d[oc, 1], in_=sc.bfB[:, 0:t]), dma=True)

    emit_proj(ctx, k, w1_d, xah, xal, NCH, NFF, w1_consumer, wbufs=wbufs)
    ctx.fence()
    ctx.sbuf_free("xal", "xah")

    f2r = ffn2_d.rearrange("(c p) t -> c p t", p=128)

    def w2_consumer(oc, ps):
        b2 = vecs[:, NCH + oc:NCH + oc + 1]
        ctx.s("scalar", lambda e, b2=b2: e.activation(
            sc.x32[:], ps[:, 0:t], AF.Identity, bias=b2, scale=1.0))
        ctx.s("gpsimd", lambda e, oc=oc: e.dma_start(out=f2r[oc], in_=sc.x32[:]), dma=True)

    emit_proj_ksweep(ctx, k, sc, w2_d, hspill_d[:, 0], hspill_d[:, 1], NFF, NCH,
                     w2_consumer, psW, wcol32, wcolh, wcoll, rbufh, rbufl)
    ctx.fence()

    x8 = ctx.sbuf("x8", [128, NCH * t], BF16)
    x1h = ctx.sbuf("x1h", [128, NCH * t], BF16)
    x1l = ctx.sbuf("x1l", [128, NCH * t], BF16)
    for c in range(NCH):
        ctx.s("sync", lambda e, c=c: e.dma_start(out=sc.x32[:], in_=ar[c]), dma=True)
        ctx.s("sync", lambda e, c=c: e.dma_start(out=sc.y32[:], in_=f2r[c]), dma=True)
        ctx.s("vector", lambda e: e.tensor_tensor(sc.x32[:], sc.x32[:], sc.y32[:], ALU.add))
        ctx.s("vector", lambda e, c=c: e.tensor_copy(x1h[:, c * t:(c + 1) * t], sc.x32[:]))
        ctx.s("vector", lambda e, c=c: e.tensor_tensor(
            x1l[:, c * t:(c + 1) * t], sc.x32[:], x1h[:, c * t:(c + 1) * t], ALU.subtract))
        ctx.fence()
    emit_ln(ctx, k, sc, x1h, x1l, x1h, x1l, psW[0], psW[1])
    ctx.fence()

    def l1qkv_consumer(oc, ps):
        if oc < NCH:
            nmh, nml, scale, o2 = "q1h", "q1l", RSQ_HD, oc
        elif oc < 2 * NCH:
            nmh, nml, scale, o2 = "k1h", "k1l", 1.0, oc - NCH
        else:
            nmh, nml, scale, o2 = "v1h", "v1l", 1.0, oc - 2 * NCH
        hr = q1[nmh].rearrange("(c p) t -> c p t", p=128)
        lr = q1[nml].rearrange("(c p) t -> c p t", p=128)
        if scale != 1.0:
            ctx.s("scalar", lambda e, scale=scale: e.activation(
                sc.x32[:], ps[:, 0:t], AF.Identity, bias=0.0, scale=scale))
            srcap = sc.x32[:]
        else:
            srcap = ps[:, 0:t]
        ctx.s("vector", lambda e, srcap=srcap: e.tensor_copy(sc.bfA[:, 0:t], srcap))
        ctx.s("vector", lambda e, srcap=srcap: e.tensor_tensor(
            sc.bfB[:, 0:t], srcap, sc.bfA[:, 0:t], ALU.subtract))
        ctx.s("gpsimd", lambda e, o2=o2, hr=hr: e.dma_start(out=hr[o2], in_=sc.bfA[:, 0:t]), dma=True)
        ctx.s("gpsimd", lambda e, o2=o2, lr=lr: e.dma_start(out=lr[o2], in_=sc.bfB[:, 0:t]), dma=True)

    if not vtm:
        emit_proj(ctx, k, wqkv_d, x1h, x1l, NCH, 3 * NCH, l1qkv_consumer, wbufs=wbufs)
    else:
        emit_proj(ctx, k, wqkv_d, x1h, x1l, NCH, 3 * NCH, l1qkv_consumer,
                  wbufs=wbufs, oc_list=range(2 * NCH))
        emit_vproj_tm(ctx, k, sc, wqkv_d, 2 * NCH, x1h, x1l, NCH, NCH,
                      dr["v1th_loc"], dr["v1tl_loc"],
                      [psW[0], psW[1], psW[2], psW[3]], wcol32, wcolh, wcoll)
    ctx.fence()

    ctx.s("sync", lambda e: e.dma_start(out=sclsb[0:1, 0:1], in_=ws_d), dma=True)
    ctx.s("sync", lambda e: e.dma_start(out=isbf[:], in_=is_d), dma=True)
    ctx.fence()
    ctx.s("vector", lambda e: e.tensor_copy(sclsb[0:1, 1:2], isbf[:]))
    ctx.s("vector", lambda e: e.tensor_tensor(
        sclsb[0:1, 2:3], sclsb[0:1, 0:1], sclsb[0:1, 1:2], ALU.mult))
    ctx.fence()
    ctx.s("tensor", lambda e: e.matmul(psW[1][:, 0:1], k.ones_row_f32, sclsb[0:1, 2:3], start=True, stop=True))
    ctx.s("vector", lambda e: e.tensor_copy(kcell[:], psW[1][:, 0:1]))
    ctx.fence()
    emit_fp8_quant(ctx, k, sc, x1h, x1l, x8, NCH, 1.0 / 0.75)
    ctx.fence()
    ctx.sbuf_free("x1l", "x1h")

    f1r = ffn1_d.rearrange("(c p) t -> c p t", p=128)
    w8r = w8_d.rearrange("(kc p) o -> kc p o", p=128)
    for oc in range(NCH):
        src = w8r[:, :, oc * 128:(oc + 1) * 128].transpose([1, 0, 2])
        ctx.s("gpsimd", lambda e, src=src: e.dma_start(
            out=wcolh[:, 0:NCH * 128].rearrange("p (kc f) -> p kc f", f=128), in_=src), dma=True)
        ctx.fence()
        for kc in range(NCH):
            for off, w in halves(t):
                st, sp = kc == 0, kc == NCH - 1
                ctx.s("tensor", lambda e, kc=kc, off=off, w=w, st=st, sp=sp: e.matmul(
                    psW[0][:, off:off + w], wcolh[:, kc * 128:(kc + 1) * 128],
                    x8[:, kc * t + off:kc * t + off + w], start=st, stop=sp))
        ctx.fence()
        ctx.s("vector", lambda e: e.tensor_scalar(
            sc.x32[:], psW[0][:, 0:t], kcell[:, 0:1], None, ALU.mult))
        ctx.s("gpsimd", lambda e, oc=oc: e.dma_start(out=f1r[oc], in_=sc.x32[:]), dma=True)
        ctx.fence()
    ctx.sbuf_free("x8")




def build_t1():
    ctx = Ctx()
    ctx.init_serial()
    t = TOK
    dr = {}
    dr["x"] = ctx.dram_in("x", [D, t])
    c_d = ctx.dram_in("consts", [128, 400])
    dr["xnh"] = ctx.dram_in("xnh", [D, t], BF16)
    dr["xnl"] = ctx.dram_in("xnl", [D, t], BF16)
    dr["fout"] = ctx.dram_in("fout", [D, t])
    dr["ath"] = ctx.dram_in("ath", [NCH, 128, t], BF16)
    dr["atl"] = ctx.dram_in("atl", [NCH, 128, t], BF16)
    dr["wo"] = ctx.dram_in("wo", [D, D])
    dr["w1"] = ctx.dram_in("w1", [D, DFF])
    dr["w2"] = ctx.dram_in("w2", [DFF, D])
    dr["wqkv_t"] = ctx.dram_in("wqkv_t", [D, 3 * D])
    dr["w8_t"] = ctx.dram_in("w8_t", [D, D], mybir.dt.float8e4)
    dr["wscale"] = ctx.dram_in("wscale", [1, 1])
    dr["iscale"] = ctx.dram_in("iscale", [1, 1], BF16)
    vec_d = ctx.dram_in("vecs", [(NCH * 4 + NFF) * 128])
    dr["accp"] = ctx.dram_out("accp", [D, t])
    dr["ffn2"] = ctx.dram_out("ffn2", [D, t])
    dr["ffn1"] = ctx.dram_out("ffn1", [D, t])
    for nm in ("q1h", "q1l", "k1h", "k1l", "v1h", "v1l"):
        dr[nm] = ctx.dram_out(nm, [D, t], BF16)
    dr["hspill"] = ctx.dram_tmp("hspill", [NFF, 2, 128, t], BF16)
    dr["xasph"] = ctx.dram_tmp("xasph", [NCH, 128, t], BF16)
    dr["xaspl"] = ctx.dram_tmp("xaspl", [NCH, 128, t], BF16)
    k = Consts(ctx)
    sc = Scratch(ctx)
    psW = [ctx.psum(f"psw{i}", [128, 1024], F32) for i in range(4)]
    vecs = ctx.sbuf("vecs_sb", [128, NCH * 4 + NFF], F32)
    wcol32 = ctx.sbuf("wcol32", [128, NCH * 128], F32)
    wcolh = ctx.sbuf("wcolh", [128, NCH * 128], BF16)
    wcoll = ctx.sbuf("wcoll", [128, NCH * 128], BF16)
    rbufh = ctx.sbuf("rbufh", [128, t], BF16)
    rbufl = ctx.sbuf("rbufl", [128, t], BF16)
    sclsb = ctx.sbuf("sclsb", [1, 4], F32)
    isbf = ctx.sbuf("isbf", [1, 1], BF16)
    kcell = ctx.sbuf("kcell", [128, 1], F32)
    k.load(ctx, c_d[:])
    emit_load_vec(ctx, vec_d, vecs, NCH * 4 + NFF)
    ctx.fence()
    emit_t1(ctx, dr, k, sc, psW, vecs, wcol32, wcolh, wcoll, rbufh, rbufl,
            sclsb, isbf, kcell)
    ctx.emit()
    return ctx



def emit_t2(ctx, dr, k, sc, psW, wcol32, wcolh, wcoll, rbufh, rbufl):
    t = TOK
    accp_d = dr["accp"]; ffn2_d = dr["ffn2"]; ffn1_d = dr["ffn1"]
    ath_d = dr["ath1"]; atl_d = dr["atl1"]; wo_d = dr["wo1"]; out_d = dr["x2"]
    acc = ctx.sbuf("t2acc", [128, NCH * t], F32)
    for drm in (accp_d, ffn2_d, ffn1_d):
        r = drm.rearrange("(c p) t -> c p t", p=128)
        for c in range(NCH):
            if drm is accp_d:
                ctx.s("sync", lambda e, c=c, r=r: e.dma_start(out=acc[:, c * t:(c + 1) * t], in_=r[c]), dma=True)
            else:
                ctx.s("sync", lambda e, c=c, r=r: e.dma_start(out=sc.x32[:], in_=r[c]), dma=True)
                ctx.s("vector", lambda e, c=c: e.tensor_tensor(
                    acc[:, c * t:(c + 1) * t], acc[:, c * t:(c + 1) * t], sc.x32[:], ALU.add))
            ctx.fence()
    outr = out_d.rearrange("(c p) t -> c p t", p=128)

    def consumer(oc, ps):
        ctx.s("vector", lambda e, oc=oc: e.tensor_tensor(
            acc[:, oc * t:(oc + 1) * t], acc[:, oc * t:(oc + 1) * t], ps[:, 0:t], ALU.add))
        ctx.s("gpsimd", lambda e, oc=oc: e.dma_start(
            out=outr[oc], in_=acc[:, oc * t:(oc + 1) * t]), dma=True)

    emit_proj_ksweep(ctx, k, sc, wo_d, ath_d, atl_d, NCH, NCH, consumer,
                     psW, wcol32, wcolh, wcoll, rbufh, rbufl)
    ctx.fence()
    ctx.sbuf_free("t2acc")


def build_t2():
    ctx = Ctx()
    ctx.init_serial()
    t = TOK
    dr = {}
    dr["accp"] = ctx.dram_in("accp", [D, t])
    dr["ffn2"] = ctx.dram_in("ffn2", [D, t])
    dr["ffn1"] = ctx.dram_in("ffn1", [D, t])
    c_d = ctx.dram_in("consts", [128, 400])
    dr["ath1"] = ctx.dram_in("ath", [NCH, 128, t], BF16)
    dr["atl1"] = ctx.dram_in("atl", [NCH, 128, t], BF16)
    dr["wo1"] = ctx.dram_in("wo1", [D, D])
    dr["x2"] = ctx.dram_out("x2", [D, t])
    k = Consts(ctx)
    sc = Scratch(ctx)
    psW = [ctx.psum(f"psw{i}", [128, 1024], F32) for i in range(4)]
    wcol32 = ctx.sbuf("wcol32", [128, 512], F32)
    wcolh = ctx.sbuf("wcolh", [128, 512], BF16)
    wcoll = ctx.sbuf("wcoll", [128, 512], BF16)
    rbufh = ctx.sbuf("rbufh", [128, t], BF16)
    rbufl = ctx.sbuf("rbufl", [128, t], BF16)
    k.load(ctx, c_d[:])
    emit_t2(ctx, dr, k, sc, psW, wcol32, wcolh, wcoll, rbufh, rbufl)
    ctx.emit()
    return ctx



_CACHE = {}


def _get(builder, name):
    if name not in _CACHE:
        ctx = builder()
        ctx.emitted = True
        _CACHE[name] = ctx
    return _CACHE[name]


def _bh_list(s_owner):
    h0 = 2 * s_owner
    return [(0, h0), (0, h0 + 1), (1, h0), (1, h0 + 1)]


def kernel(L0_attn_x, L0_attn_wq, L0_attn_bq, L0_attn_wk, L0_attn_bk,
           L0_attn_wv, L0_attn_bv, L0_attn_wo, L0_attn_bo, L0_attn_w1,
           L0_attn_b1, L0_attn_w2, L0_attn_b2, L0_attn_g1, L0_attn_be1,
           L0_attn_g2, L0_attn_be2, L0_ffn_w_gate, L0_ffn_w_up, L0_ffn_w_down,
           L1_attn_w_qkv, L1_attn_w_o, L1_ffn_weight, L1_ffn_weight_scale,
           L1_ffn_input_scale):
    import ml_dtypes
    consts = Consts.host_array()
    f32 = np.float32

    x = np.asarray(L0_attn_x, dtype=f32)
    vecs0 = np.concatenate([np.asarray(v, dtype=f32) for v in
                            (L0_attn_bq, L0_attn_bk, L0_attn_bv, L0_attn_g1, L0_attn_be1)])
    vecs1 = np.concatenate([np.asarray(v, dtype=f32) for v in
                            (L0_attn_bo, L0_attn_b2, L0_attn_g2, L0_attn_be2, L0_attn_b1)])
    wg_t = np.ascontiguousarray(np.asarray(L0_ffn_w_gate, dtype=f32).T)
    wu_t = np.ascontiguousarray(np.asarray(L0_ffn_w_up, dtype=f32).T)
    wd_t = np.ascontiguousarray(np.asarray(L0_ffn_w_down, dtype=f32).T)
    wqkv_t = np.ascontiguousarray(np.asarray(L1_attn_w_qkv, dtype=f32).T)
    wo1_t = np.ascontiguousarray(np.asarray(L1_attn_w_o, dtype=f32).T)
    w8_t = np.ascontiguousarray(
        np.asarray(L1_ffn_weight).view(np.uint8).T).view(ml_dtypes.float8_e4m3)
    wscale = np.asarray(L1_ffn_weight_scale, dtype=f32).reshape(1, 1)
    iscale = np.asarray(L1_ffn_input_scale).reshape(1, 1)

    t0 = _get(build_t0, "t0")
    in0 = []
    for c in range(NCORES):
        b, qd = divmod(c, 4)
        xs = np.ascontiguousarray(x[b, qd * TOK:(qd + 1) * TOK, :].T)
        in0.append({"x": xs, "consts": consts, "wq": np.asarray(L0_attn_wq, f32),
                    "wk": np.asarray(L0_attn_wk, f32), "wv": np.asarray(L0_attn_wv, f32),
                    "vecs": vecs0, "wg_t": wg_t, "wu_t": wu_t, "wd_t": wd_t})
    r0 = run_bass_kernel_spmd(t0.nc, in0, core_ids=list(range(NCORES)))
    o0 = r0.results

    att = _get(build_attn, "attn")
    masks0 = attn_masks_host(-1e9)

    def attn_inputs(outs, masks, names=("qh", "ql", "kh", "kl", "vh", "vl")):
        ins = []
        for s_own in range(NCORES):
            bh = _bh_list(s_own)
            d = {"consts": consts, "masks": masks}
            for nm_fm, nm_in in ((names[0], "qh"), (names[1], "ql"),
                                 (names[2], "kh"), (names[3], "kl")):
                arr = np.empty((4, 128, S), dtype=ml_dtypes.bfloat16)
                for i, (b, h) in enumerate(bh):
                    cores = [4 * b + j for j in range(4)]
                    arr[i] = np.concatenate(
                        [np.asarray(outs[cc][nm_fm])[h * 128:(h + 1) * 128, :] for cc in cores], axis=1)
                d[nm_in] = arr
            for nm_fm, nm_in in ((names[4], "vth"), (names[5], "vtl")):
                arr = np.empty((4, S, 128), dtype=ml_dtypes.bfloat16)
                for i, (b, h) in enumerate(bh):
                    cores = [4 * b + j for j in range(4)]
                    arr[i] = np.concatenate(
                        [np.ascontiguousarray(np.asarray(outs[cc][nm_fm])[h * 128:(h + 1) * 128, :].T)
                         for cc in cores], axis=0)
                d[nm_in] = arr
            ins.append(d)
        return ins

    rA0 = run_bass_kernel_spmd(att.nc, attn_inputs(o0, masks0), core_ids=list(range(NCORES)))
    oA0 = rA0.results

    def attn_out_relayout(oa):
        hs, ls = [], []
        for c in range(NCORES):
            b, qd = divmod(c, 4)
            q0 = qd * TOK
            ah = np.empty((NCH, 128, TOK), dtype=ml_dtypes.bfloat16)
            al = np.empty((NCH, 128, TOK), dtype=ml_dtypes.bfloat16)
            for h in range(16):
                owner = h // 2
                i = 2 * b + (h % 2)
                ah[h] = np.asarray(oa[owner]["ah"])[i][:, q0:q0 + TOK]
                al[h] = np.asarray(oa[owner]["al"])[i][:, q0:q0 + TOK]
            hs.append(ah)
            ls.append(al)
        return hs, ls

    ath0, atl0 = attn_out_relayout(oA0)

    t1 = _get(build_t1, "t1")
    in1 = []
    for c in range(NCORES):
        b, qd = divmod(c, 4)
        xs = np.ascontiguousarray(x[b, qd * TOK:(qd + 1) * TOK, :].T)
        in1.append({"x": xs, "consts": consts,
                    "xnh": np.asarray(o0[c]["xnh"]), "xnl": np.asarray(o0[c]["xnl"]),
                    "fout": np.asarray(o0[c]["fout"]),
                    "ath": ath0[c], "atl": atl0[c],
                    "wo": np.asarray(L0_attn_wo, f32), "w1": np.asarray(L0_attn_w1, f32),
                    "w2": np.asarray(L0_attn_w2, f32), "wqkv_t": wqkv_t,
                    "w8_t": w8_t, "wscale": wscale, "iscale": iscale, "vecs": vecs1})
    r1 = run_bass_kernel_spmd(t1.nc, in1, core_ids=list(range(NCORES)))
    o1 = r1.results

    masks1 = attn_masks_host(float(np.finfo(np.float32).min))
    rA1 = run_bass_kernel_spmd(
        att.nc, attn_inputs(o1, masks1, names=("q1h", "q1l", "k1h", "k1l", "v1h", "v1l")),
        core_ids=list(range(NCORES)))
    oA1 = rA1.results
    ath1, atl1 = attn_out_relayout(oA1)

    t2 = _get(build_t2, "t2")
    in2 = []
    for c in range(NCORES):
        in2.append({"accp": np.asarray(o1[c]["accp"]), "ffn2": np.asarray(o1[c]["ffn2"]),
                    "ffn1": np.asarray(o1[c]["ffn1"]), "consts": consts,
                    "ath": ath1[c], "atl": atl1[c], "wo1": wo1_t})
    r2 = run_bass_kernel_spmd(t2.nc, in2, core_ids=list(range(NCORES)))
    o2 = r2.results

    out = np.empty((B, S, D), dtype=np.float32)
    for c in range(NCORES):
        b, qd = divmod(c, 4)
        out[b, qd * TOK:(qd + 1) * TOK, :] = np.asarray(o2[c]["x2"], dtype=np.float32).T
    return out


def emit_vproj_tm(ctx, k, sc, w_dram, oc0, x_hi, x_lo, n_in_ch, n_out_ch,
                  vth_d, vtl_d, psW, wbuf32, wbufh, wbufl, bias_col=None, tok=TOK):
    t = tok
    dout = n_out_ch * 128
    wr = w_dram
    for tt in range(t // 128):
        for kc in range(n_in_ch):
            ctx.s("sync", lambda e, kc=kc: e.dma_start(
                out=wbuf32[:, 0:dout], in_=wr[kc * 128:(kc + 1) * 128, oc0 * 128:oc0 * 128 + dout]), dma=True)
            ctx.s("vector", lambda e: e.tensor_copy(wbufh[:, 0:dout], wbuf32[:, 0:dout]))
            ctx.s("vector", lambda e: e.tensor_tensor(
                wbufl[:, 0:dout], wbuf32[:, 0:dout], wbufh[:, 0:dout], ALU.subtract))
            ctx.fence()
            st, sp = kc == 0, kc == n_in_ch - 1
            lh = x_hi[:, kc * t + tt * 128: kc * t + (tt + 1) * 128]
            ll = x_lo[:, kc * t + tt * 128: kc * t + (tt + 1) * 128]
            for dh in range(ceil_div(dout, 512)):
                w = min(512, dout - dh * 512)
                ctx.s("tensor", lambda e, lh=lh, dh=dh, w=w, st=st: e.matmul(
                    psW[dh % 4][:, 0:w], lh, wbufh[:, dh * 512:dh * 512 + w], start=st, stop=False))
                ctx.s("tensor", lambda e, lh=lh, dh=dh, w=w: e.matmul(
                    psW[dh % 4][:, 0:w], lh, wbufl[:, dh * 512:dh * 512 + w], start=False, stop=False))
                ctx.s("tensor", lambda e, ll=ll, dh=dh, w=w, sp=sp: e.matmul(
                    psW[dh % 4][:, 0:w], ll, wbufh[:, dh * 512:dh * 512 + w], start=False, stop=sp))
            ctx.fence()
        for dh in range(ceil_div(dout, 512)):
            w = min(512, dout - dh * 512)
            if bias_col is not None:
                ctx.s("vector", lambda e, dh=dh, w=w: e.tensor_tensor(
                    sc.x32[:, 0:w], psW[dh % 4][:, 0:w], bias_col[:, dh * 512:dh * 512 + w], ALU.add))
            else:
                ctx.s("vector", lambda e, dh=dh, w=w: e.tensor_copy(sc.x32[:, 0:w], psW[dh % 4][:, 0:w]))
            ctx.s("vector", lambda e, dh=dh, w=w: e.tensor_copy(sc.bfA[:, 0:w], sc.x32[:, 0:w]))
            ctx.s("vector", lambda e, dh=dh, w=w: e.tensor_tensor(
                sc.bfB[:, 0:w], sc.x32[:, 0:w], sc.bfA[:, 0:w], ALU.subtract))
            ctx.s("gpsimd", lambda e, tt=tt, dh=dh, w=w: e.dma_start(
                out=vth_d[tt][:, dh * 512:dh * 512 + w], in_=sc.bfA[:, 0:w]), dma=True)
            ctx.s("gpsimd", lambda e, tt=tt, dh=dh, w=w: e.dma_start(
                out=vtl_d[tt][:, dh * 512:dh * 512 + w], in_=sc.bfB[:, 0:w]), dma=True)
            ctx.fence()



BASE_PERM = [0, 1, 2, 3, 6, 7, 4, 5]


def _sigma(core, d):
    return BASE_PERM[BASE_PERM[core] ^ d]


def _rdests(d):
    return [(0, d)] * 8 if d < 4 else [None] * 4 + [(0, d)] * 4


class Exch:
    def __init__(self, ctx, xidx_sb):
        self.rsem = ctx.sem("xrsem")
        self.lsem = ctx.sem("xlsem")
        self.prep = ctx.sem("xprep")
        self.bar = ctx.sem("xbar")
        self.xidx = xidx_sb
        self.col = 0

    def idx(self):
        c = self.col
        self.col += 1
        return self.xidx[:, c:c + 1]

    def barrier(self, ctx):
        ctx.fence()
        ctx.s("gpsimd", lambda e: e.remote_sem_update_broadcast(
            self.bar.h, self.lsem.h, rdests=[(0, dd) for dd in range(8)]))
        self.prep.val += 1
        ctx.ops[-1][3].append((self.prep, 1))
        ctx.s("gpsimd", lambda e: e.trigger_dma(count=1))
        ctx.ops[-1][2].append((self.prep, self.prep.val))
        self.lsem.val += 16
        self.bar.val += 16
        ctx.s("gpsimd", lambda e: e.engine_nop())
        ctx.ops[-1][2].append((self.bar, self.bar.val))
        ctx.fence()

    def send(self, ctx, d, dst_ap, src_ap):
        ctx.fence()
        ctx.s("gpsimd", lambda e, d=d, dst_ap=dst_ap, src_ap=src_ap: e.remote_dma_broadcast(
            dst_ap, src_ap, self.rsem.h, self.lsem.h, rdests=_rdests(d)))
        self.prep.val += 1
        ctx.ops[-1][3].append((self.prep, 1))
        ctx.s("gpsimd", lambda e: e.trigger_dma(count=1))
        ctx.ops[-1][2].append((self.prep, self.prep.val))
        self.lsem.val += 16
        self.rsem.val += 16 if d < 4 else 8

    def wait_recv(self, ctx):
        ctx.s("gpsimd", lambda e: e.engine_nop())
        ctx.ops[-1][2].append((self.rsem, self.rsem.val))
        ctx.fence()

    def wait_sent(self, ctx):
        ctx.s("gpsimd", lambda e: e.engine_nop())
        ctx.ops[-1][2].append((self.lsem, self.lsem.val))
        ctx.fence()


def _ind_gather(ctx, ex, out_ap, flat_dram):
    iap = ex.idx()
    ctx.s("gpsimd", lambda e, out_ap=out_ap, flat_dram=flat_dram, iap=iap:
          e.indirect_dma_start(out_ap, None, flat_dram,
                               bass.IndirectOffsetOnAxis(iap, 0)), dma=True)


def _ind_scatter(ctx, ex, flat_dram, in_ap):
    iap = ex.idx()
    ctx.s("gpsimd", lambda e, flat_dram=flat_dram, in_ap=in_ap, iap=iap:
          e.indirect_dma_start(flat_dram, bass.IndirectOffsetOnAxis(iap, 0),
                               in_ap, None), dma=True)


def emit_exchange_e1(ctx, k, sc, ex, qkv_local, vt_local, gather, stg, slots):
    t = TOK
    qk_flat = {nm: qkv_local[nm].rearrange("(r p) t -> (r p) t", p=128)
               for nm in ("qh", "ql", "kh", "kl")}
    vt_flat = [vt.rearrange("a p (g d) -> (a p g) d", d=256) for vt in vt_local]
    g_flat = {nm: gather[nm].rearrange("b p (q t) -> (b p q) t", t=t)
              for nm in ("qh", "ql", "kh", "kl")}
    gv_flat = [gather[nm].rearrange("b (r p) d -> (b r p) d", p=128)
               for nm in ("vth", "vtl")]
    for d in range(8):
        ex.barrier(ctx)
        for j2, nm in enumerate(("qh", "ql", "kh", "kl")):
            for j in range(2):
                _ind_gather(ctx, ex, stg[:, (2 * j2 + j) * t:(2 * j2 + j + 1) * t], qk_flat[nm])
        for hl in range(2):
            for tt in range(8):
                _ind_gather(ctx, ex, stg[:, 8 * t + hl * 2048 + tt * 256:
                                         8 * t + hl * 2048 + (tt + 1) * 256], vt_flat[hl])
        ctx.fence()
        par = d % 2
        ex.send(ctx, d, slots[:, par * 12288:(par + 1) * 12288], stg[:, 0:12288])
        ex.wait_recv(ctx)
        for j2, nm in enumerate(("qh", "ql", "kh", "kl")):
            for j in range(2):
                _ind_scatter(ctx, ex, g_flat[nm],
                             slots[:, par * 12288 + (2 * j2 + j) * t:
                                   par * 12288 + (2 * j2 + j + 1) * t])
        for hl in range(2):
            for j in range(2):
                for tt in range(8):
                    c0 = par * 12288 + 8 * t + hl * 2048 + tt * 256 + j * 128
                    _ind_scatter(ctx, ex, gv_flat[hl], slots[:, c0:c0 + 128])
        ctx.fence()
        ex.wait_sent(ctx)


def emit_exchange_e2(ctx, k, sc, ex, a_local, gather, stg, slots):
    t = TOK
    a_flat = [a.rearrange("b p (q t) -> (b p q) t", t=t) for a in a_local]
    g_flat = [g.rearrange("c p t -> (c p) t") for g in gather]
    for d in range(8):
        ex.barrier(ctx)
        for hl in range(2):
            for j in range(2):
                _ind_gather(ctx, ex, stg[:, (2 * hl + j) * t:(2 * hl + j + 1) * t], a_flat[hl])
        ctx.fence()
        par = d % 2
        ex.send(ctx, d, slots[:, par * 4096:(par + 1) * 4096], stg[:, 0:4096])
        ex.wait_recv(ctx)
        for hl in range(2):
            for j in range(2):
                _ind_scatter(ctx, ex, g_flat[hl],
                             slots[:, par * 4096 + (2 * hl + j) * t:
                                   par * 4096 + (2 * hl + j + 1) * t])
        ctx.fence()
        ex.wait_sent(ctx)


def build_xidx_host(core):
    cols = []
    part = np.arange(128, dtype=np.uint32)

    def e1():
        for d in range(8):
            p = _sigma(core, d)
            for j2 in range(4):
                for j in range(2):
                    cols.append((2 * p + j) * 128 + part)
            for hl in range(2):
                for tt in range(8):
                    cols.append((tt * 128 + part) * 8 + p)
            bp, tq = divmod(p, 4) if False else (p // 4, p % 4)
            for j2 in range(4):
                for j in range(2):
                    cols.append((2 * bp + j) * 512 + part * 4 + tq)
            for hl in range(2):
                for j in range(2):
                    for tt in range(8):
                        cols.append(np.full(128, (2 * bp + j) * 4096 + tq * 1024 + tt * 128, np.uint32) + part)

    def e2():
        for d in range(8):
            p = _sigma(core, d)
            bp, tq = p // 4, p % 4
            for hl in range(2):
                for j in range(2):
                    cols.append((2 * bp + j) * 512 + part * 4 + tq)
            for hl in range(2):
                for j in range(2):
                    cols.append((2 * p + j) * 128 + part)

    e1(); e2(); e1(); e2()
    return np.stack(cols).astype(np.uint32)


N_XIDX = 4 * 8 * (8 + 16) // 1 if False else None



def build_fused(upto=99, vtm=True, dbg_gathers=False):
    from concourse import library_config
    ctx = Ctx()
    ctx.init_serial()
    t = TOK
    dr = {}
    dr["x"] = ctx.dram_in("x", [D, t])
    c_d = ctx.dram_in("consts", [128, 400])
    m_d = ctx.dram_in("masks2", [8, 128, QT])
    pt_d = ctx.dram_in("xidx", [128, 1152], mybir.dt.uint32)
    for nm, sh in (("wq", [D, D]), ("wk", [D, D]), ("wv", [D, D]),
                   ("wo", [D, D]), ("w1", [D, DFF]), ("w2", [DFF, D]),
                   ("wg_t", [D, DFF]), ("wu_t", [D, DFF]), ("wd_t", [DFF, D]),
                   ("wqkv_t", [D, 3 * D]), ("wo1", [D, D])):
        dr[nm] = ctx.dram_in(nm, sh)
    dr["w8_t"] = ctx.dram_in("w8_t", [D, D], mybir.dt.float8e4)
    dr["wscale"] = ctx.dram_in("wscale", [1, 1])
    dr["iscale"] = ctx.dram_in("iscale", [1, 1], BF16)
    v0_d = ctx.dram_in("vecs0", [5 * D])
    v1_d = ctx.dram_in("vecs1", [(NCH * 4 + NFF) * 128])
    for nm in ("xnh", "xnl", "qh", "ql", "kh", "kl"):
        dr[nm] = ctx.dram_tmp(nm, [D, t], BF16)
    dr["vth_loc"] = ctx.dram_tmp("vth_loc", [8, 128, D], BF16)
    dr["vh"] = ctx.dram_tmp("vh", [D, TOK], BF16)
    dr["vl"] = ctx.dram_tmp("vl", [D, TOK], BF16)
    dr["vtl_loc"] = ctx.dram_tmp("vtl_loc", [8, 128, D], BF16)
    dr["fout"] = ctx.dram_tmp("fout", [D, t])
    dr["ispill"] = ctx.dram_tmp("ispill", [NFF, 2, 128, t], BF16)
    dr["zqspill"] = ctx.dram_tmp("zqspill", [NFF, 128, t], BF16)
    mk = ctx.dram_out if dbg_gathers else ctx.dram_tmp
    for nm in ("g_qh", "g_ql", "g_kh", "g_kl", "ah", "al",
               "g1_qh", "g1_ql", "g1_kh", "g1_kl", "ah1", "al1"):
        dr[nm] = mk(nm, [4, 128, S], BF16)
    for nm in ("g_vth", "g_vtl", "g1_vth", "g1_vtl"):
        dr[nm] = mk(nm, [4, S, 128], BF16)
    for nm in ("ath", "atl", "ath1", "atl1"):
        dr[nm] = ctx.dram_tmp(nm, [NCH, 128, t], BF16)
    dr["accp"] = ctx.dram_tmp("accp", [D, t])
    dr["ffn2"] = ctx.dram_tmp("ffn2", [D, t])
    dr["ffn1"] = ctx.dram_tmp("ffn1", [D, t])
    for nm in ("q1h", "q1l", "k1h", "k1l"):
        dr[nm] = ctx.dram_tmp(nm, [D, t], BF16)
    dr["q1vh"] = dr["q1h"]
    dr["v1h"] = ctx.dram_tmp("v1h", [D, t], BF16)
    dr["v1l"] = ctx.dram_tmp("v1l", [D, t], BF16)
    dr["v1th_loc"] = ctx.dram_tmp("v1th_loc", [8, 128, D], BF16)
    dr["v1tl_loc"] = ctx.dram_tmp("v1tl_loc", [8, 128, D], BF16)
    dr["hspill"] = ctx.dram_tmp("hspill", [NFF, 2, 128, t], BF16)
    dr["xasph"] = ctx.dram_tmp("xasph", [NCH, 128, t], BF16)
    dr["xaspl"] = ctx.dram_tmp("xaspl", [NCH, 128, t], BF16)
    dr["x2"] = ctx.dram_out("x2", [D, t])

    k = Consts(ctx)
    sc = Scratch(ctx)
    psW = [ctx.psum(f"psw{i}", [128, 1024], F32) for i in range(4)]
    psA, psB, psC = psW[0], psW[1], psW[2]
    psD = psW[3][:, 0:512]
    psE = psW[3][:, 512:1024]

    vecs0 = ctx.sbuf("vecs0_sb", [128, 5 * NCH], F32)
    vecs1 = ctx.sbuf("vecs1_sb", [128, NCH * 4 + NFF], F32)
    cellcols = ctx.sbuf("cellcols", [128, 8], F32)
    maskssb = ctx.sbuf("maskssb", [128, 8 * QT], F32)
    ptab = ctx.sbuf("xidx_sb", [128, 1152], mybir.dt.uint32)
    rbufh = ctx.sbuf("rbufh", [128, t], BF16)
    rbufl = ctx.sbuf("rbufl", [128, t], BF16)
    sclsb = ctx.sbuf("sclsb", [1, 4], F32)
    isbf = ctx.sbuf("isbf", [1, 1], BF16)
    kcell = ctx.sbuf("kcell", [128, 1], F32)

    ex = Exch(ctx, ptab)

    ctx.s("gpsimd", lambda e: (e.load_library(library_config.proxy), e.engine_nop())[1])
    k.load(ctx, c_d[:])
    emit_load_vec(ctx, v0_d, vecs0, 5 * NCH)
    emit_load_vec(ctx, v1_d, vecs1, NCH * 4 + NFF)
    ctx.s("sync", lambda e: e.dma_start(
        out=maskssb[:].rearrange("p (i f) -> p i f", f=QT), in_=m_d.transpose([1, 0, 2])), dma=True)
    ctx.s("sync", lambda e: e.dma_start(out=ptab[:], in_=pt_d), dma=True)
    ctx.fence()

    if upto <= -1:
        return _finish_fused(ctx, dr)
    wcol32 = ctx.sbuf("wcol32", [128, NCH * 128], F32)
    wcolh = ctx.sbuf("wcolh", [128, NCH * 128], BF16)
    wcoll = ctx.sbuf("wcoll", [128, NCH * 128], BF16)
    emit_t0(ctx, dr, k, sc, psA, psB, psC, psD, psE, vecs0, cellcols,
            wcol32, wcolh, wcoll, vtm=vtm)
    ctx.sbuf_free("wcoll", "wcolh", "wcol32")
    if upto <= 0:
        return _finish_fused(ctx, dr)

    stg = ctx.sbuf("xstg", [128, 12288], BF16)
    slots = ctx.sbuf("xslots", [128, 2 * 12288], BF16)
    qkv_local = {nm: dr[nm] for nm in ("qh", "ql", "kh", "kl")}
    gather0 = {"qh": dr["g_qh"], "ql": dr["g_ql"], "kh": dr["g_kh"],
               "kl": dr["g_kl"], "vth": dr["g_vth"], "vtl": dr["g_vtl"]}
    emit_exchange_e1(ctx, k, sc, ex, qkv_local, (dr["vth_loc"], dr["vtl_loc"]),
                     gather0, stg, slots)
    ctx.sbuf_free("xslots", "xstg")
    if upto <= 1:
        return _finish_fused(ctx, dr)

    adr = dict(qh=dr["g_qh"], ql=dr["g_ql"], kh=dr["g_kh"], kl=dr["g_kl"],
               vth=dr["g_vth"], vtl=dr["g_vtl"], ah=dr["ah"], al=dr["al"])
    emit_attn(ctx, adr, k, sc, psD, psE, psA[:, 0:512], psB[:, 0:512], maskssb[:, 0:4 * QT])
    if upto <= 2:
        return _finish_fused(ctx, dr)

    stg = ctx.sbuf("xstg2", [128, 4096], BF16)
    slots = ctx.sbuf("xslots2", [128, 8192], BF16)
    emit_exchange_e2(ctx, k, sc, ex, (dr["ah"], dr["al"]), (dr["ath"], dr["atl"]), stg, slots)
    ctx.sbuf_free("xslots2", "xstg2")
    if upto <= 3:
        return _finish_fused(ctx, dr)

    wcol32 = ctx.sbuf("wcol32b", [128, NCH * 128], F32)
    wcolh = ctx.sbuf("wcolhb", [128, NCH * 128], BF16)
    wcoll = ctx.sbuf("wcollb", [128, NCH * 128], BF16)
    emit_t1(ctx, dr, k, sc, psW, vecs1, wcol32, wcolh, wcoll, rbufh, rbufl,
            sclsb, isbf, kcell, vtm=True)
    ctx.sbuf_free("wcollb", "wcolhb", "wcol32b")
    if upto <= 4:
        return _finish_fused(ctx, dr)

    stg = ctx.sbuf("xstg3", [128, 12288], BF16)
    slots = ctx.sbuf("xslots3", [128, 2 * 12288], BF16)
    qkv_local1 = {"qh": dr["q1h"], "ql": dr["q1l"], "kh": dr["k1h"], "kl": dr["k1l"]}
    gather1 = {"qh": dr["g1_qh"], "ql": dr["g1_ql"], "kh": dr["g1_kh"],
               "kl": dr["g1_kl"], "vth": dr["g1_vth"], "vtl": dr["g1_vtl"]}
    emit_exchange_e1(ctx, k, sc, ex, qkv_local1, (dr["v1th_loc"], dr["v1tl_loc"]),
                     gather1, stg, slots)
    ctx.sbuf_free("xslots3", "xstg3")
    if upto <= 5:
        return _finish_fused(ctx, dr)

    adr1 = dict(qh=dr["g1_qh"], ql=dr["g1_ql"], kh=dr["g1_kh"], kl=dr["g1_kl"],
                vth=dr["g1_vth"], vtl=dr["g1_vtl"], ah=dr["ah1"], al=dr["al1"])
    emit_attn(ctx, adr1, k, sc, psD, psE, psA[:, 0:512], psB[:, 0:512], maskssb[:, 4 * QT:8 * QT])
    if upto <= 6:
        return _finish_fused(ctx, dr)

    stg = ctx.sbuf("xstg4", [128, 4096], BF16)
    slots = ctx.sbuf("xslots4", [128, 8192], BF16)
    emit_exchange_e2(ctx, k, sc, ex, (dr["ah1"], dr["al1"]), (dr["ath1"], dr["atl1"]), stg, slots)
    ctx.sbuf_free("xslots4", "xstg4")
    if upto <= 7:
        return _finish_fused(ctx, dr)

    wcol32 = ctx.sbuf("wcol32c", [128, 512], F32)
    wcolh = ctx.sbuf("wcolhc", [128, 512], BF16)
    wcoll = ctx.sbuf("wcollc", [128, 512], BF16)
    emit_t2(ctx, dr, k, sc, psW, wcol32, wcolh, wcoll, rbufh, rbufl)
    ctx.sbuf_free("wcollc", "wcolhc", "wcol32c")
    ctx.emit()
    mybir.codegen_inst_isa_subclasses(ctx.nc)
    return ctx


def _finish_fused(ctx, dr):
    t = TOK
    sc0 = ctx.sbuf("fin_z", [128, t], F32)
    outr = dr["x2"].rearrange("(c p) t -> c p t", p=128)
    ctx.s("vector", lambda e: e.memset(sc0[:], 0.0))
    for c in range(NCH):
        ctx.s("gpsimd", lambda e, c=c: e.dma_start(out=outr[c], in_=sc0[:]), dma=True)
        ctx.fence()
    ctx.emit()
    mybir.codegen_inst_isa_subclasses(ctx.nc)
    return ctx

USE_FUSED = True


def kernel_fused(L0_attn_x, L0_attn_wq, L0_attn_bq, L0_attn_wk, L0_attn_bk,
                 L0_attn_wv, L0_attn_bv, L0_attn_wo, L0_attn_bo, L0_attn_w1,
                 L0_attn_b1, L0_attn_w2, L0_attn_b2, L0_attn_g1, L0_attn_be1,
                 L0_attn_g2, L0_attn_be2, L0_ffn_w_gate, L0_ffn_w_up,
                 L0_ffn_w_down, L1_attn_w_qkv, L1_attn_w_o, L1_ffn_weight,
                 L1_ffn_weight_scale, L1_ffn_input_scale):
    import ml_dtypes
    consts = Consts.host_array()
    f32 = np.float32
    x = np.asarray(L0_attn_x, dtype=f32)
    vecs0 = np.concatenate([np.asarray(v, dtype=f32) for v in
                            (L0_attn_bq, L0_attn_bk, L0_attn_bv, L0_attn_g1, L0_attn_be1)])
    vecs1 = np.concatenate([np.asarray(v, dtype=f32) for v in
                            (L0_attn_bo, L0_attn_b2, L0_attn_g2, L0_attn_be2, L0_attn_b1)])
    masks2 = np.concatenate([attn_masks_host(-1e9),
                             attn_masks_host(float(np.finfo(np.float32).min))], axis=0)
    wg_t = np.ascontiguousarray(np.asarray(L0_ffn_w_gate, dtype=f32).T)
    wu_t = np.ascontiguousarray(np.asarray(L0_ffn_w_up, dtype=f32).T)
    wd_t = np.ascontiguousarray(np.asarray(L0_ffn_w_down, dtype=f32).T)
    wqkv_t = np.ascontiguousarray(np.asarray(L1_attn_w_qkv, dtype=f32).T)
    wo1_t = np.ascontiguousarray(np.asarray(L1_attn_w_o, dtype=f32).T)
    w8_t = np.ascontiguousarray(
        np.asarray(L1_ffn_weight).view(np.uint8).T).view(ml_dtypes.float8_e4m3)
    wscale = np.asarray(L1_ffn_weight_scale, dtype=f32).reshape(1, 1)
    iscale = np.asarray(L1_ffn_input_scale).reshape(1, 1)

    fz = _get(build_fused, "fused")
    ins = []
    for c in range(NCORES):
        b, qd = divmod(c, 4)
        xs = np.ascontiguousarray(x[b, qd * TOK:(qd + 1) * TOK, :].T)
        ins.append({"x": xs, "consts": consts, "masks2": masks2,
                    "xidx": np.ascontiguousarray(build_xidx_host(c).T),
                    "wq": np.asarray(L0_attn_wq, f32), "wk": np.asarray(L0_attn_wk, f32),
                    "wv": np.asarray(L0_attn_wv, f32), "wo": np.asarray(L0_attn_wo, f32),
                    "w1": np.asarray(L0_attn_w1, f32), "w2": np.asarray(L0_attn_w2, f32),
                    "wg_t": wg_t, "wu_t": wu_t, "wd_t": wd_t,
                    "wqkv_t": wqkv_t, "wo1": wo1_t, "w8_t": w8_t,
                    "wscale": wscale, "iscale": iscale,
                    "vecs0": vecs0, "vecs1": vecs1})
    res = run_bass_kernel_spmd(fz.nc, ins, core_ids=list(range(NCORES)))
    out = np.empty((B, S, D), dtype=np.float32)
    for c in range(NCORES):
        b, qd = divmod(c, 4)
        out[b, qd * TOK:(qd + 1) * TOK, :] = np.asarray(res.results[c]["x2"], dtype=np.float32).T
    return out


_kernel_multilaunch = kernel


def kernel(**inputs):
    if USE_FUSED:
        return kernel_fused(**inputs)
    return _kernel_multilaunch(**inputs)


# version 08
# speedup vs baseline: 1.3268x; 1.3268x over previous
import math
import numpy as np
import concourse.bass as bass
import concourse.mybir as mybir
from concourse.bass_utils import run_bass_kernel_spmd

F32 = mybir.dt.float32
BF16 = mybir.dt.bfloat16
I32 = mybir.dt.int32
AF = mybir.ActivationFunctionType
ALU = mybir.AluOpType
AX = mybir.AxisListType

NCORES = 8
B, S, D = 2, 4096, 2048
TOK = 1024
NCH = D // 128
DFF = 8192
NFF = DFF // 128
HD = 128
LN_EPS = 1e-5
RSQ_HD = 1.0 / math.sqrt(float(HD))
FWHT_NORM = 1.0 / math.sqrt(float(DFF))



class Sem:
    def __init__(self, handle, name):
        self.h = handle
        self.name = name
        self.val = 0


class Ctx:
    ENGINES = ("sync", "tensor", "vector", "scalar", "gpsimd")

    def __init__(self):
        self.nc = bass.Bass("TRN2", target_bir_lowering=False)
        self._cm = []
        self.sems = {}
        self.ops = []
        self.dram = {}
        self._uid = 0
        self._last = None
        self.sq = None

    def init_serial(self):
        self.sq = self.sem("sq")

    def uid(self):
        self._uid += 1
        return self._uid

    def dram_in(self, name, shape, dt=F32):
        t = self.nc.dram_tensor(name, list(shape), dt, kind="ExternalInput").ap()
        self.dram[name] = t
        return t

    def dram_out(self, name, shape, dt=F32):
        t = self.nc.dram_tensor(name, list(shape), dt, kind="ExternalOutput").ap()
        self.dram[name] = t
        return t

    def dram_tmp(self, name, shape, dt=F32):
        t = self.nc.dram_tensor(name, list(shape), dt, kind="Internal").ap()
        self.dram[name] = t
        return t

    def sbuf(self, name, shape, dt=F32):
        cm = self.nc.sbuf_tensor(name, list(shape), dt)
        h = cm.__enter__()
        self._cm.append(cm)
        self._sbuf_cms = getattr(self, "_sbuf_cms", {})
        self._sbuf_cms[name] = cm
        return h

    def sbuf_free(self, *names):
        for name in names:
            cm = self._sbuf_cms.pop(name)
            cm.__exit__(None, None, None)
            self._cm.remove(cm)

    def psum(self, name, shape, dt=F32):
        cm = self.nc.psum_tensor(name, list(shape), dt)
        h = cm.__enter__()
        self._cm.append(cm)
        return h

    def sem(self, name):
        if name not in self.sems:
            cm = self.nc.semaphore(name)
            h = cm.__enter__()
            self._cm.append(cm)
            self.sems[name] = Sem(h, name)
        return self.sems[name]

    FULL_SERIAL = True
    COALESCE = ("tensor",)

    def s(self, eng, fn, dma=False):
        rec = [eng, fn, [], [], dma]
        if self._last is not None:
            lrec = self._last
            if (lrec[0] == eng and not lrec[4] and not dma
                    and (eng in self.COALESCE or not self.FULL_SERIAL)):
                pass
            else:
                self._ensure_inc(lrec)
                rec[2].append((self.sq, self.sq.val))
        elif self.ops:
            rec[2].append((self.sq, self.sq.val))
        if dma:
            self.sq.val += 16
            rec[3].append((self.sq, 16))
        self.ops.append(rec)
        self._last = rec
        return rec

    def _ensure_inc(self, rec):
        if not rec[3]:
            self.sq.val += 1
            rec[3].append((self.sq, 1))

    def fence(self):
        if self._last is not None:
            self._ensure_inc(self._last)
            self._last = None

    def emit(self):
        if self._last is not None:
            self._ensure_inc(self._last)
        ops = self.ops
        with self.nc.Block() as blk:
            def runner(name):
                def run(eng):
                    for eng_name, fn, waits, incs, dma in ops:
                        if eng_name != name:
                            continue
                        for s, v in waits:
                            eng.wait_ge(s.h, v)
                        if fn is None:
                            for s, k in incs:
                                eng.sem_inc(s.h, k)
                        else:
                            instr = fn(eng)
                            for s, k in incs:
                                instr.then_inc(s.h, k)
                    eng.wait_ge(self.sq.h, self.sq.val)
                return run
            blk.sync(runner("sync"))
            blk.tensor(runner("tensor"))
            blk.vector(runner("vector"))
            blk.scalar(runner("scalar"))
            blk.gpsimd(runner("gpsimd"))

    def close(self):
        for cm in reversed(self._cm):
            cm.__exit__(None, None, None)
        self._cm = []


def ceil_div(a, b):
    return -(-a // b)


def halves(n, w=512):
    return [(i, min(w, n - i)) for i in range(0, n, w)]



class Consts:

    def __init__(self, ctx):
        self.c32 = ctx.sbuf("consts32", [128, 400], F32)
        self.ident_f32 = self.c32[:, 0:128]
        self.ones_col_bf = ctx.sbuf("ones_col_bf", [128, 1], BF16)
        self.ident_bf = ctx.sbuf("ident_bf", [128, 128], BF16)
        self.h128_bf = ctx.sbuf("h128_bf", [128, 128], BF16)
        self.ones_row_f32 = self.c32[0:1, 256:384]
        self.eps_col = self.c32[:, 385:386]
        self.eps_1 = self.c32[0:1, 384:385]
        import math as _m
        ln2 = _m.log(2.0)
        for col, val in ((386, 3.0 * ln2), (387, -3.0 * ln2), (388, 0.0), (389, 1.0)):
            ctx.nc.const_aps.register(F32, float(np.float32(val)), self.c32[:, col:col + 1])

    def load(self, ctx, dram_c):
        ctx.s("sync", lambda e: e.dma_start(out=self.c32[:], in_=dram_c), dma=True)
        ctx.s("vector", lambda e: e.tensor_copy(self.ident_bf[:], self.c32[:, 0:128]))
        ctx.s("vector", lambda e: e.tensor_copy(self.h128_bf[:], self.c32[:, 128:256]))
        ctx.s("vector", lambda e: e.tensor_copy(self.ones_col_bf[:], self.c32[:, 256:257]))
        ctx.fence()

    @staticmethod
    def host_array():
        c = np.zeros((128, 400), dtype=np.float32)
        c[:, 0:128] = np.eye(128, dtype=np.float32)
        h = np.array([[1.0]], dtype=np.float32)
        while h.shape[0] < 128:
            h = np.block([[h, h], [h, -h]])
        c[:, 128:256] = h
        c[:, 256] = 1.0
        c[0, 256:384] = 1.0
        c[0, 384] = LN_EPS
        c[:, 385] = LN_EPS
        import math as _m
        ln2 = _m.log(2.0)
        c[:, 386] = np.float32(3.0 * ln2)
        c[:, 387] = np.float32(-3.0 * ln2)
        c[:, 388] = 0.0
        c[:, 389] = 1.0
        return c


class Scratch:
    def __init__(self, ctx, tok=TOK):
        t = tok
        self.x32 = ctx.sbuf("sc_x32", [128, t], F32)
        self.y32 = ctx.sbuf("sc_y32", [128, t], F32)
        self.z32 = ctx.sbuf("sc_z32", [128, t], F32)
        self.bfA = ctx.sbuf("sc_bfA", [128, 2 * t], BF16)
        self.bfB = ctx.sbuf("sc_bfB", [128, 2 * t], BF16)
        self.i32 = ctx.sbuf("sc_i32", [128, t], I32)
        self.rows = ctx.sbuf("sc_rows", [1, 8 * t], F32)
        self.bc = ctx.sbuf("sc_bc", [128, 2 * t], F32)


def bcast_row(ctx, k, row_ap, out_ap, ps, n):
    for off, w in halves(n):
        ctx.s("tensor", lambda e, off=off, w=w: e.matmul(
            ps[:, 0:w], k.ones_row_f32, row_ap[0:1, off:off + w],
            start=True, stop=True))
        ctx.s("vector", lambda e, off=off, w=w: e.tensor_copy(
            out_ap[:, off:off + w], ps[:, 0:w]))



def emit_ln(ctx, k, sc, pair_hi, pair_lo, out_hi, out_lo, ps_a, ps_b,
            gamma=None, beta=None, nch=NCH, tok=TOK, out32_cb=None):
    t = tok
    ch = lambda sb, c: sb[:, c * t:(c + 1) * t]
    for c in range(nch):
        hi, lo = ch(pair_hi, c), ch(pair_lo, c)
        ctx.s("vector", lambda e, hi=hi, lo=lo: e.tensor_tensor(sc.x32[:], hi, lo, ALU.add))
        ctx.s("scalar", lambda e: e.activation(sc.y32[:], sc.x32[:], AF.Square))
        ctx.s("vector", lambda e: e.tensor_copy(sc.bfA[:, 0:t], sc.y32[:]))
        ctx.s("vector", lambda e: e.tensor_tensor(sc.bfB[:, 0:t], sc.y32[:], sc.bfA[:, 0:t], ALU.subtract))
        ctx.fence()
        for off, w in halves(t):
            for i, (mat, r0) in enumerate(((hi, 0), (lo, 0), (sc.bfA[:, 0:t], 32), (sc.bfB[:, 0:t], 32))):
                st = (c == 0) and (i % 2 == 0)
                sp = (c == nch - 1) and (i % 2 == 1)
                ctx.s("tensor", lambda e, mat=mat, r0=r0, off=off, w=w, st=st, sp=sp: e.matmul(
                    ps_a[r0:r0 + 1, off:off + w], k.ones_col_bf[:],
                    mat[:, off:off + w], start=st, stop=sp))
        ctx.fence()
    R = sc.rows
    ctx.s("vector", lambda e: e.tensor_copy(R[0:1, 0:t], ps_a[0:1, 0:t]))
    ctx.s("vector", lambda e: e.tensor_copy(R[0:1, t:2 * t], ps_a[32:33, 0:t]))
    invd = 1.0 / float(nch * 128)
    ctx.s("vector", lambda e: e.tensor_scalar(R[0:1, 2 * t:3 * t], R[0:1, 0:t], invd, None, ALU.mult))
    ctx.s("vector", lambda e: e.tensor_scalar(R[0:1, 3 * t:4 * t], R[0:1, t:2 * t], invd, None, ALU.mult))
    ctx.s("scalar", lambda e: e.activation(R[0:1, 0:t], R[0:1, 2 * t:3 * t], AF.Square))
    ctx.s("vector", lambda e: e.tensor_tensor(R[0:1, 3 * t:4 * t], R[0:1, 3 * t:4 * t], R[0:1, 0:t], ALU.subtract))
    ctx.s("scalar", lambda e: e.activation(R[0:1, 0:t], R[0:1, 3 * t:4 * t], AF.Sqrt, bias=k.eps_1, scale=1.0))
    ctx.s("vector", lambda e: e.reciprocal(R[0:1, 4 * t:5 * t], R[0:1, 0:t]))
    ctx.s("scalar", lambda e: e.activation(R[0:1, 0:t], R[0:1, 4 * t:5 * t], AF.Square))
    ctx.s("vector", lambda e: e.tensor_scalar(R[0:1, 3 * t:4 * t], R[0:1, 3 * t:4 * t], LN_EPS, None, ALU.add))
    ctx.s("vector", lambda e: e.tensor_tensor(R[0:1, 0:t], R[0:1, 0:t], R[0:1, 3 * t:4 * t], ALU.mult))
    ctx.s("vector", lambda e: e.tensor_scalar(R[0:1, 0:t], R[0:1, 0:t], -0.5, 1.5, ALU.mult, ALU.add))
    ctx.s("vector", lambda e: e.tensor_tensor(R[0:1, 4 * t:5 * t], R[0:1, 4 * t:5 * t], R[0:1, 0:t], ALU.mult))
    ctx.s("vector", lambda e: e.scalar_tensor_tensor(
        R[0:1, 2 * t:3 * t], R[0:1, 2 * t:3 * t], -1.0, R[0:1, 4 * t:5 * t], ALU.mult, ALU.mult))
    ctx.fence()
    bcast_row(ctx, k, R[0:1, 4 * t:5 * t], sc.bc[:, 0:t], ps_b, t)
    bcast_row(ctx, k, R[0:1, 2 * t:3 * t], sc.bc[:, t:2 * t], ps_b, t)
    ctx.fence()
    for c in range(nch):
        hi, lo = ch(pair_hi, c), ch(pair_lo, c)
        ctx.s("vector", lambda e, hi=hi, lo=lo: e.tensor_tensor(sc.x32[:], hi, lo, ALU.add))
        ctx.s("vector", lambda e: e.tensor_tensor(sc.y32[:], sc.x32[:], sc.bc[:, 0:t], ALU.mult))
        ctx.s("vector", lambda e: e.tensor_tensor(sc.y32[:], sc.y32[:], sc.bc[:, t:2 * t], ALU.add))
        if gamma is not None:
            ctx.s("vector", lambda e, c=c: e.tensor_scalar(
                sc.y32[:], sc.y32[:], gamma[:, c:c + 1], beta[:, c:c + 1], ALU.mult, ALU.add))
        oh, ol = ch(out_hi, c), ch(out_lo, c)
        ctx.s("vector", lambda e, oh=oh: e.tensor_copy(oh, sc.y32[:]))
        ctx.s("vector", lambda e, oh=oh, ol=ol: e.tensor_tensor(ol, sc.y32[:], oh, ALU.subtract))
        if out32_cb is not None:
            out32_cb(c, sc.y32[:])
        ctx.fence()



def emit_proj(ctx, k, w_dram, x_hi, x_lo, n_in_ch, n_out_ch, consumer,
              tok=TOK, wbufs=None, quant_lhs=None, oc_list=None):
    if wbufs is None:
        wbufs = {}
    u = ctx.uid()
    kw = n_in_ch * 128
    w32 = wbufs.get("w32") or ctx.sbuf(f"pj{u}_w32", [128, kw], F32)
    wh = wbufs.get("wh") or ctx.sbuf(f"pj{u}_wh", [128, kw], BF16)
    wl = wbufs.get("wl") or ctx.sbuf(f"pj{u}_wl", [128, kw], BF16)
    ps = wbufs.get("ps") or ctx.psum(f"pj{u}_ps", [128, tok], F32)
    wr = w_dram.rearrange("(kc p) o -> kc p o", p=128)
    dout = w_dram.shape[1]
    for oc in (oc_list if oc_list is not None else range(n_out_ch)):
        src = wr[:, :, oc * 128:(oc + 1) * 128].transpose([1, 0, 2])
        ctx.s("sync", lambda e, src=src: e.dma_start(
            out=w32[:].rearrange("p (kc f) -> p kc f", f=128), in_=src), dma=True)
        if quant_lhs is None:
            ctx.s("vector", lambda e: e.tensor_copy(wh[:], w32[:]))
            ctx.s("vector", lambda e: e.tensor_tensor(wl[:], w32[:], wh[:], ALU.subtract))
        else:
            quant_lhs(ctx, w32, wh, wl)
        ctx.fence()
        for kc in range(n_in_ch):
            lh = wh[:, kc * 128:(kc + 1) * 128]
            ll = wl[:, kc * 128:(kc + 1) * 128]
            for off, w in halves(tok):
                st = (kc == 0)
                sp = (kc == n_in_ch - 1)
                xh = x_hi[:, kc * tok + off: kc * tok + off + w]
                if x_lo is None:
                    ctx.s("tensor", lambda e, lh=lh, xh=xh, off=off, w=w, st=st, sp=sp: e.matmul(
                        ps[:, off:off + w], lh, xh, start=st, stop=sp))
                else:
                    xl = x_lo[:, kc * tok + off: kc * tok + off + w]
                    ctx.s("tensor", lambda e, lh=lh, xh=xh, off=off, w=w, st=st: e.matmul(
                        ps[:, off:off + w], lh, xh, start=st, stop=False))
                    ctx.s("tensor", lambda e, lh=lh, xl=xl, off=off, w=w: e.matmul(
                        ps[:, off:off + w], lh, xl, start=False, stop=False))
                    ctx.s("tensor", lambda e, ll=ll, xh=xh, off=off, w=w, sp=sp: e.matmul(
                        ps[:, off:off + w], ll, xh, start=False, stop=sp))
        ctx.fence()
        consumer(oc, ps)
        ctx.fence()



def shard_tokens_fm(x):
    out = []
    for c in range(NCORES):
        b, q = divmod(c, 4)
        out.append(np.ascontiguousarray(x[b, q * TOK:(q + 1) * TOK, :].T))
    return out


def gather_tokens_fm(shards):
    x = np.empty((B, S, D), dtype=np.float32)
    for c in range(NCORES):
        b, q = divmod(c, 4)
        x[b, q * TOK:(q + 1) * TOK, :] = np.asarray(shards[c], dtype=np.float32).T
    return x





def emit_load_vec(ctx, dram_vec, sb, nch):
    src = dram_vec.rearrange("(c p) -> c p", p=128).transpose([1, 0])
    ctx.s("sync", lambda e: e.dma_start(out=sb[:, 0:nch], in_=src,
                                        allow_slow_non_contiguous=True), dma=True)


def emit_pair_load(ctx, sc, dram_fm, hi, lo, nch, tok=TOK):
    xr = dram_fm.rearrange("(c p) t -> c p t", p=128)
    for c in range(nch):
        ctx.s("sync", lambda e, c=c: e.dma_start(out=sc.x32[:], in_=xr[c]), dma=True)
        ctx.s("vector", lambda e, c=c: e.tensor_copy(hi[:, c * tok:(c + 1) * tok], sc.x32[:]))
        ctx.s("vector", lambda e, c=c: e.tensor_tensor(
            lo[:, c * tok:(c + 1) * tok], sc.x32[:], hi[:, c * tok:(c + 1) * tok], ALU.subtract))
        ctx.fence()


def emit_pair_store(ctx, hi, lo, dram_hi, dram_lo, nch, tok=TOK):
    hr = dram_hi.rearrange("(c p) t -> c p t", p=128)
    lr = dram_lo.rearrange("(c p) t -> c p t", p=128)
    for c in range(nch):
        ctx.s("gpsimd", lambda e, c=c: e.dma_start(out=hr[c], in_=hi[:, c * tok:(c + 1) * tok]), dma=True)
        ctx.s("gpsimd", lambda e, c=c: e.dma_start(out=lr[c], in_=lo[:, c * tok:(c + 1) * tok]), dma=True)



def emit_colmax_to_row(ctx, k, m_ap, row_ap, ps, tmp_col, n, clip_min=None):
    for tt in range(ceil_div(n, 128)):
        w = min(128, n - tt * 128)
        ctx.s("tensor", lambda e, tt=tt, w=w: e.transpose(
            ps[:, 0:128], m_ap[:, tt * 128:tt * 128 + w], k.ident_f32))
        ctx.s("vector", lambda e, w=w: e.tensor_reduce(
            tmp_col[0:w, 0:1], ps[0:w, 0:128], axis=AX.X, op=ALU.max))
        ctx.fence()
        ctx.s("tensor", lambda e: e.transpose(ps[0:1, 128:256], tmp_col[:, 0:1], k.ident_f32))
        ctx.s("vector", lambda e, tt=tt, w=w: e.tensor_copy(
            row_ap[0:1, tt * 128:tt * 128 + w], ps[0:1, 128:128 + w]))
        ctx.fence()
    if clip_min is not None:
        ctx.s("vector", lambda e: e.tensor_scalar(
            row_ap[0:1, 0:n], row_ap[0:1, 0:n], clip_min, None, ALU.max))
        ctx.fence()


def emit_rne_clip_bf16(ctx, sc, src32_ap, dst_bf_ap, lo=-128.0, hi=127.0):
    n = src32_ap.shape[-1]
    ctx.s("vector", lambda e: e.tensor_copy(sc.i32[:, 0:n], src32_ap))
    ctx.s("vector", lambda e: e.tensor_copy(sc.z32[:, 0:n], sc.i32[:, 0:n]))
    ctx.s("vector", lambda e: e.tensor_scalar(dst_bf_ap, sc.z32[:, 0:n], hi, lo, ALU.min, ALU.max))



R_AMAX, R_SCL, R_TMP, R_TMP2, R_AMAX2 = 0, 1, 2, 3, 4
R_CELLS, R_KQ, R_KZQ = 5, 6, 7


def rowslot(sc, i, o=0, n=1024):
    return sc.rows[0:1, i * 1024 + o:i * 1024 + o + n]


def emit_w_mean_cells(ctx, k, sc, weights, cells, cellcols, wcol32, psE):
    wstat = sc.bc[:, 0:8]
    for wi, (w_d, rows_, cols_) in enumerate(weights):
        wr = w_d.rearrange("(s p) o -> s p o", p=128)
        first = True
        for s_ in range(rows_):
            for co in range(0, cols_, 2048):
                cw = min(2048, cols_ - co)
                ctx.s("sync", lambda e, s_=s_, co=co, cw=cw, wr=wr: e.dma_start(
                    out=wcol32[:, 0:cw], in_=wr[s_][:, co:co + cw]), dma=True)
                ctx.s("vector", lambda e, cw=cw: e.tensor_reduce(
                    sc.z32[:, 0:1], wcol32[:, 0:cw], axis=AX.X, op=ALU.add,
                    apply_absolute_value=True))
                if first:
                    ctx.s("vector", lambda e, wi=wi: e.tensor_copy(wstat[:, wi:wi + 1], sc.z32[:, 0:1]))
                    first = False
                else:
                    ctx.s("vector", lambda e, wi=wi: e.tensor_tensor(
                        wstat[:, wi:wi + 1], wstat[:, wi:wi + 1], sc.z32[:, 0:1], ALU.add))
                ctx.fence()
        ctx.s("tensor", lambda e, wi=wi: e.transpose(psE[0:1, 0:128], wstat[:, wi:wi + 1], k.ident_f32))
        ctx.s("vector", lambda e, wi=wi: e.tensor_reduce(
            cells[0:1, wi:wi + 1], psE[0:1, 0:128], axis=AX.X, op=ALU.add))
        ctx.fence()
    for wi in range(3):
        ctx.s("vector", lambda e, wi=wi: e.tensor_scalar(
            cells[0:1, wi:wi + 1], cells[0:1, wi:wi + 1], 1.0 / (2048.0 * 8192.0), 1e-5,
            ALU.mult, ALU.max))
    ctx.s("vector", lambda e: e.reciprocal(cells[0:1, 3:6], cells[0:1, 0:3]))
    ctx.fence()
    for wi in range(6):
        ctx.s("tensor", lambda e, wi=wi: e.matmul(
            psE[:, 0:1], k.ones_row_f32, cells[0:1, wi:wi + 1], start=True, stop=True))
        ctx.s("vector", lambda e, wi=wi: e.tensor_copy(cellcols[:, wi:wi + 1], psE[:, 0:1]))
        ctx.fence()


def emit_actquant(ctx, k, sc, pair_hi, pair_lo, xq, nch, psD, psE):
    t = TOK
    m = sc.bc[:, t:2 * t]
    for c in range(nch):
        ctx.s("vector", lambda e, c=c: e.tensor_tensor(
            sc.x32[:], pair_hi[:, c * t:(c + 1) * t], pair_lo[:, c * t:(c + 1) * t], ALU.add))
        ctx.s("scalar", lambda e: e.activation(sc.y32[:], sc.x32[:], AF.Abs))
        if c == 0:
            ctx.s("vector", lambda e: e.tensor_copy(m[:], sc.y32[:]))
        else:
            ctx.s("vector", lambda e: e.tensor_tensor(m[:], m[:], sc.y32[:], ALU.max))
        ctx.fence()
    emit_colmax_to_row(ctx, k, m, rowslot(sc, R_AMAX), psE, sc.z32[:, 0:1], t, clip_min=1e-5)
    ctx.s("vector", lambda e: e.reciprocal(rowslot(sc, R_SCL), rowslot(sc, R_AMAX)))
    ctx.s("vector", lambda e: e.tensor_scalar(
        rowslot(sc, R_SCL), rowslot(sc, R_SCL), 128.0, None, ALU.mult))
    ctx.s("vector", lambda e: e.tensor_scalar(
        rowslot(sc, R_KQ), rowslot(sc, R_AMAX), 1.0 / 128.0, None, ALU.mult))
    ctx.fence()
    bcast_row(ctx, k, rowslot(sc, R_SCL), sc.bc[:, 0:t], psD, t)
    ctx.fence()
    for c in range(nch):
        ctx.s("vector", lambda e, c=c: e.tensor_tensor(
            sc.x32[:], pair_hi[:, c * t:(c + 1) * t], pair_lo[:, c * t:(c + 1) * t], ALU.add))
        ctx.s("vector", lambda e: e.tensor_tensor(sc.x32[:], sc.x32[:], sc.bc[:, 0:t], ALU.mult))
        emit_rne_clip_bf16(ctx, sc, sc.x32[:], xq[:, c * t:(c + 1) * t])
        ctx.fence()


def emit_tern(ctx, sc, w32ap, dst, sw_cell, n):
    ctx.s("vector", lambda e: e.tensor_scalar(
        sc.bfA[:, 0:n], w32ap, sw_cell, 0.5, ALU.mult, ALU.is_gt))
    ctx.s("vector", lambda e: e.tensor_scalar(
        sc.bfB[:, 0:n], w32ap, sw_cell, -0.5, ALU.mult, ALU.is_lt))
    ctx.s("vector", lambda e: e.tensor_tensor(dst, sc.bfA[:, 0:n], sc.bfB[:, 0:n], ALU.subtract))


def emit_bitnet_gateup(ctx, k, sc, xq, wg_d, wu_d, ispill_d, cells, cellcols,
                       wcol32, wcolh, tern2, psA, psB, psD):
    t = TOK
    nc2 = NCH * 128
    ctx.s("vector", lambda e: e.tensor_scalar(
        rowslot(sc, R_TMP), rowslot(sc, R_KQ), cells[0:1, 0:1], None, ALU.mult))
    ctx.s("vector", lambda e: e.tensor_scalar(
        rowslot(sc, R_TMP2), rowslot(sc, R_KQ), cells[0:1, 1:2], None, ALU.mult))
    ctx.fence()
    bcast_row(ctx, k, rowslot(sc, R_TMP), sc.bc[:, 0:t], psD, t)
    bcast_row(ctx, k, rowslot(sc, R_TMP2), sc.bc[:, t:2 * t], psD, t)
    ctx.fence()
    wgr = wg_d.rearrange("(kc p) o -> kc p o", p=128)
    wur = wu_d.rearrange("(kc p) o -> kc p o", p=128)
    for oc in range(NFF):
        for (wr_, dst, swi) in ((wgr, wcolh, 3), (wur, tern2, 4)):
            src = wr_[:, :, oc * 128:(oc + 1) * 128].transpose([1, 0, 2])
            ctx.s("sync", lambda e, src=src: e.dma_start(
                out=wcol32[:, 0:nc2].rearrange("p (kc f) -> p kc f", f=128), in_=src), dma=True)
            emit_tern(ctx, sc, wcol32[:, 0:nc2], dst[:, 0:nc2], cellcols[:, swi:swi + 1], nc2)
            ctx.fence()
        for kc in range(NCH):
            for off, w in halves(t):
                st, sp = kc == 0, kc == NCH - 1
                ctx.s("tensor", lambda e, kc=kc, off=off, w=w, st=st, sp=sp: e.matmul(
                    psA[:, off:off + w], wcolh[:, kc * 128:(kc + 1) * 128],
                    xq[:, kc * t + off:kc * t + off + w], start=st, stop=sp))
                ctx.s("tensor", lambda e, kc=kc, off=off, w=w, st=st, sp=sp: e.matmul(
                    psB[:, off:off + w], tern2[:, kc * 128:(kc + 1) * 128],
                    xq[:, kc * t + off:kc * t + off + w], start=st, stop=sp))
        ctx.fence()
        ctx.s("vector", lambda e: e.tensor_tensor(sc.x32[:], psA[:, 0:t], sc.bc[:, 0:t], ALU.mult))
        ctx.s("scalar", lambda e: e.activation(sc.x32[:], sc.x32[:], AF.Silu))
        ctx.s("vector", lambda e: e.tensor_tensor(sc.y32[:], psB[:, 0:t], sc.bc[:, t:2 * t], ALU.mult))
        ctx.s("vector", lambda e: e.tensor_tensor(sc.x32[:], sc.x32[:], sc.y32[:], ALU.mult))
        ctx.s("vector", lambda e: e.tensor_copy(sc.bfA[:, 0:t], sc.x32[:]))
        ctx.s("vector", lambda e: e.tensor_tensor(sc.bfB[:, 0:t], sc.x32[:], sc.bfA[:, 0:t], ALU.subtract))
        ctx.s("gpsimd", lambda e, oc=oc: e.dma_start(out=ispill_d[oc, 0], in_=sc.bfA[:, 0:t]), dma=True)
        ctx.s("gpsimd", lambda e, oc=oc: e.dma_start(out=ispill_d[oc, 1], in_=sc.bfB[:, 0:t]), dma=True)
        ctx.fence()


def emit_bitnet_stageb(ctx, k, sc, ispill_d, zqspill_d, zbuf, ihbuf, ilbuf,
                       pbuf, ztile, psD, psE):
    TT = 128
    for tile in range(8):
        t0 = tile * TT
        for hl, buf in ((0, ihbuf), (1, ilbuf)):
            for g8 in range(0, NFF, 8):
                src = ispill_d[g8:g8 + 8, hl, :, t0:t0 + TT].transpose([1, 0, 2])
                ctx.s("sync", lambda e, src=src, g8=g8, buf=buf: e.dma_start(
                    out=buf[:, g8 * TT:(g8 + 8) * TT].rearrange("p (a f) -> p a f", f=TT),
                    in_=src), dma=True)
        ctx.fence()
        for a in range(NFF):
            ctx.s("tensor", lambda e, a=a: e.matmul(
                psD[:, 0:TT], k.h128_bf[:], ihbuf[:, a * TT:(a + 1) * TT], start=True, stop=False))
            ctx.s("tensor", lambda e, a=a: e.matmul(
                psD[:, 0:TT], k.h128_bf[:], ilbuf[:, a * TT:(a + 1) * TT], start=False, stop=True))
            ctx.s("vector", lambda e, a=a: e.tensor_copy(zbuf[:, a * TT:(a + 1) * TT], psD[:, 0:TT]))
            ctx.fence()
        TB = 32
        for sp_ in range(3):
            h1, h2 = 1 << (2 * sp_), 1 << (2 * sp_ + 1)
            for tb in range(0, TT, TB):
                for (hh, src_t, dst_t) in ((h1, "z", "p"), (h2, "p", "z")):
                    def mk(which, a0, hh=hh, tb=tb):
                        tens, width, base = (zbuf, TT, tb) if which == "z" else (pbuf, TB, 0)
                        v = tens[:, :].rearrange("p (a f) -> p a f", f=width)
                        v = v[:, :, base:base + TB]
                        v = v.rearrange("p (g two j) f -> p g two j f", two=2, j=hh)
                        return v[:, :, a0]
                    in0, in1 = mk(src_t, 0), mk(src_t, 1)
                    out0, out1 = mk(dst_t, 0), mk(dst_t, 1)
                    ctx.s("vector", lambda e, out0=out0, in0=in0, in1=in1: e.tensor_tensor(
                        out0, in0, in1, ALU.add))
                    ctx.s("vector", lambda e, out1=out1, in0=in0, in1=in1: e.tensor_tensor(
                        out1, in0, in1, ALU.subtract))
                    ctx.fence()
        zv = zbuf[:, :].rearrange("p (a f) -> p f a", f=TT)
        ctx.s("vector", lambda e, zv=zv: e.tensor_reduce(
            sc.y32[:, 0:TT], zv, axis=AX.X, op=ALU.max, apply_absolute_value=True))
        ctx.fence()
        arow = rowslot(sc, R_AMAX2, t0, TT)
        emit_colmax_to_row(ctx, k, sc.y32[:, 0:TT], arow, psE, sc.z32[:, 0:1], TT)
        srow = rowslot(sc, R_TMP, t0, TT)
        kzq = rowslot(sc, R_KZQ, t0, TT)
        ctx.s("vector", lambda e, arow=arow: e.tensor_scalar(arow, arow, FWHT_NORM, 1e-5, ALU.mult, ALU.max))
        ctx.s("vector", lambda e, srow=srow, arow=arow: e.reciprocal(srow, arow))
        ctx.s("vector", lambda e, srow=srow: e.tensor_scalar(srow, srow, 128.0 * FWHT_NORM, None, ALU.mult))
        ctx.s("vector", lambda e, kzq=kzq, arow=arow: e.tensor_scalar(kzq, arow, 1.0 / 128.0, None, ALU.mult))
        ctx.fence()
        bcast_row(ctx, k, srow, sc.bc[:, 0:TT], psE, TT)
        ctx.fence()
        for a in range(NFF):
            ctx.s("vector", lambda e, a=a: e.tensor_tensor(
                sc.x32[:, 0:TT], zbuf[:, a * TT:(a + 1) * TT], sc.bc[:, 0:TT], ALU.mult))
            emit_rne_clip_bf16(ctx, sc, sc.x32[:, 0:TT], ztile[:, a * TT:(a + 1) * TT])
            ctx.fence()
        src = ztile[:, :].rearrange("p (a f) -> p a f", f=TT)
        ctx.s("gpsimd", lambda e, t0=t0, src=src: e.dma_start(
            out=zqspill_d[:, :, t0:t0 + TT].transpose([1, 0, 2]), in_=src), dma=True)
        ctx.fence()


def emit_bitnet_down(ctx, k, sc, zqspill_d, wd_d, f_d, cells, cellcols, zqh,
                     wcol32, wcolh, psC, psD):
    nc2 = NCH * 128
    wdr = wd_d.rearrange("(kc p) o -> kc p o", p=128)
    fr = f_d.rearrange("(c p) t -> c p t", p=128)
    for th in range(2):
        th0 = th * 512
        for g8 in range(0, NFF, 8):
            src = zqspill_d[g8:g8 + 8, :, th0:th0 + 512].transpose([1, 0, 2])
            ctx.s("sync", lambda e, src=src, g8=g8: e.dma_start(
                out=zqh[:, g8 * 512:(g8 + 8) * 512].rearrange("p (a f) -> p a f", f=512),
                in_=src), dma=True)
        kd = rowslot(sc, R_TMP, 0, 512)
        ctx.s("vector", lambda e, kd=kd, th0=th0: e.tensor_scalar(
            kd, rowslot(sc, R_KZQ, th0, 512), cells[0:1, 2:3], None, ALU.mult))
        ctx.fence()
        bcast_row(ctx, k, kd, sc.bc[:, 0:512], psD, 512)
        ctx.fence()
        for oc in range(NCH):
            for q4 in range(4):
                src = wdr[q4 * NCH:(q4 + 1) * NCH, :, oc * 128:(oc + 1) * 128].transpose([1, 0, 2])
                ctx.s("sync", lambda e, src=src: e.dma_start(
                    out=wcol32[:, 0:nc2].rearrange("p (kc f) -> p kc f", f=128), in_=src), dma=True)
                emit_tern(ctx, sc, wcol32[:, 0:nc2], wcolh[:, 0:nc2], cellcols[:, 5:6], nc2)
                ctx.fence()
                for kc in range(NCH):
                    a = q4 * NCH + kc
                    st = (q4 == 0 and kc == 0)
                    sp = (q4 == 3 and kc == NCH - 1)
                    ctx.s("tensor", lambda e, a=a, kc=kc, st=st, sp=sp: e.matmul(
                        psC[:, 0:512], wcolh[:, kc * 128:(kc + 1) * 128],
                        zqh[:, a * 512:(a + 1) * 512], start=st, stop=sp))
                ctx.fence()
            ctx.s("vector", lambda e: e.tensor_tensor(sc.x32[:, 0:512], psC[:, 0:512], sc.bc[:, 0:512], ALU.mult))
            ctx.s("gpsimd", lambda e, oc=oc, th0=th0: e.dma_start(
                out=fr[oc][:, th0:th0 + 512], in_=sc.x32[:, 0:512]), dma=True)
            ctx.fence()



def emit_t0(ctx, dr, k, sc, psA, psB, psC, psD, psE, vecs, cellcols,
            wcol32, wcolh, wcoll, vtm=False):
    t = TOK
    x_d = dr["x"]; wq_d = dr["wq"]; wk_d = dr["wk"]; wv_d = dr["wv"]
    wg_d = dr["wg_t"]; wu_d = dr["wu_t"]; wd_d = dr["wd_t"]
    xnh_d = dr["xnh"]; xnl_d = dr["xnl"]
    qkv_outs = {nm: dr[nm] for nm in ("qh", "ql", "kh", "kl", "vh", "vl") if nm in dr}
    f_d = dr["fout"]; ispill_d = dr["ispill"]; zqspill_d = dr["zqspill"]

    cells = rowslot(sc, R_CELLS, 0, 8)
    emit_w_mean_cells(ctx, k, sc, ((wg_d, NCH, DFF), (wu_d, NCH, DFF), (wd_d, NFF, D)),
                      cells, cellcols, wcol32, psE)
    xq = ctx.sbuf("xq", [128, NCH * t], BF16)
    xh = ctx.sbuf("xh", [128, NCH * t], BF16)
    xl = ctx.sbuf("xl", [128, NCH * t], BF16)
    emit_pair_load(ctx, sc, x_d, xh, xl, NCH)
    emit_ln(ctx, k, sc, xh, xl, xh, xl, psA, psD)
    emit_pair_store(ctx, xh, xl, xnh_d, xnl_d, NCH)
    ctx.fence()

    emit_actquant(ctx, k, sc, xh, xl, xq, NCH, psD, psE)

    g1 = vecs[:, 3 * NCH:4 * NCH]
    be1 = vecs[:, 4 * NCH:5 * NCH]
    emit_ln(ctx, k, sc, xh, xl, xh, xl, psA, psD, gamma=g1, beta=be1)
    ctx.fence()

    wbufs = {"w32": wcol32, "wh": wcolh, "wl": wcoll, "ps": psA}

    def qkv_consumer(nm_h, nm_l, bias_idx, scale):
        hr = qkv_outs[nm_h].rearrange("(c p) t -> c p t", p=128)
        lr = qkv_outs[nm_l].rearrange("(c p) t -> c p t", p=128)

        def consumer(oc, ps):
            bcol = vecs[:, bias_idx * NCH + oc:bias_idx * NCH + oc + 1]
            ctx.s("vector", lambda e, bcol=bcol: e.tensor_scalar(
                sc.z32[:, 0:1], bcol, scale, None, ALU.mult))
            ctx.fence()
            ctx.s("scalar", lambda e: e.activation(
                sc.y32[:], ps[:, 0:t], AF.Identity, bias=sc.z32[:, 0:1], scale=scale))
            ctx.s("vector", lambda e: e.tensor_copy(sc.bfA[:, 0:t], sc.y32[:]))
            ctx.s("vector", lambda e: e.tensor_tensor(sc.bfB[:, 0:t], sc.y32[:], sc.bfA[:, 0:t], ALU.subtract))
            ctx.s("gpsimd", lambda e, oc=oc, hr=hr: e.dma_start(out=hr[oc], in_=sc.bfA[:, 0:t]), dma=True)
            ctx.s("gpsimd", lambda e, oc=oc, lr=lr: e.dma_start(out=lr[oc], in_=sc.bfB[:, 0:t]), dma=True)
        return consumer

    emit_proj(ctx, k, wq_d, xh, xl, NCH, NCH, qkv_consumer("qh", "ql", 0, RSQ_HD), wbufs=wbufs)
    emit_proj(ctx, k, wk_d, xh, xl, NCH, NCH, qkv_consumer("kh", "kl", 1, 1.0), wbufs=wbufs)
    if not vtm:
        emit_proj(ctx, k, wv_d, xh, xl, NCH, NCH, qkv_consumer("vh", "vl", 2, 1.0), wbufs=wbufs)
    else:
        bvrow = sc.bc[:, 0:D]
        for cch in range(NCH):
            ctx.s("tensor", lambda e, cch=cch: e.transpose(
                psD[0:1, 0:128], vecs[:, 2 * NCH + cch:2 * NCH + cch + 1], k.ident_f32))
            ctx.s("vector", lambda e, cch=cch: e.tensor_copy(
                sc.rows[0:1, cch * 128:(cch + 1) * 128], psD[0:1, 0:128]))
            ctx.fence()
        bcast_row(ctx, k, sc.rows[0:1, 0:512], bvrow[:, 0:512], psE, 512)
        bcast_row(ctx, k, sc.rows[0:1, 512:1024], bvrow[:, 512:1024], psE, 512)
        bcast_row(ctx, k, sc.rows[0:1, 1024:1536], bvrow[:, 1024:1536], psE, 512)
        bcast_row(ctx, k, sc.rows[0:1, 1536:2048], bvrow[:, 1536:2048], psE, 512)
        ctx.fence()
        emit_vproj_tm(ctx, k, sc, wv_d, 0, xh, xl, NCH, NCH,
                      dr["vth_loc"], dr["vtl_loc"],
                      [psA, psB, psC, psD], wcol32, wcolh, wcoll, bias_col=bvrow)
    ctx.fence()
    ctx.sbuf_free("xl", "xh")

    tern2 = ctx.sbuf("tern2", [128, NCH * 128], BF16)
    emit_bitnet_gateup(ctx, k, sc, xq, wg_d, wu_d, ispill_d, cells, cellcols,
                       wcol32, wcolh, tern2, psA, psB, psD)
    ctx.sbuf_free("tern2", "xq")
    zbuf = ctx.sbuf("zbuf", [128, NFF * 128], F32)
    ihbuf = ctx.sbuf("ihbuf", [128, NFF * 128], BF16)
    ilbuf = ctx.sbuf("ilbuf", [128, NFF * 128], BF16)
    pbuf = ctx.sbuf("pbuf", [128, NFF * 32], F32)
    ztile = ctx.sbuf("ztile", [128, NFF * 128], BF16)
    emit_bitnet_stageb(ctx, k, sc, ispill_d, zqspill_d, zbuf, ihbuf, ilbuf,
                       pbuf, ztile, psD, psE)
    ctx.sbuf_free("ztile", "pbuf", "ilbuf", "ihbuf", "zbuf")
    zqh = ctx.sbuf("zqh", [128, NFF * 512], BF16)
    emit_bitnet_down(ctx, k, sc, zqspill_d, wd_d, f_d, cells, cellcols, zqh,
                     wcol32, wcolh, psC, psD)
    ctx.sbuf_free("zqh")



def build_t0(dbg=False):
    ctx = Ctx()
    ctx.init_serial()
    t = TOK
    dr = {}
    dr["x"] = ctx.dram_in("x", [D, t])
    c_d = ctx.dram_in("consts", [128, 400])
    dr["wq"] = ctx.dram_in("wq", [D, D])
    dr["wk"] = ctx.dram_in("wk", [D, D])
    dr["wv"] = ctx.dram_in("wv", [D, D])
    vec_d = ctx.dram_in("vecs", [5 * D])
    dr["wg_t"] = ctx.dram_in("wg_t", [D, DFF])
    dr["wu_t"] = ctx.dram_in("wu_t", [D, DFF])
    dr["wd_t"] = ctx.dram_in("wd_t", [DFF, D])
    dr["xnh"] = ctx.dram_out("xnh", [D, t], BF16)
    dr["xnl"] = ctx.dram_out("xnl", [D, t], BF16)
    for nm in ("qh", "ql", "kh", "kl", "vh", "vl"):
        dr[nm] = ctx.dram_out(nm, [D, t], BF16)
    dr["fout"] = ctx.dram_out("fout", [D, t])
    dr["ispill"] = ctx.dram_tmp("ispill", [NFF, 2, 128, t], BF16)
    dr["zqspill"] = ctx.dram_tmp("zqspill", [NFF, 128, t], BF16)
    k = Consts(ctx)
    sc = Scratch(ctx)
    psA = ctx.psum("psA", [128, 1024], F32)
    psB = ctx.psum("psB", [128, 1024], F32)
    psC = ctx.psum("psC", [128, 1024], F32)
    psD = ctx.psum("psD", [128, 512], F32)
    psE = ctx.psum("psE", [128, 512], F32)
    vecs = ctx.sbuf("vecs_sb", [128, 5 * NCH], F32)
    cellcols = ctx.sbuf("cellcols", [128, 8], F32)
    wcol32 = ctx.sbuf("wcol32", [128, NCH * 128], F32)
    wcolh = ctx.sbuf("wcolh", [128, NCH * 128], BF16)
    wcoll = ctx.sbuf("wcoll", [128, NCH * 128], BF16)
    k.load(ctx, c_d[:])
    emit_load_vec(ctx, vec_d, vecs, 5 * NCH)
    ctx.fence()
    emit_t0(ctx, dr, k, sc, psA, psB, psC, psD, psE, vecs, cellcols,
            wcol32, wcolh, wcoll)
    ctx.emit()
    return ctx


QT = 256
NQT = S // QT


def emit_attn(ctx, dr, k, sc, psS, psAV, psSum, psE, masks):
    qh_d, ql_d, kh_d, kl_d = dr["qh"], dr["ql"], dr["kh"], dr["kl"]
    vth_d, vtl_d, ah_d, al_d = dr["vth"], dr["vtl"], dr["ah"], dr["al"]
    u = ctx.uid()
    qhb = ctx.sbuf(f"at{u}_qhb", [128, S], BF16)
    qlb = ctx.sbuf(f"at{u}_qlb", [128, S], BF16)
    khb = ctx.sbuf(f"at{u}_khb", [128, S], BF16)
    klb = ctx.sbuf(f"at{u}_klb", [128, S], BF16)
    vhb = ctx.sbuf(f"at{u}_vhb", [128, S], BF16)
    vlb = ctx.sbuf(f"at{u}_vlb", [128, S], BF16)
    strip = ctx.sbuf(f"at{u}_strip", [128, 2 * S], F32)
    ehb = ctx.sbuf(f"at{u}_ehb", [128, 2 * S], BF16)
    elb = ctx.sbuf(f"at{u}_elb", [128, 2 * S], BF16)
    for i in range(4):
        for (dr, sb) in ((qh_d, qhb), (ql_d, qlb), (kh_d, khb), (kl_d, klb)):
            ctx.s("sync", lambda e, dr=dr, sb=sb, i=i: e.dma_start(out=sb[:], in_=dr[i]), dma=True)
        for (dr, sb) in ((vth_d, vhb), (vtl_d, vlb)):
            src = dr[i].rearrange("(kt p) d -> p kt d", p=128)
            ctx.s("sync", lambda e, src=src, sb=sb: e.dma_start(
                out=sb[:].rearrange("p (kt d) -> p kt d", d=128), in_=src), dma=True)
        ctx.fence()
        for qt in range(NQT):
            nkt = 2 * qt + 2
            q0 = qt * QT
            for kt in range(nkt):
                ctx.s("tensor", lambda e, kt=kt, q0=q0: e.matmul(
                    psS[:, 0:QT], khb[:, kt * 128:(kt + 1) * 128], qhb[:, q0:q0 + QT],
                    start=True, stop=False))
                ctx.s("tensor", lambda e, kt=kt, q0=q0: e.matmul(
                    psS[:, 0:QT], khb[:, kt * 128:(kt + 1) * 128], qlb[:, q0:q0 + QT],
                    start=False, stop=False))
                ctx.s("tensor", lambda e, kt=kt, q0=q0: e.matmul(
                    psS[:, 0:QT], klb[:, kt * 128:(kt + 1) * 128], qhb[:, q0:q0 + QT],
                    start=False, stop=True))
                if kt >= nkt - 2:
                    pat = 0 if kt == nkt - 2 else 2
                    ctx.s("vector", lambda e, kt=kt, pat=pat: e.tensor_tensor(
                        sc.x32[:, 0:QT], psS[:, 0:QT], masks[:, pat * QT:(pat + 1) * QT], ALU.mult))
                    ctx.s("vector", lambda e, kt=kt, pat=pat: e.tensor_tensor(
                        strip[:, kt * QT:(kt + 1) * QT], sc.x32[:, 0:QT],
                        masks[:, (pat + 1) * QT:(pat + 2) * QT], ALU.add))
                else:
                    ctx.s("scalar", lambda e, kt=kt: e.activation(
                        strip[:, kt * QT:(kt + 1) * QT], psS[:, 0:QT], AF.Identity))
                ctx.fence()
            ctx.s("vector", lambda e, nkt=nkt: e.tensor_reduce(
                sc.z32[:, 0:1], strip[:, 0:nkt * QT], axis=AX.X, op=ALU.max))
            ctx.fence()
            ctx.s("tensor", lambda e: e.transpose(psE[0:1, 0:128], sc.z32[:, 0:1], k.ident_f32))
            ctx.s("vector", lambda e: e.tensor_reduce(
                sc.rows[0:1, 0:1], psE[0:1, 0:128], axis=AX.X, op=ALU.max))
            ctx.s("vector", lambda e: e.tensor_scalar(
                sc.rows[0:1, 0:1], sc.rows[0:1, 0:1], -1.0, None, ALU.mult))
            ctx.fence()
            ctx.s("tensor", lambda e: e.matmul(
                psE[:, 0:1], k.ones_row_f32, sc.rows[0:1, 0:1], start=True, stop=True))
            ctx.s("vector", lambda e: e.tensor_copy(sc.z32[:, 0:1], psE[:, 0:1]))
            ctx.fence()
            for kt in range(nkt):
                sl = slice(kt * QT, (kt + 1) * QT)
                ctx.s("scalar", lambda e, sl=sl: e.activation(
                    sc.x32[:, 0:QT], strip[:, sl], AF.Exp, bias=sc.z32[:, 0:1], scale=1.0))
                ctx.s("vector", lambda e, sl=sl: e.tensor_copy(ehb[:, sl], sc.x32[:, 0:QT]))
                ctx.s("vector", lambda e, sl=sl: e.tensor_tensor(
                    elb[:, sl], sc.x32[:, 0:QT], ehb[:, sl], ALU.subtract))
                ctx.fence()
                st, sp = (kt == 0), (kt == nkt - 1)
                ctx.s("tensor", lambda e, kt=kt, sl=sl, st=st: e.matmul(
                    psAV[:, 0:QT], vhb[:, kt * 128:(kt + 1) * 128], ehb[:, sl], start=st, stop=False))
                ctx.s("tensor", lambda e, kt=kt, sl=sl: e.matmul(
                    psAV[:, 0:QT], vhb[:, kt * 128:(kt + 1) * 128], elb[:, sl], start=False, stop=False))
                ctx.s("tensor", lambda e, kt=kt, sl=sl, sp=sp: e.matmul(
                    psAV[:, 0:QT], vlb[:, kt * 128:(kt + 1) * 128], ehb[:, sl], start=False, stop=sp))
                ctx.s("tensor", lambda e, sl=sl, st=st: e.matmul(
                    psSum[0:1, 0:QT], k.ones_col_bf[:], ehb[:, sl], start=st, stop=False))
                ctx.s("tensor", lambda e, sl=sl, sp=sp: e.matmul(
                    psSum[0:1, 0:QT], k.ones_col_bf[:], elb[:, sl], start=False, stop=sp))
                ctx.fence()
            ctx.s("vector", lambda e: e.reciprocal(sc.rows[0:1, 0:QT], psSum[0:1, 0:QT]))
            ctx.fence()
            ctx.s("tensor", lambda e: e.matmul(
                psE[:, 0:QT], k.ones_row_f32, sc.rows[0:1, 0:QT], start=True, stop=True))
            ctx.s("vector", lambda e: e.tensor_copy(sc.bc[:, 0:QT], psE[:, 0:QT]))
            ctx.fence()
            ctx.s("vector", lambda e: e.tensor_tensor(
                sc.x32[:, 0:QT], psAV[:, 0:QT], sc.bc[:, 0:QT], ALU.mult))
            ctx.s("vector", lambda e: e.tensor_copy(sc.bfA[:, 0:QT], sc.x32[:, 0:QT]))
            ctx.s("vector", lambda e: e.tensor_tensor(
                sc.bfB[:, 0:QT], sc.x32[:, 0:QT], sc.bfA[:, 0:QT], ALU.subtract))
            ctx.s("gpsimd", lambda e, i=i, q0=q0: e.dma_start(
                out=ah_d[i][:, q0:q0 + QT], in_=sc.bfA[:, 0:QT]), dma=True)
            ctx.s("gpsimd", lambda e, i=i, q0=q0: e.dma_start(
                out=al_d[i][:, q0:q0 + QT], in_=sc.bfB[:, 0:QT]), dma=True)
            ctx.fence()
    for nm in ("elb", "ehb", "strip", "vlb", "vhb", "klb", "khb", "qlb", "qhb"):
        ctx.sbuf_free(f"at{u}_{nm}")


def build_attn():
    ctx = Ctx()
    ctx.init_serial()
    c_d = ctx.dram_in("consts", [128, 400])
    mask_d = ctx.dram_in("masks", [4, 128, QT])
    dr = {}
    for nm in ("qh", "ql", "kh", "kl"):
        dr[nm] = ctx.dram_in(nm, [4, 128, S], BF16)
    for nm in ("vth", "vtl"):
        dr[nm] = ctx.dram_in(nm, [4, S, 128], BF16)
    dr["ah"] = ctx.dram_out("ah", [4, 128, S], BF16)
    dr["al"] = ctx.dram_out("al", [4, 128, S], BF16)
    k = Consts(ctx)
    sc = Scratch(ctx)
    psS = ctx.psum("psS", [128, 512], F32)
    psAV = ctx.psum("psAV", [128, 512], F32)
    psSum = ctx.psum("psSum", [128, 512], F32)
    psE = ctx.psum("psE", [128, 512], F32)
    masks = ctx.sbuf("masks_sb", [128, 4 * QT], F32)
    k.load(ctx, c_d[:])
    ctx.s("sync", lambda e: e.dma_start(
        out=masks[:].rearrange("p (i f) -> p i f", f=QT), in_=mask_d.transpose([1, 0, 2])), dma=True)
    ctx.fence()
    emit_attn(ctx, dr, k, sc, psS, psAV, psSum, psE, masks)
    ctx.emit()
    return ctx


def attn_masks_host(maskval):
    out = np.zeros((4, 128, QT), dtype=np.float32)
    kloc = np.arange(128)[:, None]
    qloc = np.arange(QT)[None, :]
    mA = (kloc > qloc)
    mB = (128 + kloc > qloc)
    out[0] = (~mA).astype(np.float32)
    out[1] = np.where(mA, np.float32(maskval), np.float32(0.0))
    out[2] = (~mB).astype(np.float32)
    out[3] = np.where(mB, np.float32(maskval), np.float32(0.0))
    return out


def emit_proj_ksweep(ctx, k, sc, w_dram, rhsh_d, rhsl_d, n_in_ch, n_out_ch,
                     consumer, psW, wbuf32, wbufh, wbufl, rbufh, rbufl, tok=TOK):
    wr = w_dram.rearrange("(kc p) o -> kc p o", p=128)
    for s4 in range(0, n_out_ch, 4):
        for kc in range(n_in_ch):
            ctx.s("sync", lambda e, kc=kc, s4=s4: e.dma_start(
                out=wbuf32[:, 0:512], in_=wr[kc][:, s4 * 128:(s4 + 4) * 128]), dma=True)
            ctx.s("vector", lambda e: e.tensor_copy(wbufh[:, 0:512], wbuf32[:, 0:512]))
            ctx.s("vector", lambda e: e.tensor_tensor(
                wbufl[:, 0:512], wbuf32[:, 0:512], wbufh[:, 0:512], ALU.subtract))
            ctx.s("sync", lambda e, kc=kc: e.dma_start(out=rbufh[:, 0:tok], in_=rhsh_d[kc]), dma=True)
            ctx.s("sync", lambda e, kc=kc: e.dma_start(out=rbufl[:, 0:tok], in_=rhsl_d[kc]), dma=True)
            ctx.fence()
            for j in range(4):
                if s4 + j >= n_out_ch:
                    continue
                lh = wbufh[:, j * 128:(j + 1) * 128]
                ll = wbufl[:, j * 128:(j + 1) * 128]
                for off, w in halves(tok):
                    st, sp = kc == 0, kc == n_in_ch - 1
                    ctx.s("tensor", lambda e, j=j, lh=lh, off=off, w=w, st=st: e.matmul(
                        psW[j][:, off:off + w], lh, rbufh[:, off:off + w], start=st, stop=False))
                    ctx.s("tensor", lambda e, j=j, lh=lh, off=off, w=w: e.matmul(
                        psW[j][:, off:off + w], lh, rbufl[:, off:off + w], start=False, stop=False))
                    ctx.s("tensor", lambda e, j=j, ll=ll, off=off, w=w, sp=sp: e.matmul(
                        psW[j][:, off:off + w], ll, rbufh[:, off:off + w], start=False, stop=sp))
            ctx.fence()
        for j in range(4):
            if s4 + j >= n_out_ch:
                continue
            consumer(s4 + j, psW[j])
            ctx.fence()



LOG2E_INV = float(np.log(2.0))


def emit_fp8_quant(ctx, k, sc, pair_hi, pair_lo, out_bf, nch, inv_s, tok=TOK):
    t = tok
    for c in range(nch):
        ctx.s("vector", lambda e, c=c: e.tensor_tensor(
            sc.x32[:], pair_hi[:, c * t:(c + 1) * t], pair_lo[:, c * t:(c + 1) * t], ALU.add))
        ctx.s("scalar", lambda e: e.activation(sc.y32[:], sc.x32[:], AF.Abs, scale=inv_s))
        ctx.s("vector", lambda e: e.tensor_scalar(sc.y32[:], sc.y32[:], 448.0, None, ALU.min))
        ctx.s("scalar", lambda e: e.activation(sc.z32[:], sc.x32[:], AF.Sign))
        ctx.s("scalar", lambda e: e.activation(sc.x32[:], sc.y32[:], AF.Ln))
        ctx.s("vector", lambda e: e.tensor_scalar(
            sc.x32[:], sc.x32[:], 1.0 / LOG2E_INV, 63.5, ALU.mult, ALU.add))
        ctx.s("vector", lambda e: e.tensor_copy(sc.i32[:], sc.x32[:]))
        ctx.s("vector", lambda e: e.tensor_copy(sc.x32[:], sc.i32[:]))
        ctx.s("vector", lambda e: e.tensor_scalar(
            sc.x32[:], sc.x32[:], -64.0, -6.0, ALU.add, ALU.max))
        ctx.s("scalar", lambda e: e.activation(
            sc.bc[:, 0:t], sc.x32[:], AF.Exp, bias=float(np.float32(3.0 * LOG2E_INV)), scale=-LOG2E_INV))
        ctx.s("scalar", lambda e: e.activation(
            sc.x32[:], sc.x32[:], AF.Exp, bias=float(np.float32(-3.0 * LOG2E_INV)), scale=LOG2E_INV))
        ctx.s("vector", lambda e: e.tensor_tensor(sc.y32[:], sc.y32[:], sc.bc[:, 0:t], ALU.mult))
        ctx.s("vector", lambda e: e.tensor_copy(sc.i32[:], sc.y32[:]))
        ctx.s("vector", lambda e: e.tensor_copy(sc.y32[:], sc.i32[:]))
        ctx.s("vector", lambda e: e.tensor_tensor(sc.y32[:], sc.y32[:], sc.x32[:], ALU.mult))
        ctx.s("vector", lambda e, c=c: e.tensor_tensor(
            out_bf[:, c * t:(c + 1) * t], sc.y32[:], sc.z32[:], ALU.mult))
        ctx.fence()



def emit_t1(ctx, dr, k, sc, psW, vecs, wcol32, wcolh, wcoll, rbufh, rbufl,
            sclsb, isbf, kcell, vtm=False):
    t = TOK
    x_d = dr["x"]; xnh_d = dr["xnh"]; xnl_d = dr["xnl"]; f_d = dr["fout"]
    ath_d = dr["ath"]; atl_d = dr["atl"]; wo_d = dr["wo"]; w1_d = dr["w1"]
    w2_d = dr["w2"]; wqkv_d = dr["wqkv_t"]; w8_d = dr["w8_t"]
    ws_d = dr["wscale"]; is_d = dr["iscale"]
    accp_d = dr["accp"]; ffn2_d = dr["ffn2"]; ffn1_d = dr["ffn1"]
    q1 = {nm: dr[nm] for nm in ("q1h", "q1l", "k1h", "k1l", "v1h", "v1l")}
    hspill_d = dr["hspill"]; xasp_h = dr["xasph"]; xasp_l = dr["xaspl"]
    acc = ctx.sbuf("acc", [128, NCH * t], F32)
    xr = x_d.rearrange("(c p) t -> c p t", p=128)
    fr2 = f_d.rearrange("(c p) t -> c p t", p=128)
    for c in range(NCH):
        ctx.s("sync", lambda e, c=c: e.dma_start(out=sc.x32[:], in_=xr[c]), dma=True)
        ctx.s("sync", lambda e, c=c: e.dma_start(out=sc.y32[:], in_=fr2[c]), dma=True)
        ctx.s("vector", lambda e, c=c: e.tensor_tensor(
            acc[:, c * t:(c + 1) * t], sc.x32[:], sc.y32[:], ALU.add))
        ctx.fence()
    xnhr = xnh_d.rearrange("(c p) t -> c p t", p=128)
    xnlr = xnl_d.rearrange("(c p) t -> c p t", p=128)

    def oproj_consumer(oc, ps):
        bo = vecs[:, oc:oc + 1]
        ctx.s("scalar", lambda e, bo=bo: e.activation(
            sc.x32[:], ps[:, 0:t], AF.Identity, bias=bo, scale=1.0))
        ctx.s("sync", lambda e, oc=oc: e.dma_start(out=sc.bfA[:, 0:t], in_=xnhr[oc]), dma=True)
        ctx.s("sync", lambda e, oc=oc: e.dma_start(out=sc.bfB[:, 0:t], in_=xnlr[oc]), dma=True)
        ctx.s("vector", lambda e: e.tensor_tensor(sc.x32[:], sc.x32[:], sc.bfA[:, 0:t], ALU.add))
        ctx.s("vector", lambda e: e.tensor_tensor(sc.x32[:], sc.x32[:], sc.bfB[:, 0:t], ALU.add))
        ctx.s("vector", lambda e, oc=oc: e.tensor_tensor(
            acc[:, oc * t:(oc + 1) * t], acc[:, oc * t:(oc + 1) * t], sc.x32[:], ALU.add))
        ctx.s("vector", lambda e: e.tensor_copy(sc.bfA[:, 0:t], sc.x32[:]))
        ctx.s("vector", lambda e: e.tensor_tensor(sc.bfB[:, 0:t], sc.x32[:], sc.bfA[:, 0:t], ALU.subtract))
        ctx.s("gpsimd", lambda e, oc=oc: e.dma_start(out=xasp_h[oc], in_=sc.bfA[:, 0:t]), dma=True)
        ctx.s("gpsimd", lambda e, oc=oc: e.dma_start(out=xasp_l[oc], in_=sc.bfB[:, 0:t]), dma=True)

    emit_proj_ksweep(ctx, k, sc, wo_d, ath_d, atl_d, NCH, NCH, oproj_consumer,
                     psW, wcol32, wcolh, wcoll, rbufh, rbufl)
    ctx.fence()
    ar = accp_d.rearrange("(c p) t -> c p t", p=128)
    for c in range(NCH):
        ctx.s("gpsimd", lambda e, c=c: e.dma_start(out=ar[c], in_=acc[:, c * t:(c + 1) * t]), dma=True)
    ctx.fence()
    ctx.sbuf_free("acc")

    xah = ctx.sbuf("xah", [128, NCH * t], BF16)
    xal = ctx.sbuf("xal", [128, NCH * t], BF16)
    for c in range(NCH):
        ctx.s("sync", lambda e, c=c: e.dma_start(out=xah[:, c * t:(c + 1) * t], in_=xasp_h[c]), dma=True)
        ctx.s("sync", lambda e, c=c: e.dma_start(out=xal[:, c * t:(c + 1) * t], in_=xasp_l[c]), dma=True)
    ctx.fence()
    g2 = vecs[:, 2 * NCH:3 * NCH]
    be2 = vecs[:, 3 * NCH:4 * NCH]
    emit_ln(ctx, k, sc, xah, xal, xah, xal, psW[0], psW[1], gamma=g2, beta=be2)
    ctx.fence()
    wbufs = {"w32": wcol32, "wh": wcolh, "wl": wcoll, "ps": psW[0]}

    def w1_consumer(oc, ps):
        b1 = vecs[:, 4 * NCH + oc:4 * NCH + oc + 1]
        ctx.s("scalar", lambda e, b1=b1: e.activation(
            sc.x32[:], ps[:, 0:t], AF.Gelu, bias=b1, scale=1.0))
        ctx.s("vector", lambda e: e.tensor_copy(sc.bfA[:, 0:t], sc.x32[:]))
        ctx.s("vector", lambda e: e.tensor_tensor(sc.bfB[:, 0:t], sc.x32[:], sc.bfA[:, 0:t], ALU.subtract))
        ctx.s("gpsimd", lambda e, oc=oc: e.dma_start(out=hspill_d[oc, 0], in_=sc.bfA[:, 0:t]), dma=True)
        ctx.s("gpsimd", lambda e, oc=oc: e.dma_start(out=hspill_d[oc, 1], in_=sc.bfB[:, 0:t]), dma=True)

    emit_proj(ctx, k, w1_d, xah, xal, NCH, NFF, w1_consumer, wbufs=wbufs)
    ctx.fence()
    ctx.sbuf_free("xal", "xah")

    f2r = ffn2_d.rearrange("(c p) t -> c p t", p=128)

    def w2_consumer(oc, ps):
        b2 = vecs[:, NCH + oc:NCH + oc + 1]
        ctx.s("scalar", lambda e, b2=b2: e.activation(
            sc.x32[:], ps[:, 0:t], AF.Identity, bias=b2, scale=1.0))
        ctx.s("gpsimd", lambda e, oc=oc: e.dma_start(out=f2r[oc], in_=sc.x32[:]), dma=True)

    emit_proj_ksweep(ctx, k, sc, w2_d, hspill_d[:, 0], hspill_d[:, 1], NFF, NCH,
                     w2_consumer, psW, wcol32, wcolh, wcoll, rbufh, rbufl)
    ctx.fence()

    x8 = ctx.sbuf("x8", [128, NCH * t], BF16)
    x1h = ctx.sbuf("x1h", [128, NCH * t], BF16)
    x1l = ctx.sbuf("x1l", [128, NCH * t], BF16)
    for c in range(NCH):
        ctx.s("sync", lambda e, c=c: e.dma_start(out=sc.x32[:], in_=ar[c]), dma=True)
        ctx.s("sync", lambda e, c=c: e.dma_start(out=sc.y32[:], in_=f2r[c]), dma=True)
        ctx.s("vector", lambda e: e.tensor_tensor(sc.x32[:], sc.x32[:], sc.y32[:], ALU.add))
        ctx.s("vector", lambda e, c=c: e.tensor_copy(x1h[:, c * t:(c + 1) * t], sc.x32[:]))
        ctx.s("vector", lambda e, c=c: e.tensor_tensor(
            x1l[:, c * t:(c + 1) * t], sc.x32[:], x1h[:, c * t:(c + 1) * t], ALU.subtract))
        ctx.fence()
    emit_ln(ctx, k, sc, x1h, x1l, x1h, x1l, psW[0], psW[1])
    ctx.fence()

    def l1qkv_consumer(oc, ps):
        if oc < NCH:
            nmh, nml, scale, o2 = "q1h", "q1l", RSQ_HD, oc
        elif oc < 2 * NCH:
            nmh, nml, scale, o2 = "k1h", "k1l", 1.0, oc - NCH
        else:
            nmh, nml, scale, o2 = "v1h", "v1l", 1.0, oc - 2 * NCH
        hr = q1[nmh].rearrange("(c p) t -> c p t", p=128)
        lr = q1[nml].rearrange("(c p) t -> c p t", p=128)
        if scale != 1.0:
            ctx.s("scalar", lambda e, scale=scale: e.activation(
                sc.x32[:], ps[:, 0:t], AF.Identity, bias=0.0, scale=scale))
            srcap = sc.x32[:]
        else:
            srcap = ps[:, 0:t]
        ctx.s("vector", lambda e, srcap=srcap: e.tensor_copy(sc.bfA[:, 0:t], srcap))
        ctx.s("vector", lambda e, srcap=srcap: e.tensor_tensor(
            sc.bfB[:, 0:t], srcap, sc.bfA[:, 0:t], ALU.subtract))
        ctx.s("gpsimd", lambda e, o2=o2, hr=hr: e.dma_start(out=hr[o2], in_=sc.bfA[:, 0:t]), dma=True)
        ctx.s("gpsimd", lambda e, o2=o2, lr=lr: e.dma_start(out=lr[o2], in_=sc.bfB[:, 0:t]), dma=True)

    if not vtm:
        emit_proj(ctx, k, wqkv_d, x1h, x1l, NCH, 3 * NCH, l1qkv_consumer, wbufs=wbufs)
    else:
        emit_proj(ctx, k, wqkv_d, x1h, x1l, NCH, 3 * NCH, l1qkv_consumer,
                  wbufs=wbufs, oc_list=range(2 * NCH))
        emit_vproj_tm(ctx, k, sc, wqkv_d, 2 * NCH, x1h, x1l, NCH, NCH,
                      dr["v1th_loc"], dr["v1tl_loc"],
                      [psW[0], psW[1], psW[2], psW[3]], wcol32, wcolh, wcoll)
    ctx.fence()

    ctx.s("sync", lambda e: e.dma_start(out=sclsb[0:1, 0:1], in_=ws_d), dma=True)
    ctx.s("sync", lambda e: e.dma_start(out=isbf[:], in_=is_d), dma=True)
    ctx.fence()
    ctx.s("vector", lambda e: e.tensor_copy(sclsb[0:1, 1:2], isbf[:]))
    ctx.s("vector", lambda e: e.tensor_tensor(
        sclsb[0:1, 2:3], sclsb[0:1, 0:1], sclsb[0:1, 1:2], ALU.mult))
    ctx.fence()
    ctx.s("tensor", lambda e: e.matmul(psW[1][:, 0:1], k.ones_row_f32, sclsb[0:1, 2:3], start=True, stop=True))
    ctx.s("vector", lambda e: e.tensor_copy(kcell[:], psW[1][:, 0:1]))
    ctx.fence()
    emit_fp8_quant(ctx, k, sc, x1h, x1l, x8, NCH, 1.0 / 0.75)
    ctx.fence()
    ctx.sbuf_free("x1l", "x1h")

    f1r = ffn1_d.rearrange("(c p) t -> c p t", p=128)
    w8r = w8_d.rearrange("(kc p) o -> kc p o", p=128)
    for oc in range(NCH):
        src = w8r[:, :, oc * 128:(oc + 1) * 128].transpose([1, 0, 2])
        ctx.s("gpsimd", lambda e, src=src: e.dma_start(
            out=wcolh[:, 0:NCH * 128].rearrange("p (kc f) -> p kc f", f=128), in_=src), dma=True)
        ctx.fence()
        for kc in range(NCH):
            for off, w in halves(t):
                st, sp = kc == 0, kc == NCH - 1
                ctx.s("tensor", lambda e, kc=kc, off=off, w=w, st=st, sp=sp: e.matmul(
                    psW[0][:, off:off + w], wcolh[:, kc * 128:(kc + 1) * 128],
                    x8[:, kc * t + off:kc * t + off + w], start=st, stop=sp))
        ctx.fence()
        ctx.s("vector", lambda e: e.tensor_scalar(
            sc.x32[:], psW[0][:, 0:t], kcell[:, 0:1], None, ALU.mult))
        ctx.s("gpsimd", lambda e, oc=oc: e.dma_start(out=f1r[oc], in_=sc.x32[:]), dma=True)
        ctx.fence()
    ctx.sbuf_free("x8")




def build_t1():
    ctx = Ctx()
    ctx.init_serial()
    t = TOK
    dr = {}
    dr["x"] = ctx.dram_in("x", [D, t])
    c_d = ctx.dram_in("consts", [128, 400])
    dr["xnh"] = ctx.dram_in("xnh", [D, t], BF16)
    dr["xnl"] = ctx.dram_in("xnl", [D, t], BF16)
    dr["fout"] = ctx.dram_in("fout", [D, t])
    dr["ath"] = ctx.dram_in("ath", [NCH, 128, t], BF16)
    dr["atl"] = ctx.dram_in("atl", [NCH, 128, t], BF16)
    dr["wo"] = ctx.dram_in("wo", [D, D])
    dr["w1"] = ctx.dram_in("w1", [D, DFF])
    dr["w2"] = ctx.dram_in("w2", [DFF, D])
    dr["wqkv_t"] = ctx.dram_in("wqkv_t", [D, 3 * D])
    dr["w8_t"] = ctx.dram_in("w8_t", [D, D], mybir.dt.float8e4)
    dr["wscale"] = ctx.dram_in("wscale", [1, 1])
    dr["iscale"] = ctx.dram_in("iscale", [1, 1], BF16)
    vec_d = ctx.dram_in("vecs", [(NCH * 4 + NFF) * 128])
    dr["accp"] = ctx.dram_out("accp", [D, t])
    dr["ffn2"] = ctx.dram_out("ffn2", [D, t])
    dr["ffn1"] = ctx.dram_out("ffn1", [D, t])
    for nm in ("q1h", "q1l", "k1h", "k1l", "v1h", "v1l"):
        dr[nm] = ctx.dram_out(nm, [D, t], BF16)
    dr["hspill"] = ctx.dram_tmp("hspill", [NFF, 2, 128, t], BF16)
    dr["xasph"] = ctx.dram_tmp("xasph", [NCH, 128, t], BF16)
    dr["xaspl"] = ctx.dram_tmp("xaspl", [NCH, 128, t], BF16)
    k = Consts(ctx)
    sc = Scratch(ctx)
    psW = [ctx.psum(f"psw{i}", [128, 1024], F32) for i in range(4)]
    vecs = ctx.sbuf("vecs_sb", [128, NCH * 4 + NFF], F32)
    wcol32 = ctx.sbuf("wcol32", [128, NCH * 128], F32)
    wcolh = ctx.sbuf("wcolh", [128, NCH * 128], BF16)
    wcoll = ctx.sbuf("wcoll", [128, NCH * 128], BF16)
    rbufh = ctx.sbuf("rbufh", [128, t], BF16)
    rbufl = ctx.sbuf("rbufl", [128, t], BF16)
    sclsb = ctx.sbuf("sclsb", [1, 4], F32)
    isbf = ctx.sbuf("isbf", [1, 1], BF16)
    kcell = ctx.sbuf("kcell", [128, 1], F32)
    k.load(ctx, c_d[:])
    emit_load_vec(ctx, vec_d, vecs, NCH * 4 + NFF)
    ctx.fence()
    emit_t1(ctx, dr, k, sc, psW, vecs, wcol32, wcolh, wcoll, rbufh, rbufl,
            sclsb, isbf, kcell)
    ctx.emit()
    return ctx



def emit_t2(ctx, dr, k, sc, psW, wcol32, wcolh, wcoll, rbufh, rbufl):
    t = TOK
    accp_d = dr["accp"]; ffn2_d = dr["ffn2"]; ffn1_d = dr["ffn1"]
    ath_d = dr["ath1"]; atl_d = dr["atl1"]; wo_d = dr["wo1"]; out_d = dr["x2"]
    acc = ctx.sbuf("t2acc", [128, NCH * t], F32)
    for drm in (accp_d, ffn2_d, ffn1_d):
        r = drm.rearrange("(c p) t -> c p t", p=128)
        for c in range(NCH):
            if drm is accp_d:
                ctx.s("sync", lambda e, c=c, r=r: e.dma_start(out=acc[:, c * t:(c + 1) * t], in_=r[c]), dma=True)
            else:
                ctx.s("sync", lambda e, c=c, r=r: e.dma_start(out=sc.x32[:], in_=r[c]), dma=True)
                ctx.s("vector", lambda e, c=c: e.tensor_tensor(
                    acc[:, c * t:(c + 1) * t], acc[:, c * t:(c + 1) * t], sc.x32[:], ALU.add))
            ctx.fence()
    outr = out_d.rearrange("(c p) t -> c p t", p=128)

    def consumer(oc, ps):
        ctx.s("vector", lambda e, oc=oc: e.tensor_tensor(
            acc[:, oc * t:(oc + 1) * t], acc[:, oc * t:(oc + 1) * t], ps[:, 0:t], ALU.add))
        ctx.s("gpsimd", lambda e, oc=oc: e.dma_start(
            out=outr[oc], in_=acc[:, oc * t:(oc + 1) * t]), dma=True)

    emit_proj_ksweep(ctx, k, sc, wo_d, ath_d, atl_d, NCH, NCH, consumer,
                     psW, wcol32, wcolh, wcoll, rbufh, rbufl)
    ctx.fence()
    ctx.sbuf_free("t2acc")


def build_t2():
    ctx = Ctx()
    ctx.init_serial()
    t = TOK
    dr = {}
    dr["accp"] = ctx.dram_in("accp", [D, t])
    dr["ffn2"] = ctx.dram_in("ffn2", [D, t])
    dr["ffn1"] = ctx.dram_in("ffn1", [D, t])
    c_d = ctx.dram_in("consts", [128, 400])
    dr["ath1"] = ctx.dram_in("ath", [NCH, 128, t], BF16)
    dr["atl1"] = ctx.dram_in("atl", [NCH, 128, t], BF16)
    dr["wo1"] = ctx.dram_in("wo1", [D, D])
    dr["x2"] = ctx.dram_out("x2", [D, t])
    k = Consts(ctx)
    sc = Scratch(ctx)
    psW = [ctx.psum(f"psw{i}", [128, 1024], F32) for i in range(4)]
    wcol32 = ctx.sbuf("wcol32", [128, 512], F32)
    wcolh = ctx.sbuf("wcolh", [128, 512], BF16)
    wcoll = ctx.sbuf("wcoll", [128, 512], BF16)
    rbufh = ctx.sbuf("rbufh", [128, t], BF16)
    rbufl = ctx.sbuf("rbufl", [128, t], BF16)
    k.load(ctx, c_d[:])
    emit_t2(ctx, dr, k, sc, psW, wcol32, wcolh, wcoll, rbufh, rbufl)
    ctx.emit()
    return ctx



_CACHE = {}


def _get(builder, name):
    if name not in _CACHE:
        ctx = builder()
        ctx.emitted = True
        _CACHE[name] = ctx
    return _CACHE[name]


def _bh_list(s_owner):
    h0 = 2 * s_owner
    return [(0, h0), (0, h0 + 1), (1, h0), (1, h0 + 1)]


def kernel(L0_attn_x, L0_attn_wq, L0_attn_bq, L0_attn_wk, L0_attn_bk,
           L0_attn_wv, L0_attn_bv, L0_attn_wo, L0_attn_bo, L0_attn_w1,
           L0_attn_b1, L0_attn_w2, L0_attn_b2, L0_attn_g1, L0_attn_be1,
           L0_attn_g2, L0_attn_be2, L0_ffn_w_gate, L0_ffn_w_up, L0_ffn_w_down,
           L1_attn_w_qkv, L1_attn_w_o, L1_ffn_weight, L1_ffn_weight_scale,
           L1_ffn_input_scale):
    import ml_dtypes
    consts = Consts.host_array()
    f32 = np.float32

    x = np.asarray(L0_attn_x, dtype=f32)
    vecs0 = np.concatenate([np.asarray(v, dtype=f32) for v in
                            (L0_attn_bq, L0_attn_bk, L0_attn_bv, L0_attn_g1, L0_attn_be1)])
    vecs1 = np.concatenate([np.asarray(v, dtype=f32) for v in
                            (L0_attn_bo, L0_attn_b2, L0_attn_g2, L0_attn_be2, L0_attn_b1)])
    wg_t = np.ascontiguousarray(np.asarray(L0_ffn_w_gate, dtype=f32).T)
    wu_t = np.ascontiguousarray(np.asarray(L0_ffn_w_up, dtype=f32).T)
    wd_t = np.ascontiguousarray(np.asarray(L0_ffn_w_down, dtype=f32).T)
    wqkv_t = np.ascontiguousarray(np.asarray(L1_attn_w_qkv, dtype=f32).T)
    wo1_t = np.ascontiguousarray(np.asarray(L1_attn_w_o, dtype=f32).T)
    w8_t = np.ascontiguousarray(
        np.asarray(L1_ffn_weight).view(np.uint8).T).view(ml_dtypes.float8_e4m3)
    wscale = np.asarray(L1_ffn_weight_scale, dtype=f32).reshape(1, 1)
    iscale = np.asarray(L1_ffn_input_scale).reshape(1, 1)

    t0 = _get(build_t0, "t0")
    in0 = []
    for c in range(NCORES):
        b, qd = divmod(c, 4)
        xs = np.ascontiguousarray(x[b, qd * TOK:(qd + 1) * TOK, :].T)
        in0.append({"x": xs, "consts": consts, "wq": np.asarray(L0_attn_wq, f32),
                    "wk": np.asarray(L0_attn_wk, f32), "wv": np.asarray(L0_attn_wv, f32),
                    "vecs": vecs0, "wg_t": wg_t, "wu_t": wu_t, "wd_t": wd_t})
    r0 = run_bass_kernel_spmd(t0.nc, in0, core_ids=list(range(NCORES)))
    o0 = r0.results

    att = _get(build_attn, "attn")
    masks0 = attn_masks_host(-1e9)

    def attn_inputs(outs, masks, names=("qh", "ql", "kh", "kl", "vh", "vl")):
        ins = []
        for s_own in range(NCORES):
            bh = _bh_list(s_own)
            d = {"consts": consts, "masks": masks}
            for nm_fm, nm_in in ((names[0], "qh"), (names[1], "ql"),
                                 (names[2], "kh"), (names[3], "kl")):
                arr = np.empty((4, 128, S), dtype=ml_dtypes.bfloat16)
                for i, (b, h) in enumerate(bh):
                    cores = [4 * b + j for j in range(4)]
                    arr[i] = np.concatenate(
                        [np.asarray(outs[cc][nm_fm])[h * 128:(h + 1) * 128, :] for cc in cores], axis=1)
                d[nm_in] = arr
            for nm_fm, nm_in in ((names[4], "vth"), (names[5], "vtl")):
                arr = np.empty((4, S, 128), dtype=ml_dtypes.bfloat16)
                for i, (b, h) in enumerate(bh):
                    cores = [4 * b + j for j in range(4)]
                    arr[i] = np.concatenate(
                        [np.ascontiguousarray(np.asarray(outs[cc][nm_fm])[h * 128:(h + 1) * 128, :].T)
                         for cc in cores], axis=0)
                d[nm_in] = arr
            ins.append(d)
        return ins

    rA0 = run_bass_kernel_spmd(att.nc, attn_inputs(o0, masks0), core_ids=list(range(NCORES)))
    oA0 = rA0.results

    def attn_out_relayout(oa):
        hs, ls = [], []
        for c in range(NCORES):
            b, qd = divmod(c, 4)
            q0 = qd * TOK
            ah = np.empty((NCH, 128, TOK), dtype=ml_dtypes.bfloat16)
            al = np.empty((NCH, 128, TOK), dtype=ml_dtypes.bfloat16)
            for h in range(16):
                owner = h // 2
                i = 2 * b + (h % 2)
                ah[h] = np.asarray(oa[owner]["ah"])[i][:, q0:q0 + TOK]
                al[h] = np.asarray(oa[owner]["al"])[i][:, q0:q0 + TOK]
            hs.append(ah)
            ls.append(al)
        return hs, ls

    ath0, atl0 = attn_out_relayout(oA0)

    t1 = _get(build_t1, "t1")
    in1 = []
    for c in range(NCORES):
        b, qd = divmod(c, 4)
        xs = np.ascontiguousarray(x[b, qd * TOK:(qd + 1) * TOK, :].T)
        in1.append({"x": xs, "consts": consts,
                    "xnh": np.asarray(o0[c]["xnh"]), "xnl": np.asarray(o0[c]["xnl"]),
                    "fout": np.asarray(o0[c]["fout"]),
                    "ath": ath0[c], "atl": atl0[c],
                    "wo": np.asarray(L0_attn_wo, f32), "w1": np.asarray(L0_attn_w1, f32),
                    "w2": np.asarray(L0_attn_w2, f32), "wqkv_t": wqkv_t,
                    "w8_t": w8_t, "wscale": wscale, "iscale": iscale, "vecs": vecs1})
    r1 = run_bass_kernel_spmd(t1.nc, in1, core_ids=list(range(NCORES)))
    o1 = r1.results

    masks1 = attn_masks_host(float(np.finfo(np.float32).min))
    rA1 = run_bass_kernel_spmd(
        att.nc, attn_inputs(o1, masks1, names=("q1h", "q1l", "k1h", "k1l", "v1h", "v1l")),
        core_ids=list(range(NCORES)))
    oA1 = rA1.results
    ath1, atl1 = attn_out_relayout(oA1)

    t2 = _get(build_t2, "t2")
    in2 = []
    for c in range(NCORES):
        in2.append({"accp": np.asarray(o1[c]["accp"]), "ffn2": np.asarray(o1[c]["ffn2"]),
                    "ffn1": np.asarray(o1[c]["ffn1"]), "consts": consts,
                    "ath": ath1[c], "atl": atl1[c], "wo1": wo1_t})
    r2 = run_bass_kernel_spmd(t2.nc, in2, core_ids=list(range(NCORES)))
    o2 = r2.results

    out = np.empty((B, S, D), dtype=np.float32)
    for c in range(NCORES):
        b, qd = divmod(c, 4)
        out[b, qd * TOK:(qd + 1) * TOK, :] = np.asarray(o2[c]["x2"], dtype=np.float32).T
    return out


def emit_vproj_tm(ctx, k, sc, w_dram, oc0, x_hi, x_lo, n_in_ch, n_out_ch,
                  vth_d, vtl_d, psW, wbuf32, wbufh, wbufl, bias_col=None, tok=TOK):
    t = tok
    dout = n_out_ch * 128
    wr = w_dram
    for tt in range(t // 128):
        for kc in range(n_in_ch):
            ctx.s("sync", lambda e, kc=kc: e.dma_start(
                out=wbuf32[:, 0:dout], in_=wr[kc * 128:(kc + 1) * 128, oc0 * 128:oc0 * 128 + dout]), dma=True)
            ctx.s("vector", lambda e: e.tensor_copy(wbufh[:, 0:dout], wbuf32[:, 0:dout]))
            ctx.s("vector", lambda e: e.tensor_tensor(
                wbufl[:, 0:dout], wbuf32[:, 0:dout], wbufh[:, 0:dout], ALU.subtract))
            ctx.fence()
            st, sp = kc == 0, kc == n_in_ch - 1
            lh = x_hi[:, kc * t + tt * 128: kc * t + (tt + 1) * 128]
            ll = x_lo[:, kc * t + tt * 128: kc * t + (tt + 1) * 128]
            for dh in range(ceil_div(dout, 512)):
                w = min(512, dout - dh * 512)
                ctx.s("tensor", lambda e, lh=lh, dh=dh, w=w, st=st: e.matmul(
                    psW[dh % 4][:, 0:w], lh, wbufh[:, dh * 512:dh * 512 + w], start=st, stop=False))
                ctx.s("tensor", lambda e, lh=lh, dh=dh, w=w: e.matmul(
                    psW[dh % 4][:, 0:w], lh, wbufl[:, dh * 512:dh * 512 + w], start=False, stop=False))
                ctx.s("tensor", lambda e, ll=ll, dh=dh, w=w, sp=sp: e.matmul(
                    psW[dh % 4][:, 0:w], ll, wbufh[:, dh * 512:dh * 512 + w], start=False, stop=sp))
            ctx.fence()
        for dh in range(ceil_div(dout, 512)):
            w = min(512, dout - dh * 512)
            if bias_col is not None:
                ctx.s("vector", lambda e, dh=dh, w=w: e.tensor_tensor(
                    sc.x32[:, 0:w], psW[dh % 4][:, 0:w], bias_col[:, dh * 512:dh * 512 + w], ALU.add))
            else:
                ctx.s("vector", lambda e, dh=dh, w=w: e.tensor_copy(sc.x32[:, 0:w], psW[dh % 4][:, 0:w]))
            ctx.s("vector", lambda e, dh=dh, w=w: e.tensor_copy(sc.bfA[:, 0:w], sc.x32[:, 0:w]))
            ctx.s("vector", lambda e, dh=dh, w=w: e.tensor_tensor(
                sc.bfB[:, 0:w], sc.x32[:, 0:w], sc.bfA[:, 0:w], ALU.subtract))
            ctx.s("gpsimd", lambda e, tt=tt, dh=dh, w=w: e.dma_start(
                out=vth_d[tt][:, dh * 512:dh * 512 + w], in_=sc.bfA[:, 0:w]), dma=True)
            ctx.s("gpsimd", lambda e, tt=tt, dh=dh, w=w: e.dma_start(
                out=vtl_d[tt][:, dh * 512:dh * 512 + w], in_=sc.bfB[:, 0:w]), dma=True)
            ctx.fence()



BASE_PERM = [0, 1, 2, 3, 6, 7, 4, 5]


def _sigma(core, d):
    return BASE_PERM[BASE_PERM[core] ^ d]


def _rdests(d):
    return [(0, d)] * 8 if d < 4 else [None] * 4 + [(0, d)] * 4


class Exch:
    def __init__(self, ctx, xidx_sb):
        self.rsem = ctx.sem("xrsem")
        self.lsem = ctx.sem("xlsem")
        self.prep = ctx.sem("xprep")
        self.bar = ctx.sem("xbar")
        self.xidx = xidx_sb
        self.col = 0

    def idx(self):
        c = self.col
        self.col += 1
        return self.xidx[:, c:c + 1]

    def barrier(self, ctx):
        ctx.fence()
        ctx.s("gpsimd", lambda e: e.remote_sem_update_broadcast(
            self.bar.h, self.lsem.h, rdests=[(0, dd) for dd in range(8)]))
        self.prep.val += 1
        ctx.ops[-1][3].append((self.prep, 1))
        ctx.s("gpsimd", lambda e: e.trigger_dma(count=1))
        ctx.ops[-1][2].append((self.prep, self.prep.val))
        self.lsem.val += 16
        self.bar.val += 16
        ctx.s("gpsimd", lambda e: e.engine_nop())
        ctx.ops[-1][2].append((self.bar, self.bar.val))
        ctx.fence()

    def send(self, ctx, d, dst_ap, src_ap):
        ctx.fence()
        ctx.s("gpsimd", lambda e, d=d, dst_ap=dst_ap, src_ap=src_ap: e.remote_dma_broadcast(
            dst_ap, src_ap, self.rsem.h, self.lsem.h, rdests=_rdests(d)))
        self.prep.val += 1
        ctx.ops[-1][3].append((self.prep, 1))
        ctx.s("gpsimd", lambda e: e.trigger_dma(count=1))
        ctx.ops[-1][2].append((self.prep, self.prep.val))
        self.lsem.val += 16
        self.rsem.val += 16 if d < 4 else 8

    def wait_recv(self, ctx):
        ctx.s("gpsimd", lambda e: e.engine_nop())
        ctx.ops[-1][2].append((self.rsem, self.rsem.val))
        ctx.fence()

    def wait_sent(self, ctx):
        ctx.s("gpsimd", lambda e: e.engine_nop())
        ctx.ops[-1][2].append((self.lsem, self.lsem.val))
        ctx.fence()


def _ind_gather(ctx, ex, out_ap, flat_dram):
    iap = ex.idx()
    ctx.s("gpsimd", lambda e, out_ap=out_ap, flat_dram=flat_dram, iap=iap:
          e.indirect_dma_start(out_ap, None, flat_dram,
                               bass.IndirectOffsetOnAxis(iap, 0)), dma=True)


def _ind_scatter(ctx, ex, flat_dram, in_ap):
    iap = ex.idx()
    ctx.s("gpsimd", lambda e, flat_dram=flat_dram, in_ap=in_ap, iap=iap:
          e.indirect_dma_start(flat_dram, bass.IndirectOffsetOnAxis(iap, 0),
                               in_ap, None), dma=True)


def emit_exchange_e1(ctx, k, sc, ex, qkv_local, vt_local, gather, stg, slots):
    t = TOK
    qk_flat = {nm: qkv_local[nm].rearrange("(r p) t -> (r p) t", p=128)
               for nm in ("qh", "ql", "kh", "kl")}
    vt_flat = [vt.rearrange("a p (g d) -> (a p g) d", d=256) for vt in vt_local]
    g_flat = {nm: gather[nm].rearrange("b p (q t) -> (b p q) t", t=t)
              for nm in ("qh", "ql", "kh", "kl")}
    gv_flat = [gather[nm].rearrange("b (r p) d -> (b r p) d", p=128)
               for nm in ("vth", "vtl")]
    for d in range(8):
        ex.barrier(ctx)
        for j2, nm in enumerate(("qh", "ql", "kh", "kl")):
            for j in range(2):
                _ind_gather(ctx, ex, stg[:, (2 * j2 + j) * t:(2 * j2 + j + 1) * t], qk_flat[nm])
        for hl in range(2):
            for tt in range(8):
                _ind_gather(ctx, ex, stg[:, 8 * t + hl * 2048 + tt * 256:
                                         8 * t + hl * 2048 + (tt + 1) * 256], vt_flat[hl])
        ctx.fence()
        par = d % 2
        ex.send(ctx, d, slots[:, par * 12288:(par + 1) * 12288], stg[:, 0:12288])
        ex.wait_recv(ctx)
        for j2, nm in enumerate(("qh", "ql", "kh", "kl")):
            for j in range(2):
                _ind_scatter(ctx, ex, g_flat[nm],
                             slots[:, par * 12288 + (2 * j2 + j) * t:
                                   par * 12288 + (2 * j2 + j + 1) * t])
        for hl in range(2):
            for j in range(2):
                for tt in range(8):
                    c0 = par * 12288 + 8 * t + hl * 2048 + tt * 256 + j * 128
                    _ind_scatter(ctx, ex, gv_flat[hl], slots[:, c0:c0 + 128])
        ctx.fence()
        ex.wait_sent(ctx)


def emit_exchange_e2(ctx, k, sc, ex, a_local, gather, stg, slots):
    t = TOK
    a_flat = [a.rearrange("b p (q t) -> (b p q) t", t=t) for a in a_local]
    g_flat = [g.rearrange("c p t -> (c p) t") for g in gather]
    for d in range(8):
        ex.barrier(ctx)
        for hl in range(2):
            for j in range(2):
                _ind_gather(ctx, ex, stg[:, (2 * hl + j) * t:(2 * hl + j + 1) * t], a_flat[hl])
        ctx.fence()
        par = d % 2
        ex.send(ctx, d, slots[:, par * 4096:(par + 1) * 4096], stg[:, 0:4096])
        ex.wait_recv(ctx)
        for hl in range(2):
            for j in range(2):
                _ind_scatter(ctx, ex, g_flat[hl],
                             slots[:, par * 4096 + (2 * hl + j) * t:
                                   par * 4096 + (2 * hl + j + 1) * t])
        ctx.fence()
        ex.wait_sent(ctx)


def build_xidx_host(core):
    cols = []
    part = np.arange(128, dtype=np.uint32)

    def e1():
        for d in range(8):
            p = _sigma(core, d)
            for j2 in range(4):
                for j in range(2):
                    cols.append((2 * p + j) * 128 + part)
            for hl in range(2):
                for tt in range(8):
                    cols.append((tt * 128 + part) * 8 + p)
            bp, tq = divmod(p, 4) if False else (p // 4, p % 4)
            for j2 in range(4):
                for j in range(2):
                    cols.append((2 * bp + j) * 512 + part * 4 + tq)
            for hl in range(2):
                for j in range(2):
                    for tt in range(8):
                        cols.append(np.full(128, (2 * bp + j) * 4096 + tq * 1024 + tt * 128, np.uint32) + part)

    def e2():
        for d in range(8):
            p = _sigma(core, d)
            bp, tq = p // 4, p % 4
            for hl in range(2):
                for j in range(2):
                    cols.append((2 * bp + j) * 512 + part * 4 + tq)
            for hl in range(2):
                for j in range(2):
                    cols.append((2 * p + j) * 128 + part)

    e1(); e2(); e1(); e2()
    return np.stack(cols).astype(np.uint32)


N_XIDX = 4 * 8 * (8 + 16) // 1 if False else None



def build_fused(upto=99, vtm=True, dbg_gathers=False):
    from concourse import library_config
    ctx = Ctx()
    ctx.init_serial()
    t = TOK
    dr = {}
    dr["x"] = ctx.dram_in("x", [D, t])
    c_d = ctx.dram_in("consts", [128, 400])
    m_d = ctx.dram_in("masks2", [8, 128, QT])
    pt_d = ctx.dram_in("xidx", [128, 1152], mybir.dt.uint32)
    for nm, sh in (("wq", [D, D]), ("wk", [D, D]), ("wv", [D, D]),
                   ("wo", [D, D]), ("w1", [D, DFF]), ("w2", [DFF, D]),
                   ("wg_t", [D, DFF]), ("wu_t", [D, DFF]), ("wd_t", [DFF, D]),
                   ("wqkv_t", [D, 3 * D]), ("wo1", [D, D])):
        dr[nm] = ctx.dram_in(nm, sh)
    dr["w8_t"] = ctx.dram_in("w8_t", [D, D], mybir.dt.float8e4)
    dr["wscale"] = ctx.dram_in("wscale", [1, 1])
    dr["iscale"] = ctx.dram_in("iscale", [1, 1], BF16)
    v0_d = ctx.dram_in("vecs0", [5 * D])
    v1_d = ctx.dram_in("vecs1", [(NCH * 4 + NFF) * 128])
    for nm in ("xnh", "xnl", "qh", "ql", "kh", "kl"):
        dr[nm] = ctx.dram_tmp(nm, [D, t], BF16)
    dr["vth_loc"] = ctx.dram_tmp("vth_loc", [8, 128, D], BF16)
    dr["vh"] = ctx.dram_tmp("vh", [D, TOK], BF16)
    dr["vl"] = ctx.dram_tmp("vl", [D, TOK], BF16)
    dr["vtl_loc"] = ctx.dram_tmp("vtl_loc", [8, 128, D], BF16)
    dr["fout"] = ctx.dram_tmp("fout", [D, t])
    dr["ispill"] = ctx.dram_tmp("ispill", [NFF, 2, 128, t], BF16)
    dr["zqspill"] = ctx.dram_tmp("zqspill", [NFF, 128, t], BF16)
    mk = ctx.dram_out if dbg_gathers else ctx.dram_tmp
    for nm in ("g_qh", "g_ql", "g_kh", "g_kl", "ah", "al",
               "g1_qh", "g1_ql", "g1_kh", "g1_kl", "ah1", "al1"):
        dr[nm] = mk(nm, [4, 128, S], BF16)
    for nm in ("g_vth", "g_vtl", "g1_vth", "g1_vtl"):
        dr[nm] = mk(nm, [4, S, 128], BF16)
    for nm in ("ath", "atl", "ath1", "atl1"):
        dr[nm] = ctx.dram_tmp(nm, [NCH, 128, t], BF16)
    dr["accp"] = ctx.dram_tmp("accp", [D, t])
    dr["ffn2"] = ctx.dram_tmp("ffn2", [D, t])
    dr["ffn1"] = ctx.dram_tmp("ffn1", [D, t])
    for nm in ("q1h", "q1l", "k1h", "k1l"):
        dr[nm] = ctx.dram_tmp(nm, [D, t], BF16)
    dr["q1vh"] = dr["q1h"]
    dr["v1h"] = ctx.dram_tmp("v1h", [D, t], BF16)
    dr["v1l"] = ctx.dram_tmp("v1l", [D, t], BF16)
    dr["v1th_loc"] = ctx.dram_tmp("v1th_loc", [8, 128, D], BF16)
    dr["v1tl_loc"] = ctx.dram_tmp("v1tl_loc", [8, 128, D], BF16)
    dr["hspill"] = ctx.dram_tmp("hspill", [NFF, 2, 128, t], BF16)
    dr["xasph"] = ctx.dram_tmp("xasph", [NCH, 128, t], BF16)
    dr["xaspl"] = ctx.dram_tmp("xaspl", [NCH, 128, t], BF16)
    dr["x2"] = ctx.dram_out("x2", [D, t])

    k = Consts(ctx)
    sc = Scratch(ctx)
    psW = [ctx.psum(f"psw{i}", [128, 1024], F32) for i in range(4)]
    psA, psB, psC = psW[0], psW[1], psW[2]
    psD = psW[3][:, 0:512]
    psE = psW[3][:, 512:1024]

    vecs0 = ctx.sbuf("vecs0_sb", [128, 5 * NCH], F32)
    vecs1 = ctx.sbuf("vecs1_sb", [128, NCH * 4 + NFF], F32)
    cellcols = ctx.sbuf("cellcols", [128, 8], F32)
    maskssb = ctx.sbuf("maskssb", [128, 8 * QT], F32)
    ptab = ctx.sbuf("xidx_sb", [128, 1152], mybir.dt.uint32)
    rbufh = ctx.sbuf("rbufh", [128, t], BF16)
    rbufl = ctx.sbuf("rbufl", [128, t], BF16)
    sclsb = ctx.sbuf("sclsb", [1, 4], F32)
    isbf = ctx.sbuf("isbf", [1, 1], BF16)
    kcell = ctx.sbuf("kcell", [128, 1], F32)

    ex = Exch(ctx, ptab)

    ctx.s("gpsimd", lambda e: (e.load_library(library_config.proxy), e.engine_nop())[1])
    k.load(ctx, c_d[:])
    emit_load_vec(ctx, v0_d, vecs0, 5 * NCH)
    emit_load_vec(ctx, v1_d, vecs1, NCH * 4 + NFF)
    ctx.s("sync", lambda e: e.dma_start(
        out=maskssb[:].rearrange("p (i f) -> p i f", f=QT), in_=m_d.transpose([1, 0, 2])), dma=True)
    ctx.s("sync", lambda e: e.dma_start(out=ptab[:], in_=pt_d), dma=True)
    ctx.fence()

    if upto <= -1:
        return _finish_fused(ctx, dr)
    wcol32 = ctx.sbuf("wcol32", [128, NCH * 128], F32)
    wcolh = ctx.sbuf("wcolh", [128, NCH * 128], BF16)
    wcoll = ctx.sbuf("wcoll", [128, NCH * 128], BF16)
    emit_t0(ctx, dr, k, sc, psA, psB, psC, psD, psE, vecs0, cellcols,
            wcol32, wcolh, wcoll, vtm=vtm)
    ctx.sbuf_free("wcoll", "wcolh", "wcol32")
    if upto <= 0:
        return _finish_fused(ctx, dr)

    stg = ctx.sbuf("xstg", [128, 12288], BF16)
    slots = ctx.sbuf("xslots", [128, 2 * 12288], BF16)
    qkv_local = {nm: dr[nm] for nm in ("qh", "ql", "kh", "kl")}
    gather0 = {"qh": dr["g_qh"], "ql": dr["g_ql"], "kh": dr["g_kh"],
               "kl": dr["g_kl"], "vth": dr["g_vth"], "vtl": dr["g_vtl"]}
    emit_exchange_e1(ctx, k, sc, ex, qkv_local, (dr["vth_loc"], dr["vtl_loc"]),
                     gather0, stg, slots)
    ctx.sbuf_free("xslots", "xstg")
    if upto <= 1:
        return _finish_fused(ctx, dr)

    adr = dict(qh=dr["g_qh"], ql=dr["g_ql"], kh=dr["g_kh"], kl=dr["g_kl"],
               vth=dr["g_vth"], vtl=dr["g_vtl"], ah=dr["ah"], al=dr["al"])
    emit_attn(ctx, adr, k, sc, psD, psE, psA[:, 0:512], psB[:, 0:512], maskssb[:, 0:4 * QT])
    if upto <= 2:
        return _finish_fused(ctx, dr)

    stg = ctx.sbuf("xstg2", [128, 4096], BF16)
    slots = ctx.sbuf("xslots2", [128, 8192], BF16)
    emit_exchange_e2(ctx, k, sc, ex, (dr["ah"], dr["al"]), (dr["ath"], dr["atl"]), stg, slots)
    ctx.sbuf_free("xslots2", "xstg2")
    if upto <= 3:
        return _finish_fused(ctx, dr)

    wcol32 = ctx.sbuf("wcol32b", [128, NCH * 128], F32)
    wcolh = ctx.sbuf("wcolhb", [128, NCH * 128], BF16)
    wcoll = ctx.sbuf("wcollb", [128, NCH * 128], BF16)
    emit_t1(ctx, dr, k, sc, psW, vecs1, wcol32, wcolh, wcoll, rbufh, rbufl,
            sclsb, isbf, kcell, vtm=True)
    ctx.sbuf_free("wcollb", "wcolhb", "wcol32b")
    if upto <= 4:
        return _finish_fused(ctx, dr)

    stg = ctx.sbuf("xstg3", [128, 12288], BF16)
    slots = ctx.sbuf("xslots3", [128, 2 * 12288], BF16)
    qkv_local1 = {"qh": dr["q1h"], "ql": dr["q1l"], "kh": dr["k1h"], "kl": dr["k1l"]}
    gather1 = {"qh": dr["g1_qh"], "ql": dr["g1_ql"], "kh": dr["g1_kh"],
               "kl": dr["g1_kl"], "vth": dr["g1_vth"], "vtl": dr["g1_vtl"]}
    emit_exchange_e1(ctx, k, sc, ex, qkv_local1, (dr["v1th_loc"], dr["v1tl_loc"]),
                     gather1, stg, slots)
    ctx.sbuf_free("xslots3", "xstg3")
    if upto <= 5:
        return _finish_fused(ctx, dr)

    adr1 = dict(qh=dr["g1_qh"], ql=dr["g1_ql"], kh=dr["g1_kh"], kl=dr["g1_kl"],
                vth=dr["g1_vth"], vtl=dr["g1_vtl"], ah=dr["ah1"], al=dr["al1"])
    emit_attn(ctx, adr1, k, sc, psD, psE, psA[:, 0:512], psB[:, 0:512], maskssb[:, 4 * QT:8 * QT])
    if upto <= 6:
        return _finish_fused(ctx, dr)

    stg = ctx.sbuf("xstg4", [128, 4096], BF16)
    slots = ctx.sbuf("xslots4", [128, 8192], BF16)
    emit_exchange_e2(ctx, k, sc, ex, (dr["ah1"], dr["al1"]), (dr["ath1"], dr["atl1"]), stg, slots)
    ctx.sbuf_free("xslots4", "xstg4")
    if upto <= 7:
        return _finish_fused(ctx, dr)

    wcol32 = ctx.sbuf("wcol32c", [128, 512], F32)
    wcolh = ctx.sbuf("wcolhc", [128, 512], BF16)
    wcoll = ctx.sbuf("wcollc", [128, 512], BF16)
    emit_t2(ctx, dr, k, sc, psW, wcol32, wcolh, wcoll, rbufh, rbufl)
    ctx.sbuf_free("wcollc", "wcolhc", "wcol32c")
    ctx.emit()
    mybir.codegen_inst_isa_subclasses(ctx.nc)
    return ctx


def _finish_fused(ctx, dr):
    t = TOK
    sc0 = ctx.sbuf("fin_z", [128, t], F32)
    outr = dr["x2"].rearrange("(c p) t -> c p t", p=128)
    ctx.s("vector", lambda e: e.memset(sc0[:], 0.0))
    for c in range(NCH):
        ctx.s("gpsimd", lambda e, c=c: e.dma_start(out=outr[c], in_=sc0[:]), dma=True)
        ctx.fence()
    ctx.emit()
    mybir.codegen_inst_isa_subclasses(ctx.nc)
    return ctx

USE_FUSED = True


def kernel_fused(L0_attn_x, L0_attn_wq, L0_attn_bq, L0_attn_wk, L0_attn_bk,
                 L0_attn_wv, L0_attn_bv, L0_attn_wo, L0_attn_bo, L0_attn_w1,
                 L0_attn_b1, L0_attn_w2, L0_attn_b2, L0_attn_g1, L0_attn_be1,
                 L0_attn_g2, L0_attn_be2, L0_ffn_w_gate, L0_ffn_w_up,
                 L0_ffn_w_down, L1_attn_w_qkv, L1_attn_w_o, L1_ffn_weight,
                 L1_ffn_weight_scale, L1_ffn_input_scale):
    import ml_dtypes
    consts = Consts.host_array()
    f32 = np.float32
    x = np.asarray(L0_attn_x, dtype=f32)
    vecs0 = np.concatenate([np.asarray(v, dtype=f32) for v in
                            (L0_attn_bq, L0_attn_bk, L0_attn_bv, L0_attn_g1, L0_attn_be1)])
    vecs1 = np.concatenate([np.asarray(v, dtype=f32) for v in
                            (L0_attn_bo, L0_attn_b2, L0_attn_g2, L0_attn_be2, L0_attn_b1)])
    masks2 = np.concatenate([attn_masks_host(-1e9),
                             attn_masks_host(float(np.finfo(np.float32).min))], axis=0)
    wg_t = np.ascontiguousarray(np.asarray(L0_ffn_w_gate, dtype=f32).T)
    wu_t = np.ascontiguousarray(np.asarray(L0_ffn_w_up, dtype=f32).T)
    wd_t = np.ascontiguousarray(np.asarray(L0_ffn_w_down, dtype=f32).T)
    wqkv_t = np.ascontiguousarray(np.asarray(L1_attn_w_qkv, dtype=f32).T)
    wo1_t = np.ascontiguousarray(np.asarray(L1_attn_w_o, dtype=f32).T)
    w8_t = np.ascontiguousarray(
        np.asarray(L1_ffn_weight).view(np.uint8).T).view(ml_dtypes.float8_e4m3)
    wscale = np.asarray(L1_ffn_weight_scale, dtype=f32).reshape(1, 1)
    iscale = np.asarray(L1_ffn_input_scale).reshape(1, 1)

    fz = _get(build_fused, "fused")
    ins = []
    for c in range(NCORES):
        b, qd = divmod(c, 4)
        xs = np.ascontiguousarray(x[b, qd * TOK:(qd + 1) * TOK, :].T)
        ins.append({"x": xs, "consts": consts, "masks2": masks2,
                    "xidx": np.ascontiguousarray(build_xidx_host(c).T),
                    "wq": np.asarray(L0_attn_wq, f32), "wk": np.asarray(L0_attn_wk, f32),
                    "wv": np.asarray(L0_attn_wv, f32), "wo": np.asarray(L0_attn_wo, f32),
                    "w1": np.asarray(L0_attn_w1, f32), "w2": np.asarray(L0_attn_w2, f32),
                    "wg_t": wg_t, "wu_t": wu_t, "wd_t": wd_t,
                    "wqkv_t": wqkv_t, "wo1": wo1_t, "w8_t": w8_t,
                    "wscale": wscale, "iscale": iscale,
                    "vecs0": vecs0, "vecs1": vecs1})
    res = run_bass_kernel_spmd(fz.nc, ins, core_ids=list(range(NCORES)))
    out = np.empty((B, S, D), dtype=np.float32)
    for c in range(NCORES):
        b, qd = divmod(c, 4)
        out[b, qd * TOK:(qd + 1) * TOK, :] = np.asarray(res.results[c]["x2"], dtype=np.float32).T
    return out


_kernel_multilaunch = kernel


def kernel(**inputs):
    if USE_FUSED:
        return kernel_fused(**inputs)
    return _kernel_multilaunch(**inputs)


# version 09
# speedup vs baseline: 1.3698x; 1.0324x over previous
import math
import numpy as np
import concourse.bass as bass
import concourse.mybir as mybir
from concourse.bass_utils import run_bass_kernel_spmd

F32 = mybir.dt.float32
BF16 = mybir.dt.bfloat16
I32 = mybir.dt.int32
AF = mybir.ActivationFunctionType
ALU = mybir.AluOpType
AX = mybir.AxisListType

NCORES = 8
B, S, D = 2, 4096, 2048
TOK = 1024
NCH = D // 128
DFF = 8192
NFF = DFF // 128
HD = 128
LN_EPS = 1e-5
RSQ_HD = 1.0 / math.sqrt(float(HD))
FWHT_NORM = 1.0 / math.sqrt(float(DFF))



class Sem:
    def __init__(self, handle, name):
        self.h = handle
        self.name = name
        self.val = 0


class Ctx:
    ENGINES = ("sync", "tensor", "vector", "scalar", "gpsimd")

    def __init__(self):
        self.nc = bass.Bass("TRN2", target_bir_lowering=False)
        self._cm = []
        self.sems = {}
        self.ops = []
        self.dram = {}
        self._uid = 0
        self._last = None
        self.sq = None

    def init_serial(self):
        self.sq = self.sem("sq")

    def uid(self):
        self._uid += 1
        return self._uid

    def dram_in(self, name, shape, dt=F32):
        t = self.nc.dram_tensor(name, list(shape), dt, kind="ExternalInput").ap()
        self.dram[name] = t
        return t

    def dram_out(self, name, shape, dt=F32):
        t = self.nc.dram_tensor(name, list(shape), dt, kind="ExternalOutput").ap()
        self.dram[name] = t
        return t

    def dram_tmp(self, name, shape, dt=F32):
        t = self.nc.dram_tensor(name, list(shape), dt, kind="Internal").ap()
        self.dram[name] = t
        return t

    def sbuf(self, name, shape, dt=F32):
        cm = self.nc.sbuf_tensor(name, list(shape), dt)
        h = cm.__enter__()
        self._cm.append(cm)
        self._sbuf_cms = getattr(self, "_sbuf_cms", {})
        self._sbuf_cms[name] = cm
        return h

    def sbuf_free(self, *names):
        for name in names:
            cm = self._sbuf_cms.pop(name)
            cm.__exit__(None, None, None)
            self._cm.remove(cm)

    def psum(self, name, shape, dt=F32):
        cm = self.nc.psum_tensor(name, list(shape), dt)
        h = cm.__enter__()
        self._cm.append(cm)
        return h

    def sem(self, name):
        if name not in self.sems:
            cm = self.nc.semaphore(name)
            h = cm.__enter__()
            self._cm.append(cm)
            self.sems[name] = Sem(h, name)
        return self.sems[name]

    FULL_SERIAL = True
    COALESCE = ("tensor",)

    def s(self, eng, fn, dma=False):
        rec = [eng, fn, [], [], dma]
        if self._last is not None:
            lrec = self._last
            if (lrec[0] == eng and not lrec[4] and not dma
                    and (eng in self.COALESCE or not self.FULL_SERIAL)):
                pass
            else:
                self._ensure_inc(lrec)
                rec[2].append((self.sq, self.sq.val))
        elif self.ops:
            rec[2].append((self.sq, self.sq.val))
        if dma:
            self.sq.val += 16
            rec[3].append((self.sq, 16))
        self.ops.append(rec)
        self._last = rec
        return rec

    def _ensure_inc(self, rec):
        if not rec[3]:
            self.sq.val += 1
            rec[3].append((self.sq, 1))

    def fence(self):
        if self._last is not None:
            self._ensure_inc(self._last)
            self._last = None

    def emit(self):
        if self._last is not None:
            self._ensure_inc(self._last)
        ops = self.ops
        with self.nc.Block() as blk:
            def runner(name):
                def run(eng):
                    for eng_name, fn, waits, incs, dma in ops:
                        if eng_name != name:
                            continue
                        for s, v in waits:
                            eng.wait_ge(s.h, v)
                        if fn is None:
                            for s, k in incs:
                                eng.sem_inc(s.h, k)
                        else:
                            instr = fn(eng)
                            for s, k in incs:
                                instr.then_inc(s.h, k)
                    eng.wait_ge(self.sq.h, self.sq.val)
                return run
            blk.sync(runner("sync"))
            blk.tensor(runner("tensor"))
            blk.vector(runner("vector"))
            blk.scalar(runner("scalar"))
            blk.gpsimd(runner("gpsimd"))

    def close(self):
        for cm in reversed(self._cm):
            cm.__exit__(None, None, None)
        self._cm = []


def ceil_div(a, b):
    return -(-a // b)


def halves(n, w=512):
    return [(i, min(w, n - i)) for i in range(0, n, w)]



class Consts:

    def __init__(self, ctx):
        self.c32 = ctx.sbuf("consts32", [128, 400], F32)
        self.ident_f32 = self.c32[:, 0:128]
        self.ones_col_bf = ctx.sbuf("ones_col_bf", [128, 1], BF16)
        self.ident_bf = ctx.sbuf("ident_bf", [128, 128], BF16)
        self.h128_bf = ctx.sbuf("h128_bf", [128, 128], BF16)
        self.ones_row_f32 = self.c32[0:1, 256:384]
        self.eps_col = self.c32[:, 385:386]
        self.eps_1 = self.c32[0:1, 384:385]
        import math as _m
        ln2 = _m.log(2.0)
        for col, val in ((386, 3.0 * ln2), (387, -3.0 * ln2), (388, 0.0), (389, 1.0)):
            ctx.nc.const_aps.register(F32, float(np.float32(val)), self.c32[:, col:col + 1])

    def load(self, ctx, dram_c):
        ctx.s("sync", lambda e: e.dma_start(out=self.c32[:], in_=dram_c), dma=True)
        ctx.s("vector", lambda e: e.tensor_copy(self.ident_bf[:], self.c32[:, 0:128]))
        ctx.s("vector", lambda e: e.tensor_copy(self.h128_bf[:], self.c32[:, 128:256]))
        ctx.s("vector", lambda e: e.tensor_copy(self.ones_col_bf[:], self.c32[:, 256:257]))
        ctx.fence()

    @staticmethod
    def host_array():
        c = np.zeros((128, 400), dtype=np.float32)
        c[:, 0:128] = np.eye(128, dtype=np.float32)
        h = np.array([[1.0]], dtype=np.float32)
        while h.shape[0] < 128:
            h = np.block([[h, h], [h, -h]])
        c[:, 128:256] = h
        c[:, 256] = 1.0
        c[0, 256:384] = 1.0
        c[0, 384] = LN_EPS
        c[:, 385] = LN_EPS
        import math as _m
        ln2 = _m.log(2.0)
        c[:, 386] = np.float32(3.0 * ln2)
        c[:, 387] = np.float32(-3.0 * ln2)
        c[:, 388] = 0.0
        c[:, 389] = 1.0
        return c


class Scratch:
    def __init__(self, ctx, tok=TOK):
        t = tok
        self.x32 = ctx.sbuf("sc_x32", [128, t], F32)
        self.y32 = ctx.sbuf("sc_y32", [128, t], F32)
        self.z32 = ctx.sbuf("sc_z32", [128, t], F32)
        self.bfA = ctx.sbuf("sc_bfA", [128, 2 * t], BF16)
        self.bfB = ctx.sbuf("sc_bfB", [128, 2 * t], BF16)
        self.i32 = ctx.sbuf("sc_i32", [128, t], I32)
        self.rows = ctx.sbuf("sc_rows", [1, 8 * t], F32)
        self.bc = ctx.sbuf("sc_bc", [128, 2 * t], F32)


def bcast_row(ctx, k, row_ap, out_ap, ps, n):
    for off, w in halves(n):
        ctx.s("tensor", lambda e, off=off, w=w: e.matmul(
            ps[:, 0:w], k.ones_row_f32, row_ap[0:1, off:off + w],
            start=True, stop=True))
        ctx.s("vector", lambda e, off=off, w=w: e.tensor_copy(
            out_ap[:, off:off + w], ps[:, 0:w]))



def emit_ln(ctx, k, sc, pair_hi, pair_lo, out_hi, out_lo, ps_a, ps_b,
            gamma=None, beta=None, nch=NCH, tok=TOK, out32_cb=None):
    t = tok
    ch = lambda sb, c: sb[:, c * t:(c + 1) * t]
    for c in range(nch):
        hi, lo = ch(pair_hi, c), ch(pair_lo, c)
        ctx.s("vector", lambda e, hi=hi, lo=lo: e.tensor_tensor(sc.x32[:], hi, lo, ALU.add))
        ctx.s("scalar", lambda e: e.activation(sc.y32[:], sc.x32[:], AF.Square))
        ctx.s("vector", lambda e: e.tensor_copy(sc.bfA[:, 0:t], sc.y32[:]))
        ctx.s("vector", lambda e: e.tensor_tensor(sc.bfB[:, 0:t], sc.y32[:], sc.bfA[:, 0:t], ALU.subtract))
        ctx.fence()
        for off, w in halves(t):
            for i, (mat, r0) in enumerate(((hi, 0), (lo, 0), (sc.bfA[:, 0:t], 32), (sc.bfB[:, 0:t], 32))):
                st = (c == 0) and (i % 2 == 0)
                sp = (c == nch - 1) and (i % 2 == 1)
                ctx.s("tensor", lambda e, mat=mat, r0=r0, off=off, w=w, st=st, sp=sp: e.matmul(
                    ps_a[r0:r0 + 1, off:off + w], k.ones_col_bf[:],
                    mat[:, off:off + w], start=st, stop=sp))
        ctx.fence()
    R = sc.rows
    ctx.s("vector", lambda e: e.tensor_copy(R[0:1, 0:t], ps_a[0:1, 0:t]))
    ctx.s("vector", lambda e: e.tensor_copy(R[0:1, t:2 * t], ps_a[32:33, 0:t]))
    invd = 1.0 / float(nch * 128)
    ctx.s("vector", lambda e: e.tensor_scalar(R[0:1, 2 * t:3 * t], R[0:1, 0:t], invd, None, ALU.mult))
    ctx.s("vector", lambda e: e.tensor_scalar(R[0:1, 3 * t:4 * t], R[0:1, t:2 * t], invd, None, ALU.mult))
    ctx.s("scalar", lambda e: e.activation(R[0:1, 0:t], R[0:1, 2 * t:3 * t], AF.Square))
    ctx.s("vector", lambda e: e.tensor_tensor(R[0:1, 3 * t:4 * t], R[0:1, 3 * t:4 * t], R[0:1, 0:t], ALU.subtract))
    ctx.s("scalar", lambda e: e.activation(R[0:1, 0:t], R[0:1, 3 * t:4 * t], AF.Sqrt, bias=k.eps_1, scale=1.0))
    ctx.s("vector", lambda e: e.reciprocal(R[0:1, 4 * t:5 * t], R[0:1, 0:t]))
    ctx.s("scalar", lambda e: e.activation(R[0:1, 0:t], R[0:1, 4 * t:5 * t], AF.Square))
    ctx.s("vector", lambda e: e.tensor_scalar(R[0:1, 3 * t:4 * t], R[0:1, 3 * t:4 * t], LN_EPS, None, ALU.add))
    ctx.s("vector", lambda e: e.tensor_tensor(R[0:1, 0:t], R[0:1, 0:t], R[0:1, 3 * t:4 * t], ALU.mult))
    ctx.s("vector", lambda e: e.tensor_scalar(R[0:1, 0:t], R[0:1, 0:t], -0.5, 1.5, ALU.mult, ALU.add))
    ctx.s("vector", lambda e: e.tensor_tensor(R[0:1, 4 * t:5 * t], R[0:1, 4 * t:5 * t], R[0:1, 0:t], ALU.mult))
    ctx.s("vector", lambda e: e.scalar_tensor_tensor(
        R[0:1, 2 * t:3 * t], R[0:1, 2 * t:3 * t], -1.0, R[0:1, 4 * t:5 * t], ALU.mult, ALU.mult))
    ctx.fence()
    bcast_row(ctx, k, R[0:1, 4 * t:5 * t], sc.bc[:, 0:t], ps_b, t)
    bcast_row(ctx, k, R[0:1, 2 * t:3 * t], sc.bc[:, t:2 * t], ps_b, t)
    ctx.fence()
    for c in range(nch):
        hi, lo = ch(pair_hi, c), ch(pair_lo, c)
        ctx.s("vector", lambda e, hi=hi, lo=lo: e.tensor_tensor(sc.x32[:], hi, lo, ALU.add))
        ctx.s("vector", lambda e: e.tensor_tensor(sc.y32[:], sc.x32[:], sc.bc[:, 0:t], ALU.mult))
        ctx.s("vector", lambda e: e.tensor_tensor(sc.y32[:], sc.y32[:], sc.bc[:, t:2 * t], ALU.add))
        if gamma is not None:
            ctx.s("vector", lambda e, c=c: e.tensor_scalar(
                sc.y32[:], sc.y32[:], gamma[:, c:c + 1], beta[:, c:c + 1], ALU.mult, ALU.add))
        oh, ol = ch(out_hi, c), ch(out_lo, c)
        ctx.s("vector", lambda e, oh=oh: e.tensor_copy(oh, sc.y32[:]))
        ctx.s("vector", lambda e, oh=oh, ol=ol: e.tensor_tensor(ol, sc.y32[:], oh, ALU.subtract))
        if out32_cb is not None:
            out32_cb(c, sc.y32[:])
        ctx.fence()



def emit_proj(ctx, k, w_dram, x_hi, x_lo, n_in_ch, n_out_ch, consumer,
              tok=TOK, wbufs=None, quant_lhs=None, oc_list=None):
    if wbufs is None:
        wbufs = {}
    u = ctx.uid()
    kw = n_in_ch * 128
    w32a = wbufs.get("w32") or ctx.sbuf(f"pj{u}_w32", [128, kw], F32)
    wh = wbufs.get("wh") or ctx.sbuf(f"pj{u}_wh", [128, kw], BF16)
    wl = wbufs.get("wl") or ctx.sbuf(f"pj{u}_wl", [128, kw], BF16)
    ps = wbufs.get("ps") or ctx.psum(f"pj{u}_ps", [128, tok], F32)
    w32b_name = f"pj{u}_w32b"
    w32b = wbufs.get("w32b") or ctx.sbuf(w32b_name, [128, kw], F32)
    wr = w_dram.rearrange("(kc p) o -> kc p o", p=128)
    ocs = list(oc_list if oc_list is not None else range(n_out_ch))
    n = len(ocs)
    s_ld = ctx.sem("pjld")
    ctx.fence()
    entry_val = ctx.sq.val
    frv = {}

    def dma_async(i):
        w32 = (w32a, w32b)[i % 2]
        src = wr[:, :, ocs[i] * 128:(ocs[i] + 1) * 128].transpose([1, 0, 2])
        wait_v = entry_val if i < 2 else frv[i - 2]
        s_ld.val += 16
        ctx.ops.append(["sync",
                        (lambda e, src=src, w32=w32: e.dma_start(
                            out=w32[:].rearrange("p (kc f) -> p kc f", f=128), in_=src)),
                        [(ctx.sq, wait_v)], [(s_ld, 16)], True])

    ld_base0 = s_ld.val
    dma_async(0)
    if n > 1:
        dma_async(1)
    for i, oc in enumerate(ocs):
        w32 = (w32a, w32b)[i % 2]
        if quant_lhs is None:
            r1 = ctx.s("vector", lambda e, w32=w32: e.tensor_copy(wh[:], w32[:]))
            ctx.s("vector", lambda e, w32=w32: e.tensor_tensor(wl[:], w32[:], wh[:], ALU.subtract))
        else:
            r1 = quant_lhs(ctx, w32, wh, wl)
        r1[2].append((s_ld, ld_base0 + 16 * (i + 1)))
        ctx.fence()
        frv[i] = ctx.sq.val
        for kc in range(n_in_ch):
            lh = wh[:, kc * 128:(kc + 1) * 128]
            ll = wl[:, kc * 128:(kc + 1) * 128]
            for off, w in halves(tok):
                st = (kc == 0)
                sp = (kc == n_in_ch - 1)
                xh = x_hi[:, kc * tok + off: kc * tok + off + w]
                if x_lo is None:
                    ctx.s("tensor", lambda e, lh=lh, xh=xh, off=off, w=w, st=st, sp=sp: e.matmul(
                        ps[:, off:off + w], lh, xh, start=st, stop=sp))
                else:
                    xl = x_lo[:, kc * tok + off: kc * tok + off + w]
                    ctx.s("tensor", lambda e, lh=lh, xh=xh, off=off, w=w, st=st: e.matmul(
                        ps[:, off:off + w], lh, xh, start=st, stop=False))
                    ctx.s("tensor", lambda e, lh=lh, xl=xl, off=off, w=w: e.matmul(
                        ps[:, off:off + w], lh, xl, start=False, stop=False))
                    ctx.s("tensor", lambda e, ll=ll, xh=xh, off=off, w=w, sp=sp: e.matmul(
                        ps[:, off:off + w], ll, xh, start=False, stop=sp))
        ctx.fence()
        consumer(oc, ps)
        ctx.fence()
        if i + 2 < n:
            dma_async(i + 2)
    if "w32b" not in wbufs:
        ctx.sbuf_free(w32b_name)



def shard_tokens_fm(x):
    out = []
    for c in range(NCORES):
        b, q = divmod(c, 4)
        out.append(np.ascontiguousarray(x[b, q * TOK:(q + 1) * TOK, :].T))
    return out


def gather_tokens_fm(shards):
    x = np.empty((B, S, D), dtype=np.float32)
    for c in range(NCORES):
        b, q = divmod(c, 4)
        x[b, q * TOK:(q + 1) * TOK, :] = np.asarray(shards[c], dtype=np.float32).T
    return x





def emit_load_vec(ctx, dram_vec, sb, nch):
    src = dram_vec.rearrange("(c p) -> c p", p=128).transpose([1, 0])
    ctx.s("sync", lambda e: e.dma_start(out=sb[:, 0:nch], in_=src,
                                        allow_slow_non_contiguous=True), dma=True)


def emit_pair_load(ctx, sc, dram_fm, hi, lo, nch, tok=TOK):
    xr = dram_fm.rearrange("(c p) t -> c p t", p=128)
    for c in range(nch):
        ctx.s("sync", lambda e, c=c: e.dma_start(out=sc.x32[:], in_=xr[c]), dma=True)
        ctx.s("vector", lambda e, c=c: e.tensor_copy(hi[:, c * tok:(c + 1) * tok], sc.x32[:]))
        ctx.s("vector", lambda e, c=c: e.tensor_tensor(
            lo[:, c * tok:(c + 1) * tok], sc.x32[:], hi[:, c * tok:(c + 1) * tok], ALU.subtract))
        ctx.fence()


def emit_pair_store(ctx, hi, lo, dram_hi, dram_lo, nch, tok=TOK):
    hr = dram_hi.rearrange("(c p) t -> c p t", p=128)
    lr = dram_lo.rearrange("(c p) t -> c p t", p=128)
    for c in range(nch):
        ctx.s("gpsimd", lambda e, c=c: e.dma_start(out=hr[c], in_=hi[:, c * tok:(c + 1) * tok]), dma=True)
        ctx.s("gpsimd", lambda e, c=c: e.dma_start(out=lr[c], in_=lo[:, c * tok:(c + 1) * tok]), dma=True)



def emit_colmax_to_row(ctx, k, m_ap, row_ap, ps, tmp_col, n, clip_min=None):
    for tt in range(ceil_div(n, 128)):
        w = min(128, n - tt * 128)
        ctx.s("tensor", lambda e, tt=tt, w=w: e.transpose(
            ps[:, 0:128], m_ap[:, tt * 128:tt * 128 + w], k.ident_f32))
        ctx.s("vector", lambda e, w=w: e.tensor_reduce(
            tmp_col[0:w, 0:1], ps[0:w, 0:128], axis=AX.X, op=ALU.max))
        ctx.fence()
        ctx.s("tensor", lambda e: e.transpose(ps[0:1, 128:256], tmp_col[:, 0:1], k.ident_f32))
        ctx.s("vector", lambda e, tt=tt, w=w: e.tensor_copy(
            row_ap[0:1, tt * 128:tt * 128 + w], ps[0:1, 128:128 + w]))
        ctx.fence()
    if clip_min is not None:
        ctx.s("vector", lambda e: e.tensor_scalar(
            row_ap[0:1, 0:n], row_ap[0:1, 0:n], clip_min, None, ALU.max))
        ctx.fence()


def emit_rne_clip_bf16(ctx, sc, src32_ap, dst_bf_ap, lo=-128.0, hi=127.0):
    n = src32_ap.shape[-1]
    ctx.s("vector", lambda e: e.tensor_copy(sc.i32[:, 0:n], src32_ap))
    ctx.s("vector", lambda e: e.tensor_copy(sc.z32[:, 0:n], sc.i32[:, 0:n]))
    ctx.s("vector", lambda e: e.tensor_scalar(dst_bf_ap, sc.z32[:, 0:n], hi, lo, ALU.min, ALU.max))



R_AMAX, R_SCL, R_TMP, R_TMP2, R_AMAX2 = 0, 1, 2, 3, 4
R_CELLS, R_KQ, R_KZQ = 5, 6, 7


def rowslot(sc, i, o=0, n=1024):
    return sc.rows[0:1, i * 1024 + o:i * 1024 + o + n]


def emit_w_mean_cells(ctx, k, sc, weights, cells, cellcols, wcol32, psE):
    wstat = sc.bc[:, 0:8]
    for wi, (w_d, rows_, cols_) in enumerate(weights):
        wr = w_d.rearrange("(s p) o -> s p o", p=128)
        first = True
        for s_ in range(rows_):
            for co in range(0, cols_, 2048):
                cw = min(2048, cols_ - co)
                ctx.s("sync", lambda e, s_=s_, co=co, cw=cw, wr=wr: e.dma_start(
                    out=wcol32[:, 0:cw], in_=wr[s_][:, co:co + cw]), dma=True)
                ctx.s("vector", lambda e, cw=cw: e.tensor_reduce(
                    sc.z32[:, 0:1], wcol32[:, 0:cw], axis=AX.X, op=ALU.add,
                    apply_absolute_value=True))
                if first:
                    ctx.s("vector", lambda e, wi=wi: e.tensor_copy(wstat[:, wi:wi + 1], sc.z32[:, 0:1]))
                    first = False
                else:
                    ctx.s("vector", lambda e, wi=wi: e.tensor_tensor(
                        wstat[:, wi:wi + 1], wstat[:, wi:wi + 1], sc.z32[:, 0:1], ALU.add))
                ctx.fence()
        ctx.s("tensor", lambda e, wi=wi: e.transpose(psE[0:1, 0:128], wstat[:, wi:wi + 1], k.ident_f32))
        ctx.s("vector", lambda e, wi=wi: e.tensor_reduce(
            cells[0:1, wi:wi + 1], psE[0:1, 0:128], axis=AX.X, op=ALU.add))
        ctx.fence()
    for wi in range(3):
        ctx.s("vector", lambda e, wi=wi: e.tensor_scalar(
            cells[0:1, wi:wi + 1], cells[0:1, wi:wi + 1], 1.0 / (2048.0 * 8192.0), 1e-5,
            ALU.mult, ALU.max))
    ctx.s("vector", lambda e: e.reciprocal(cells[0:1, 3:6], cells[0:1, 0:3]))
    ctx.fence()
    for wi in range(6):
        ctx.s("tensor", lambda e, wi=wi: e.matmul(
            psE[:, 0:1], k.ones_row_f32, cells[0:1, wi:wi + 1], start=True, stop=True))
        ctx.s("vector", lambda e, wi=wi: e.tensor_copy(cellcols[:, wi:wi + 1], psE[:, 0:1]))
        ctx.fence()


def emit_actquant(ctx, k, sc, pair_hi, pair_lo, xq, nch, psD, psE):
    t = TOK
    m = sc.bc[:, t:2 * t]
    for c in range(nch):
        ctx.s("vector", lambda e, c=c: e.tensor_tensor(
            sc.x32[:], pair_hi[:, c * t:(c + 1) * t], pair_lo[:, c * t:(c + 1) * t], ALU.add))
        ctx.s("scalar", lambda e: e.activation(sc.y32[:], sc.x32[:], AF.Abs))
        if c == 0:
            ctx.s("vector", lambda e: e.tensor_copy(m[:], sc.y32[:]))
        else:
            ctx.s("vector", lambda e: e.tensor_tensor(m[:], m[:], sc.y32[:], ALU.max))
        ctx.fence()
    emit_colmax_to_row(ctx, k, m, rowslot(sc, R_AMAX), psE, sc.z32[:, 0:1], t, clip_min=1e-5)
    ctx.s("vector", lambda e: e.reciprocal(rowslot(sc, R_SCL), rowslot(sc, R_AMAX)))
    ctx.s("vector", lambda e: e.tensor_scalar(
        rowslot(sc, R_SCL), rowslot(sc, R_SCL), 128.0, None, ALU.mult))
    ctx.s("vector", lambda e: e.tensor_scalar(
        rowslot(sc, R_KQ), rowslot(sc, R_AMAX), 1.0 / 128.0, None, ALU.mult))
    ctx.fence()
    bcast_row(ctx, k, rowslot(sc, R_SCL), sc.bc[:, 0:t], psD, t)
    ctx.fence()
    for c in range(nch):
        ctx.s("vector", lambda e, c=c: e.tensor_tensor(
            sc.x32[:], pair_hi[:, c * t:(c + 1) * t], pair_lo[:, c * t:(c + 1) * t], ALU.add))
        ctx.s("vector", lambda e: e.tensor_tensor(sc.x32[:], sc.x32[:], sc.bc[:, 0:t], ALU.mult))
        emit_rne_clip_bf16(ctx, sc, sc.x32[:], xq[:, c * t:(c + 1) * t])
        ctx.fence()


def emit_tern(ctx, sc, w32ap, dst, sw_cell, n):
    ctx.s("vector", lambda e: e.tensor_scalar(
        sc.bfA[:, 0:n], w32ap, sw_cell, 0.5, ALU.mult, ALU.is_gt))
    ctx.s("vector", lambda e: e.tensor_scalar(
        sc.bfB[:, 0:n], w32ap, sw_cell, -0.5, ALU.mult, ALU.is_lt))
    ctx.s("vector", lambda e: e.tensor_tensor(dst, sc.bfA[:, 0:n], sc.bfB[:, 0:n], ALU.subtract))


def emit_bitnet_gateup(ctx, k, sc, xq, wg_d, wu_d, ispill_d, cells, cellcols,
                       wcol32, wcolh, tern2, psA, psB, psD):
    t = TOK
    nc2 = NCH * 128
    ctx.s("vector", lambda e: e.tensor_scalar(
        rowslot(sc, R_TMP), rowslot(sc, R_KQ), cells[0:1, 0:1], None, ALU.mult))
    ctx.s("vector", lambda e: e.tensor_scalar(
        rowslot(sc, R_TMP2), rowslot(sc, R_KQ), cells[0:1, 1:2], None, ALU.mult))
    ctx.fence()
    bcast_row(ctx, k, rowslot(sc, R_TMP), sc.bc[:, 0:t], psD, t)
    bcast_row(ctx, k, rowslot(sc, R_TMP2), sc.bc[:, t:2 * t], psD, t)
    ctx.fence()
    wgr = wg_d.rearrange("(kc p) o -> kc p o", p=128)
    wur = wu_d.rearrange("(kc p) o -> kc p o", p=128)
    for oc in range(NFF):
        for (wr_, dst, swi) in ((wgr, wcolh, 3), (wur, tern2, 4)):
            src = wr_[:, :, oc * 128:(oc + 1) * 128].transpose([1, 0, 2])
            ctx.s("sync", lambda e, src=src: e.dma_start(
                out=wcol32[:, 0:nc2].rearrange("p (kc f) -> p kc f", f=128), in_=src), dma=True)
            emit_tern(ctx, sc, wcol32[:, 0:nc2], dst[:, 0:nc2], cellcols[:, swi:swi + 1], nc2)
            ctx.fence()
        for kc in range(NCH):
            for off, w in halves(t):
                st, sp = kc == 0, kc == NCH - 1
                ctx.s("tensor", lambda e, kc=kc, off=off, w=w, st=st, sp=sp: e.matmul(
                    psA[:, off:off + w], wcolh[:, kc * 128:(kc + 1) * 128],
                    xq[:, kc * t + off:kc * t + off + w], start=st, stop=sp))
                ctx.s("tensor", lambda e, kc=kc, off=off, w=w, st=st, sp=sp: e.matmul(
                    psB[:, off:off + w], tern2[:, kc * 128:(kc + 1) * 128],
                    xq[:, kc * t + off:kc * t + off + w], start=st, stop=sp))
        ctx.fence()
        ctx.s("vector", lambda e: e.tensor_tensor(sc.x32[:], psA[:, 0:t], sc.bc[:, 0:t], ALU.mult))
        ctx.s("scalar", lambda e: e.activation(sc.x32[:], sc.x32[:], AF.Silu))
        ctx.s("vector", lambda e: e.tensor_tensor(sc.y32[:], psB[:, 0:t], sc.bc[:, t:2 * t], ALU.mult))
        ctx.s("vector", lambda e: e.tensor_tensor(sc.x32[:], sc.x32[:], sc.y32[:], ALU.mult))
        ctx.s("vector", lambda e: e.tensor_copy(sc.bfA[:, 0:t], sc.x32[:]))
        ctx.s("vector", lambda e: e.tensor_tensor(sc.bfB[:, 0:t], sc.x32[:], sc.bfA[:, 0:t], ALU.subtract))
        ctx.s("gpsimd", lambda e, oc=oc: e.dma_start(out=ispill_d[oc, 0], in_=sc.bfA[:, 0:t]), dma=True)
        ctx.s("gpsimd", lambda e, oc=oc: e.dma_start(out=ispill_d[oc, 1], in_=sc.bfB[:, 0:t]), dma=True)
        ctx.fence()


def emit_bitnet_stageb(ctx, k, sc, ispill_d, zqspill_d, zbuf, ihbuf, ilbuf,
                       pbuf, ztile, psD, psE):
    TT = 128
    for tile in range(8):
        t0 = tile * TT
        for hl, buf in ((0, ihbuf), (1, ilbuf)):
            for g8 in range(0, NFF, 8):
                src = ispill_d[g8:g8 + 8, hl, :, t0:t0 + TT].transpose([1, 0, 2])
                ctx.s("sync", lambda e, src=src, g8=g8, buf=buf: e.dma_start(
                    out=buf[:, g8 * TT:(g8 + 8) * TT].rearrange("p (a f) -> p a f", f=TT),
                    in_=src), dma=True)
        ctx.fence()
        for a in range(NFF):
            ctx.s("tensor", lambda e, a=a: e.matmul(
                psD[:, 0:TT], k.h128_bf[:], ihbuf[:, a * TT:(a + 1) * TT], start=True, stop=False))
            ctx.s("tensor", lambda e, a=a: e.matmul(
                psD[:, 0:TT], k.h128_bf[:], ilbuf[:, a * TT:(a + 1) * TT], start=False, stop=True))
            ctx.s("vector", lambda e, a=a: e.tensor_copy(zbuf[:, a * TT:(a + 1) * TT], psD[:, 0:TT]))
            ctx.fence()
        TB = 32
        for sp_ in range(3):
            h1, h2 = 1 << (2 * sp_), 1 << (2 * sp_ + 1)
            for tb in range(0, TT, TB):
                for (hh, src_t, dst_t) in ((h1, "z", "p"), (h2, "p", "z")):
                    def mk(which, a0, hh=hh, tb=tb):
                        tens, width, base = (zbuf, TT, tb) if which == "z" else (pbuf, TB, 0)
                        v = tens[:, :].rearrange("p (a f) -> p a f", f=width)
                        v = v[:, :, base:base + TB]
                        v = v.rearrange("p (g two j) f -> p g two j f", two=2, j=hh)
                        return v[:, :, a0]
                    in0, in1 = mk(src_t, 0), mk(src_t, 1)
                    out0, out1 = mk(dst_t, 0), mk(dst_t, 1)
                    ctx.s("vector", lambda e, out0=out0, in0=in0, in1=in1: e.tensor_tensor(
                        out0, in0, in1, ALU.add))
                    ctx.s("vector", lambda e, out1=out1, in0=in0, in1=in1: e.tensor_tensor(
                        out1, in0, in1, ALU.subtract))
                    ctx.fence()
        zv = zbuf[:, :].rearrange("p (a f) -> p f a", f=TT)
        ctx.s("vector", lambda e, zv=zv: e.tensor_reduce(
            sc.y32[:, 0:TT], zv, axis=AX.X, op=ALU.max, apply_absolute_value=True))
        ctx.fence()
        arow = rowslot(sc, R_AMAX2, t0, TT)
        emit_colmax_to_row(ctx, k, sc.y32[:, 0:TT], arow, psE, sc.z32[:, 0:1], TT)
        srow = rowslot(sc, R_TMP, t0, TT)
        kzq = rowslot(sc, R_KZQ, t0, TT)
        ctx.s("vector", lambda e, arow=arow: e.tensor_scalar(arow, arow, FWHT_NORM, 1e-5, ALU.mult, ALU.max))
        ctx.s("vector", lambda e, srow=srow, arow=arow: e.reciprocal(srow, arow))
        ctx.s("vector", lambda e, srow=srow: e.tensor_scalar(srow, srow, 128.0 * FWHT_NORM, None, ALU.mult))
        ctx.s("vector", lambda e, kzq=kzq, arow=arow: e.tensor_scalar(kzq, arow, 1.0 / 128.0, None, ALU.mult))
        ctx.fence()
        bcast_row(ctx, k, srow, sc.bc[:, 0:TT], psE, TT)
        ctx.fence()
        for a in range(NFF):
            ctx.s("vector", lambda e, a=a: e.tensor_tensor(
                sc.x32[:, 0:TT], zbuf[:, a * TT:(a + 1) * TT], sc.bc[:, 0:TT], ALU.mult))
            emit_rne_clip_bf16(ctx, sc, sc.x32[:, 0:TT], ztile[:, a * TT:(a + 1) * TT])
            ctx.fence()
        src = ztile[:, :].rearrange("p (a f) -> p a f", f=TT)
        ctx.s("gpsimd", lambda e, t0=t0, src=src: e.dma_start(
            out=zqspill_d[:, :, t0:t0 + TT].transpose([1, 0, 2]), in_=src), dma=True)
        ctx.fence()


def emit_bitnet_down(ctx, k, sc, zqspill_d, wd_d, f_d, cells, cellcols, zqh,
                     wcol32, wcolh, psC, psD):
    nc2 = NCH * 128
    wdr = wd_d.rearrange("(kc p) o -> kc p o", p=128)
    fr = f_d.rearrange("(c p) t -> c p t", p=128)
    for th in range(2):
        th0 = th * 512
        for g8 in range(0, NFF, 8):
            src = zqspill_d[g8:g8 + 8, :, th0:th0 + 512].transpose([1, 0, 2])
            ctx.s("sync", lambda e, src=src, g8=g8: e.dma_start(
                out=zqh[:, g8 * 512:(g8 + 8) * 512].rearrange("p (a f) -> p a f", f=512),
                in_=src), dma=True)
        kd = rowslot(sc, R_TMP, 0, 512)
        ctx.s("vector", lambda e, kd=kd, th0=th0: e.tensor_scalar(
            kd, rowslot(sc, R_KZQ, th0, 512), cells[0:1, 2:3], None, ALU.mult))
        ctx.fence()
        bcast_row(ctx, k, kd, sc.bc[:, 0:512], psD, 512)
        ctx.fence()
        for oc in range(NCH):
            for q4 in range(4):
                src = wdr[q4 * NCH:(q4 + 1) * NCH, :, oc * 128:(oc + 1) * 128].transpose([1, 0, 2])
                ctx.s("sync", lambda e, src=src: e.dma_start(
                    out=wcol32[:, 0:nc2].rearrange("p (kc f) -> p kc f", f=128), in_=src), dma=True)
                emit_tern(ctx, sc, wcol32[:, 0:nc2], wcolh[:, 0:nc2], cellcols[:, 5:6], nc2)
                ctx.fence()
                for kc in range(NCH):
                    a = q4 * NCH + kc
                    st = (q4 == 0 and kc == 0)
                    sp = (q4 == 3 and kc == NCH - 1)
                    ctx.s("tensor", lambda e, a=a, kc=kc, st=st, sp=sp: e.matmul(
                        psC[:, 0:512], wcolh[:, kc * 128:(kc + 1) * 128],
                        zqh[:, a * 512:(a + 1) * 512], start=st, stop=sp))
                ctx.fence()
            ctx.s("vector", lambda e: e.tensor_tensor(sc.x32[:, 0:512], psC[:, 0:512], sc.bc[:, 0:512], ALU.mult))
            ctx.s("gpsimd", lambda e, oc=oc, th0=th0: e.dma_start(
                out=fr[oc][:, th0:th0 + 512], in_=sc.x32[:, 0:512]), dma=True)
            ctx.fence()



def emit_t0(ctx, dr, k, sc, psA, psB, psC, psD, psE, vecs, cellcols,
            wcol32, wcolh, wcoll, vtm=False):
    t = TOK
    x_d = dr["x"]; wq_d = dr["wq"]; wk_d = dr["wk"]; wv_d = dr["wv"]
    wg_d = dr["wg_t"]; wu_d = dr["wu_t"]; wd_d = dr["wd_t"]
    xnh_d = dr["xnh"]; xnl_d = dr["xnl"]
    qkv_outs = {nm: dr[nm] for nm in ("qh", "ql", "kh", "kl", "vh", "vl") if nm in dr}
    f_d = dr["fout"]; ispill_d = dr["ispill"]; zqspill_d = dr["zqspill"]

    cells = rowslot(sc, R_CELLS, 0, 8)
    emit_w_mean_cells(ctx, k, sc, ((wg_d, NCH, DFF), (wu_d, NCH, DFF), (wd_d, NFF, D)),
                      cells, cellcols, wcol32, psE)
    xq = ctx.sbuf("xq", [128, NCH * t], BF16)
    xh = ctx.sbuf("xh", [128, NCH * t], BF16)
    xl = ctx.sbuf("xl", [128, NCH * t], BF16)
    emit_pair_load(ctx, sc, x_d, xh, xl, NCH)
    emit_ln(ctx, k, sc, xh, xl, xh, xl, psA, psD)
    emit_pair_store(ctx, xh, xl, xnh_d, xnl_d, NCH)
    ctx.fence()

    emit_actquant(ctx, k, sc, xh, xl, xq, NCH, psD, psE)

    g1 = vecs[:, 3 * NCH:4 * NCH]
    be1 = vecs[:, 4 * NCH:5 * NCH]
    emit_ln(ctx, k, sc, xh, xl, xh, xl, psA, psD, gamma=g1, beta=be1)
    ctx.fence()

    wbufs = {"w32": wcol32, "wh": wcolh, "wl": wcoll, "ps": psA}

    def qkv_consumer(nm_h, nm_l, bias_idx, scale):
        hr = qkv_outs[nm_h].rearrange("(c p) t -> c p t", p=128)
        lr = qkv_outs[nm_l].rearrange("(c p) t -> c p t", p=128)

        def consumer(oc, ps):
            bcol = vecs[:, bias_idx * NCH + oc:bias_idx * NCH + oc + 1]
            ctx.s("vector", lambda e, bcol=bcol: e.tensor_scalar(
                sc.z32[:, 0:1], bcol, scale, None, ALU.mult))
            ctx.fence()
            ctx.s("scalar", lambda e: e.activation(
                sc.y32[:], ps[:, 0:t], AF.Identity, bias=sc.z32[:, 0:1], scale=scale))
            ctx.s("vector", lambda e: e.tensor_copy(sc.bfA[:, 0:t], sc.y32[:]))
            ctx.s("vector", lambda e: e.tensor_tensor(sc.bfB[:, 0:t], sc.y32[:], sc.bfA[:, 0:t], ALU.subtract))
            ctx.s("gpsimd", lambda e, oc=oc, hr=hr: e.dma_start(out=hr[oc], in_=sc.bfA[:, 0:t]), dma=True)
            ctx.s("gpsimd", lambda e, oc=oc, lr=lr: e.dma_start(out=lr[oc], in_=sc.bfB[:, 0:t]), dma=True)
        return consumer

    emit_proj(ctx, k, wq_d, xh, xl, NCH, NCH, qkv_consumer("qh", "ql", 0, RSQ_HD), wbufs=wbufs)
    emit_proj(ctx, k, wk_d, xh, xl, NCH, NCH, qkv_consumer("kh", "kl", 1, 1.0), wbufs=wbufs)
    if not vtm:
        emit_proj(ctx, k, wv_d, xh, xl, NCH, NCH, qkv_consumer("vh", "vl", 2, 1.0), wbufs=wbufs)
    else:
        bvrow = sc.bc[:, 0:D]
        for cch in range(NCH):
            ctx.s("tensor", lambda e, cch=cch: e.transpose(
                psD[0:1, 0:128], vecs[:, 2 * NCH + cch:2 * NCH + cch + 1], k.ident_f32))
            ctx.s("vector", lambda e, cch=cch: e.tensor_copy(
                sc.rows[0:1, cch * 128:(cch + 1) * 128], psD[0:1, 0:128]))
            ctx.fence()
        bcast_row(ctx, k, sc.rows[0:1, 0:512], bvrow[:, 0:512], psE, 512)
        bcast_row(ctx, k, sc.rows[0:1, 512:1024], bvrow[:, 512:1024], psE, 512)
        bcast_row(ctx, k, sc.rows[0:1, 1024:1536], bvrow[:, 1024:1536], psE, 512)
        bcast_row(ctx, k, sc.rows[0:1, 1536:2048], bvrow[:, 1536:2048], psE, 512)
        ctx.fence()
        emit_vproj_tm(ctx, k, sc, wv_d, 0, xh, xl, NCH, NCH,
                      dr["vth_loc"], dr["vtl_loc"],
                      [psA, psB, psC, psD], wcol32, wcolh, wcoll, bias_col=bvrow)
    ctx.fence()
    ctx.sbuf_free("xl", "xh")

    tern2 = ctx.sbuf("tern2", [128, NCH * 128], BF16)
    emit_bitnet_gateup(ctx, k, sc, xq, wg_d, wu_d, ispill_d, cells, cellcols,
                       wcol32, wcolh, tern2, psA, psB, psD)
    ctx.sbuf_free("tern2", "xq")
    zbuf = ctx.sbuf("zbuf", [128, NFF * 128], F32)
    ihbuf = ctx.sbuf("ihbuf", [128, NFF * 128], BF16)
    ilbuf = ctx.sbuf("ilbuf", [128, NFF * 128], BF16)
    pbuf = ctx.sbuf("pbuf", [128, NFF * 32], F32)
    ztile = ctx.sbuf("ztile", [128, NFF * 128], BF16)
    emit_bitnet_stageb(ctx, k, sc, ispill_d, zqspill_d, zbuf, ihbuf, ilbuf,
                       pbuf, ztile, psD, psE)
    ctx.sbuf_free("ztile", "pbuf", "ilbuf", "ihbuf", "zbuf")
    zqh = ctx.sbuf("zqh", [128, NFF * 512], BF16)
    emit_bitnet_down(ctx, k, sc, zqspill_d, wd_d, f_d, cells, cellcols, zqh,
                     wcol32, wcolh, psC, psD)
    ctx.sbuf_free("zqh")



def build_t0(dbg=False):
    ctx = Ctx()
    ctx.init_serial()
    t = TOK
    dr = {}
    dr["x"] = ctx.dram_in("x", [D, t])
    c_d = ctx.dram_in("consts", [128, 400])
    dr["wq"] = ctx.dram_in("wq", [D, D])
    dr["wk"] = ctx.dram_in("wk", [D, D])
    dr["wv"] = ctx.dram_in("wv", [D, D])
    vec_d = ctx.dram_in("vecs", [5 * D])
    dr["wg_t"] = ctx.dram_in("wg_t", [D, DFF])
    dr["wu_t"] = ctx.dram_in("wu_t", [D, DFF])
    dr["wd_t"] = ctx.dram_in("wd_t", [DFF, D])
    dr["xnh"] = ctx.dram_out("xnh", [D, t], BF16)
    dr["xnl"] = ctx.dram_out("xnl", [D, t], BF16)
    for nm in ("qh", "ql", "kh", "kl", "vh", "vl"):
        dr[nm] = ctx.dram_out(nm, [D, t], BF16)
    dr["fout"] = ctx.dram_out("fout", [D, t])
    dr["ispill"] = ctx.dram_tmp("ispill", [NFF, 2, 128, t], BF16)
    dr["zqspill"] = ctx.dram_tmp("zqspill", [NFF, 128, t], BF16)
    k = Consts(ctx)
    sc = Scratch(ctx)
    psA = ctx.psum("psA", [128, 1024], F32)
    psB = ctx.psum("psB", [128, 1024], F32)
    psC = ctx.psum("psC", [128, 1024], F32)
    psD = ctx.psum("psD", [128, 512], F32)
    psE = ctx.psum("psE", [128, 512], F32)
    vecs = ctx.sbuf("vecs_sb", [128, 5 * NCH], F32)
    cellcols = ctx.sbuf("cellcols", [128, 8], F32)
    wcol32 = ctx.sbuf("wcol32", [128, NCH * 128], F32)
    wcolh = ctx.sbuf("wcolh", [128, NCH * 128], BF16)
    wcoll = ctx.sbuf("wcoll", [128, NCH * 128], BF16)
    k.load(ctx, c_d[:])
    emit_load_vec(ctx, vec_d, vecs, 5 * NCH)
    ctx.fence()
    emit_t0(ctx, dr, k, sc, psA, psB, psC, psD, psE, vecs, cellcols,
            wcol32, wcolh, wcoll)
    ctx.emit()
    return ctx


QT = 256
NQT = S // QT


def emit_attn(ctx, dr, k, sc, psS, psAV, psSum, psE, masks):
    qh_d, ql_d, kh_d, kl_d = dr["qh"], dr["ql"], dr["kh"], dr["kl"]
    vth_d, vtl_d, ah_d, al_d = dr["vth"], dr["vtl"], dr["ah"], dr["al"]
    u = ctx.uid()
    qhb = ctx.sbuf(f"at{u}_qhb", [128, S], BF16)
    qlb = ctx.sbuf(f"at{u}_qlb", [128, S], BF16)
    khb = ctx.sbuf(f"at{u}_khb", [128, S], BF16)
    klb = ctx.sbuf(f"at{u}_klb", [128, S], BF16)
    vhb = ctx.sbuf(f"at{u}_vhb", [128, S], BF16)
    vlb = ctx.sbuf(f"at{u}_vlb", [128, S], BF16)
    strip = ctx.sbuf(f"at{u}_strip", [128, 2 * S], F32)
    ehb = ctx.sbuf(f"at{u}_ehb", [128, 2 * S], BF16)
    elb = ctx.sbuf(f"at{u}_elb", [128, 2 * S], BF16)
    for i in range(4):
        for (dr, sb) in ((qh_d, qhb), (ql_d, qlb), (kh_d, khb), (kl_d, klb)):
            ctx.s("sync", lambda e, dr=dr, sb=sb, i=i: e.dma_start(out=sb[:], in_=dr[i]), dma=True)
        for (dr, sb) in ((vth_d, vhb), (vtl_d, vlb)):
            src = dr[i].rearrange("(kt p) d -> p kt d", p=128)
            ctx.s("sync", lambda e, src=src, sb=sb: e.dma_start(
                out=sb[:].rearrange("p (kt d) -> p kt d", d=128), in_=src), dma=True)
        ctx.fence()
        for qt in range(NQT):
            nkt = 2 * qt + 2
            q0 = qt * QT
            for kt in range(nkt):
                ctx.s("tensor", lambda e, kt=kt, q0=q0: e.matmul(
                    psS[:, 0:QT], khb[:, kt * 128:(kt + 1) * 128], qhb[:, q0:q0 + QT],
                    start=True, stop=False))
                ctx.s("tensor", lambda e, kt=kt, q0=q0: e.matmul(
                    psS[:, 0:QT], khb[:, kt * 128:(kt + 1) * 128], qlb[:, q0:q0 + QT],
                    start=False, stop=False))
                ctx.s("tensor", lambda e, kt=kt, q0=q0: e.matmul(
                    psS[:, 0:QT], klb[:, kt * 128:(kt + 1) * 128], qhb[:, q0:q0 + QT],
                    start=False, stop=True))
                if kt >= nkt - 2:
                    pat = 0 if kt == nkt - 2 else 2
                    ctx.s("vector", lambda e, kt=kt, pat=pat: e.tensor_tensor(
                        sc.x32[:, 0:QT], psS[:, 0:QT], masks[:, pat * QT:(pat + 1) * QT], ALU.mult))
                    ctx.s("vector", lambda e, kt=kt, pat=pat: e.tensor_tensor(
                        strip[:, kt * QT:(kt + 1) * QT], sc.x32[:, 0:QT],
                        masks[:, (pat + 1) * QT:(pat + 2) * QT], ALU.add))
                else:
                    ctx.s("scalar", lambda e, kt=kt: e.activation(
                        strip[:, kt * QT:(kt + 1) * QT], psS[:, 0:QT], AF.Identity))
                ctx.fence()
            ctx.s("vector", lambda e, nkt=nkt: e.tensor_reduce(
                sc.z32[:, 0:1], strip[:, 0:nkt * QT], axis=AX.X, op=ALU.max))
            ctx.fence()
            ctx.s("tensor", lambda e: e.transpose(psE[0:1, 0:128], sc.z32[:, 0:1], k.ident_f32))
            ctx.s("vector", lambda e: e.tensor_reduce(
                sc.rows[0:1, 0:1], psE[0:1, 0:128], axis=AX.X, op=ALU.max))
            ctx.s("vector", lambda e: e.tensor_scalar(
                sc.rows[0:1, 0:1], sc.rows[0:1, 0:1], -1.0, None, ALU.mult))
            ctx.fence()
            ctx.s("tensor", lambda e: e.matmul(
                psE[:, 0:1], k.ones_row_f32, sc.rows[0:1, 0:1], start=True, stop=True))
            ctx.s("vector", lambda e: e.tensor_copy(sc.z32[:, 0:1], psE[:, 0:1]))
            ctx.fence()
            for kt in range(nkt):
                sl = slice(kt * QT, (kt + 1) * QT)
                ctx.s("scalar", lambda e, sl=sl: e.activation(
                    sc.x32[:, 0:QT], strip[:, sl], AF.Exp, bias=sc.z32[:, 0:1], scale=1.0))
                ctx.s("vector", lambda e, sl=sl: e.tensor_copy(ehb[:, sl], sc.x32[:, 0:QT]))
                ctx.s("vector", lambda e, sl=sl: e.tensor_tensor(
                    elb[:, sl], sc.x32[:, 0:QT], ehb[:, sl], ALU.subtract))
                ctx.fence()
                st, sp = (kt == 0), (kt == nkt - 1)
                ctx.s("tensor", lambda e, kt=kt, sl=sl, st=st: e.matmul(
                    psAV[:, 0:QT], vhb[:, kt * 128:(kt + 1) * 128], ehb[:, sl], start=st, stop=False))
                ctx.s("tensor", lambda e, kt=kt, sl=sl: e.matmul(
                    psAV[:, 0:QT], vhb[:, kt * 128:(kt + 1) * 128], elb[:, sl], start=False, stop=False))
                ctx.s("tensor", lambda e, kt=kt, sl=sl, sp=sp: e.matmul(
                    psAV[:, 0:QT], vlb[:, kt * 128:(kt + 1) * 128], ehb[:, sl], start=False, stop=sp))
                ctx.s("tensor", lambda e, sl=sl, st=st: e.matmul(
                    psSum[0:1, 0:QT], k.ones_col_bf[:], ehb[:, sl], start=st, stop=False))
                ctx.s("tensor", lambda e, sl=sl, sp=sp: e.matmul(
                    psSum[0:1, 0:QT], k.ones_col_bf[:], elb[:, sl], start=False, stop=sp))
                ctx.fence()
            ctx.s("vector", lambda e: e.reciprocal(sc.rows[0:1, 0:QT], psSum[0:1, 0:QT]))
            ctx.fence()
            ctx.s("tensor", lambda e: e.matmul(
                psE[:, 0:QT], k.ones_row_f32, sc.rows[0:1, 0:QT], start=True, stop=True))
            ctx.s("vector", lambda e: e.tensor_copy(sc.bc[:, 0:QT], psE[:, 0:QT]))
            ctx.fence()
            ctx.s("vector", lambda e: e.tensor_tensor(
                sc.x32[:, 0:QT], psAV[:, 0:QT], sc.bc[:, 0:QT], ALU.mult))
            ctx.s("vector", lambda e: e.tensor_copy(sc.bfA[:, 0:QT], sc.x32[:, 0:QT]))
            ctx.s("vector", lambda e: e.tensor_tensor(
                sc.bfB[:, 0:QT], sc.x32[:, 0:QT], sc.bfA[:, 0:QT], ALU.subtract))
            ctx.s("gpsimd", lambda e, i=i, q0=q0: e.dma_start(
                out=ah_d[i][:, q0:q0 + QT], in_=sc.bfA[:, 0:QT]), dma=True)
            ctx.s("gpsimd", lambda e, i=i, q0=q0: e.dma_start(
                out=al_d[i][:, q0:q0 + QT], in_=sc.bfB[:, 0:QT]), dma=True)
            ctx.fence()
    for nm in ("elb", "ehb", "strip", "vlb", "vhb", "klb", "khb", "qlb", "qhb"):
        ctx.sbuf_free(f"at{u}_{nm}")


def build_attn():
    ctx = Ctx()
    ctx.init_serial()
    c_d = ctx.dram_in("consts", [128, 400])
    mask_d = ctx.dram_in("masks", [4, 128, QT])
    dr = {}
    for nm in ("qh", "ql", "kh", "kl"):
        dr[nm] = ctx.dram_in(nm, [4, 128, S], BF16)
    for nm in ("vth", "vtl"):
        dr[nm] = ctx.dram_in(nm, [4, S, 128], BF16)
    dr["ah"] = ctx.dram_out("ah", [4, 128, S], BF16)
    dr["al"] = ctx.dram_out("al", [4, 128, S], BF16)
    k = Consts(ctx)
    sc = Scratch(ctx)
    psS = ctx.psum("psS", [128, 512], F32)
    psAV = ctx.psum("psAV", [128, 512], F32)
    psSum = ctx.psum("psSum", [128, 512], F32)
    psE = ctx.psum("psE", [128, 512], F32)
    masks = ctx.sbuf("masks_sb", [128, 4 * QT], F32)
    k.load(ctx, c_d[:])
    ctx.s("sync", lambda e: e.dma_start(
        out=masks[:].rearrange("p (i f) -> p i f", f=QT), in_=mask_d.transpose([1, 0, 2])), dma=True)
    ctx.fence()
    emit_attn(ctx, dr, k, sc, psS, psAV, psSum, psE, masks)
    ctx.emit()
    return ctx


def attn_masks_host(maskval):
    out = np.zeros((4, 128, QT), dtype=np.float32)
    kloc = np.arange(128)[:, None]
    qloc = np.arange(QT)[None, :]
    mA = (kloc > qloc)
    mB = (128 + kloc > qloc)
    out[0] = (~mA).astype(np.float32)
    out[1] = np.where(mA, np.float32(maskval), np.float32(0.0))
    out[2] = (~mB).astype(np.float32)
    out[3] = np.where(mB, np.float32(maskval), np.float32(0.0))
    return out


def emit_proj_ksweep(ctx, k, sc, w_dram, rhsh_d, rhsl_d, n_in_ch, n_out_ch,
                     consumer, psW, wbuf32, wbufh, wbufl, rbufh, rbufl, tok=TOK):
    wr = w_dram.rearrange("(kc p) o -> kc p o", p=128)
    for s4 in range(0, n_out_ch, 4):
        for kc in range(n_in_ch):
            ctx.s("sync", lambda e, kc=kc, s4=s4: e.dma_start(
                out=wbuf32[:, 0:512], in_=wr[kc][:, s4 * 128:(s4 + 4) * 128]), dma=True)
            ctx.s("vector", lambda e: e.tensor_copy(wbufh[:, 0:512], wbuf32[:, 0:512]))
            ctx.s("vector", lambda e: e.tensor_tensor(
                wbufl[:, 0:512], wbuf32[:, 0:512], wbufh[:, 0:512], ALU.subtract))
            ctx.s("sync", lambda e, kc=kc: e.dma_start(out=rbufh[:, 0:tok], in_=rhsh_d[kc]), dma=True)
            ctx.s("sync", lambda e, kc=kc: e.dma_start(out=rbufl[:, 0:tok], in_=rhsl_d[kc]), dma=True)
            ctx.fence()
            for j in range(4):
                if s4 + j >= n_out_ch:
                    continue
                lh = wbufh[:, j * 128:(j + 1) * 128]
                ll = wbufl[:, j * 128:(j + 1) * 128]
                for off, w in halves(tok):
                    st, sp = kc == 0, kc == n_in_ch - 1
                    ctx.s("tensor", lambda e, j=j, lh=lh, off=off, w=w, st=st: e.matmul(
                        psW[j][:, off:off + w], lh, rbufh[:, off:off + w], start=st, stop=False))
                    ctx.s("tensor", lambda e, j=j, lh=lh, off=off, w=w: e.matmul(
                        psW[j][:, off:off + w], lh, rbufl[:, off:off + w], start=False, stop=False))
                    ctx.s("tensor", lambda e, j=j, ll=ll, off=off, w=w, sp=sp: e.matmul(
                        psW[j][:, off:off + w], ll, rbufh[:, off:off + w], start=False, stop=sp))
            ctx.fence()
        for j in range(4):
            if s4 + j >= n_out_ch:
                continue
            consumer(s4 + j, psW[j])
            ctx.fence()



LOG2E_INV = float(np.log(2.0))


def emit_fp8_quant(ctx, k, sc, pair_hi, pair_lo, out_bf, nch, inv_s, tok=TOK):
    t = tok
    for c in range(nch):
        ctx.s("vector", lambda e, c=c: e.tensor_tensor(
            sc.x32[:], pair_hi[:, c * t:(c + 1) * t], pair_lo[:, c * t:(c + 1) * t], ALU.add))
        ctx.s("scalar", lambda e: e.activation(sc.y32[:], sc.x32[:], AF.Abs, scale=inv_s))
        ctx.s("vector", lambda e: e.tensor_scalar(sc.y32[:], sc.y32[:], 448.0, None, ALU.min))
        ctx.s("scalar", lambda e: e.activation(sc.z32[:], sc.x32[:], AF.Sign))
        ctx.s("scalar", lambda e: e.activation(sc.x32[:], sc.y32[:], AF.Ln))
        ctx.s("vector", lambda e: e.tensor_scalar(
            sc.x32[:], sc.x32[:], 1.0 / LOG2E_INV, 63.5, ALU.mult, ALU.add))
        ctx.s("vector", lambda e: e.tensor_copy(sc.i32[:], sc.x32[:]))
        ctx.s("vector", lambda e: e.tensor_copy(sc.x32[:], sc.i32[:]))
        ctx.s("vector", lambda e: e.tensor_scalar(
            sc.x32[:], sc.x32[:], -64.0, -6.0, ALU.add, ALU.max))
        ctx.s("scalar", lambda e: e.activation(
            sc.bc[:, 0:t], sc.x32[:], AF.Exp, bias=float(np.float32(3.0 * LOG2E_INV)), scale=-LOG2E_INV))
        ctx.s("scalar", lambda e: e.activation(
            sc.x32[:], sc.x32[:], AF.Exp, bias=float(np.float32(-3.0 * LOG2E_INV)), scale=LOG2E_INV))
        ctx.s("vector", lambda e: e.tensor_tensor(sc.y32[:], sc.y32[:], sc.bc[:, 0:t], ALU.mult))
        ctx.s("vector", lambda e: e.tensor_copy(sc.i32[:], sc.y32[:]))
        ctx.s("vector", lambda e: e.tensor_copy(sc.y32[:], sc.i32[:]))
        ctx.s("vector", lambda e: e.tensor_tensor(sc.y32[:], sc.y32[:], sc.x32[:], ALU.mult))
        ctx.s("vector", lambda e, c=c: e.tensor_tensor(
            out_bf[:, c * t:(c + 1) * t], sc.y32[:], sc.z32[:], ALU.mult))
        ctx.fence()



def emit_t1(ctx, dr, k, sc, psW, vecs, wcol32, wcolh, wcoll, rbufh, rbufl,
            sclsb, isbf, kcell, vtm=False):
    t = TOK
    x_d = dr["x"]; xnh_d = dr["xnh"]; xnl_d = dr["xnl"]; f_d = dr["fout"]
    ath_d = dr["ath"]; atl_d = dr["atl"]; wo_d = dr["wo"]; w1_d = dr["w1"]
    w2_d = dr["w2"]; wqkv_d = dr["wqkv_t"]; w8_d = dr["w8_t"]
    ws_d = dr["wscale"]; is_d = dr["iscale"]
    accp_d = dr["accp"]; ffn2_d = dr["ffn2"]; ffn1_d = dr["ffn1"]
    q1 = {nm: dr[nm] for nm in ("q1h", "q1l", "k1h", "k1l", "v1h", "v1l")}
    hspill_d = dr["hspill"]; xasp_h = dr["xasph"]; xasp_l = dr["xaspl"]
    acc = ctx.sbuf("acc", [128, NCH * t], F32)
    xr = x_d.rearrange("(c p) t -> c p t", p=128)
    fr2 = f_d.rearrange("(c p) t -> c p t", p=128)
    for c in range(NCH):
        ctx.s("sync", lambda e, c=c: e.dma_start(out=sc.x32[:], in_=xr[c]), dma=True)
        ctx.s("sync", lambda e, c=c: e.dma_start(out=sc.y32[:], in_=fr2[c]), dma=True)
        ctx.s("vector", lambda e, c=c: e.tensor_tensor(
            acc[:, c * t:(c + 1) * t], sc.x32[:], sc.y32[:], ALU.add))
        ctx.fence()
    xnhr = xnh_d.rearrange("(c p) t -> c p t", p=128)
    xnlr = xnl_d.rearrange("(c p) t -> c p t", p=128)

    def oproj_consumer(oc, ps):
        bo = vecs[:, oc:oc + 1]
        ctx.s("scalar", lambda e, bo=bo: e.activation(
            sc.x32[:], ps[:, 0:t], AF.Identity, bias=bo, scale=1.0))
        ctx.s("sync", lambda e, oc=oc: e.dma_start(out=sc.bfA[:, 0:t], in_=xnhr[oc]), dma=True)
        ctx.s("sync", lambda e, oc=oc: e.dma_start(out=sc.bfB[:, 0:t], in_=xnlr[oc]), dma=True)
        ctx.s("vector", lambda e: e.tensor_tensor(sc.x32[:], sc.x32[:], sc.bfA[:, 0:t], ALU.add))
        ctx.s("vector", lambda e: e.tensor_tensor(sc.x32[:], sc.x32[:], sc.bfB[:, 0:t], ALU.add))
        ctx.s("vector", lambda e, oc=oc: e.tensor_tensor(
            acc[:, oc * t:(oc + 1) * t], acc[:, oc * t:(oc + 1) * t], sc.x32[:], ALU.add))
        ctx.s("vector", lambda e: e.tensor_copy(sc.bfA[:, 0:t], sc.x32[:]))
        ctx.s("vector", lambda e: e.tensor_tensor(sc.bfB[:, 0:t], sc.x32[:], sc.bfA[:, 0:t], ALU.subtract))
        ctx.s("gpsimd", lambda e, oc=oc: e.dma_start(out=xasp_h[oc], in_=sc.bfA[:, 0:t]), dma=True)
        ctx.s("gpsimd", lambda e, oc=oc: e.dma_start(out=xasp_l[oc], in_=sc.bfB[:, 0:t]), dma=True)

    emit_proj_ksweep(ctx, k, sc, wo_d, ath_d, atl_d, NCH, NCH, oproj_consumer,
                     psW, wcol32, wcolh, wcoll, rbufh, rbufl)
    ctx.fence()
    ar = accp_d.rearrange("(c p) t -> c p t", p=128)
    for c in range(NCH):
        ctx.s("gpsimd", lambda e, c=c: e.dma_start(out=ar[c], in_=acc[:, c * t:(c + 1) * t]), dma=True)
    ctx.fence()
    ctx.sbuf_free("acc")

    xah = ctx.sbuf("xah", [128, NCH * t], BF16)
    xal = ctx.sbuf("xal", [128, NCH * t], BF16)
    for c in range(NCH):
        ctx.s("sync", lambda e, c=c: e.dma_start(out=xah[:, c * t:(c + 1) * t], in_=xasp_h[c]), dma=True)
        ctx.s("sync", lambda e, c=c: e.dma_start(out=xal[:, c * t:(c + 1) * t], in_=xasp_l[c]), dma=True)
    ctx.fence()
    g2 = vecs[:, 2 * NCH:3 * NCH]
    be2 = vecs[:, 3 * NCH:4 * NCH]
    emit_ln(ctx, k, sc, xah, xal, xah, xal, psW[0], psW[1], gamma=g2, beta=be2)
    ctx.fence()
    wbufs = {"w32": wcol32, "wh": wcolh, "wl": wcoll, "ps": psW[0]}

    def w1_consumer(oc, ps):
        b1 = vecs[:, 4 * NCH + oc:4 * NCH + oc + 1]
        ctx.s("scalar", lambda e, b1=b1: e.activation(
            sc.x32[:], ps[:, 0:t], AF.Gelu, bias=b1, scale=1.0))
        ctx.s("vector", lambda e: e.tensor_copy(sc.bfA[:, 0:t], sc.x32[:]))
        ctx.s("vector", lambda e: e.tensor_tensor(sc.bfB[:, 0:t], sc.x32[:], sc.bfA[:, 0:t], ALU.subtract))
        ctx.s("gpsimd", lambda e, oc=oc: e.dma_start(out=hspill_d[oc, 0], in_=sc.bfA[:, 0:t]), dma=True)
        ctx.s("gpsimd", lambda e, oc=oc: e.dma_start(out=hspill_d[oc, 1], in_=sc.bfB[:, 0:t]), dma=True)

    emit_proj(ctx, k, w1_d, xah, xal, NCH, NFF, w1_consumer, wbufs=wbufs)
    ctx.fence()
    ctx.sbuf_free("xal", "xah")

    f2r = ffn2_d.rearrange("(c p) t -> c p t", p=128)

    def w2_consumer(oc, ps):
        b2 = vecs[:, NCH + oc:NCH + oc + 1]
        ctx.s("scalar", lambda e, b2=b2: e.activation(
            sc.x32[:], ps[:, 0:t], AF.Identity, bias=b2, scale=1.0))
        ctx.s("gpsimd", lambda e, oc=oc: e.dma_start(out=f2r[oc], in_=sc.x32[:]), dma=True)

    emit_proj_ksweep(ctx, k, sc, w2_d, hspill_d[:, 0], hspill_d[:, 1], NFF, NCH,
                     w2_consumer, psW, wcol32, wcolh, wcoll, rbufh, rbufl)
    ctx.fence()

    x8 = ctx.sbuf("x8", [128, NCH * t], BF16)
    x1h = ctx.sbuf("x1h", [128, NCH * t], BF16)
    x1l = ctx.sbuf("x1l", [128, NCH * t], BF16)
    for c in range(NCH):
        ctx.s("sync", lambda e, c=c: e.dma_start(out=sc.x32[:], in_=ar[c]), dma=True)
        ctx.s("sync", lambda e, c=c: e.dma_start(out=sc.y32[:], in_=f2r[c]), dma=True)
        ctx.s("vector", lambda e: e.tensor_tensor(sc.x32[:], sc.x32[:], sc.y32[:], ALU.add))
        ctx.s("vector", lambda e, c=c: e.tensor_copy(x1h[:, c * t:(c + 1) * t], sc.x32[:]))
        ctx.s("vector", lambda e, c=c: e.tensor_tensor(
            x1l[:, c * t:(c + 1) * t], sc.x32[:], x1h[:, c * t:(c + 1) * t], ALU.subtract))
        ctx.fence()
    emit_ln(ctx, k, sc, x1h, x1l, x1h, x1l, psW[0], psW[1])
    ctx.fence()

    def l1qkv_consumer(oc, ps):
        if oc < NCH:
            nmh, nml, scale, o2 = "q1h", "q1l", RSQ_HD, oc
        elif oc < 2 * NCH:
            nmh, nml, scale, o2 = "k1h", "k1l", 1.0, oc - NCH
        else:
            nmh, nml, scale, o2 = "v1h", "v1l", 1.0, oc - 2 * NCH
        hr = q1[nmh].rearrange("(c p) t -> c p t", p=128)
        lr = q1[nml].rearrange("(c p) t -> c p t", p=128)
        if scale != 1.0:
            ctx.s("scalar", lambda e, scale=scale: e.activation(
                sc.x32[:], ps[:, 0:t], AF.Identity, bias=0.0, scale=scale))
            srcap = sc.x32[:]
        else:
            srcap = ps[:, 0:t]
        ctx.s("vector", lambda e, srcap=srcap: e.tensor_copy(sc.bfA[:, 0:t], srcap))
        ctx.s("vector", lambda e, srcap=srcap: e.tensor_tensor(
            sc.bfB[:, 0:t], srcap, sc.bfA[:, 0:t], ALU.subtract))
        ctx.s("gpsimd", lambda e, o2=o2, hr=hr: e.dma_start(out=hr[o2], in_=sc.bfA[:, 0:t]), dma=True)
        ctx.s("gpsimd", lambda e, o2=o2, lr=lr: e.dma_start(out=lr[o2], in_=sc.bfB[:, 0:t]), dma=True)

    if not vtm:
        emit_proj(ctx, k, wqkv_d, x1h, x1l, NCH, 3 * NCH, l1qkv_consumer, wbufs=wbufs)
    else:
        emit_proj(ctx, k, wqkv_d, x1h, x1l, NCH, 3 * NCH, l1qkv_consumer,
                  wbufs=wbufs, oc_list=range(2 * NCH))
        emit_vproj_tm(ctx, k, sc, wqkv_d, 2 * NCH, x1h, x1l, NCH, NCH,
                      dr["v1th_loc"], dr["v1tl_loc"],
                      [psW[0], psW[1], psW[2], psW[3]], wcol32, wcolh, wcoll)
    ctx.fence()

    ctx.s("sync", lambda e: e.dma_start(out=sclsb[0:1, 0:1], in_=ws_d), dma=True)
    ctx.s("sync", lambda e: e.dma_start(out=isbf[:], in_=is_d), dma=True)
    ctx.fence()
    ctx.s("vector", lambda e: e.tensor_copy(sclsb[0:1, 1:2], isbf[:]))
    ctx.s("vector", lambda e: e.tensor_tensor(
        sclsb[0:1, 2:3], sclsb[0:1, 0:1], sclsb[0:1, 1:2], ALU.mult))
    ctx.fence()
    ctx.s("tensor", lambda e: e.matmul(psW[1][:, 0:1], k.ones_row_f32, sclsb[0:1, 2:3], start=True, stop=True))
    ctx.s("vector", lambda e: e.tensor_copy(kcell[:], psW[1][:, 0:1]))
    ctx.fence()
    emit_fp8_quant(ctx, k, sc, x1h, x1l, x8, NCH, 1.0 / 0.75)
    ctx.fence()
    ctx.sbuf_free("x1l", "x1h")

    f1r = ffn1_d.rearrange("(c p) t -> c p t", p=128)
    w8r = w8_d.rearrange("(kc p) o -> kc p o", p=128)
    for oc in range(NCH):
        src = w8r[:, :, oc * 128:(oc + 1) * 128].transpose([1, 0, 2])
        ctx.s("gpsimd", lambda e, src=src: e.dma_start(
            out=wcolh[:, 0:NCH * 128].rearrange("p (kc f) -> p kc f", f=128), in_=src), dma=True)
        ctx.fence()
        for kc in range(NCH):
            for off, w in halves(t):
                st, sp = kc == 0, kc == NCH - 1
                ctx.s("tensor", lambda e, kc=kc, off=off, w=w, st=st, sp=sp: e.matmul(
                    psW[0][:, off:off + w], wcolh[:, kc * 128:(kc + 1) * 128],
                    x8[:, kc * t + off:kc * t + off + w], start=st, stop=sp))
        ctx.fence()
        ctx.s("vector", lambda e: e.tensor_scalar(
            sc.x32[:], psW[0][:, 0:t], kcell[:, 0:1], None, ALU.mult))
        ctx.s("gpsimd", lambda e, oc=oc: e.dma_start(out=f1r[oc], in_=sc.x32[:]), dma=True)
        ctx.fence()
    ctx.sbuf_free("x8")




def build_t1():
    ctx = Ctx()
    ctx.init_serial()
    t = TOK
    dr = {}
    dr["x"] = ctx.dram_in("x", [D, t])
    c_d = ctx.dram_in("consts", [128, 400])
    dr["xnh"] = ctx.dram_in("xnh", [D, t], BF16)
    dr["xnl"] = ctx.dram_in("xnl", [D, t], BF16)
    dr["fout"] = ctx.dram_in("fout", [D, t])
    dr["ath"] = ctx.dram_in("ath", [NCH, 128, t], BF16)
    dr["atl"] = ctx.dram_in("atl", [NCH, 128, t], BF16)
    dr["wo"] = ctx.dram_in("wo", [D, D])
    dr["w1"] = ctx.dram_in("w1", [D, DFF])
    dr["w2"] = ctx.dram_in("w2", [DFF, D])
    dr["wqkv_t"] = ctx.dram_in("wqkv_t", [D, 3 * D])
    dr["w8_t"] = ctx.dram_in("w8_t", [D, D], mybir.dt.float8e4)
    dr["wscale"] = ctx.dram_in("wscale", [1, 1])
    dr["iscale"] = ctx.dram_in("iscale", [1, 1], BF16)
    vec_d = ctx.dram_in("vecs", [(NCH * 4 + NFF) * 128])
    dr["accp"] = ctx.dram_out("accp", [D, t])
    dr["ffn2"] = ctx.dram_out("ffn2", [D, t])
    dr["ffn1"] = ctx.dram_out("ffn1", [D, t])
    for nm in ("q1h", "q1l", "k1h", "k1l", "v1h", "v1l"):
        dr[nm] = ctx.dram_out(nm, [D, t], BF16)
    dr["hspill"] = ctx.dram_tmp("hspill", [NFF, 2, 128, t], BF16)
    dr["xasph"] = ctx.dram_tmp("xasph", [NCH, 128, t], BF16)
    dr["xaspl"] = ctx.dram_tmp("xaspl", [NCH, 128, t], BF16)
    k = Consts(ctx)
    sc = Scratch(ctx)
    psW = [ctx.psum(f"psw{i}", [128, 1024], F32) for i in range(4)]
    vecs = ctx.sbuf("vecs_sb", [128, NCH * 4 + NFF], F32)
    wcol32 = ctx.sbuf("wcol32", [128, NCH * 128], F32)
    wcolh = ctx.sbuf("wcolh", [128, NCH * 128], BF16)
    wcoll = ctx.sbuf("wcoll", [128, NCH * 128], BF16)
    rbufh = ctx.sbuf("rbufh", [128, t], BF16)
    rbufl = ctx.sbuf("rbufl", [128, t], BF16)
    sclsb = ctx.sbuf("sclsb", [1, 4], F32)
    isbf = ctx.sbuf("isbf", [1, 1], BF16)
    kcell = ctx.sbuf("kcell", [128, 1], F32)
    k.load(ctx, c_d[:])
    emit_load_vec(ctx, vec_d, vecs, NCH * 4 + NFF)
    ctx.fence()
    emit_t1(ctx, dr, k, sc, psW, vecs, wcol32, wcolh, wcoll, rbufh, rbufl,
            sclsb, isbf, kcell)
    ctx.emit()
    return ctx



def emit_t2(ctx, dr, k, sc, psW, wcol32, wcolh, wcoll, rbufh, rbufl):
    t = TOK
    accp_d = dr["accp"]; ffn2_d = dr["ffn2"]; ffn1_d = dr["ffn1"]
    ath_d = dr["ath1"]; atl_d = dr["atl1"]; wo_d = dr["wo1"]; out_d = dr["x2"]
    acc = ctx.sbuf("t2acc", [128, NCH * t], F32)
    for drm in (accp_d, ffn2_d, ffn1_d):
        r = drm.rearrange("(c p) t -> c p t", p=128)
        for c in range(NCH):
            if drm is accp_d:
                ctx.s("sync", lambda e, c=c, r=r: e.dma_start(out=acc[:, c * t:(c + 1) * t], in_=r[c]), dma=True)
            else:
                ctx.s("sync", lambda e, c=c, r=r: e.dma_start(out=sc.x32[:], in_=r[c]), dma=True)
                ctx.s("vector", lambda e, c=c: e.tensor_tensor(
                    acc[:, c * t:(c + 1) * t], acc[:, c * t:(c + 1) * t], sc.x32[:], ALU.add))
            ctx.fence()
    outr = out_d.rearrange("(c p) t -> c p t", p=128)

    def consumer(oc, ps):
        ctx.s("vector", lambda e, oc=oc: e.tensor_tensor(
            acc[:, oc * t:(oc + 1) * t], acc[:, oc * t:(oc + 1) * t], ps[:, 0:t], ALU.add))
        ctx.s("gpsimd", lambda e, oc=oc: e.dma_start(
            out=outr[oc], in_=acc[:, oc * t:(oc + 1) * t]), dma=True)

    emit_proj_ksweep(ctx, k, sc, wo_d, ath_d, atl_d, NCH, NCH, consumer,
                     psW, wcol32, wcolh, wcoll, rbufh, rbufl)
    ctx.fence()
    ctx.sbuf_free("t2acc")


def build_t2():
    ctx = Ctx()
    ctx.init_serial()
    t = TOK
    dr = {}
    dr["accp"] = ctx.dram_in("accp", [D, t])
    dr["ffn2"] = ctx.dram_in("ffn2", [D, t])
    dr["ffn1"] = ctx.dram_in("ffn1", [D, t])
    c_d = ctx.dram_in("consts", [128, 400])
    dr["ath1"] = ctx.dram_in("ath", [NCH, 128, t], BF16)
    dr["atl1"] = ctx.dram_in("atl", [NCH, 128, t], BF16)
    dr["wo1"] = ctx.dram_in("wo1", [D, D])
    dr["x2"] = ctx.dram_out("x2", [D, t])
    k = Consts(ctx)
    sc = Scratch(ctx)
    psW = [ctx.psum(f"psw{i}", [128, 1024], F32) for i in range(4)]
    wcol32 = ctx.sbuf("wcol32", [128, 512], F32)
    wcolh = ctx.sbuf("wcolh", [128, 512], BF16)
    wcoll = ctx.sbuf("wcoll", [128, 512], BF16)
    rbufh = ctx.sbuf("rbufh", [128, t], BF16)
    rbufl = ctx.sbuf("rbufl", [128, t], BF16)
    k.load(ctx, c_d[:])
    emit_t2(ctx, dr, k, sc, psW, wcol32, wcolh, wcoll, rbufh, rbufl)
    ctx.emit()
    return ctx



_CACHE = {}


def _get(builder, name):
    if name not in _CACHE:
        ctx = builder()
        ctx.emitted = True
        _CACHE[name] = ctx
    return _CACHE[name]


def _bh_list(s_owner):
    h0 = 2 * s_owner
    return [(0, h0), (0, h0 + 1), (1, h0), (1, h0 + 1)]


def kernel(L0_attn_x, L0_attn_wq, L0_attn_bq, L0_attn_wk, L0_attn_bk,
           L0_attn_wv, L0_attn_bv, L0_attn_wo, L0_attn_bo, L0_attn_w1,
           L0_attn_b1, L0_attn_w2, L0_attn_b2, L0_attn_g1, L0_attn_be1,
           L0_attn_g2, L0_attn_be2, L0_ffn_w_gate, L0_ffn_w_up, L0_ffn_w_down,
           L1_attn_w_qkv, L1_attn_w_o, L1_ffn_weight, L1_ffn_weight_scale,
           L1_ffn_input_scale):
    import ml_dtypes
    consts = Consts.host_array()
    f32 = np.float32

    x = np.asarray(L0_attn_x, dtype=f32)
    vecs0 = np.concatenate([np.asarray(v, dtype=f32) for v in
                            (L0_attn_bq, L0_attn_bk, L0_attn_bv, L0_attn_g1, L0_attn_be1)])
    vecs1 = np.concatenate([np.asarray(v, dtype=f32) for v in
                            (L0_attn_bo, L0_attn_b2, L0_attn_g2, L0_attn_be2, L0_attn_b1)])
    wg_t = np.ascontiguousarray(np.asarray(L0_ffn_w_gate, dtype=f32).T)
    wu_t = np.ascontiguousarray(np.asarray(L0_ffn_w_up, dtype=f32).T)
    wd_t = np.ascontiguousarray(np.asarray(L0_ffn_w_down, dtype=f32).T)
    wqkv_t = np.ascontiguousarray(np.asarray(L1_attn_w_qkv, dtype=f32).T)
    wo1_t = np.ascontiguousarray(np.asarray(L1_attn_w_o, dtype=f32).T)
    w8_t = np.ascontiguousarray(
        np.asarray(L1_ffn_weight).view(np.uint8).T).view(ml_dtypes.float8_e4m3)
    wscale = np.asarray(L1_ffn_weight_scale, dtype=f32).reshape(1, 1)
    iscale = np.asarray(L1_ffn_input_scale).reshape(1, 1)

    t0 = _get(build_t0, "t0")
    in0 = []
    for c in range(NCORES):
        b, qd = divmod(c, 4)
        xs = np.ascontiguousarray(x[b, qd * TOK:(qd + 1) * TOK, :].T)
        in0.append({"x": xs, "consts": consts, "wq": np.asarray(L0_attn_wq, f32),
                    "wk": np.asarray(L0_attn_wk, f32), "wv": np.asarray(L0_attn_wv, f32),
                    "vecs": vecs0, "wg_t": wg_t, "wu_t": wu_t, "wd_t": wd_t})
    r0 = run_bass_kernel_spmd(t0.nc, in0, core_ids=list(range(NCORES)))
    o0 = r0.results

    att = _get(build_attn, "attn")
    masks0 = attn_masks_host(-1e9)

    def attn_inputs(outs, masks, names=("qh", "ql", "kh", "kl", "vh", "vl")):
        ins = []
        for s_own in range(NCORES):
            bh = _bh_list(s_own)
            d = {"consts": consts, "masks": masks}
            for nm_fm, nm_in in ((names[0], "qh"), (names[1], "ql"),
                                 (names[2], "kh"), (names[3], "kl")):
                arr = np.empty((4, 128, S), dtype=ml_dtypes.bfloat16)
                for i, (b, h) in enumerate(bh):
                    cores = [4 * b + j for j in range(4)]
                    arr[i] = np.concatenate(
                        [np.asarray(outs[cc][nm_fm])[h * 128:(h + 1) * 128, :] for cc in cores], axis=1)
                d[nm_in] = arr
            for nm_fm, nm_in in ((names[4], "vth"), (names[5], "vtl")):
                arr = np.empty((4, S, 128), dtype=ml_dtypes.bfloat16)
                for i, (b, h) in enumerate(bh):
                    cores = [4 * b + j for j in range(4)]
                    arr[i] = np.concatenate(
                        [np.ascontiguousarray(np.asarray(outs[cc][nm_fm])[h * 128:(h + 1) * 128, :].T)
                         for cc in cores], axis=0)
                d[nm_in] = arr
            ins.append(d)
        return ins

    rA0 = run_bass_kernel_spmd(att.nc, attn_inputs(o0, masks0), core_ids=list(range(NCORES)))
    oA0 = rA0.results

    def attn_out_relayout(oa):
        hs, ls = [], []
        for c in range(NCORES):
            b, qd = divmod(c, 4)
            q0 = qd * TOK
            ah = np.empty((NCH, 128, TOK), dtype=ml_dtypes.bfloat16)
            al = np.empty((NCH, 128, TOK), dtype=ml_dtypes.bfloat16)
            for h in range(16):
                owner = h // 2
                i = 2 * b + (h % 2)
                ah[h] = np.asarray(oa[owner]["ah"])[i][:, q0:q0 + TOK]
                al[h] = np.asarray(oa[owner]["al"])[i][:, q0:q0 + TOK]
            hs.append(ah)
            ls.append(al)
        return hs, ls

    ath0, atl0 = attn_out_relayout(oA0)

    t1 = _get(build_t1, "t1")
    in1 = []
    for c in range(NCORES):
        b, qd = divmod(c, 4)
        xs = np.ascontiguousarray(x[b, qd * TOK:(qd + 1) * TOK, :].T)
        in1.append({"x": xs, "consts": consts,
                    "xnh": np.asarray(o0[c]["xnh"]), "xnl": np.asarray(o0[c]["xnl"]),
                    "fout": np.asarray(o0[c]["fout"]),
                    "ath": ath0[c], "atl": atl0[c],
                    "wo": np.asarray(L0_attn_wo, f32), "w1": np.asarray(L0_attn_w1, f32),
                    "w2": np.asarray(L0_attn_w2, f32), "wqkv_t": wqkv_t,
                    "w8_t": w8_t, "wscale": wscale, "iscale": iscale, "vecs": vecs1})
    r1 = run_bass_kernel_spmd(t1.nc, in1, core_ids=list(range(NCORES)))
    o1 = r1.results

    masks1 = attn_masks_host(float(np.finfo(np.float32).min))
    rA1 = run_bass_kernel_spmd(
        att.nc, attn_inputs(o1, masks1, names=("q1h", "q1l", "k1h", "k1l", "v1h", "v1l")),
        core_ids=list(range(NCORES)))
    oA1 = rA1.results
    ath1, atl1 = attn_out_relayout(oA1)

    t2 = _get(build_t2, "t2")
    in2 = []
    for c in range(NCORES):
        in2.append({"accp": np.asarray(o1[c]["accp"]), "ffn2": np.asarray(o1[c]["ffn2"]),
                    "ffn1": np.asarray(o1[c]["ffn1"]), "consts": consts,
                    "ath": ath1[c], "atl": atl1[c], "wo1": wo1_t})
    r2 = run_bass_kernel_spmd(t2.nc, in2, core_ids=list(range(NCORES)))
    o2 = r2.results

    out = np.empty((B, S, D), dtype=np.float32)
    for c in range(NCORES):
        b, qd = divmod(c, 4)
        out[b, qd * TOK:(qd + 1) * TOK, :] = np.asarray(o2[c]["x2"], dtype=np.float32).T
    return out


def emit_vproj_tm(ctx, k, sc, w_dram, oc0, x_hi, x_lo, n_in_ch, n_out_ch,
                  vth_d, vtl_d, psW, wbuf32, wbufh, wbufl, bias_col=None, tok=TOK):
    t = tok
    dout = n_out_ch * 128
    wr = w_dram
    for tt in range(t // 128):
        for kc in range(n_in_ch):
            ctx.s("sync", lambda e, kc=kc: e.dma_start(
                out=wbuf32[:, 0:dout], in_=wr[kc * 128:(kc + 1) * 128, oc0 * 128:oc0 * 128 + dout]), dma=True)
            ctx.s("vector", lambda e: e.tensor_copy(wbufh[:, 0:dout], wbuf32[:, 0:dout]))
            ctx.s("vector", lambda e: e.tensor_tensor(
                wbufl[:, 0:dout], wbuf32[:, 0:dout], wbufh[:, 0:dout], ALU.subtract))
            ctx.fence()
            st, sp = kc == 0, kc == n_in_ch - 1
            lh = x_hi[:, kc * t + tt * 128: kc * t + (tt + 1) * 128]
            ll = x_lo[:, kc * t + tt * 128: kc * t + (tt + 1) * 128]
            for dh in range(ceil_div(dout, 512)):
                w = min(512, dout - dh * 512)
                ctx.s("tensor", lambda e, lh=lh, dh=dh, w=w, st=st: e.matmul(
                    psW[dh % 4][:, 0:w], lh, wbufh[:, dh * 512:dh * 512 + w], start=st, stop=False))
                ctx.s("tensor", lambda e, lh=lh, dh=dh, w=w: e.matmul(
                    psW[dh % 4][:, 0:w], lh, wbufl[:, dh * 512:dh * 512 + w], start=False, stop=False))
                ctx.s("tensor", lambda e, ll=ll, dh=dh, w=w, sp=sp: e.matmul(
                    psW[dh % 4][:, 0:w], ll, wbufh[:, dh * 512:dh * 512 + w], start=False, stop=sp))
            ctx.fence()
        for dh in range(ceil_div(dout, 512)):
            w = min(512, dout - dh * 512)
            if bias_col is not None:
                ctx.s("vector", lambda e, dh=dh, w=w: e.tensor_tensor(
                    sc.x32[:, 0:w], psW[dh % 4][:, 0:w], bias_col[:, dh * 512:dh * 512 + w], ALU.add))
            else:
                ctx.s("vector", lambda e, dh=dh, w=w: e.tensor_copy(sc.x32[:, 0:w], psW[dh % 4][:, 0:w]))
            ctx.s("vector", lambda e, dh=dh, w=w: e.tensor_copy(sc.bfA[:, 0:w], sc.x32[:, 0:w]))
            ctx.s("vector", lambda e, dh=dh, w=w: e.tensor_tensor(
                sc.bfB[:, 0:w], sc.x32[:, 0:w], sc.bfA[:, 0:w], ALU.subtract))
            ctx.s("gpsimd", lambda e, tt=tt, dh=dh, w=w: e.dma_start(
                out=vth_d[tt][:, dh * 512:dh * 512 + w], in_=sc.bfA[:, 0:w]), dma=True)
            ctx.s("gpsimd", lambda e, tt=tt, dh=dh, w=w: e.dma_start(
                out=vtl_d[tt][:, dh * 512:dh * 512 + w], in_=sc.bfB[:, 0:w]), dma=True)
            ctx.fence()



BASE_PERM = [0, 1, 2, 3, 6, 7, 4, 5]


def _sigma(core, d):
    return BASE_PERM[BASE_PERM[core] ^ d]


def _rdests(d):
    return [(0, d)] * 8 if d < 4 else [None] * 4 + [(0, d)] * 4


class Exch:
    def __init__(self, ctx, xidx_sb):
        self.rsem = ctx.sem("xrsem")
        self.lsem = ctx.sem("xlsem")
        self.prep = ctx.sem("xprep")
        self.bar = ctx.sem("xbar")
        self.xidx = xidx_sb
        self.col = 0

    def idx(self):
        c = self.col
        self.col += 1
        return self.xidx[:, c:c + 1]

    def barrier(self, ctx):
        ctx.fence()
        ctx.s("gpsimd", lambda e: e.remote_sem_update_broadcast(
            self.bar.h, self.lsem.h, rdests=[(0, dd) for dd in range(8)]))
        self.prep.val += 1
        ctx.ops[-1][3].append((self.prep, 1))
        ctx.s("gpsimd", lambda e: e.trigger_dma(count=1))
        ctx.ops[-1][2].append((self.prep, self.prep.val))
        self.lsem.val += 16
        self.bar.val += 16
        ctx.s("gpsimd", lambda e: e.engine_nop())
        ctx.ops[-1][2].append((self.bar, self.bar.val))
        ctx.fence()

    def send(self, ctx, d, dst_ap, src_ap):
        ctx.fence()
        ctx.s("gpsimd", lambda e, d=d, dst_ap=dst_ap, src_ap=src_ap: e.remote_dma_broadcast(
            dst_ap, src_ap, self.rsem.h, self.lsem.h, rdests=_rdests(d)))
        self.prep.val += 1
        ctx.ops[-1][3].append((self.prep, 1))
        ctx.s("gpsimd", lambda e: e.trigger_dma(count=1))
        ctx.ops[-1][2].append((self.prep, self.prep.val))
        self.lsem.val += 16
        self.rsem.val += 16 if d < 4 else 8

    def wait_recv(self, ctx):
        ctx.s("gpsimd", lambda e: e.engine_nop())
        ctx.ops[-1][2].append((self.rsem, self.rsem.val))
        ctx.fence()

    def wait_sent(self, ctx):
        ctx.s("gpsimd", lambda e: e.engine_nop())
        ctx.ops[-1][2].append((self.lsem, self.lsem.val))
        ctx.fence()


def _ind_gather(ctx, ex, out_ap, flat_dram):
    iap = ex.idx()
    ctx.s("gpsimd", lambda e, out_ap=out_ap, flat_dram=flat_dram, iap=iap:
          e.indirect_dma_start(out_ap, None, flat_dram,
                               bass.IndirectOffsetOnAxis(iap, 0)), dma=True)


def _ind_scatter(ctx, ex, flat_dram, in_ap):
    iap = ex.idx()
    ctx.s("gpsimd", lambda e, flat_dram=flat_dram, in_ap=in_ap, iap=iap:
          e.indirect_dma_start(flat_dram, bass.IndirectOffsetOnAxis(iap, 0),
                               in_ap, None), dma=True)


def emit_exchange_e1(ctx, k, sc, ex, qkv_local, vt_local, gather, stg, slots):
    t = TOK
    qk_flat = {nm: qkv_local[nm].rearrange("(r p) t -> (r p) t", p=128)
               for nm in ("qh", "ql", "kh", "kl")}
    vt_flat = [vt.rearrange("a p (g d) -> (a p g) d", d=256) for vt in vt_local]
    g_flat = {nm: gather[nm].rearrange("b p (q t) -> (b p q) t", t=t)
              for nm in ("qh", "ql", "kh", "kl")}
    gv_flat = [gather[nm].rearrange("b (r p) d -> (b r p) d", p=128)
               for nm in ("vth", "vtl")]
    for d in range(8):
        ex.barrier(ctx)
        for j2, nm in enumerate(("qh", "ql", "kh", "kl")):
            for j in range(2):
                _ind_gather(ctx, ex, stg[:, (2 * j2 + j) * t:(2 * j2 + j + 1) * t], qk_flat[nm])
        for hl in range(2):
            for tt in range(8):
                _ind_gather(ctx, ex, stg[:, 8 * t + hl * 2048 + tt * 256:
                                         8 * t + hl * 2048 + (tt + 1) * 256], vt_flat[hl])
        ctx.fence()
        par = d % 2
        ex.send(ctx, d, slots[:, par * 12288:(par + 1) * 12288], stg[:, 0:12288])
        ex.wait_recv(ctx)
        for j2, nm in enumerate(("qh", "ql", "kh", "kl")):
            for j in range(2):
                _ind_scatter(ctx, ex, g_flat[nm],
                             slots[:, par * 12288 + (2 * j2 + j) * t:
                                   par * 12288 + (2 * j2 + j + 1) * t])
        for hl in range(2):
            for j in range(2):
                for tt in range(8):
                    c0 = par * 12288 + 8 * t + hl * 2048 + tt * 256 + j * 128
                    _ind_scatter(ctx, ex, gv_flat[hl], slots[:, c0:c0 + 128])
        ctx.fence()
        ex.wait_sent(ctx)


def emit_exchange_e2(ctx, k, sc, ex, a_local, gather, stg, slots):
    t = TOK
    a_flat = [a.rearrange("b p (q t) -> (b p q) t", t=t) for a in a_local]
    g_flat = [g.rearrange("c p t -> (c p) t") for g in gather]
    for d in range(8):
        ex.barrier(ctx)
        for hl in range(2):
            for j in range(2):
                _ind_gather(ctx, ex, stg[:, (2 * hl + j) * t:(2 * hl + j + 1) * t], a_flat[hl])
        ctx.fence()
        par = d % 2
        ex.send(ctx, d, slots[:, par * 4096:(par + 1) * 4096], stg[:, 0:4096])
        ex.wait_recv(ctx)
        for hl in range(2):
            for j in range(2):
                _ind_scatter(ctx, ex, g_flat[hl],
                             slots[:, par * 4096 + (2 * hl + j) * t:
                                   par * 4096 + (2 * hl + j + 1) * t])
        ctx.fence()
        ex.wait_sent(ctx)


def build_xidx_host(core):
    cols = []
    part = np.arange(128, dtype=np.uint32)

    def e1():
        for d in range(8):
            p = _sigma(core, d)
            for j2 in range(4):
                for j in range(2):
                    cols.append((2 * p + j) * 128 + part)
            for hl in range(2):
                for tt in range(8):
                    cols.append((tt * 128 + part) * 8 + p)
            bp, tq = divmod(p, 4) if False else (p // 4, p % 4)
            for j2 in range(4):
                for j in range(2):
                    cols.append((2 * bp + j) * 512 + part * 4 + tq)
            for hl in range(2):
                for j in range(2):
                    for tt in range(8):
                        cols.append(np.full(128, (2 * bp + j) * 4096 + tq * 1024 + tt * 128, np.uint32) + part)

    def e2():
        for d in range(8):
            p = _sigma(core, d)
            bp, tq = p // 4, p % 4
            for hl in range(2):
                for j in range(2):
                    cols.append((2 * bp + j) * 512 + part * 4 + tq)
            for hl in range(2):
                for j in range(2):
                    cols.append((2 * p + j) * 128 + part)

    e1(); e2(); e1(); e2()
    return np.stack(cols).astype(np.uint32)


N_XIDX = 4 * 8 * (8 + 16) // 1 if False else None



def build_fused(upto=99, vtm=True, dbg_gathers=False):
    from concourse import library_config
    ctx = Ctx()
    ctx.init_serial()
    t = TOK
    dr = {}
    dr["x"] = ctx.dram_in("x", [D, t])
    c_d = ctx.dram_in("consts", [128, 400])
    m_d = ctx.dram_in("masks2", [8, 128, QT])
    pt_d = ctx.dram_in("xidx", [128, 1152], mybir.dt.uint32)
    for nm, sh in (("wq", [D, D]), ("wk", [D, D]), ("wv", [D, D]),
                   ("wo", [D, D]), ("w1", [D, DFF]), ("w2", [DFF, D]),
                   ("wg_t", [D, DFF]), ("wu_t", [D, DFF]), ("wd_t", [DFF, D]),
                   ("wqkv_t", [D, 3 * D]), ("wo1", [D, D])):
        dr[nm] = ctx.dram_in(nm, sh)
    dr["w8_t"] = ctx.dram_in("w8_t", [D, D], mybir.dt.float8e4)
    dr["wscale"] = ctx.dram_in("wscale", [1, 1])
    dr["iscale"] = ctx.dram_in("iscale", [1, 1], BF16)
    v0_d = ctx.dram_in("vecs0", [5 * D])
    v1_d = ctx.dram_in("vecs1", [(NCH * 4 + NFF) * 128])
    for nm in ("xnh", "xnl", "qh", "ql", "kh", "kl"):
        dr[nm] = ctx.dram_tmp(nm, [D, t], BF16)
    dr["vth_loc"] = ctx.dram_tmp("vth_loc", [8, 128, D], BF16)
    dr["vh"] = ctx.dram_tmp("vh", [D, TOK], BF16)
    dr["vl"] = ctx.dram_tmp("vl", [D, TOK], BF16)
    dr["vtl_loc"] = ctx.dram_tmp("vtl_loc", [8, 128, D], BF16)
    dr["fout"] = ctx.dram_tmp("fout", [D, t])
    dr["ispill"] = ctx.dram_tmp("ispill", [NFF, 2, 128, t], BF16)
    dr["zqspill"] = ctx.dram_tmp("zqspill", [NFF, 128, t], BF16)
    mk = ctx.dram_out if dbg_gathers else ctx.dram_tmp
    for nm in ("g_qh", "g_ql", "g_kh", "g_kl", "ah", "al",
               "g1_qh", "g1_ql", "g1_kh", "g1_kl", "ah1", "al1"):
        dr[nm] = mk(nm, [4, 128, S], BF16)
    for nm in ("g_vth", "g_vtl", "g1_vth", "g1_vtl"):
        dr[nm] = mk(nm, [4, S, 128], BF16)
    for nm in ("ath", "atl", "ath1", "atl1"):
        dr[nm] = ctx.dram_tmp(nm, [NCH, 128, t], BF16)
    dr["accp"] = ctx.dram_tmp("accp", [D, t])
    dr["ffn2"] = ctx.dram_tmp("ffn2", [D, t])
    dr["ffn1"] = ctx.dram_tmp("ffn1", [D, t])
    for nm in ("q1h", "q1l", "k1h", "k1l"):
        dr[nm] = ctx.dram_tmp(nm, [D, t], BF16)
    dr["q1vh"] = dr["q1h"]
    dr["v1h"] = ctx.dram_tmp("v1h", [D, t], BF16)
    dr["v1l"] = ctx.dram_tmp("v1l", [D, t], BF16)
    dr["v1th_loc"] = ctx.dram_tmp("v1th_loc", [8, 128, D], BF16)
    dr["v1tl_loc"] = ctx.dram_tmp("v1tl_loc", [8, 128, D], BF16)
    dr["hspill"] = ctx.dram_tmp("hspill", [NFF, 2, 128, t], BF16)
    dr["xasph"] = ctx.dram_tmp("xasph", [NCH, 128, t], BF16)
    dr["xaspl"] = ctx.dram_tmp("xaspl", [NCH, 128, t], BF16)
    dr["x2"] = ctx.dram_out("x2", [D, t])

    k = Consts(ctx)
    sc = Scratch(ctx)
    psW = [ctx.psum(f"psw{i}", [128, 1024], F32) for i in range(4)]
    psA, psB, psC = psW[0], psW[1], psW[2]
    psD = psW[3][:, 0:512]
    psE = psW[3][:, 512:1024]

    vecs0 = ctx.sbuf("vecs0_sb", [128, 5 * NCH], F32)
    vecs1 = ctx.sbuf("vecs1_sb", [128, NCH * 4 + NFF], F32)
    cellcols = ctx.sbuf("cellcols", [128, 8], F32)
    maskssb = ctx.sbuf("maskssb", [128, 8 * QT], F32)
    ptab = ctx.sbuf("xidx_sb", [128, 1152], mybir.dt.uint32)
    rbufh = ctx.sbuf("rbufh", [128, t], BF16)
    rbufl = ctx.sbuf("rbufl", [128, t], BF16)
    sclsb = ctx.sbuf("sclsb", [1, 4], F32)
    isbf = ctx.sbuf("isbf", [1, 1], BF16)
    kcell = ctx.sbuf("kcell", [128, 1], F32)

    ex = Exch(ctx, ptab)

    ctx.s("gpsimd", lambda e: (e.load_library(library_config.proxy), e.engine_nop())[1])
    k.load(ctx, c_d[:])
    emit_load_vec(ctx, v0_d, vecs0, 5 * NCH)
    emit_load_vec(ctx, v1_d, vecs1, NCH * 4 + NFF)
    ctx.s("sync", lambda e: e.dma_start(
        out=maskssb[:].rearrange("p (i f) -> p i f", f=QT), in_=m_d.transpose([1, 0, 2])), dma=True)
    ctx.s("sync", lambda e: e.dma_start(out=ptab[:], in_=pt_d), dma=True)
    ctx.fence()

    if upto <= -1:
        return _finish_fused(ctx, dr)
    wcol32 = ctx.sbuf("wcol32", [128, NCH * 128], F32)
    wcolh = ctx.sbuf("wcolh", [128, NCH * 128], BF16)
    wcoll = ctx.sbuf("wcoll", [128, NCH * 128], BF16)
    emit_t0(ctx, dr, k, sc, psA, psB, psC, psD, psE, vecs0, cellcols,
            wcol32, wcolh, wcoll, vtm=vtm)
    ctx.sbuf_free("wcoll", "wcolh", "wcol32")
    if upto <= 0:
        return _finish_fused(ctx, dr)

    stg = ctx.sbuf("xstg", [128, 12288], BF16)
    slots = ctx.sbuf("xslots", [128, 2 * 12288], BF16)
    qkv_local = {nm: dr[nm] for nm in ("qh", "ql", "kh", "kl")}
    gather0 = {"qh": dr["g_qh"], "ql": dr["g_ql"], "kh": dr["g_kh"],
               "kl": dr["g_kl"], "vth": dr["g_vth"], "vtl": dr["g_vtl"]}
    emit_exchange_e1(ctx, k, sc, ex, qkv_local, (dr["vth_loc"], dr["vtl_loc"]),
                     gather0, stg, slots)
    ctx.sbuf_free("xslots", "xstg")
    if upto <= 1:
        return _finish_fused(ctx, dr)

    adr = dict(qh=dr["g_qh"], ql=dr["g_ql"], kh=dr["g_kh"], kl=dr["g_kl"],
               vth=dr["g_vth"], vtl=dr["g_vtl"], ah=dr["ah"], al=dr["al"])
    emit_attn(ctx, adr, k, sc, psD, psE, psA[:, 0:512], psB[:, 0:512], maskssb[:, 0:4 * QT])
    if upto <= 2:
        return _finish_fused(ctx, dr)

    stg = ctx.sbuf("xstg2", [128, 4096], BF16)
    slots = ctx.sbuf("xslots2", [128, 8192], BF16)
    emit_exchange_e2(ctx, k, sc, ex, (dr["ah"], dr["al"]), (dr["ath"], dr["atl"]), stg, slots)
    ctx.sbuf_free("xslots2", "xstg2")
    if upto <= 3:
        return _finish_fused(ctx, dr)

    wcol32 = ctx.sbuf("wcol32b", [128, NCH * 128], F32)
    wcolh = ctx.sbuf("wcolhb", [128, NCH * 128], BF16)
    wcoll = ctx.sbuf("wcollb", [128, NCH * 128], BF16)
    emit_t1(ctx, dr, k, sc, psW, vecs1, wcol32, wcolh, wcoll, rbufh, rbufl,
            sclsb, isbf, kcell, vtm=True)
    ctx.sbuf_free("wcollb", "wcolhb", "wcol32b")
    if upto <= 4:
        return _finish_fused(ctx, dr)

    stg = ctx.sbuf("xstg3", [128, 12288], BF16)
    slots = ctx.sbuf("xslots3", [128, 2 * 12288], BF16)
    qkv_local1 = {"qh": dr["q1h"], "ql": dr["q1l"], "kh": dr["k1h"], "kl": dr["k1l"]}
    gather1 = {"qh": dr["g1_qh"], "ql": dr["g1_ql"], "kh": dr["g1_kh"],
               "kl": dr["g1_kl"], "vth": dr["g1_vth"], "vtl": dr["g1_vtl"]}
    emit_exchange_e1(ctx, k, sc, ex, qkv_local1, (dr["v1th_loc"], dr["v1tl_loc"]),
                     gather1, stg, slots)
    ctx.sbuf_free("xslots3", "xstg3")
    if upto <= 5:
        return _finish_fused(ctx, dr)

    adr1 = dict(qh=dr["g1_qh"], ql=dr["g1_ql"], kh=dr["g1_kh"], kl=dr["g1_kl"],
                vth=dr["g1_vth"], vtl=dr["g1_vtl"], ah=dr["ah1"], al=dr["al1"])
    emit_attn(ctx, adr1, k, sc, psD, psE, psA[:, 0:512], psB[:, 0:512], maskssb[:, 4 * QT:8 * QT])
    if upto <= 6:
        return _finish_fused(ctx, dr)

    stg = ctx.sbuf("xstg4", [128, 4096], BF16)
    slots = ctx.sbuf("xslots4", [128, 8192], BF16)
    emit_exchange_e2(ctx, k, sc, ex, (dr["ah1"], dr["al1"]), (dr["ath1"], dr["atl1"]), stg, slots)
    ctx.sbuf_free("xslots4", "xstg4")
    if upto <= 7:
        return _finish_fused(ctx, dr)

    wcol32 = ctx.sbuf("wcol32c", [128, 512], F32)
    wcolh = ctx.sbuf("wcolhc", [128, 512], BF16)
    wcoll = ctx.sbuf("wcollc", [128, 512], BF16)
    emit_t2(ctx, dr, k, sc, psW, wcol32, wcolh, wcoll, rbufh, rbufl)
    ctx.sbuf_free("wcollc", "wcolhc", "wcol32c")
    ctx.emit()
    mybir.codegen_inst_isa_subclasses(ctx.nc)
    return ctx


def _finish_fused(ctx, dr):
    t = TOK
    sc0 = ctx.sbuf("fin_z", [128, t], F32)
    outr = dr["x2"].rearrange("(c p) t -> c p t", p=128)
    ctx.s("vector", lambda e: e.memset(sc0[:], 0.0))
    for c in range(NCH):
        ctx.s("gpsimd", lambda e, c=c: e.dma_start(out=outr[c], in_=sc0[:]), dma=True)
        ctx.fence()
    ctx.emit()
    mybir.codegen_inst_isa_subclasses(ctx.nc)
    return ctx

USE_FUSED = True


def kernel_fused(L0_attn_x, L0_attn_wq, L0_attn_bq, L0_attn_wk, L0_attn_bk,
                 L0_attn_wv, L0_attn_bv, L0_attn_wo, L0_attn_bo, L0_attn_w1,
                 L0_attn_b1, L0_attn_w2, L0_attn_b2, L0_attn_g1, L0_attn_be1,
                 L0_attn_g2, L0_attn_be2, L0_ffn_w_gate, L0_ffn_w_up,
                 L0_ffn_w_down, L1_attn_w_qkv, L1_attn_w_o, L1_ffn_weight,
                 L1_ffn_weight_scale, L1_ffn_input_scale):
    import ml_dtypes
    consts = Consts.host_array()
    f32 = np.float32
    x = np.asarray(L0_attn_x, dtype=f32)
    vecs0 = np.concatenate([np.asarray(v, dtype=f32) for v in
                            (L0_attn_bq, L0_attn_bk, L0_attn_bv, L0_attn_g1, L0_attn_be1)])
    vecs1 = np.concatenate([np.asarray(v, dtype=f32) for v in
                            (L0_attn_bo, L0_attn_b2, L0_attn_g2, L0_attn_be2, L0_attn_b1)])
    masks2 = np.concatenate([attn_masks_host(-1e9),
                             attn_masks_host(float(np.finfo(np.float32).min))], axis=0)
    wg_t = np.ascontiguousarray(np.asarray(L0_ffn_w_gate, dtype=f32).T)
    wu_t = np.ascontiguousarray(np.asarray(L0_ffn_w_up, dtype=f32).T)
    wd_t = np.ascontiguousarray(np.asarray(L0_ffn_w_down, dtype=f32).T)
    wqkv_t = np.ascontiguousarray(np.asarray(L1_attn_w_qkv, dtype=f32).T)
    wo1_t = np.ascontiguousarray(np.asarray(L1_attn_w_o, dtype=f32).T)
    w8_t = np.ascontiguousarray(
        np.asarray(L1_ffn_weight).view(np.uint8).T).view(ml_dtypes.float8_e4m3)
    wscale = np.asarray(L1_ffn_weight_scale, dtype=f32).reshape(1, 1)
    iscale = np.asarray(L1_ffn_input_scale).reshape(1, 1)

    fz = _get(build_fused, "fused")
    ins = []
    for c in range(NCORES):
        b, qd = divmod(c, 4)
        xs = np.ascontiguousarray(x[b, qd * TOK:(qd + 1) * TOK, :].T)
        ins.append({"x": xs, "consts": consts, "masks2": masks2,
                    "xidx": np.ascontiguousarray(build_xidx_host(c).T),
                    "wq": np.asarray(L0_attn_wq, f32), "wk": np.asarray(L0_attn_wk, f32),
                    "wv": np.asarray(L0_attn_wv, f32), "wo": np.asarray(L0_attn_wo, f32),
                    "w1": np.asarray(L0_attn_w1, f32), "w2": np.asarray(L0_attn_w2, f32),
                    "wg_t": wg_t, "wu_t": wu_t, "wd_t": wd_t,
                    "wqkv_t": wqkv_t, "wo1": wo1_t, "w8_t": w8_t,
                    "wscale": wscale, "iscale": iscale,
                    "vecs0": vecs0, "vecs1": vecs1})
    res = run_bass_kernel_spmd(fz.nc, ins, core_ids=list(range(NCORES)))
    out = np.empty((B, S, D), dtype=np.float32)
    for c in range(NCORES):
        b, qd = divmod(c, 4)
        out[b, qd * TOK:(qd + 1) * TOK, :] = np.asarray(res.results[c]["x2"], dtype=np.float32).T
    return out


_kernel_multilaunch = kernel


def kernel(**inputs):
    if USE_FUSED:
        return kernel_fused(**inputs)
    return _kernel_multilaunch(**inputs)


# version 10
# speedup vs baseline: 1.4622x; 1.0674x over previous
import math
import numpy as np
import concourse.bass as bass
import concourse.mybir as mybir
from concourse.bass_utils import run_bass_kernel_spmd

F32 = mybir.dt.float32
BF16 = mybir.dt.bfloat16
I32 = mybir.dt.int32
AF = mybir.ActivationFunctionType
ALU = mybir.AluOpType
AX = mybir.AxisListType

NCORES = 8
B, S, D = 2, 4096, 2048
TOK = 1024
NCH = D // 128
DFF = 8192
NFF = DFF // 128
HD = 128
LN_EPS = 1e-5
RSQ_HD = 1.0 / math.sqrt(float(HD))
FWHT_NORM = 1.0 / math.sqrt(float(DFF))



class Sem:
    def __init__(self, handle, name):
        self.h = handle
        self.name = name
        self.val = 0


class Ctx:
    ENGINES = ("sync", "tensor", "vector", "scalar", "gpsimd")

    def __init__(self):
        self.nc = bass.Bass("TRN2", target_bir_lowering=False)
        self._cm = []
        self.sems = {}
        self.ops = []
        self.dram = {}
        self._uid = 0
        self._last = None
        self.sq = None

    def init_serial(self):
        self.sq = self.sem("sq")

    def uid(self):
        self._uid += 1
        return self._uid

    def dram_in(self, name, shape, dt=F32):
        t = self.nc.dram_tensor(name, list(shape), dt, kind="ExternalInput").ap()
        self.dram[name] = t
        return t

    def dram_out(self, name, shape, dt=F32):
        t = self.nc.dram_tensor(name, list(shape), dt, kind="ExternalOutput").ap()
        self.dram[name] = t
        return t

    def dram_tmp(self, name, shape, dt=F32):
        t = self.nc.dram_tensor(name, list(shape), dt, kind="Internal").ap()
        self.dram[name] = t
        return t

    def sbuf(self, name, shape, dt=F32):
        cm = self.nc.sbuf_tensor(name, list(shape), dt)
        h = cm.__enter__()
        self._cm.append(cm)
        self._sbuf_cms = getattr(self, "_sbuf_cms", {})
        self._sbuf_cms[name] = cm
        return h

    def sbuf_free(self, *names):
        for name in names:
            cm = self._sbuf_cms.pop(name)
            cm.__exit__(None, None, None)
            self._cm.remove(cm)

    def psum(self, name, shape, dt=F32):
        cm = self.nc.psum_tensor(name, list(shape), dt)
        h = cm.__enter__()
        self._cm.append(cm)
        return h

    def sem(self, name):
        if name not in self.sems:
            cm = self.nc.semaphore(name)
            h = cm.__enter__()
            self._cm.append(cm)
            self.sems[name] = Sem(h, name)
        return self.sems[name]

    FULL_SERIAL = True
    COALESCE = ("tensor",)

    def s(self, eng, fn, dma=False):
        rec = [eng, fn, [], [], dma]
        if self._last is not None:
            lrec = self._last
            if (lrec[0] == eng and not lrec[4] and not dma
                    and (eng in self.COALESCE or not self.FULL_SERIAL)):
                pass
            else:
                self._ensure_inc(lrec)
                rec[2].append((self.sq, self.sq.val))
        elif self.ops:
            rec[2].append((self.sq, self.sq.val))
        if dma:
            self.sq.val += 16
            rec[3].append((self.sq, 16))
        self.ops.append(rec)
        self._last = rec
        return rec

    def _ensure_inc(self, rec):
        if not rec[3]:
            self.sq.val += 1
            rec[3].append((self.sq, 1))

    def fence(self):
        if self._last is not None:
            self._ensure_inc(self._last)
            self._last = None

    def emit(self):
        if self._last is not None:
            self._ensure_inc(self._last)
        ops = self.ops
        with self.nc.Block() as blk:
            def runner(name):
                def run(eng):
                    for eng_name, fn, waits, incs, dma in ops:
                        if eng_name != name:
                            continue
                        for s, v in waits:
                            eng.wait_ge(s.h, v)
                        if fn is None:
                            for s, k in incs:
                                eng.sem_inc(s.h, k)
                        else:
                            instr = fn(eng)
                            for s, k in incs:
                                instr.then_inc(s.h, k)
                    eng.wait_ge(self.sq.h, self.sq.val)
                return run
            blk.sync(runner("sync"))
            blk.tensor(runner("tensor"))
            blk.vector(runner("vector"))
            blk.scalar(runner("scalar"))
            blk.gpsimd(runner("gpsimd"))

    def close(self):
        for cm in reversed(self._cm):
            cm.__exit__(None, None, None)
        self._cm = []


def ceil_div(a, b):
    return -(-a // b)


def halves(n, w=512):
    return [(i, min(w, n - i)) for i in range(0, n, w)]



class Consts:

    def __init__(self, ctx):
        self.c32 = ctx.sbuf("consts32", [128, 400], F32)
        self.ident_f32 = self.c32[:, 0:128]
        self.ones_col_bf = ctx.sbuf("ones_col_bf", [128, 1], BF16)
        self.ident_bf = ctx.sbuf("ident_bf", [128, 128], BF16)
        self.h128_bf = ctx.sbuf("h128_bf", [128, 128], BF16)
        self.ones_row_f32 = self.c32[0:1, 256:384]
        self.eps_col = self.c32[:, 385:386]
        self.eps_1 = self.c32[0:1, 384:385]
        import math as _m
        ln2 = _m.log(2.0)
        for col, val in ((386, 3.0 * ln2), (387, -3.0 * ln2), (388, 0.0), (389, 1.0)):
            ctx.nc.const_aps.register(F32, float(np.float32(val)), self.c32[:, col:col + 1])

    def load(self, ctx, dram_c):
        ctx.s("sync", lambda e: e.dma_start(out=self.c32[:], in_=dram_c), dma=True)
        ctx.s("vector", lambda e: e.tensor_copy(self.ident_bf[:], self.c32[:, 0:128]))
        ctx.s("vector", lambda e: e.tensor_copy(self.h128_bf[:], self.c32[:, 128:256]))
        ctx.s("vector", lambda e: e.tensor_copy(self.ones_col_bf[:], self.c32[:, 256:257]))
        ctx.fence()

    @staticmethod
    def host_array():
        c = np.zeros((128, 400), dtype=np.float32)
        c[:, 0:128] = np.eye(128, dtype=np.float32)
        h = np.array([[1.0]], dtype=np.float32)
        while h.shape[0] < 128:
            h = np.block([[h, h], [h, -h]])
        c[:, 128:256] = h
        c[:, 256] = 1.0
        c[0, 256:384] = 1.0
        c[0, 384] = LN_EPS
        c[:, 385] = LN_EPS
        import math as _m
        ln2 = _m.log(2.0)
        c[:, 386] = np.float32(3.0 * ln2)
        c[:, 387] = np.float32(-3.0 * ln2)
        c[:, 388] = 0.0
        c[:, 389] = 1.0
        return c


class Scratch:
    def __init__(self, ctx, tok=TOK):
        t = tok
        self.x32 = ctx.sbuf("sc_x32", [128, t], F32)
        self.y32 = ctx.sbuf("sc_y32", [128, t], F32)
        self.z32 = ctx.sbuf("sc_z32", [128, t], F32)
        self.bfA = ctx.sbuf("sc_bfA", [128, 2 * t], BF16)
        self.bfB = ctx.sbuf("sc_bfB", [128, 2 * t], BF16)
        self.i32 = ctx.sbuf("sc_i32", [128, t], I32)
        self.rows = ctx.sbuf("sc_rows", [1, 8 * t], F32)
        self.bc = ctx.sbuf("sc_bc", [128, 2 * t], F32)


def bcast_row(ctx, k, row_ap, out_ap, ps, n):
    for off, w in halves(n):
        ctx.s("tensor", lambda e, off=off, w=w: e.matmul(
            ps[:, 0:w], k.ones_row_f32, row_ap[0:1, off:off + w],
            start=True, stop=True))
        ctx.s("vector", lambda e, off=off, w=w: e.tensor_copy(
            out_ap[:, off:off + w], ps[:, 0:w]))



def emit_ln(ctx, k, sc, pair_hi, pair_lo, out_hi, out_lo, ps_a, ps_b,
            gamma=None, beta=None, nch=NCH, tok=TOK, out32_cb=None):
    t = tok
    ch = lambda sb, c: sb[:, c * t:(c + 1) * t]
    for c in range(nch):
        hi, lo = ch(pair_hi, c), ch(pair_lo, c)
        ctx.s("vector", lambda e, hi=hi, lo=lo: e.tensor_tensor(sc.x32[:], hi, lo, ALU.add))
        ctx.s("scalar", lambda e: e.activation(sc.y32[:], sc.x32[:], AF.Square))
        ctx.s("vector", lambda e: e.tensor_copy(sc.bfA[:, 0:t], sc.y32[:]))
        ctx.s("vector", lambda e: e.tensor_tensor(sc.bfB[:, 0:t], sc.y32[:], sc.bfA[:, 0:t], ALU.subtract))
        ctx.fence()
        for off, w in halves(t):
            for i, (mat, r0) in enumerate(((hi, 0), (lo, 0), (sc.bfA[:, 0:t], 32), (sc.bfB[:, 0:t], 32))):
                st = (c == 0) and (i % 2 == 0)
                sp = (c == nch - 1) and (i % 2 == 1)
                ctx.s("tensor", lambda e, mat=mat, r0=r0, off=off, w=w, st=st, sp=sp: e.matmul(
                    ps_a[r0:r0 + 1, off:off + w], k.ones_col_bf[:],
                    mat[:, off:off + w], start=st, stop=sp))
        ctx.fence()
    R = sc.rows
    ctx.s("vector", lambda e: e.tensor_copy(R[0:1, 0:t], ps_a[0:1, 0:t]))
    ctx.s("vector", lambda e: e.tensor_copy(R[0:1, t:2 * t], ps_a[32:33, 0:t]))
    invd = 1.0 / float(nch * 128)
    ctx.s("vector", lambda e: e.tensor_scalar(R[0:1, 2 * t:3 * t], R[0:1, 0:t], invd, None, ALU.mult))
    ctx.s("vector", lambda e: e.tensor_scalar(R[0:1, 3 * t:4 * t], R[0:1, t:2 * t], invd, None, ALU.mult))
    ctx.s("scalar", lambda e: e.activation(R[0:1, 0:t], R[0:1, 2 * t:3 * t], AF.Square))
    ctx.s("vector", lambda e: e.tensor_tensor(R[0:1, 3 * t:4 * t], R[0:1, 3 * t:4 * t], R[0:1, 0:t], ALU.subtract))
    ctx.s("scalar", lambda e: e.activation(R[0:1, 0:t], R[0:1, 3 * t:4 * t], AF.Sqrt, bias=k.eps_1, scale=1.0))
    ctx.s("vector", lambda e: e.reciprocal(R[0:1, 4 * t:5 * t], R[0:1, 0:t]))
    ctx.s("scalar", lambda e: e.activation(R[0:1, 0:t], R[0:1, 4 * t:5 * t], AF.Square))
    ctx.s("vector", lambda e: e.tensor_scalar(R[0:1, 3 * t:4 * t], R[0:1, 3 * t:4 * t], LN_EPS, None, ALU.add))
    ctx.s("vector", lambda e: e.tensor_tensor(R[0:1, 0:t], R[0:1, 0:t], R[0:1, 3 * t:4 * t], ALU.mult))
    ctx.s("vector", lambda e: e.tensor_scalar(R[0:1, 0:t], R[0:1, 0:t], -0.5, 1.5, ALU.mult, ALU.add))
    ctx.s("vector", lambda e: e.tensor_tensor(R[0:1, 4 * t:5 * t], R[0:1, 4 * t:5 * t], R[0:1, 0:t], ALU.mult))
    ctx.s("vector", lambda e: e.scalar_tensor_tensor(
        R[0:1, 2 * t:3 * t], R[0:1, 2 * t:3 * t], -1.0, R[0:1, 4 * t:5 * t], ALU.mult, ALU.mult))
    ctx.fence()
    bcast_row(ctx, k, R[0:1, 4 * t:5 * t], sc.bc[:, 0:t], ps_b, t)
    bcast_row(ctx, k, R[0:1, 2 * t:3 * t], sc.bc[:, t:2 * t], ps_b, t)
    ctx.fence()
    for c in range(nch):
        hi, lo = ch(pair_hi, c), ch(pair_lo, c)
        ctx.s("vector", lambda e, hi=hi, lo=lo: e.tensor_tensor(sc.x32[:], hi, lo, ALU.add))
        ctx.s("vector", lambda e: e.tensor_tensor(sc.y32[:], sc.x32[:], sc.bc[:, 0:t], ALU.mult))
        ctx.s("vector", lambda e: e.tensor_tensor(sc.y32[:], sc.y32[:], sc.bc[:, t:2 * t], ALU.add))
        if gamma is not None:
            ctx.s("vector", lambda e, c=c: e.tensor_scalar(
                sc.y32[:], sc.y32[:], gamma[:, c:c + 1], beta[:, c:c + 1], ALU.mult, ALU.add))
        oh, ol = ch(out_hi, c), ch(out_lo, c)
        ctx.s("vector", lambda e, oh=oh: e.tensor_copy(oh, sc.y32[:]))
        ctx.s("vector", lambda e, oh=oh, ol=ol: e.tensor_tensor(ol, sc.y32[:], oh, ALU.subtract))
        if out32_cb is not None:
            out32_cb(c, sc.y32[:])
        ctx.fence()



def emit_proj(ctx, k, w_dram, x_hi, x_lo, n_in_ch, n_out_ch, consumer,
              tok=TOK, wbufs=None, quant_lhs=None, oc_list=None):
    if wbufs is None:
        wbufs = {}
    u = ctx.uid()
    kw = n_in_ch * 128
    w32a = wbufs.get("w32") or ctx.sbuf(f"pj{u}_w32", [128, kw], F32)
    wh = wbufs.get("wh") or ctx.sbuf(f"pj{u}_wh", [128, kw], BF16)
    wl = wbufs.get("wl") or ctx.sbuf(f"pj{u}_wl", [128, kw], BF16)
    ps = wbufs.get("ps") or ctx.psum(f"pj{u}_ps", [128, tok], F32)
    w32b_name = f"pj{u}_w32b"
    w32b = wbufs.get("w32b") or ctx.sbuf(w32b_name, [128, kw], F32)
    wr = w_dram.rearrange("(kc p) o -> kc p o", p=128)
    ocs = list(oc_list if oc_list is not None else range(n_out_ch))
    n = len(ocs)
    s_ld = ctx.sem("pjld")
    ctx.fence()
    entry_val = ctx.sq.val
    frv = {}

    def dma_async(i):
        w32 = (w32a, w32b)[i % 2]
        src = wr[:, :, ocs[i] * 128:(ocs[i] + 1) * 128].transpose([1, 0, 2])
        wait_v = entry_val if i < 2 else frv[i - 2]
        s_ld.val += 16
        ctx.ops.append(["sync",
                        (lambda e, src=src, w32=w32: e.dma_start(
                            out=w32[:].rearrange("p (kc f) -> p kc f", f=128), in_=src)),
                        [(ctx.sq, wait_v)], [(s_ld, 16)], True])

    ld_base0 = s_ld.val
    dma_async(0)
    if n > 1:
        dma_async(1)
    for i, oc in enumerate(ocs):
        w32 = (w32a, w32b)[i % 2]
        if quant_lhs is None:
            r1 = ctx.s("vector", lambda e, w32=w32: e.tensor_copy(wh[:], w32[:]))
            ctx.s("vector", lambda e, w32=w32: e.tensor_tensor(wl[:], w32[:], wh[:], ALU.subtract))
        else:
            r1 = quant_lhs(ctx, w32, wh, wl)
        r1[2].append((s_ld, ld_base0 + 16 * (i + 1)))
        ctx.fence()
        frv[i] = ctx.sq.val
        for kc in range(n_in_ch):
            lh = wh[:, kc * 128:(kc + 1) * 128]
            ll = wl[:, kc * 128:(kc + 1) * 128]
            for off, w in halves(tok):
                st = (kc == 0)
                sp = (kc == n_in_ch - 1)
                xh = x_hi[:, kc * tok + off: kc * tok + off + w]
                if x_lo is None:
                    ctx.s("tensor", lambda e, lh=lh, xh=xh, off=off, w=w, st=st, sp=sp: e.matmul(
                        ps[:, off:off + w], lh, xh, start=st, stop=sp))
                else:
                    xl = x_lo[:, kc * tok + off: kc * tok + off + w]
                    ctx.s("tensor", lambda e, lh=lh, xh=xh, off=off, w=w, st=st: e.matmul(
                        ps[:, off:off + w], lh, xh, start=st, stop=False))
                    ctx.s("tensor", lambda e, lh=lh, xl=xl, off=off, w=w: e.matmul(
                        ps[:, off:off + w], lh, xl, start=False, stop=False))
                    ctx.s("tensor", lambda e, ll=ll, xh=xh, off=off, w=w, sp=sp: e.matmul(
                        ps[:, off:off + w], ll, xh, start=False, stop=sp))
        ctx.fence()
        consumer(oc, ps)
        ctx.fence()
        if i + 2 < n:
            dma_async(i + 2)
    if "w32b" not in wbufs:
        ctx.sbuf_free(w32b_name)



def shard_tokens_fm(x):
    out = []
    for c in range(NCORES):
        b, q = divmod(c, 4)
        out.append(np.ascontiguousarray(x[b, q * TOK:(q + 1) * TOK, :].T))
    return out


def gather_tokens_fm(shards):
    x = np.empty((B, S, D), dtype=np.float32)
    for c in range(NCORES):
        b, q = divmod(c, 4)
        x[b, q * TOK:(q + 1) * TOK, :] = np.asarray(shards[c], dtype=np.float32).T
    return x





def emit_load_vec(ctx, dram_vec, sb, nch):
    src = dram_vec.rearrange("(c p) -> c p", p=128).transpose([1, 0])
    ctx.s("sync", lambda e: e.dma_start(out=sb[:, 0:nch], in_=src,
                                        allow_slow_non_contiguous=True), dma=True)


def emit_pair_load(ctx, sc, dram_fm, hi, lo, nch, tok=TOK):
    xr = dram_fm.rearrange("(c p) t -> c p t", p=128)
    for c in range(nch):
        ctx.s("sync", lambda e, c=c: e.dma_start(out=sc.x32[:], in_=xr[c]), dma=True)
        ctx.s("vector", lambda e, c=c: e.tensor_copy(hi[:, c * tok:(c + 1) * tok], sc.x32[:]))
        ctx.s("vector", lambda e, c=c: e.tensor_tensor(
            lo[:, c * tok:(c + 1) * tok], sc.x32[:], hi[:, c * tok:(c + 1) * tok], ALU.subtract))
        ctx.fence()


def emit_pair_store(ctx, hi, lo, dram_hi, dram_lo, nch, tok=TOK):
    hr = dram_hi.rearrange("(c p) t -> c p t", p=128)
    lr = dram_lo.rearrange("(c p) t -> c p t", p=128)
    for c in range(nch):
        ctx.s("gpsimd", lambda e, c=c: e.dma_start(out=hr[c], in_=hi[:, c * tok:(c + 1) * tok]), dma=True)
        ctx.s("gpsimd", lambda e, c=c: e.dma_start(out=lr[c], in_=lo[:, c * tok:(c + 1) * tok]), dma=True)



def emit_colmax_to_row(ctx, k, m_ap, row_ap, ps, tmp_col, n, clip_min=None):
    for tt in range(ceil_div(n, 128)):
        w = min(128, n - tt * 128)
        ctx.s("tensor", lambda e, tt=tt, w=w: e.transpose(
            ps[:, 0:128], m_ap[:, tt * 128:tt * 128 + w], k.ident_f32))
        ctx.s("vector", lambda e, w=w: e.tensor_reduce(
            tmp_col[0:w, 0:1], ps[0:w, 0:128], axis=AX.X, op=ALU.max))
        ctx.fence()
        ctx.s("tensor", lambda e: e.transpose(ps[0:1, 128:256], tmp_col[:, 0:1], k.ident_f32))
        ctx.s("vector", lambda e, tt=tt, w=w: e.tensor_copy(
            row_ap[0:1, tt * 128:tt * 128 + w], ps[0:1, 128:128 + w]))
        ctx.fence()
    if clip_min is not None:
        ctx.s("vector", lambda e: e.tensor_scalar(
            row_ap[0:1, 0:n], row_ap[0:1, 0:n], clip_min, None, ALU.max))
        ctx.fence()


def emit_rne_clip_bf16(ctx, sc, src32_ap, dst_bf_ap, lo=-128.0, hi=127.0):
    n = src32_ap.shape[-1]
    ctx.s("vector", lambda e: e.tensor_copy(sc.i32[:, 0:n], src32_ap))
    ctx.s("vector", lambda e: e.tensor_copy(sc.z32[:, 0:n], sc.i32[:, 0:n]))
    ctx.s("vector", lambda e: e.tensor_scalar(dst_bf_ap, sc.z32[:, 0:n], hi, lo, ALU.min, ALU.max))



R_AMAX, R_SCL, R_TMP, R_TMP2, R_AMAX2 = 0, 1, 2, 3, 4
R_CELLS, R_KQ, R_KZQ = 5, 6, 7


def rowslot(sc, i, o=0, n=1024):
    return sc.rows[0:1, i * 1024 + o:i * 1024 + o + n]


def emit_w_mean_cells(ctx, k, sc, weights, cells, cellcols, wcol32, psE):
    wstat = sc.bc[:, 0:8]
    for wi, (w_d, rows_, cols_) in enumerate(weights):
        wr = w_d.rearrange("(s p) o -> s p o", p=128)
        first = True
        for s_ in range(rows_):
            for co in range(0, cols_, 2048):
                cw = min(2048, cols_ - co)
                ctx.s("sync", lambda e, s_=s_, co=co, cw=cw, wr=wr: e.dma_start(
                    out=wcol32[:, 0:cw], in_=wr[s_][:, co:co + cw]), dma=True)
                ctx.s("vector", lambda e, cw=cw: e.tensor_reduce(
                    sc.z32[:, 0:1], wcol32[:, 0:cw], axis=AX.X, op=ALU.add,
                    apply_absolute_value=True))
                if first:
                    ctx.s("vector", lambda e, wi=wi: e.tensor_copy(wstat[:, wi:wi + 1], sc.z32[:, 0:1]))
                    first = False
                else:
                    ctx.s("vector", lambda e, wi=wi: e.tensor_tensor(
                        wstat[:, wi:wi + 1], wstat[:, wi:wi + 1], sc.z32[:, 0:1], ALU.add))
                ctx.fence()
        ctx.s("tensor", lambda e, wi=wi: e.transpose(psE[0:1, 0:128], wstat[:, wi:wi + 1], k.ident_f32))
        ctx.s("vector", lambda e, wi=wi: e.tensor_reduce(
            cells[0:1, wi:wi + 1], psE[0:1, 0:128], axis=AX.X, op=ALU.add))
        ctx.fence()
    for wi in range(3):
        ctx.s("vector", lambda e, wi=wi: e.tensor_scalar(
            cells[0:1, wi:wi + 1], cells[0:1, wi:wi + 1], 1.0 / (2048.0 * 8192.0), 1e-5,
            ALU.mult, ALU.max))
    ctx.s("vector", lambda e: e.reciprocal(cells[0:1, 3:6], cells[0:1, 0:3]))
    ctx.fence()
    for wi in range(6):
        ctx.s("tensor", lambda e, wi=wi: e.matmul(
            psE[:, 0:1], k.ones_row_f32, cells[0:1, wi:wi + 1], start=True, stop=True))
        ctx.s("vector", lambda e, wi=wi: e.tensor_copy(cellcols[:, wi:wi + 1], psE[:, 0:1]))
        ctx.fence()


def emit_actquant(ctx, k, sc, pair_hi, pair_lo, xq, nch, psD, psE):
    t = TOK
    m = sc.bc[:, t:2 * t]
    for c in range(nch):
        ctx.s("vector", lambda e, c=c: e.tensor_tensor(
            sc.x32[:], pair_hi[:, c * t:(c + 1) * t], pair_lo[:, c * t:(c + 1) * t], ALU.add))
        ctx.s("scalar", lambda e: e.activation(sc.y32[:], sc.x32[:], AF.Abs))
        if c == 0:
            ctx.s("vector", lambda e: e.tensor_copy(m[:], sc.y32[:]))
        else:
            ctx.s("vector", lambda e: e.tensor_tensor(m[:], m[:], sc.y32[:], ALU.max))
        ctx.fence()
    emit_colmax_to_row(ctx, k, m, rowslot(sc, R_AMAX), psE, sc.z32[:, 0:1], t, clip_min=1e-5)
    ctx.s("vector", lambda e: e.reciprocal(rowslot(sc, R_SCL), rowslot(sc, R_AMAX)))
    ctx.s("vector", lambda e: e.tensor_scalar(
        rowslot(sc, R_SCL), rowslot(sc, R_SCL), 128.0, None, ALU.mult))
    ctx.s("vector", lambda e: e.tensor_scalar(
        rowslot(sc, R_KQ), rowslot(sc, R_AMAX), 1.0 / 128.0, None, ALU.mult))
    ctx.fence()
    bcast_row(ctx, k, rowslot(sc, R_SCL), sc.bc[:, 0:t], psD, t)
    ctx.fence()
    for c in range(nch):
        ctx.s("vector", lambda e, c=c: e.tensor_tensor(
            sc.x32[:], pair_hi[:, c * t:(c + 1) * t], pair_lo[:, c * t:(c + 1) * t], ALU.add))
        ctx.s("vector", lambda e: e.tensor_tensor(sc.x32[:], sc.x32[:], sc.bc[:, 0:t], ALU.mult))
        emit_rne_clip_bf16(ctx, sc, sc.x32[:], xq[:, c * t:(c + 1) * t])
        ctx.fence()


def emit_tern(ctx, sc, w32ap, dst, sw_cell, n):
    ctx.s("vector", lambda e: e.tensor_scalar(
        sc.bfA[:, 0:n], w32ap, sw_cell, 0.5, ALU.mult, ALU.is_gt))
    ctx.s("vector", lambda e: e.tensor_scalar(
        sc.bfB[:, 0:n], w32ap, sw_cell, -0.5, ALU.mult, ALU.is_lt))
    ctx.s("vector", lambda e: e.tensor_tensor(dst, sc.bfA[:, 0:n], sc.bfB[:, 0:n], ALU.subtract))


def emit_bitnet_gateup(ctx, k, sc, xq, wg_d, wu_d, ispill_d, cells, cellcols,
                       wcol32, wcolh, tern2, psA, psB, psD):
    t = TOK
    nc2 = NCH * 128
    ctx.s("vector", lambda e: e.tensor_scalar(
        rowslot(sc, R_TMP), rowslot(sc, R_KQ), cells[0:1, 0:1], None, ALU.mult))
    ctx.s("vector", lambda e: e.tensor_scalar(
        rowslot(sc, R_TMP2), rowslot(sc, R_KQ), cells[0:1, 1:2], None, ALU.mult))
    ctx.fence()
    bcast_row(ctx, k, rowslot(sc, R_TMP), sc.bc[:, 0:t], psD, t)
    bcast_row(ctx, k, rowslot(sc, R_TMP2), sc.bc[:, t:2 * t], psD, t)
    ctx.fence()
    wgr = wg_d.rearrange("(kc p) o -> kc p o", p=128)
    wur = wu_d.rearrange("(kc p) o -> kc p o", p=128)
    for oc in range(NFF):
        for (wr_, dst, swi) in ((wgr, wcolh, 3), (wur, tern2, 4)):
            src = wr_[:, :, oc * 128:(oc + 1) * 128].transpose([1, 0, 2])
            ctx.s("sync", lambda e, src=src: e.dma_start(
                out=wcol32[:, 0:nc2].rearrange("p (kc f) -> p kc f", f=128), in_=src), dma=True)
            emit_tern(ctx, sc, wcol32[:, 0:nc2], dst[:, 0:nc2], cellcols[:, swi:swi + 1], nc2)
            ctx.fence()
        for kc in range(NCH):
            for off, w in halves(t):
                st, sp = kc == 0, kc == NCH - 1
                ctx.s("tensor", lambda e, kc=kc, off=off, w=w, st=st, sp=sp: e.matmul(
                    psA[:, off:off + w], wcolh[:, kc * 128:(kc + 1) * 128],
                    xq[:, kc * t + off:kc * t + off + w], start=st, stop=sp))
                ctx.s("tensor", lambda e, kc=kc, off=off, w=w, st=st, sp=sp: e.matmul(
                    psB[:, off:off + w], tern2[:, kc * 128:(kc + 1) * 128],
                    xq[:, kc * t + off:kc * t + off + w], start=st, stop=sp))
        ctx.fence()
        ctx.s("vector", lambda e: e.tensor_tensor(sc.x32[:], psA[:, 0:t], sc.bc[:, 0:t], ALU.mult))
        ctx.s("scalar", lambda e: e.activation(sc.x32[:], sc.x32[:], AF.Silu))
        ctx.s("vector", lambda e: e.tensor_tensor(sc.y32[:], psB[:, 0:t], sc.bc[:, t:2 * t], ALU.mult))
        ctx.s("vector", lambda e: e.tensor_tensor(sc.x32[:], sc.x32[:], sc.y32[:], ALU.mult))
        ctx.s("vector", lambda e: e.tensor_copy(sc.bfA[:, 0:t], sc.x32[:]))
        ctx.s("vector", lambda e: e.tensor_tensor(sc.bfB[:, 0:t], sc.x32[:], sc.bfA[:, 0:t], ALU.subtract))
        ctx.s("gpsimd", lambda e, oc=oc: e.dma_start(out=ispill_d[oc, 0], in_=sc.bfA[:, 0:t]), dma=True)
        ctx.s("gpsimd", lambda e, oc=oc: e.dma_start(out=ispill_d[oc, 1], in_=sc.bfB[:, 0:t]), dma=True)
        ctx.fence()


def emit_bitnet_stageb(ctx, k, sc, ispill_d, zqspill_d, zbuf, ihbuf, ilbuf,
                       pbuf, ztile, psD, psE):
    TT = 128
    for tile in range(8):
        t0 = tile * TT
        for hl, buf in ((0, ihbuf), (1, ilbuf)):
            for g8 in range(0, NFF, 8):
                src = ispill_d[g8:g8 + 8, hl, :, t0:t0 + TT].transpose([1, 0, 2])
                ctx.s("sync", lambda e, src=src, g8=g8, buf=buf: e.dma_start(
                    out=buf[:, g8 * TT:(g8 + 8) * TT].rearrange("p (a f) -> p a f", f=TT),
                    in_=src), dma=True)
        ctx.fence()
        for a in range(NFF):
            ctx.s("tensor", lambda e, a=a: e.matmul(
                psD[:, 0:TT], k.h128_bf[:], ihbuf[:, a * TT:(a + 1) * TT], start=True, stop=False))
            ctx.s("tensor", lambda e, a=a: e.matmul(
                psD[:, 0:TT], k.h128_bf[:], ilbuf[:, a * TT:(a + 1) * TT], start=False, stop=True))
            ctx.s("vector", lambda e, a=a: e.tensor_copy(zbuf[:, a * TT:(a + 1) * TT], psD[:, 0:TT]))
            ctx.fence()
        TB = 32
        for sp_ in range(3):
            h1, h2 = 1 << (2 * sp_), 1 << (2 * sp_ + 1)
            for tb in range(0, TT, TB):
                for (hh, src_t, dst_t) in ((h1, "z", "p"), (h2, "p", "z")):
                    def mk(which, a0, hh=hh, tb=tb):
                        tens, width, base = (zbuf, TT, tb) if which == "z" else (pbuf, TB, 0)
                        v = tens[:, :].rearrange("p (a f) -> p a f", f=width)
                        v = v[:, :, base:base + TB]
                        v = v.rearrange("p (g two j) f -> p g two j f", two=2, j=hh)
                        return v[:, :, a0]
                    in0, in1 = mk(src_t, 0), mk(src_t, 1)
                    out0, out1 = mk(dst_t, 0), mk(dst_t, 1)
                    ctx.s("vector", lambda e, out0=out0, in0=in0, in1=in1: e.tensor_tensor(
                        out0, in0, in1, ALU.add))
                    ctx.s("vector", lambda e, out1=out1, in0=in0, in1=in1: e.tensor_tensor(
                        out1, in0, in1, ALU.subtract))
                    ctx.fence()
        zv = zbuf[:, :].rearrange("p (a f) -> p f a", f=TT)
        ctx.s("vector", lambda e, zv=zv: e.tensor_reduce(
            sc.y32[:, 0:TT], zv, axis=AX.X, op=ALU.max, apply_absolute_value=True))
        ctx.fence()
        arow = rowslot(sc, R_AMAX2, t0, TT)
        emit_colmax_to_row(ctx, k, sc.y32[:, 0:TT], arow, psE, sc.z32[:, 0:1], TT)
        srow = rowslot(sc, R_TMP, t0, TT)
        kzq = rowslot(sc, R_KZQ, t0, TT)
        ctx.s("vector", lambda e, arow=arow: e.tensor_scalar(arow, arow, FWHT_NORM, 1e-5, ALU.mult, ALU.max))
        ctx.s("vector", lambda e, srow=srow, arow=arow: e.reciprocal(srow, arow))
        ctx.s("vector", lambda e, srow=srow: e.tensor_scalar(srow, srow, 128.0 * FWHT_NORM, None, ALU.mult))
        ctx.s("vector", lambda e, kzq=kzq, arow=arow: e.tensor_scalar(kzq, arow, 1.0 / 128.0, None, ALU.mult))
        ctx.fence()
        bcast_row(ctx, k, srow, sc.bc[:, 0:TT], psE, TT)
        ctx.fence()
        for a in range(NFF):
            ctx.s("vector", lambda e, a=a: e.tensor_tensor(
                sc.x32[:, 0:TT], zbuf[:, a * TT:(a + 1) * TT], sc.bc[:, 0:TT], ALU.mult))
            emit_rne_clip_bf16(ctx, sc, sc.x32[:, 0:TT], ztile[:, a * TT:(a + 1) * TT])
            ctx.fence()
        src = ztile[:, :].rearrange("p (a f) -> p a f", f=TT)
        ctx.s("gpsimd", lambda e, t0=t0, src=src: e.dma_start(
            out=zqspill_d[:, :, t0:t0 + TT].transpose([1, 0, 2]), in_=src), dma=True)
        ctx.fence()


def emit_bitnet_down(ctx, k, sc, zqspill_d, wd_d, f_d, cells, cellcols, zqh,
                     wcol32, wcolh, psC, psD):
    nc2 = NCH * 128
    wdr = wd_d.rearrange("(kc p) o -> kc p o", p=128)
    fr = f_d.rearrange("(c p) t -> c p t", p=128)
    for th in range(2):
        th0 = th * 512
        for g8 in range(0, NFF, 8):
            src = zqspill_d[g8:g8 + 8, :, th0:th0 + 512].transpose([1, 0, 2])
            ctx.s("sync", lambda e, src=src, g8=g8: e.dma_start(
                out=zqh[:, g8 * 512:(g8 + 8) * 512].rearrange("p (a f) -> p a f", f=512),
                in_=src), dma=True)
        kd = rowslot(sc, R_TMP, 0, 512)
        ctx.s("vector", lambda e, kd=kd, th0=th0: e.tensor_scalar(
            kd, rowslot(sc, R_KZQ, th0, 512), cells[0:1, 2:3], None, ALU.mult))
        ctx.fence()
        bcast_row(ctx, k, kd, sc.bc[:, 0:512], psD, 512)
        ctx.fence()
        for oc in range(NCH):
            for q4 in range(4):
                src = wdr[q4 * NCH:(q4 + 1) * NCH, :, oc * 128:(oc + 1) * 128].transpose([1, 0, 2])
                ctx.s("sync", lambda e, src=src: e.dma_start(
                    out=wcol32[:, 0:nc2].rearrange("p (kc f) -> p kc f", f=128), in_=src), dma=True)
                emit_tern(ctx, sc, wcol32[:, 0:nc2], wcolh[:, 0:nc2], cellcols[:, 5:6], nc2)
                ctx.fence()
                for kc in range(NCH):
                    a = q4 * NCH + kc
                    st = (q4 == 0 and kc == 0)
                    sp = (q4 == 3 and kc == NCH - 1)
                    ctx.s("tensor", lambda e, a=a, kc=kc, st=st, sp=sp: e.matmul(
                        psC[:, 0:512], wcolh[:, kc * 128:(kc + 1) * 128],
                        zqh[:, a * 512:(a + 1) * 512], start=st, stop=sp))
                ctx.fence()
            ctx.s("vector", lambda e: e.tensor_tensor(sc.x32[:, 0:512], psC[:, 0:512], sc.bc[:, 0:512], ALU.mult))
            ctx.s("gpsimd", lambda e, oc=oc, th0=th0: e.dma_start(
                out=fr[oc][:, th0:th0 + 512], in_=sc.x32[:, 0:512]), dma=True)
            ctx.fence()



def emit_t0(ctx, dr, k, sc, psA, psB, psC, psD, psE, vecs, cellcols,
            wcol32, wcolh, wcoll, vtm=False):
    t = TOK
    x_d = dr["x"]; wq_d = dr["wq"]; wk_d = dr["wk"]; wv_d = dr["wv"]
    wg_d = dr["wg_t"]; wu_d = dr["wu_t"]; wd_d = dr["wd_t"]
    xnh_d = dr["xnh"]; xnl_d = dr["xnl"]
    qkv_outs = {nm: dr[nm] for nm in ("qh", "ql", "kh", "kl", "vh", "vl") if nm in dr}
    f_d = dr["fout"]; ispill_d = dr["ispill"]; zqspill_d = dr["zqspill"]

    cells = rowslot(sc, R_CELLS, 0, 8)
    emit_w_mean_cells(ctx, k, sc, ((wg_d, NCH, DFF), (wu_d, NCH, DFF), (wd_d, NFF, D)),
                      cells, cellcols, wcol32, psE)
    xq = ctx.sbuf("xq", [128, NCH * t], BF16)
    xh = ctx.sbuf("xh", [128, NCH * t], BF16)
    xl = ctx.sbuf("xl", [128, NCH * t], BF16)
    emit_pair_load(ctx, sc, x_d, xh, xl, NCH)
    emit_ln(ctx, k, sc, xh, xl, xh, xl, psA, psD)
    emit_pair_store(ctx, xh, xl, xnh_d, xnl_d, NCH)
    ctx.fence()

    emit_actquant(ctx, k, sc, xh, xl, xq, NCH, psD, psE)

    g1 = vecs[:, 3 * NCH:4 * NCH]
    be1 = vecs[:, 4 * NCH:5 * NCH]
    emit_ln(ctx, k, sc, xh, xl, xh, xl, psA, psD, gamma=g1, beta=be1)
    ctx.fence()

    wbufs = {"w32": wcol32, "wh": wcolh, "wl": wcoll, "ps": psA}

    def qkv_consumer(nm_h, nm_l, bias_idx, scale):
        hr = qkv_outs[nm_h].rearrange("(c p) t -> c p t", p=128)
        lr = qkv_outs[nm_l].rearrange("(c p) t -> c p t", p=128)

        def consumer(oc, ps):
            bcol = vecs[:, bias_idx * NCH + oc:bias_idx * NCH + oc + 1]
            ctx.s("vector", lambda e, bcol=bcol: e.tensor_scalar(
                sc.z32[:, 0:1], bcol, scale, None, ALU.mult))
            ctx.fence()
            ctx.s("scalar", lambda e: e.activation(
                sc.y32[:], ps[:, 0:t], AF.Identity, bias=sc.z32[:, 0:1], scale=scale))
            ctx.s("vector", lambda e: e.tensor_copy(sc.bfA[:, 0:t], sc.y32[:]))
            ctx.s("vector", lambda e: e.tensor_tensor(sc.bfB[:, 0:t], sc.y32[:], sc.bfA[:, 0:t], ALU.subtract))
            ctx.s("gpsimd", lambda e, oc=oc, hr=hr: e.dma_start(out=hr[oc], in_=sc.bfA[:, 0:t]), dma=True)
            ctx.s("gpsimd", lambda e, oc=oc, lr=lr: e.dma_start(out=lr[oc], in_=sc.bfB[:, 0:t]), dma=True)
        return consumer

    emit_proj(ctx, k, wq_d, xh, xl, NCH, NCH, qkv_consumer("qh", "ql", 0, RSQ_HD), wbufs=wbufs)
    emit_proj(ctx, k, wk_d, xh, xl, NCH, NCH, qkv_consumer("kh", "kl", 1, 1.0), wbufs=wbufs)
    if not vtm:
        emit_proj(ctx, k, wv_d, xh, xl, NCH, NCH, qkv_consumer("vh", "vl", 2, 1.0), wbufs=wbufs)
    else:
        bvrow = sc.bc[:, 0:D]
        for cch in range(NCH):
            ctx.s("tensor", lambda e, cch=cch: e.transpose(
                psD[0:1, 0:128], vecs[:, 2 * NCH + cch:2 * NCH + cch + 1], k.ident_f32))
            ctx.s("vector", lambda e, cch=cch: e.tensor_copy(
                sc.rows[0:1, cch * 128:(cch + 1) * 128], psD[0:1, 0:128]))
            ctx.fence()
        bcast_row(ctx, k, sc.rows[0:1, 0:512], bvrow[:, 0:512], psE, 512)
        bcast_row(ctx, k, sc.rows[0:1, 512:1024], bvrow[:, 512:1024], psE, 512)
        bcast_row(ctx, k, sc.rows[0:1, 1024:1536], bvrow[:, 1024:1536], psE, 512)
        bcast_row(ctx, k, sc.rows[0:1, 1536:2048], bvrow[:, 1536:2048], psE, 512)
        ctx.fence()
        emit_vproj_tm(ctx, k, sc, wv_d, 0, xh, xl, NCH, NCH,
                      dr["vth_loc"], dr["vtl_loc"],
                      [psA, psB, psC, psD], wcol32, wcolh, wcoll, bias_col=bvrow)
    ctx.fence()
    ctx.sbuf_free("xl", "xh")

    tern2 = ctx.sbuf("tern2", [128, NCH * 128], BF16)
    emit_bitnet_gateup(ctx, k, sc, xq, wg_d, wu_d, ispill_d, cells, cellcols,
                       wcol32, wcolh, tern2, psA, psB, psD)
    ctx.sbuf_free("tern2", "xq")
    zbuf = ctx.sbuf("zbuf", [128, NFF * 128], F32)
    ihbuf = ctx.sbuf("ihbuf", [128, NFF * 128], BF16)
    ilbuf = ctx.sbuf("ilbuf", [128, NFF * 128], BF16)
    pbuf = ctx.sbuf("pbuf", [128, NFF * 32], F32)
    ztile = ctx.sbuf("ztile", [128, NFF * 128], BF16)
    emit_bitnet_stageb(ctx, k, sc, ispill_d, zqspill_d, zbuf, ihbuf, ilbuf,
                       pbuf, ztile, psD, psE)
    ctx.sbuf_free("ztile", "pbuf", "ilbuf", "ihbuf", "zbuf")
    zqh = ctx.sbuf("zqh", [128, NFF * 512], BF16)
    emit_bitnet_down(ctx, k, sc, zqspill_d, wd_d, f_d, cells, cellcols, zqh,
                     wcol32, wcolh, psC, psD)
    ctx.sbuf_free("zqh")



def build_t0(dbg=False):
    ctx = Ctx()
    ctx.init_serial()
    t = TOK
    dr = {}
    dr["x"] = ctx.dram_in("x", [D, t])
    c_d = ctx.dram_in("consts", [128, 400])
    dr["wq"] = ctx.dram_in("wq", [D, D])
    dr["wk"] = ctx.dram_in("wk", [D, D])
    dr["wv"] = ctx.dram_in("wv", [D, D])
    vec_d = ctx.dram_in("vecs", [5 * D])
    dr["wg_t"] = ctx.dram_in("wg_t", [D, DFF])
    dr["wu_t"] = ctx.dram_in("wu_t", [D, DFF])
    dr["wd_t"] = ctx.dram_in("wd_t", [DFF, D])
    dr["xnh"] = ctx.dram_out("xnh", [D, t], BF16)
    dr["xnl"] = ctx.dram_out("xnl", [D, t], BF16)
    for nm in ("qh", "ql", "kh", "kl", "vh", "vl"):
        dr[nm] = ctx.dram_out(nm, [D, t], BF16)
    dr["fout"] = ctx.dram_out("fout", [D, t])
    dr["ispill"] = ctx.dram_tmp("ispill", [NFF, 2, 128, t], BF16)
    dr["zqspill"] = ctx.dram_tmp("zqspill", [NFF, 128, t], BF16)
    k = Consts(ctx)
    sc = Scratch(ctx)
    psA = ctx.psum("psA", [128, 1024], F32)
    psB = ctx.psum("psB", [128, 1024], F32)
    psC = ctx.psum("psC", [128, 1024], F32)
    psD = ctx.psum("psD", [128, 512], F32)
    psE = ctx.psum("psE", [128, 512], F32)
    vecs = ctx.sbuf("vecs_sb", [128, 5 * NCH], F32)
    cellcols = ctx.sbuf("cellcols", [128, 8], F32)
    wcol32 = ctx.sbuf("wcol32", [128, NCH * 128], F32)
    wcolh = ctx.sbuf("wcolh", [128, NCH * 128], BF16)
    wcoll = ctx.sbuf("wcoll", [128, NCH * 128], BF16)
    k.load(ctx, c_d[:])
    emit_load_vec(ctx, vec_d, vecs, 5 * NCH)
    ctx.fence()
    emit_t0(ctx, dr, k, sc, psA, psB, psC, psD, psE, vecs, cellcols,
            wcol32, wcolh, wcoll)
    ctx.emit()
    return ctx


QT = 256
NQT = S // QT


def emit_attn(ctx, dr, k, sc, psS, psAV, psSum, psE, masks):
    qh_d, ql_d, kh_d, kl_d = dr["qh"], dr["ql"], dr["kh"], dr["kl"]
    vth_d, vtl_d, ah_d, al_d = dr["vth"], dr["vtl"], dr["ah"], dr["al"]
    u = ctx.uid()
    qhb = ctx.sbuf(f"at{u}_qhb", [128, S], BF16)
    qlb = ctx.sbuf(f"at{u}_qlb", [128, S], BF16)
    khb = ctx.sbuf(f"at{u}_khb", [128, S], BF16)
    klb = ctx.sbuf(f"at{u}_klb", [128, S], BF16)
    vhb = ctx.sbuf(f"at{u}_vhb", [128, S], BF16)
    vlb = ctx.sbuf(f"at{u}_vlb", [128, S], BF16)
    strip = ctx.sbuf(f"at{u}_strip", [128, 2 * S], F32)
    ehb = ctx.sbuf(f"at{u}_ehb", [128, 2 * S], BF16)
    elb = ctx.sbuf(f"at{u}_elb", [128, 2 * S], BF16)
    for i in range(4):
        for (dr, sb) in ((qh_d, qhb), (ql_d, qlb), (kh_d, khb), (kl_d, klb)):
            ctx.s("sync", lambda e, dr=dr, sb=sb, i=i: e.dma_start(out=sb[:], in_=dr[i]), dma=True)
        for (dr, sb) in ((vth_d, vhb), (vtl_d, vlb)):
            src = dr[i].rearrange("(kt p) d -> p kt d", p=128)
            ctx.s("sync", lambda e, src=src, sb=sb: e.dma_start(
                out=sb[:].rearrange("p (kt d) -> p kt d", d=128), in_=src), dma=True)
        ctx.fence()
        for qt in range(NQT):
            nkt = 2 * qt + 2
            q0 = qt * QT
            for kt in range(nkt):
                ctx.s("tensor", lambda e, kt=kt, q0=q0: e.matmul(
                    psS[:, 0:QT], khb[:, kt * 128:(kt + 1) * 128], qhb[:, q0:q0 + QT],
                    start=True, stop=False))
                ctx.s("tensor", lambda e, kt=kt, q0=q0: e.matmul(
                    psS[:, 0:QT], khb[:, kt * 128:(kt + 1) * 128], qlb[:, q0:q0 + QT],
                    start=False, stop=False))
                ctx.s("tensor", lambda e, kt=kt, q0=q0: e.matmul(
                    psS[:, 0:QT], klb[:, kt * 128:(kt + 1) * 128], qhb[:, q0:q0 + QT],
                    start=False, stop=True))
                if kt >= nkt - 2:
                    pat = 0 if kt == nkt - 2 else 2
                    ctx.s("vector", lambda e, kt=kt, pat=pat: e.tensor_tensor(
                        sc.x32[:, 0:QT], psS[:, 0:QT], masks[:, pat * QT:(pat + 1) * QT], ALU.mult))
                    ctx.s("vector", lambda e, kt=kt, pat=pat: e.tensor_tensor(
                        strip[:, kt * QT:(kt + 1) * QT], sc.x32[:, 0:QT],
                        masks[:, (pat + 1) * QT:(pat + 2) * QT], ALU.add))
                else:
                    ctx.s("scalar", lambda e, kt=kt: e.activation(
                        strip[:, kt * QT:(kt + 1) * QT], psS[:, 0:QT], AF.Identity))
                ctx.fence()
            ctx.s("vector", lambda e, nkt=nkt: e.tensor_reduce(
                sc.z32[:, 0:1], strip[:, 0:nkt * QT], axis=AX.X, op=ALU.max))
            ctx.fence()
            ctx.s("tensor", lambda e: e.transpose(psE[0:1, 0:128], sc.z32[:, 0:1], k.ident_f32))
            ctx.s("vector", lambda e: e.tensor_reduce(
                sc.rows[0:1, 0:1], psE[0:1, 0:128], axis=AX.X, op=ALU.max))
            ctx.s("vector", lambda e: e.tensor_scalar(
                sc.rows[0:1, 0:1], sc.rows[0:1, 0:1], -1.0, None, ALU.mult))
            ctx.fence()
            ctx.s("tensor", lambda e: e.matmul(
                psE[:, 0:1], k.ones_row_f32, sc.rows[0:1, 0:1], start=True, stop=True))
            ctx.s("vector", lambda e: e.tensor_copy(sc.z32[:, 0:1], psE[:, 0:1]))
            ctx.fence()
            for kt in range(nkt):
                sl = slice(kt * QT, (kt + 1) * QT)
                ctx.s("scalar", lambda e, sl=sl: e.activation(
                    sc.x32[:, 0:QT], strip[:, sl], AF.Exp, bias=sc.z32[:, 0:1], scale=1.0))
                ctx.s("vector", lambda e, sl=sl: e.tensor_copy(ehb[:, sl], sc.x32[:, 0:QT]))
                ctx.s("vector", lambda e, sl=sl: e.tensor_tensor(
                    elb[:, sl], sc.x32[:, 0:QT], ehb[:, sl], ALU.subtract))
                ctx.fence()
                st, sp = (kt == 0), (kt == nkt - 1)
                ctx.s("tensor", lambda e, kt=kt, sl=sl, st=st: e.matmul(
                    psAV[:, 0:QT], vhb[:, kt * 128:(kt + 1) * 128], ehb[:, sl], start=st, stop=False))
                ctx.s("tensor", lambda e, kt=kt, sl=sl: e.matmul(
                    psAV[:, 0:QT], vhb[:, kt * 128:(kt + 1) * 128], elb[:, sl], start=False, stop=False))
                ctx.s("tensor", lambda e, kt=kt, sl=sl, sp=sp: e.matmul(
                    psAV[:, 0:QT], vlb[:, kt * 128:(kt + 1) * 128], ehb[:, sl], start=False, stop=sp))
                ctx.s("tensor", lambda e, sl=sl, st=st: e.matmul(
                    psSum[0:1, 0:QT], k.ones_col_bf[:], ehb[:, sl], start=st, stop=False))
                ctx.s("tensor", lambda e, sl=sl, sp=sp: e.matmul(
                    psSum[0:1, 0:QT], k.ones_col_bf[:], elb[:, sl], start=False, stop=sp))
                ctx.fence()
            ctx.s("vector", lambda e: e.reciprocal(sc.rows[0:1, 0:QT], psSum[0:1, 0:QT]))
            ctx.fence()
            ctx.s("tensor", lambda e: e.matmul(
                psE[:, 0:QT], k.ones_row_f32, sc.rows[0:1, 0:QT], start=True, stop=True))
            ctx.s("vector", lambda e: e.tensor_copy(sc.bc[:, 0:QT], psE[:, 0:QT]))
            ctx.fence()
            ctx.s("vector", lambda e: e.tensor_tensor(
                sc.x32[:, 0:QT], psAV[:, 0:QT], sc.bc[:, 0:QT], ALU.mult))
            ctx.s("vector", lambda e: e.tensor_copy(sc.bfA[:, 0:QT], sc.x32[:, 0:QT]))
            ctx.s("vector", lambda e: e.tensor_tensor(
                sc.bfB[:, 0:QT], sc.x32[:, 0:QT], sc.bfA[:, 0:QT], ALU.subtract))
            ctx.s("gpsimd", lambda e, i=i, q0=q0: e.dma_start(
                out=ah_d[i][:, q0:q0 + QT], in_=sc.bfA[:, 0:QT]), dma=True)
            ctx.s("gpsimd", lambda e, i=i, q0=q0: e.dma_start(
                out=al_d[i][:, q0:q0 + QT], in_=sc.bfB[:, 0:QT]), dma=True)
            ctx.fence()
    for nm in ("elb", "ehb", "strip", "vlb", "vhb", "klb", "khb", "qlb", "qhb"):
        ctx.sbuf_free(f"at{u}_{nm}")


def build_attn():
    ctx = Ctx()
    ctx.init_serial()
    c_d = ctx.dram_in("consts", [128, 400])
    mask_d = ctx.dram_in("masks", [4, 128, QT])
    dr = {}
    for nm in ("qh", "ql", "kh", "kl"):
        dr[nm] = ctx.dram_in(nm, [4, 128, S], BF16)
    for nm in ("vth", "vtl"):
        dr[nm] = ctx.dram_in(nm, [4, S, 128], BF16)
    dr["ah"] = ctx.dram_out("ah", [4, 128, S], BF16)
    dr["al"] = ctx.dram_out("al", [4, 128, S], BF16)
    k = Consts(ctx)
    sc = Scratch(ctx)
    psS = ctx.psum("psS", [128, 512], F32)
    psAV = ctx.psum("psAV", [128, 512], F32)
    psSum = ctx.psum("psSum", [128, 512], F32)
    psE = ctx.psum("psE", [128, 512], F32)
    masks = ctx.sbuf("masks_sb", [128, 4 * QT], F32)
    k.load(ctx, c_d[:])
    ctx.s("sync", lambda e: e.dma_start(
        out=masks[:].rearrange("p (i f) -> p i f", f=QT), in_=mask_d.transpose([1, 0, 2])), dma=True)
    ctx.fence()
    emit_attn(ctx, dr, k, sc, psS, psAV, psSum, psE, masks)
    ctx.emit()
    return ctx


def attn_masks_host(maskval):
    out = np.zeros((4, 128, QT), dtype=np.float32)
    kloc = np.arange(128)[:, None]
    qloc = np.arange(QT)[None, :]
    mA = (kloc > qloc)
    mB = (128 + kloc > qloc)
    out[0] = (~mA).astype(np.float32)
    out[1] = np.where(mA, np.float32(maskval), np.float32(0.0))
    out[2] = (~mB).astype(np.float32)
    out[3] = np.where(mB, np.float32(maskval), np.float32(0.0))
    return out


def emit_proj_ksweep(ctx, k, sc, w_dram, rhsh_d, rhsl_d, n_in_ch, n_out_ch,
                     consumer, psW, wbuf32, wbufh, wbufl, rbufh, rbufl, tok=TOK):
    wr = w_dram.rearrange("(kc p) o -> kc p o", p=128)
    u = ctx.uid()
    wb2 = ctx.sbuf(f"ks{u}_w32b", [128, 512], F32)
    rbh2 = ctx.sbuf(f"ks{u}_rbh", [128, tok], BF16)
    rbl2 = ctx.sbuf(f"ks{u}_rbl", [128, tok], BF16)
    s_ld = ctx.sem("pjld")
    ctx.fence()
    entry_val = ctx.sq.val
    frv = {}
    iters = [(s4, kc) for s4 in range(0, n_out_ch, 4) for kc in range(n_in_ch)]
    nit = len(iters)
    ld0 = s_ld.val

    def bufs(t):
        return (wbuf32, wb2)[t % 2], (rbufh, rbh2)[t % 2], (rbufl, rbl2)[t % 2]

    def dma_async(t):
        s4, kc = iters[t]
        w32, rh, rl = bufs(t)
        wait_v = entry_val if t < 2 else frv[t - 2]
        for o, i in ((w32[:, 0:512], wr[kc][:, s4 * 128:(s4 + 4) * 128]),
                     (rh[:, 0:tok], rhsh_d[kc]),
                     (rl[:, 0:tok], rhsl_d[kc])):
            s_ld.val += 16
            ctx.ops.append(["sync",
                            (lambda e, o=o, i=i: e.dma_start(out=o, in_=i)),
                            [(ctx.sq, wait_v)], [(s_ld, 16)], True])

    dma_async(0)
    if nit > 1:
        dma_async(1)
    for t, (s4, kc) in enumerate(iters):
        w32, rh, rl = bufs(t)
        r1 = ctx.s("vector", lambda e, w32=w32: e.tensor_copy(wbufh[:, 0:512], w32[:, 0:512]))
        r1[2].append((s_ld, ld0 + 48 * t + 16))
        ctx.s("vector", lambda e, w32=w32: e.tensor_tensor(
            wbufl[:, 0:512], w32[:, 0:512], wbufh[:, 0:512], ALU.subtract))
        ctx.fence()
        first_mm = None
        for j in range(4):
            if s4 + j >= n_out_ch:
                continue
            lh = wbufh[:, j * 128:(j + 1) * 128]
            ll = wbufl[:, j * 128:(j + 1) * 128]
            for off, w in halves(tok):
                st, sp = kc == 0, kc == n_in_ch - 1
                rm = ctx.s("tensor", lambda e, j=j, lh=lh, rh=rh, off=off, w=w, st=st: e.matmul(
                    psW[j][:, off:off + w], lh, rh[:, off:off + w], start=st, stop=False))
                if first_mm is None:
                    first_mm = rm
                ctx.s("tensor", lambda e, j=j, lh=lh, rl=rl, off=off, w=w: e.matmul(
                    psW[j][:, off:off + w], lh, rl[:, off:off + w], start=False, stop=False))
                ctx.s("tensor", lambda e, j=j, ll=ll, rh=rh, off=off, w=w, sp=sp: e.matmul(
                    psW[j][:, off:off + w], ll, rh[:, off:off + w], start=False, stop=sp))
        if first_mm is not None:
            first_mm[2].append((s_ld, ld0 + 48 * t + 48))
        ctx.fence()
        frv[t] = ctx.sq.val
        if t + 2 < nit:
            dma_async(t + 2)
        if kc == n_in_ch - 1:
            for j in range(4):
                if s4 + j >= n_out_ch:
                    continue
                consumer(s4 + j, psW[j])
                ctx.fence()
    ctx.sbuf_free(f"ks{u}_rbl", f"ks{u}_rbh", f"ks{u}_w32b")



LOG2E_INV = float(np.log(2.0))


def emit_fp8_quant(ctx, k, sc, pair_hi, pair_lo, out_bf, nch, inv_s, tok=TOK):
    t = tok
    for c in range(nch):
        ctx.s("vector", lambda e, c=c: e.tensor_tensor(
            sc.x32[:], pair_hi[:, c * t:(c + 1) * t], pair_lo[:, c * t:(c + 1) * t], ALU.add))
        ctx.s("scalar", lambda e: e.activation(sc.y32[:], sc.x32[:], AF.Abs, scale=inv_s))
        ctx.s("vector", lambda e: e.tensor_scalar(sc.y32[:], sc.y32[:], 448.0, None, ALU.min))
        ctx.s("scalar", lambda e: e.activation(sc.z32[:], sc.x32[:], AF.Sign))
        ctx.s("scalar", lambda e: e.activation(sc.x32[:], sc.y32[:], AF.Ln))
        ctx.s("vector", lambda e: e.tensor_scalar(
            sc.x32[:], sc.x32[:], 1.0 / LOG2E_INV, 63.5, ALU.mult, ALU.add))
        ctx.s("vector", lambda e: e.tensor_copy(sc.i32[:], sc.x32[:]))
        ctx.s("vector", lambda e: e.tensor_copy(sc.x32[:], sc.i32[:]))
        ctx.s("vector", lambda e: e.tensor_scalar(
            sc.x32[:], sc.x32[:], -64.0, -6.0, ALU.add, ALU.max))
        ctx.s("scalar", lambda e: e.activation(
            sc.bc[:, 0:t], sc.x32[:], AF.Exp, bias=float(np.float32(3.0 * LOG2E_INV)), scale=-LOG2E_INV))
        ctx.s("scalar", lambda e: e.activation(
            sc.x32[:], sc.x32[:], AF.Exp, bias=float(np.float32(-3.0 * LOG2E_INV)), scale=LOG2E_INV))
        ctx.s("vector", lambda e: e.tensor_tensor(sc.y32[:], sc.y32[:], sc.bc[:, 0:t], ALU.mult))
        ctx.s("vector", lambda e: e.tensor_copy(sc.i32[:], sc.y32[:]))
        ctx.s("vector", lambda e: e.tensor_copy(sc.y32[:], sc.i32[:]))
        ctx.s("vector", lambda e: e.tensor_tensor(sc.y32[:], sc.y32[:], sc.x32[:], ALU.mult))
        ctx.s("vector", lambda e, c=c: e.tensor_tensor(
            out_bf[:, c * t:(c + 1) * t], sc.y32[:], sc.z32[:], ALU.mult))
        ctx.fence()



def emit_t1(ctx, dr, k, sc, psW, vecs, wcol32, wcolh, wcoll, rbufh, rbufl,
            sclsb, isbf, kcell, vtm=False):
    t = TOK
    x_d = dr["x"]; xnh_d = dr["xnh"]; xnl_d = dr["xnl"]; f_d = dr["fout"]
    ath_d = dr["ath"]; atl_d = dr["atl"]; wo_d = dr["wo"]; w1_d = dr["w1"]
    w2_d = dr["w2"]; wqkv_d = dr["wqkv_t"]; w8_d = dr["w8_t"]
    ws_d = dr["wscale"]; is_d = dr["iscale"]
    accp_d = dr["accp"]; ffn2_d = dr["ffn2"]; ffn1_d = dr["ffn1"]
    q1 = {nm: dr[nm] for nm in ("q1h", "q1l", "k1h", "k1l", "v1h", "v1l")}
    hspill_d = dr["hspill"]; xasp_h = dr["xasph"]; xasp_l = dr["xaspl"]
    acc = ctx.sbuf("acc", [128, NCH * t], F32)
    xr = x_d.rearrange("(c p) t -> c p t", p=128)
    fr2 = f_d.rearrange("(c p) t -> c p t", p=128)
    for c in range(NCH):
        ctx.s("sync", lambda e, c=c: e.dma_start(out=sc.x32[:], in_=xr[c]), dma=True)
        ctx.s("sync", lambda e, c=c: e.dma_start(out=sc.y32[:], in_=fr2[c]), dma=True)
        ctx.s("vector", lambda e, c=c: e.tensor_tensor(
            acc[:, c * t:(c + 1) * t], sc.x32[:], sc.y32[:], ALU.add))
        ctx.fence()
    xnhr = xnh_d.rearrange("(c p) t -> c p t", p=128)
    xnlr = xnl_d.rearrange("(c p) t -> c p t", p=128)

    def oproj_consumer(oc, ps):
        bo = vecs[:, oc:oc + 1]
        ctx.s("scalar", lambda e, bo=bo: e.activation(
            sc.x32[:], ps[:, 0:t], AF.Identity, bias=bo, scale=1.0))
        ctx.s("sync", lambda e, oc=oc: e.dma_start(out=sc.bfA[:, 0:t], in_=xnhr[oc]), dma=True)
        ctx.s("sync", lambda e, oc=oc: e.dma_start(out=sc.bfB[:, 0:t], in_=xnlr[oc]), dma=True)
        ctx.s("vector", lambda e: e.tensor_tensor(sc.x32[:], sc.x32[:], sc.bfA[:, 0:t], ALU.add))
        ctx.s("vector", lambda e: e.tensor_tensor(sc.x32[:], sc.x32[:], sc.bfB[:, 0:t], ALU.add))
        ctx.s("vector", lambda e, oc=oc: e.tensor_tensor(
            acc[:, oc * t:(oc + 1) * t], acc[:, oc * t:(oc + 1) * t], sc.x32[:], ALU.add))
        ctx.s("vector", lambda e: e.tensor_copy(sc.bfA[:, 0:t], sc.x32[:]))
        ctx.s("vector", lambda e: e.tensor_tensor(sc.bfB[:, 0:t], sc.x32[:], sc.bfA[:, 0:t], ALU.subtract))
        ctx.s("gpsimd", lambda e, oc=oc: e.dma_start(out=xasp_h[oc], in_=sc.bfA[:, 0:t]), dma=True)
        ctx.s("gpsimd", lambda e, oc=oc: e.dma_start(out=xasp_l[oc], in_=sc.bfB[:, 0:t]), dma=True)

    emit_proj_ksweep(ctx, k, sc, wo_d, ath_d, atl_d, NCH, NCH, oproj_consumer,
                     psW, wcol32, wcolh, wcoll, rbufh, rbufl)
    ctx.fence()
    ar = accp_d.rearrange("(c p) t -> c p t", p=128)
    for c in range(NCH):
        ctx.s("gpsimd", lambda e, c=c: e.dma_start(out=ar[c], in_=acc[:, c * t:(c + 1) * t]), dma=True)
    ctx.fence()
    ctx.sbuf_free("acc")

    xah = ctx.sbuf("xah", [128, NCH * t], BF16)
    xal = ctx.sbuf("xal", [128, NCH * t], BF16)
    for c in range(NCH):
        ctx.s("sync", lambda e, c=c: e.dma_start(out=xah[:, c * t:(c + 1) * t], in_=xasp_h[c]), dma=True)
        ctx.s("sync", lambda e, c=c: e.dma_start(out=xal[:, c * t:(c + 1) * t], in_=xasp_l[c]), dma=True)
    ctx.fence()
    g2 = vecs[:, 2 * NCH:3 * NCH]
    be2 = vecs[:, 3 * NCH:4 * NCH]
    emit_ln(ctx, k, sc, xah, xal, xah, xal, psW[0], psW[1], gamma=g2, beta=be2)
    ctx.fence()
    wbufs = {"w32": wcol32, "wh": wcolh, "wl": wcoll, "ps": psW[0]}

    def w1_consumer(oc, ps):
        b1 = vecs[:, 4 * NCH + oc:4 * NCH + oc + 1]
        ctx.s("scalar", lambda e, b1=b1: e.activation(
            sc.x32[:], ps[:, 0:t], AF.Gelu, bias=b1, scale=1.0))
        ctx.s("vector", lambda e: e.tensor_copy(sc.bfA[:, 0:t], sc.x32[:]))
        ctx.s("vector", lambda e: e.tensor_tensor(sc.bfB[:, 0:t], sc.x32[:], sc.bfA[:, 0:t], ALU.subtract))
        ctx.s("gpsimd", lambda e, oc=oc: e.dma_start(out=hspill_d[oc, 0], in_=sc.bfA[:, 0:t]), dma=True)
        ctx.s("gpsimd", lambda e, oc=oc: e.dma_start(out=hspill_d[oc, 1], in_=sc.bfB[:, 0:t]), dma=True)

    emit_proj(ctx, k, w1_d, xah, xal, NCH, NFF, w1_consumer, wbufs=wbufs)
    ctx.fence()
    ctx.sbuf_free("xal", "xah")

    f2r = ffn2_d.rearrange("(c p) t -> c p t", p=128)

    def w2_consumer(oc, ps):
        b2 = vecs[:, NCH + oc:NCH + oc + 1]
        ctx.s("scalar", lambda e, b2=b2: e.activation(
            sc.x32[:], ps[:, 0:t], AF.Identity, bias=b2, scale=1.0))
        ctx.s("gpsimd", lambda e, oc=oc: e.dma_start(out=f2r[oc], in_=sc.x32[:]), dma=True)

    emit_proj_ksweep(ctx, k, sc, w2_d, hspill_d[:, 0], hspill_d[:, 1], NFF, NCH,
                     w2_consumer, psW, wcol32, wcolh, wcoll, rbufh, rbufl)
    ctx.fence()

    x8 = ctx.sbuf("x8", [128, NCH * t], BF16)
    x1h = ctx.sbuf("x1h", [128, NCH * t], BF16)
    x1l = ctx.sbuf("x1l", [128, NCH * t], BF16)
    for c in range(NCH):
        ctx.s("sync", lambda e, c=c: e.dma_start(out=sc.x32[:], in_=ar[c]), dma=True)
        ctx.s("sync", lambda e, c=c: e.dma_start(out=sc.y32[:], in_=f2r[c]), dma=True)
        ctx.s("vector", lambda e: e.tensor_tensor(sc.x32[:], sc.x32[:], sc.y32[:], ALU.add))
        ctx.s("vector", lambda e, c=c: e.tensor_copy(x1h[:, c * t:(c + 1) * t], sc.x32[:]))
        ctx.s("vector", lambda e, c=c: e.tensor_tensor(
            x1l[:, c * t:(c + 1) * t], sc.x32[:], x1h[:, c * t:(c + 1) * t], ALU.subtract))
        ctx.fence()
    emit_ln(ctx, k, sc, x1h, x1l, x1h, x1l, psW[0], psW[1])
    ctx.fence()

    def l1qkv_consumer(oc, ps):
        if oc < NCH:
            nmh, nml, scale, o2 = "q1h", "q1l", RSQ_HD, oc
        elif oc < 2 * NCH:
            nmh, nml, scale, o2 = "k1h", "k1l", 1.0, oc - NCH
        else:
            nmh, nml, scale, o2 = "v1h", "v1l", 1.0, oc - 2 * NCH
        hr = q1[nmh].rearrange("(c p) t -> c p t", p=128)
        lr = q1[nml].rearrange("(c p) t -> c p t", p=128)
        if scale != 1.0:
            ctx.s("scalar", lambda e, scale=scale: e.activation(
                sc.x32[:], ps[:, 0:t], AF.Identity, bias=0.0, scale=scale))
            srcap = sc.x32[:]
        else:
            srcap = ps[:, 0:t]
        ctx.s("vector", lambda e, srcap=srcap: e.tensor_copy(sc.bfA[:, 0:t], srcap))
        ctx.s("vector", lambda e, srcap=srcap: e.tensor_tensor(
            sc.bfB[:, 0:t], srcap, sc.bfA[:, 0:t], ALU.subtract))
        ctx.s("gpsimd", lambda e, o2=o2, hr=hr: e.dma_start(out=hr[o2], in_=sc.bfA[:, 0:t]), dma=True)
        ctx.s("gpsimd", lambda e, o2=o2, lr=lr: e.dma_start(out=lr[o2], in_=sc.bfB[:, 0:t]), dma=True)

    if not vtm:
        emit_proj(ctx, k, wqkv_d, x1h, x1l, NCH, 3 * NCH, l1qkv_consumer, wbufs=wbufs)
    else:
        emit_proj(ctx, k, wqkv_d, x1h, x1l, NCH, 3 * NCH, l1qkv_consumer,
                  wbufs=wbufs, oc_list=range(2 * NCH))
        emit_vproj_tm(ctx, k, sc, wqkv_d, 2 * NCH, x1h, x1l, NCH, NCH,
                      dr["v1th_loc"], dr["v1tl_loc"],
                      [psW[0], psW[1], psW[2], psW[3]], wcol32, wcolh, wcoll)
    ctx.fence()

    ctx.s("sync", lambda e: e.dma_start(out=sclsb[0:1, 0:1], in_=ws_d), dma=True)
    ctx.s("sync", lambda e: e.dma_start(out=isbf[:], in_=is_d), dma=True)
    ctx.fence()
    ctx.s("vector", lambda e: e.tensor_copy(sclsb[0:1, 1:2], isbf[:]))
    ctx.s("vector", lambda e: e.tensor_tensor(
        sclsb[0:1, 2:3], sclsb[0:1, 0:1], sclsb[0:1, 1:2], ALU.mult))
    ctx.fence()
    ctx.s("tensor", lambda e: e.matmul(psW[1][:, 0:1], k.ones_row_f32, sclsb[0:1, 2:3], start=True, stop=True))
    ctx.s("vector", lambda e: e.tensor_copy(kcell[:], psW[1][:, 0:1]))
    ctx.fence()
    emit_fp8_quant(ctx, k, sc, x1h, x1l, x8, NCH, 1.0 / 0.75)
    ctx.fence()
    ctx.sbuf_free("x1l", "x1h")

    f1r = ffn1_d.rearrange("(c p) t -> c p t", p=128)
    w8r = w8_d.rearrange("(kc p) o -> kc p o", p=128)
    for oc in range(NCH):
        src = w8r[:, :, oc * 128:(oc + 1) * 128].transpose([1, 0, 2])
        ctx.s("gpsimd", lambda e, src=src: e.dma_start(
            out=wcolh[:, 0:NCH * 128].rearrange("p (kc f) -> p kc f", f=128), in_=src), dma=True)
        ctx.fence()
        for kc in range(NCH):
            for off, w in halves(t):
                st, sp = kc == 0, kc == NCH - 1
                ctx.s("tensor", lambda e, kc=kc, off=off, w=w, st=st, sp=sp: e.matmul(
                    psW[0][:, off:off + w], wcolh[:, kc * 128:(kc + 1) * 128],
                    x8[:, kc * t + off:kc * t + off + w], start=st, stop=sp))
        ctx.fence()
        ctx.s("vector", lambda e: e.tensor_scalar(
            sc.x32[:], psW[0][:, 0:t], kcell[:, 0:1], None, ALU.mult))
        ctx.s("gpsimd", lambda e, oc=oc: e.dma_start(out=f1r[oc], in_=sc.x32[:]), dma=True)
        ctx.fence()
    ctx.sbuf_free("x8")




def build_t1():
    ctx = Ctx()
    ctx.init_serial()
    t = TOK
    dr = {}
    dr["x"] = ctx.dram_in("x", [D, t])
    c_d = ctx.dram_in("consts", [128, 400])
    dr["xnh"] = ctx.dram_in("xnh", [D, t], BF16)
    dr["xnl"] = ctx.dram_in("xnl", [D, t], BF16)
    dr["fout"] = ctx.dram_in("fout", [D, t])
    dr["ath"] = ctx.dram_in("ath", [NCH, 128, t], BF16)
    dr["atl"] = ctx.dram_in("atl", [NCH, 128, t], BF16)
    dr["wo"] = ctx.dram_in("wo", [D, D])
    dr["w1"] = ctx.dram_in("w1", [D, DFF])
    dr["w2"] = ctx.dram_in("w2", [DFF, D])
    dr["wqkv_t"] = ctx.dram_in("wqkv_t", [D, 3 * D])
    dr["w8_t"] = ctx.dram_in("w8_t", [D, D], mybir.dt.float8e4)
    dr["wscale"] = ctx.dram_in("wscale", [1, 1])
    dr["iscale"] = ctx.dram_in("iscale", [1, 1], BF16)
    vec_d = ctx.dram_in("vecs", [(NCH * 4 + NFF) * 128])
    dr["accp"] = ctx.dram_out("accp", [D, t])
    dr["ffn2"] = ctx.dram_out("ffn2", [D, t])
    dr["ffn1"] = ctx.dram_out("ffn1", [D, t])
    for nm in ("q1h", "q1l", "k1h", "k1l", "v1h", "v1l"):
        dr[nm] = ctx.dram_out(nm, [D, t], BF16)
    dr["hspill"] = ctx.dram_tmp("hspill", [NFF, 2, 128, t], BF16)
    dr["xasph"] = ctx.dram_tmp("xasph", [NCH, 128, t], BF16)
    dr["xaspl"] = ctx.dram_tmp("xaspl", [NCH, 128, t], BF16)
    k = Consts(ctx)
    sc = Scratch(ctx)
    psW = [ctx.psum(f"psw{i}", [128, 1024], F32) for i in range(4)]
    vecs = ctx.sbuf("vecs_sb", [128, NCH * 4 + NFF], F32)
    wcol32 = ctx.sbuf("wcol32", [128, NCH * 128], F32)
    wcolh = ctx.sbuf("wcolh", [128, NCH * 128], BF16)
    wcoll = ctx.sbuf("wcoll", [128, NCH * 128], BF16)
    rbufh = ctx.sbuf("rbufh", [128, t], BF16)
    rbufl = ctx.sbuf("rbufl", [128, t], BF16)
    sclsb = ctx.sbuf("sclsb", [1, 4], F32)
    isbf = ctx.sbuf("isbf", [1, 1], BF16)
    kcell = ctx.sbuf("kcell", [128, 1], F32)
    k.load(ctx, c_d[:])
    emit_load_vec(ctx, vec_d, vecs, NCH * 4 + NFF)
    ctx.fence()
    emit_t1(ctx, dr, k, sc, psW, vecs, wcol32, wcolh, wcoll, rbufh, rbufl,
            sclsb, isbf, kcell)
    ctx.emit()
    return ctx



def emit_t2(ctx, dr, k, sc, psW, wcol32, wcolh, wcoll, rbufh, rbufl):
    t = TOK
    accp_d = dr["accp"]; ffn2_d = dr["ffn2"]; ffn1_d = dr["ffn1"]
    ath_d = dr["ath1"]; atl_d = dr["atl1"]; wo_d = dr["wo1"]; out_d = dr["x2"]
    acc = ctx.sbuf("t2acc", [128, NCH * t], F32)
    for drm in (accp_d, ffn2_d, ffn1_d):
        r = drm.rearrange("(c p) t -> c p t", p=128)
        for c in range(NCH):
            if drm is accp_d:
                ctx.s("sync", lambda e, c=c, r=r: e.dma_start(out=acc[:, c * t:(c + 1) * t], in_=r[c]), dma=True)
            else:
                ctx.s("sync", lambda e, c=c, r=r: e.dma_start(out=sc.x32[:], in_=r[c]), dma=True)
                ctx.s("vector", lambda e, c=c: e.tensor_tensor(
                    acc[:, c * t:(c + 1) * t], acc[:, c * t:(c + 1) * t], sc.x32[:], ALU.add))
            ctx.fence()
    outr = out_d.rearrange("(c p) t -> c p t", p=128)

    def consumer(oc, ps):
        ctx.s("vector", lambda e, oc=oc: e.tensor_tensor(
            acc[:, oc * t:(oc + 1) * t], acc[:, oc * t:(oc + 1) * t], ps[:, 0:t], ALU.add))
        ctx.s("gpsimd", lambda e, oc=oc: e.dma_start(
            out=outr[oc], in_=acc[:, oc * t:(oc + 1) * t]), dma=True)

    emit_proj_ksweep(ctx, k, sc, wo_d, ath_d, atl_d, NCH, NCH, consumer,
                     psW, wcol32, wcolh, wcoll, rbufh, rbufl)
    ctx.fence()
    ctx.sbuf_free("t2acc")


def build_t2():
    ctx = Ctx()
    ctx.init_serial()
    t = TOK
    dr = {}
    dr["accp"] = ctx.dram_in("accp", [D, t])
    dr["ffn2"] = ctx.dram_in("ffn2", [D, t])
    dr["ffn1"] = ctx.dram_in("ffn1", [D, t])
    c_d = ctx.dram_in("consts", [128, 400])
    dr["ath1"] = ctx.dram_in("ath", [NCH, 128, t], BF16)
    dr["atl1"] = ctx.dram_in("atl", [NCH, 128, t], BF16)
    dr["wo1"] = ctx.dram_in("wo1", [D, D])
    dr["x2"] = ctx.dram_out("x2", [D, t])
    k = Consts(ctx)
    sc = Scratch(ctx)
    psW = [ctx.psum(f"psw{i}", [128, 1024], F32) for i in range(4)]
    wcol32 = ctx.sbuf("wcol32", [128, 512], F32)
    wcolh = ctx.sbuf("wcolh", [128, 512], BF16)
    wcoll = ctx.sbuf("wcoll", [128, 512], BF16)
    rbufh = ctx.sbuf("rbufh", [128, t], BF16)
    rbufl = ctx.sbuf("rbufl", [128, t], BF16)
    k.load(ctx, c_d[:])
    emit_t2(ctx, dr, k, sc, psW, wcol32, wcolh, wcoll, rbufh, rbufl)
    ctx.emit()
    return ctx



_CACHE = {}


def _get(builder, name):
    if name not in _CACHE:
        ctx = builder()
        ctx.emitted = True
        _CACHE[name] = ctx
    return _CACHE[name]


def _bh_list(s_owner):
    h0 = 2 * s_owner
    return [(0, h0), (0, h0 + 1), (1, h0), (1, h0 + 1)]


def kernel(L0_attn_x, L0_attn_wq, L0_attn_bq, L0_attn_wk, L0_attn_bk,
           L0_attn_wv, L0_attn_bv, L0_attn_wo, L0_attn_bo, L0_attn_w1,
           L0_attn_b1, L0_attn_w2, L0_attn_b2, L0_attn_g1, L0_attn_be1,
           L0_attn_g2, L0_attn_be2, L0_ffn_w_gate, L0_ffn_w_up, L0_ffn_w_down,
           L1_attn_w_qkv, L1_attn_w_o, L1_ffn_weight, L1_ffn_weight_scale,
           L1_ffn_input_scale):
    import ml_dtypes
    consts = Consts.host_array()
    f32 = np.float32

    x = np.asarray(L0_attn_x, dtype=f32)
    vecs0 = np.concatenate([np.asarray(v, dtype=f32) for v in
                            (L0_attn_bq, L0_attn_bk, L0_attn_bv, L0_attn_g1, L0_attn_be1)])
    vecs1 = np.concatenate([np.asarray(v, dtype=f32) for v in
                            (L0_attn_bo, L0_attn_b2, L0_attn_g2, L0_attn_be2, L0_attn_b1)])
    wg_t = np.ascontiguousarray(np.asarray(L0_ffn_w_gate, dtype=f32).T)
    wu_t = np.ascontiguousarray(np.asarray(L0_ffn_w_up, dtype=f32).T)
    wd_t = np.ascontiguousarray(np.asarray(L0_ffn_w_down, dtype=f32).T)
    wqkv_t = np.ascontiguousarray(np.asarray(L1_attn_w_qkv, dtype=f32).T)
    wo1_t = np.ascontiguousarray(np.asarray(L1_attn_w_o, dtype=f32).T)
    w8_t = np.ascontiguousarray(
        np.asarray(L1_ffn_weight).view(np.uint8).T).view(ml_dtypes.float8_e4m3)
    wscale = np.asarray(L1_ffn_weight_scale, dtype=f32).reshape(1, 1)
    iscale = np.asarray(L1_ffn_input_scale).reshape(1, 1)

    t0 = _get(build_t0, "t0")
    in0 = []
    for c in range(NCORES):
        b, qd = divmod(c, 4)
        xs = np.ascontiguousarray(x[b, qd * TOK:(qd + 1) * TOK, :].T)
        in0.append({"x": xs, "consts": consts, "wq": np.asarray(L0_attn_wq, f32),
                    "wk": np.asarray(L0_attn_wk, f32), "wv": np.asarray(L0_attn_wv, f32),
                    "vecs": vecs0, "wg_t": wg_t, "wu_t": wu_t, "wd_t": wd_t})
    r0 = run_bass_kernel_spmd(t0.nc, in0, core_ids=list(range(NCORES)))
    o0 = r0.results

    att = _get(build_attn, "attn")
    masks0 = attn_masks_host(-1e9)

    def attn_inputs(outs, masks, names=("qh", "ql", "kh", "kl", "vh", "vl")):
        ins = []
        for s_own in range(NCORES):
            bh = _bh_list(s_own)
            d = {"consts": consts, "masks": masks}
            for nm_fm, nm_in in ((names[0], "qh"), (names[1], "ql"),
                                 (names[2], "kh"), (names[3], "kl")):
                arr = np.empty((4, 128, S), dtype=ml_dtypes.bfloat16)
                for i, (b, h) in enumerate(bh):
                    cores = [4 * b + j for j in range(4)]
                    arr[i] = np.concatenate(
                        [np.asarray(outs[cc][nm_fm])[h * 128:(h + 1) * 128, :] for cc in cores], axis=1)
                d[nm_in] = arr
            for nm_fm, nm_in in ((names[4], "vth"), (names[5], "vtl")):
                arr = np.empty((4, S, 128), dtype=ml_dtypes.bfloat16)
                for i, (b, h) in enumerate(bh):
                    cores = [4 * b + j for j in range(4)]
                    arr[i] = np.concatenate(
                        [np.ascontiguousarray(np.asarray(outs[cc][nm_fm])[h * 128:(h + 1) * 128, :].T)
                         for cc in cores], axis=0)
                d[nm_in] = arr
            ins.append(d)
        return ins

    rA0 = run_bass_kernel_spmd(att.nc, attn_inputs(o0, masks0), core_ids=list(range(NCORES)))
    oA0 = rA0.results

    def attn_out_relayout(oa):
        hs, ls = [], []
        for c in range(NCORES):
            b, qd = divmod(c, 4)
            q0 = qd * TOK
            ah = np.empty((NCH, 128, TOK), dtype=ml_dtypes.bfloat16)
            al = np.empty((NCH, 128, TOK), dtype=ml_dtypes.bfloat16)
            for h in range(16):
                owner = h // 2
                i = 2 * b + (h % 2)
                ah[h] = np.asarray(oa[owner]["ah"])[i][:, q0:q0 + TOK]
                al[h] = np.asarray(oa[owner]["al"])[i][:, q0:q0 + TOK]
            hs.append(ah)
            ls.append(al)
        return hs, ls

    ath0, atl0 = attn_out_relayout(oA0)

    t1 = _get(build_t1, "t1")
    in1 = []
    for c in range(NCORES):
        b, qd = divmod(c, 4)
        xs = np.ascontiguousarray(x[b, qd * TOK:(qd + 1) * TOK, :].T)
        in1.append({"x": xs, "consts": consts,
                    "xnh": np.asarray(o0[c]["xnh"]), "xnl": np.asarray(o0[c]["xnl"]),
                    "fout": np.asarray(o0[c]["fout"]),
                    "ath": ath0[c], "atl": atl0[c],
                    "wo": np.asarray(L0_attn_wo, f32), "w1": np.asarray(L0_attn_w1, f32),
                    "w2": np.asarray(L0_attn_w2, f32), "wqkv_t": wqkv_t,
                    "w8_t": w8_t, "wscale": wscale, "iscale": iscale, "vecs": vecs1})
    r1 = run_bass_kernel_spmd(t1.nc, in1, core_ids=list(range(NCORES)))
    o1 = r1.results

    masks1 = attn_masks_host(float(np.finfo(np.float32).min))
    rA1 = run_bass_kernel_spmd(
        att.nc, attn_inputs(o1, masks1, names=("q1h", "q1l", "k1h", "k1l", "v1h", "v1l")),
        core_ids=list(range(NCORES)))
    oA1 = rA1.results
    ath1, atl1 = attn_out_relayout(oA1)

    t2 = _get(build_t2, "t2")
    in2 = []
    for c in range(NCORES):
        in2.append({"accp": np.asarray(o1[c]["accp"]), "ffn2": np.asarray(o1[c]["ffn2"]),
                    "ffn1": np.asarray(o1[c]["ffn1"]), "consts": consts,
                    "ath": ath1[c], "atl": atl1[c], "wo1": wo1_t})
    r2 = run_bass_kernel_spmd(t2.nc, in2, core_ids=list(range(NCORES)))
    o2 = r2.results

    out = np.empty((B, S, D), dtype=np.float32)
    for c in range(NCORES):
        b, qd = divmod(c, 4)
        out[b, qd * TOK:(qd + 1) * TOK, :] = np.asarray(o2[c]["x2"], dtype=np.float32).T
    return out


def emit_vproj_tm(ctx, k, sc, w_dram, oc0, x_hi, x_lo, n_in_ch, n_out_ch,
                  vth_d, vtl_d, psW, wbuf32, wbufh, wbufl, bias_col=None, tok=TOK):
    t = tok
    dout = n_out_ch * 128
    wr = w_dram
    for tt in range(t // 128):
        for kc in range(n_in_ch):
            ctx.s("sync", lambda e, kc=kc: e.dma_start(
                out=wbuf32[:, 0:dout], in_=wr[kc * 128:(kc + 1) * 128, oc0 * 128:oc0 * 128 + dout]), dma=True)
            ctx.s("vector", lambda e: e.tensor_copy(wbufh[:, 0:dout], wbuf32[:, 0:dout]))
            ctx.s("vector", lambda e: e.tensor_tensor(
                wbufl[:, 0:dout], wbuf32[:, 0:dout], wbufh[:, 0:dout], ALU.subtract))
            ctx.fence()
            st, sp = kc == 0, kc == n_in_ch - 1
            lh = x_hi[:, kc * t + tt * 128: kc * t + (tt + 1) * 128]
            ll = x_lo[:, kc * t + tt * 128: kc * t + (tt + 1) * 128]
            for dh in range(ceil_div(dout, 512)):
                w = min(512, dout - dh * 512)
                ctx.s("tensor", lambda e, lh=lh, dh=dh, w=w, st=st: e.matmul(
                    psW[dh % 4][:, 0:w], lh, wbufh[:, dh * 512:dh * 512 + w], start=st, stop=False))
                ctx.s("tensor", lambda e, lh=lh, dh=dh, w=w: e.matmul(
                    psW[dh % 4][:, 0:w], lh, wbufl[:, dh * 512:dh * 512 + w], start=False, stop=False))
                ctx.s("tensor", lambda e, ll=ll, dh=dh, w=w, sp=sp: e.matmul(
                    psW[dh % 4][:, 0:w], ll, wbufh[:, dh * 512:dh * 512 + w], start=False, stop=sp))
            ctx.fence()
        for dh in range(ceil_div(dout, 512)):
            w = min(512, dout - dh * 512)
            if bias_col is not None:
                ctx.s("vector", lambda e, dh=dh, w=w: e.tensor_tensor(
                    sc.x32[:, 0:w], psW[dh % 4][:, 0:w], bias_col[:, dh * 512:dh * 512 + w], ALU.add))
            else:
                ctx.s("vector", lambda e, dh=dh, w=w: e.tensor_copy(sc.x32[:, 0:w], psW[dh % 4][:, 0:w]))
            ctx.s("vector", lambda e, dh=dh, w=w: e.tensor_copy(sc.bfA[:, 0:w], sc.x32[:, 0:w]))
            ctx.s("vector", lambda e, dh=dh, w=w: e.tensor_tensor(
                sc.bfB[:, 0:w], sc.x32[:, 0:w], sc.bfA[:, 0:w], ALU.subtract))
            ctx.s("gpsimd", lambda e, tt=tt, dh=dh, w=w: e.dma_start(
                out=vth_d[tt][:, dh * 512:dh * 512 + w], in_=sc.bfA[:, 0:w]), dma=True)
            ctx.s("gpsimd", lambda e, tt=tt, dh=dh, w=w: e.dma_start(
                out=vtl_d[tt][:, dh * 512:dh * 512 + w], in_=sc.bfB[:, 0:w]), dma=True)
            ctx.fence()



BASE_PERM = [0, 1, 2, 3, 6, 7, 4, 5]


def _sigma(core, d):
    return BASE_PERM[BASE_PERM[core] ^ d]


def _rdests(d):
    return [(0, d)] * 8 if d < 4 else [None] * 4 + [(0, d)] * 4


class Exch:
    def __init__(self, ctx, xidx_sb):
        self.rsem = ctx.sem("xrsem")
        self.lsem = ctx.sem("xlsem")
        self.prep = ctx.sem("xprep")
        self.bar = ctx.sem("xbar")
        self.xidx = xidx_sb
        self.col = 0

    def idx(self):
        c = self.col
        self.col += 1
        return self.xidx[:, c:c + 1]

    def barrier(self, ctx):
        ctx.fence()
        ctx.s("gpsimd", lambda e: e.remote_sem_update_broadcast(
            self.bar.h, self.lsem.h, rdests=[(0, dd) for dd in range(8)]))
        self.prep.val += 1
        ctx.ops[-1][3].append((self.prep, 1))
        ctx.s("gpsimd", lambda e: e.trigger_dma(count=1))
        ctx.ops[-1][2].append((self.prep, self.prep.val))
        self.lsem.val += 16
        self.bar.val += 16
        ctx.s("gpsimd", lambda e: e.engine_nop())
        ctx.ops[-1][2].append((self.bar, self.bar.val))
        ctx.fence()

    def send(self, ctx, d, dst_ap, src_ap):
        ctx.fence()
        ctx.s("gpsimd", lambda e, d=d, dst_ap=dst_ap, src_ap=src_ap: e.remote_dma_broadcast(
            dst_ap, src_ap, self.rsem.h, self.lsem.h, rdests=_rdests(d)))
        self.prep.val += 1
        ctx.ops[-1][3].append((self.prep, 1))
        ctx.s("gpsimd", lambda e: e.trigger_dma(count=1))
        ctx.ops[-1][2].append((self.prep, self.prep.val))
        self.lsem.val += 16
        self.rsem.val += 16 if d < 4 else 8

    def wait_recv(self, ctx):
        ctx.s("gpsimd", lambda e: e.engine_nop())
        ctx.ops[-1][2].append((self.rsem, self.rsem.val))
        ctx.fence()

    def wait_sent(self, ctx):
        ctx.s("gpsimd", lambda e: e.engine_nop())
        ctx.ops[-1][2].append((self.lsem, self.lsem.val))
        ctx.fence()


def _ind_gather(ctx, ex, out_ap, flat_dram):
    iap = ex.idx()
    ctx.s("gpsimd", lambda e, out_ap=out_ap, flat_dram=flat_dram, iap=iap:
          e.indirect_dma_start(out_ap, None, flat_dram,
                               bass.IndirectOffsetOnAxis(iap, 0)), dma=True)


def _ind_scatter(ctx, ex, flat_dram, in_ap):
    iap = ex.idx()
    ctx.s("gpsimd", lambda e, flat_dram=flat_dram, in_ap=in_ap, iap=iap:
          e.indirect_dma_start(flat_dram, bass.IndirectOffsetOnAxis(iap, 0),
                               in_ap, None), dma=True)


def emit_exchange_e1(ctx, k, sc, ex, qkv_local, vt_local, gather, stg, slots):
    t = TOK
    qk_flat = {nm: qkv_local[nm].rearrange("(r p) t -> (r p) t", p=128)
               for nm in ("qh", "ql", "kh", "kl")}
    vt_flat = [vt.rearrange("a p (g d) -> (a p g) d", d=256) for vt in vt_local]
    g_flat = {nm: gather[nm].rearrange("b p (q t) -> (b p q) t", t=t)
              for nm in ("qh", "ql", "kh", "kl")}
    gv_flat = [gather[nm].rearrange("b (r p) d -> (b r p) d", p=128)
               for nm in ("vth", "vtl")]
    for d in range(8):
        ex.barrier(ctx)
        for j2, nm in enumerate(("qh", "ql", "kh", "kl")):
            for j in range(2):
                _ind_gather(ctx, ex, stg[:, (2 * j2 + j) * t:(2 * j2 + j + 1) * t], qk_flat[nm])
        for hl in range(2):
            for tt in range(8):
                _ind_gather(ctx, ex, stg[:, 8 * t + hl * 2048 + tt * 256:
                                         8 * t + hl * 2048 + (tt + 1) * 256], vt_flat[hl])
        ctx.fence()
        par = d % 2
        ex.send(ctx, d, slots[:, par * 12288:(par + 1) * 12288], stg[:, 0:12288])
        ex.wait_recv(ctx)
        for j2, nm in enumerate(("qh", "ql", "kh", "kl")):
            for j in range(2):
                _ind_scatter(ctx, ex, g_flat[nm],
                             slots[:, par * 12288 + (2 * j2 + j) * t:
                                   par * 12288 + (2 * j2 + j + 1) * t])
        for hl in range(2):
            for j in range(2):
                for tt in range(8):
                    c0 = par * 12288 + 8 * t + hl * 2048 + tt * 256 + j * 128
                    _ind_scatter(ctx, ex, gv_flat[hl], slots[:, c0:c0 + 128])
        ctx.fence()
        ex.wait_sent(ctx)


def emit_exchange_e2(ctx, k, sc, ex, a_local, gather, stg, slots):
    t = TOK
    a_flat = [a.rearrange("b p (q t) -> (b p q) t", t=t) for a in a_local]
    g_flat = [g.rearrange("c p t -> (c p) t") for g in gather]
    for d in range(8):
        ex.barrier(ctx)
        for hl in range(2):
            for j in range(2):
                _ind_gather(ctx, ex, stg[:, (2 * hl + j) * t:(2 * hl + j + 1) * t], a_flat[hl])
        ctx.fence()
        par = d % 2
        ex.send(ctx, d, slots[:, par * 4096:(par + 1) * 4096], stg[:, 0:4096])
        ex.wait_recv(ctx)
        for hl in range(2):
            for j in range(2):
                _ind_scatter(ctx, ex, g_flat[hl],
                             slots[:, par * 4096 + (2 * hl + j) * t:
                                   par * 4096 + (2 * hl + j + 1) * t])
        ctx.fence()
        ex.wait_sent(ctx)


def build_xidx_host(core):
    cols = []
    part = np.arange(128, dtype=np.uint32)

    def e1():
        for d in range(8):
            p = _sigma(core, d)
            for j2 in range(4):
                for j in range(2):
                    cols.append((2 * p + j) * 128 + part)
            for hl in range(2):
                for tt in range(8):
                    cols.append((tt * 128 + part) * 8 + p)
            bp, tq = divmod(p, 4) if False else (p // 4, p % 4)
            for j2 in range(4):
                for j in range(2):
                    cols.append((2 * bp + j) * 512 + part * 4 + tq)
            for hl in range(2):
                for j in range(2):
                    for tt in range(8):
                        cols.append(np.full(128, (2 * bp + j) * 4096 + tq * 1024 + tt * 128, np.uint32) + part)

    def e2():
        for d in range(8):
            p = _sigma(core, d)
            bp, tq = p // 4, p % 4
            for hl in range(2):
                for j in range(2):
                    cols.append((2 * bp + j) * 512 + part * 4 + tq)
            for hl in range(2):
                for j in range(2):
                    cols.append((2 * p + j) * 128 + part)

    e1(); e2(); e1(); e2()
    return np.stack(cols).astype(np.uint32)


N_XIDX = 4 * 8 * (8 + 16) // 1 if False else None



def build_fused(upto=99, vtm=True, dbg_gathers=False):
    from concourse import library_config
    ctx = Ctx()
    ctx.init_serial()
    t = TOK
    dr = {}
    dr["x"] = ctx.dram_in("x", [D, t])
    c_d = ctx.dram_in("consts", [128, 400])
    m_d = ctx.dram_in("masks2", [8, 128, QT])
    pt_d = ctx.dram_in("xidx", [128, 1152], mybir.dt.uint32)
    for nm, sh in (("wq", [D, D]), ("wk", [D, D]), ("wv", [D, D]),
                   ("wo", [D, D]), ("w1", [D, DFF]), ("w2", [DFF, D]),
                   ("wg_t", [D, DFF]), ("wu_t", [D, DFF]), ("wd_t", [DFF, D]),
                   ("wqkv_t", [D, 3 * D]), ("wo1", [D, D])):
        dr[nm] = ctx.dram_in(nm, sh)
    dr["w8_t"] = ctx.dram_in("w8_t", [D, D], mybir.dt.float8e4)
    dr["wscale"] = ctx.dram_in("wscale", [1, 1])
    dr["iscale"] = ctx.dram_in("iscale", [1, 1], BF16)
    v0_d = ctx.dram_in("vecs0", [5 * D])
    v1_d = ctx.dram_in("vecs1", [(NCH * 4 + NFF) * 128])
    for nm in ("xnh", "xnl", "qh", "ql", "kh", "kl"):
        dr[nm] = ctx.dram_tmp(nm, [D, t], BF16)
    dr["vth_loc"] = ctx.dram_tmp("vth_loc", [8, 128, D], BF16)
    dr["vh"] = ctx.dram_tmp("vh", [D, TOK], BF16)
    dr["vl"] = ctx.dram_tmp("vl", [D, TOK], BF16)
    dr["vtl_loc"] = ctx.dram_tmp("vtl_loc", [8, 128, D], BF16)
    dr["fout"] = ctx.dram_tmp("fout", [D, t])
    dr["ispill"] = ctx.dram_tmp("ispill", [NFF, 2, 128, t], BF16)
    dr["zqspill"] = ctx.dram_tmp("zqspill", [NFF, 128, t], BF16)
    mk = ctx.dram_out if dbg_gathers else ctx.dram_tmp
    for nm in ("g_qh", "g_ql", "g_kh", "g_kl", "ah", "al",
               "g1_qh", "g1_ql", "g1_kh", "g1_kl", "ah1", "al1"):
        dr[nm] = mk(nm, [4, 128, S], BF16)
    for nm in ("g_vth", "g_vtl", "g1_vth", "g1_vtl"):
        dr[nm] = mk(nm, [4, S, 128], BF16)
    for nm in ("ath", "atl", "ath1", "atl1"):
        dr[nm] = ctx.dram_tmp(nm, [NCH, 128, t], BF16)
    dr["accp"] = ctx.dram_tmp("accp", [D, t])
    dr["ffn2"] = ctx.dram_tmp("ffn2", [D, t])
    dr["ffn1"] = ctx.dram_tmp("ffn1", [D, t])
    for nm in ("q1h", "q1l", "k1h", "k1l"):
        dr[nm] = ctx.dram_tmp(nm, [D, t], BF16)
    dr["q1vh"] = dr["q1h"]
    dr["v1h"] = ctx.dram_tmp("v1h", [D, t], BF16)
    dr["v1l"] = ctx.dram_tmp("v1l", [D, t], BF16)
    dr["v1th_loc"] = ctx.dram_tmp("v1th_loc", [8, 128, D], BF16)
    dr["v1tl_loc"] = ctx.dram_tmp("v1tl_loc", [8, 128, D], BF16)
    dr["hspill"] = ctx.dram_tmp("hspill", [NFF, 2, 128, t], BF16)
    dr["xasph"] = ctx.dram_tmp("xasph", [NCH, 128, t], BF16)
    dr["xaspl"] = ctx.dram_tmp("xaspl", [NCH, 128, t], BF16)
    dr["x2"] = ctx.dram_out("x2", [D, t])

    k = Consts(ctx)
    sc = Scratch(ctx)
    psW = [ctx.psum(f"psw{i}", [128, 1024], F32) for i in range(4)]
    psA, psB, psC = psW[0], psW[1], psW[2]
    psD = psW[3][:, 0:512]
    psE = psW[3][:, 512:1024]

    vecs0 = ctx.sbuf("vecs0_sb", [128, 5 * NCH], F32)
    vecs1 = ctx.sbuf("vecs1_sb", [128, NCH * 4 + NFF], F32)
    cellcols = ctx.sbuf("cellcols", [128, 8], F32)
    maskssb = ctx.sbuf("maskssb", [128, 8 * QT], F32)
    ptab = ctx.sbuf("xidx_sb", [128, 1152], mybir.dt.uint32)
    rbufh = ctx.sbuf("rbufh", [128, t], BF16)
    rbufl = ctx.sbuf("rbufl", [128, t], BF16)
    sclsb = ctx.sbuf("sclsb", [1, 4], F32)
    isbf = ctx.sbuf("isbf", [1, 1], BF16)
    kcell = ctx.sbuf("kcell", [128, 1], F32)

    ex = Exch(ctx, ptab)

    ctx.s("gpsimd", lambda e: (e.load_library(library_config.proxy), e.engine_nop())[1])
    k.load(ctx, c_d[:])
    emit_load_vec(ctx, v0_d, vecs0, 5 * NCH)
    emit_load_vec(ctx, v1_d, vecs1, NCH * 4 + NFF)
    ctx.s("sync", lambda e: e.dma_start(
        out=maskssb[:].rearrange("p (i f) -> p i f", f=QT), in_=m_d.transpose([1, 0, 2])), dma=True)
    ctx.s("sync", lambda e: e.dma_start(out=ptab[:], in_=pt_d), dma=True)
    ctx.fence()

    if upto <= -1:
        return _finish_fused(ctx, dr)
    wcol32 = ctx.sbuf("wcol32", [128, NCH * 128], F32)
    wcolh = ctx.sbuf("wcolh", [128, NCH * 128], BF16)
    wcoll = ctx.sbuf("wcoll", [128, NCH * 128], BF16)
    emit_t0(ctx, dr, k, sc, psA, psB, psC, psD, psE, vecs0, cellcols,
            wcol32, wcolh, wcoll, vtm=vtm)
    ctx.sbuf_free("wcoll", "wcolh", "wcol32")
    if upto <= 0:
        return _finish_fused(ctx, dr)

    stg = ctx.sbuf("xstg", [128, 12288], BF16)
    slots = ctx.sbuf("xslots", [128, 2 * 12288], BF16)
    qkv_local = {nm: dr[nm] for nm in ("qh", "ql", "kh", "kl")}
    gather0 = {"qh": dr["g_qh"], "ql": dr["g_ql"], "kh": dr["g_kh"],
               "kl": dr["g_kl"], "vth": dr["g_vth"], "vtl": dr["g_vtl"]}
    emit_exchange_e1(ctx, k, sc, ex, qkv_local, (dr["vth_loc"], dr["vtl_loc"]),
                     gather0, stg, slots)
    ctx.sbuf_free("xslots", "xstg")
    if upto <= 1:
        return _finish_fused(ctx, dr)

    adr = dict(qh=dr["g_qh"], ql=dr["g_ql"], kh=dr["g_kh"], kl=dr["g_kl"],
               vth=dr["g_vth"], vtl=dr["g_vtl"], ah=dr["ah"], al=dr["al"])
    emit_attn(ctx, adr, k, sc, psD, psE, psA[:, 0:512], psB[:, 0:512], maskssb[:, 0:4 * QT])
    if upto <= 2:
        return _finish_fused(ctx, dr)

    stg = ctx.sbuf("xstg2", [128, 4096], BF16)
    slots = ctx.sbuf("xslots2", [128, 8192], BF16)
    emit_exchange_e2(ctx, k, sc, ex, (dr["ah"], dr["al"]), (dr["ath"], dr["atl"]), stg, slots)
    ctx.sbuf_free("xslots2", "xstg2")
    if upto <= 3:
        return _finish_fused(ctx, dr)

    wcol32 = ctx.sbuf("wcol32b", [128, NCH * 128], F32)
    wcolh = ctx.sbuf("wcolhb", [128, NCH * 128], BF16)
    wcoll = ctx.sbuf("wcollb", [128, NCH * 128], BF16)
    emit_t1(ctx, dr, k, sc, psW, vecs1, wcol32, wcolh, wcoll, rbufh, rbufl,
            sclsb, isbf, kcell, vtm=True)
    ctx.sbuf_free("wcollb", "wcolhb", "wcol32b")
    if upto <= 4:
        return _finish_fused(ctx, dr)

    stg = ctx.sbuf("xstg3", [128, 12288], BF16)
    slots = ctx.sbuf("xslots3", [128, 2 * 12288], BF16)
    qkv_local1 = {"qh": dr["q1h"], "ql": dr["q1l"], "kh": dr["k1h"], "kl": dr["k1l"]}
    gather1 = {"qh": dr["g1_qh"], "ql": dr["g1_ql"], "kh": dr["g1_kh"],
               "kl": dr["g1_kl"], "vth": dr["g1_vth"], "vtl": dr["g1_vtl"]}
    emit_exchange_e1(ctx, k, sc, ex, qkv_local1, (dr["v1th_loc"], dr["v1tl_loc"]),
                     gather1, stg, slots)
    ctx.sbuf_free("xslots3", "xstg3")
    if upto <= 5:
        return _finish_fused(ctx, dr)

    adr1 = dict(qh=dr["g1_qh"], ql=dr["g1_ql"], kh=dr["g1_kh"], kl=dr["g1_kl"],
                vth=dr["g1_vth"], vtl=dr["g1_vtl"], ah=dr["ah1"], al=dr["al1"])
    emit_attn(ctx, adr1, k, sc, psD, psE, psA[:, 0:512], psB[:, 0:512], maskssb[:, 4 * QT:8 * QT])
    if upto <= 6:
        return _finish_fused(ctx, dr)

    stg = ctx.sbuf("xstg4", [128, 4096], BF16)
    slots = ctx.sbuf("xslots4", [128, 8192], BF16)
    emit_exchange_e2(ctx, k, sc, ex, (dr["ah1"], dr["al1"]), (dr["ath1"], dr["atl1"]), stg, slots)
    ctx.sbuf_free("xslots4", "xstg4")
    if upto <= 7:
        return _finish_fused(ctx, dr)

    wcol32 = ctx.sbuf("wcol32c", [128, 512], F32)
    wcolh = ctx.sbuf("wcolhc", [128, 512], BF16)
    wcoll = ctx.sbuf("wcollc", [128, 512], BF16)
    emit_t2(ctx, dr, k, sc, psW, wcol32, wcolh, wcoll, rbufh, rbufl)
    ctx.sbuf_free("wcollc", "wcolhc", "wcol32c")
    ctx.emit()
    mybir.codegen_inst_isa_subclasses(ctx.nc)
    return ctx


def _finish_fused(ctx, dr):
    t = TOK
    sc0 = ctx.sbuf("fin_z", [128, t], F32)
    outr = dr["x2"].rearrange("(c p) t -> c p t", p=128)
    ctx.s("vector", lambda e: e.memset(sc0[:], 0.0))
    for c in range(NCH):
        ctx.s("gpsimd", lambda e, c=c: e.dma_start(out=outr[c], in_=sc0[:]), dma=True)
        ctx.fence()
    ctx.emit()
    mybir.codegen_inst_isa_subclasses(ctx.nc)
    return ctx

USE_FUSED = True


def kernel_fused(L0_attn_x, L0_attn_wq, L0_attn_bq, L0_attn_wk, L0_attn_bk,
                 L0_attn_wv, L0_attn_bv, L0_attn_wo, L0_attn_bo, L0_attn_w1,
                 L0_attn_b1, L0_attn_w2, L0_attn_b2, L0_attn_g1, L0_attn_be1,
                 L0_attn_g2, L0_attn_be2, L0_ffn_w_gate, L0_ffn_w_up,
                 L0_ffn_w_down, L1_attn_w_qkv, L1_attn_w_o, L1_ffn_weight,
                 L1_ffn_weight_scale, L1_ffn_input_scale):
    import ml_dtypes
    consts = Consts.host_array()
    f32 = np.float32
    x = np.asarray(L0_attn_x, dtype=f32)
    vecs0 = np.concatenate([np.asarray(v, dtype=f32) for v in
                            (L0_attn_bq, L0_attn_bk, L0_attn_bv, L0_attn_g1, L0_attn_be1)])
    vecs1 = np.concatenate([np.asarray(v, dtype=f32) for v in
                            (L0_attn_bo, L0_attn_b2, L0_attn_g2, L0_attn_be2, L0_attn_b1)])
    masks2 = np.concatenate([attn_masks_host(-1e9),
                             attn_masks_host(float(np.finfo(np.float32).min))], axis=0)
    wg_t = np.ascontiguousarray(np.asarray(L0_ffn_w_gate, dtype=f32).T)
    wu_t = np.ascontiguousarray(np.asarray(L0_ffn_w_up, dtype=f32).T)
    wd_t = np.ascontiguousarray(np.asarray(L0_ffn_w_down, dtype=f32).T)
    wqkv_t = np.ascontiguousarray(np.asarray(L1_attn_w_qkv, dtype=f32).T)
    wo1_t = np.ascontiguousarray(np.asarray(L1_attn_w_o, dtype=f32).T)
    w8_t = np.ascontiguousarray(
        np.asarray(L1_ffn_weight).view(np.uint8).T).view(ml_dtypes.float8_e4m3)
    wscale = np.asarray(L1_ffn_weight_scale, dtype=f32).reshape(1, 1)
    iscale = np.asarray(L1_ffn_input_scale).reshape(1, 1)

    fz = _get(build_fused, "fused")
    ins = []
    for c in range(NCORES):
        b, qd = divmod(c, 4)
        xs = np.ascontiguousarray(x[b, qd * TOK:(qd + 1) * TOK, :].T)
        ins.append({"x": xs, "consts": consts, "masks2": masks2,
                    "xidx": np.ascontiguousarray(build_xidx_host(c).T),
                    "wq": np.asarray(L0_attn_wq, f32), "wk": np.asarray(L0_attn_wk, f32),
                    "wv": np.asarray(L0_attn_wv, f32), "wo": np.asarray(L0_attn_wo, f32),
                    "w1": np.asarray(L0_attn_w1, f32), "w2": np.asarray(L0_attn_w2, f32),
                    "wg_t": wg_t, "wu_t": wu_t, "wd_t": wd_t,
                    "wqkv_t": wqkv_t, "wo1": wo1_t, "w8_t": w8_t,
                    "wscale": wscale, "iscale": iscale,
                    "vecs0": vecs0, "vecs1": vecs1})
    res = run_bass_kernel_spmd(fz.nc, ins, core_ids=list(range(NCORES)))
    out = np.empty((B, S, D), dtype=np.float32)
    for c in range(NCORES):
        b, qd = divmod(c, 4)
        out[b, qd * TOK:(qd + 1) * TOK, :] = np.asarray(res.results[c]["x2"], dtype=np.float32).T
    return out


_kernel_multilaunch = kernel


def kernel(**inputs):
    if USE_FUSED:
        return kernel_fused(**inputs)
    return _kernel_multilaunch(**inputs)
